# version 03
# baseline (speedup 1.0000x reference)
import contextlib
import numpy as np
import ml_dtypes
import concourse.bass as bass
import concourse.mybir as mybir
from concourse.bass_utils import run_bass_kernel_spmd

F32 = mybir.dt.float32
BF16 = mybir.dt.bfloat16
AF = mybir.ActivationFunctionType
ALU = mybir.AluOpType
AX = mybir.AxisListType
BF = ml_dtypes.bfloat16

NCORES = 8
S = 2048
D = 1024
BPC = 2
TOK = BPC * S
NE = 66
IPAD = 768
SEC = 8 * IPAD
BLOBW = 3 * SEC
CH = 512
NCH = TOK // CH
NGRP = 4
GTOK = TOK // NGRP
GCH = GTOK // CH
KB = 1024
DYN_BYTES = 200 * KB

SPLIT_MODE = False
DEBUG_EXPORT = False

ENGS = ("sync", "gpsimd", "tensor", "vector", "scalar")


class Buf:
    __slots__ = ("w", "r", "small")

    def __init__(self, small=False):
        self.w = None
        self.r = []
        self.small = small


class Prog:
    def __init__(self):
        self.items = {e: [] for e in ENGS}
        self.cnt = {e: 0 for e in ENGS}
        self.pend = {e: [] for e in ENGS}

    def barrier(self, skip=()):
        toks = [(e, self.cnt[e]) for e in ENGS if self.cnt[e] > 0]
        for e in ENGS:
            if e in skip:
                continue
            self.pend[e] = self.pend[e] + [(tk, True) for tk in toks]

    def op(self, e, fn, reads=(), writes=(), extra=(), inc=1):
        deps = []
        for b in reads:
            if b.w is not None:
                deps.append((b.w, b.small))
        for b in writes:
            for tk in b.r:
                deps.append((tk, b.small))
            if b.w is not None:
                deps.append((b.w, b.small))
        for tk in extra:
            deps.append((tk, True))
        deps += self.pend[e]
        self.pend[e] = []
        self.cnt[e] += inc
        tok = (e, self.cnt[e])
        fd = {}
        for (pe, v), force in deps:
            if pe == e and (not force or v >= tok[1]):
                continue
            if v > fd.get(pe, 0):
                fd[pe] = v
        self.items[e].append((fd, fn, inc))
        for b in reads:
            b.r.append(tok)
        for b in writes:
            b.r = []
            b.w = tok
        return tok


def _tile_kxm(w):
    K, M = w.shape
    return np.ascontiguousarray(
        w.reshape(K // 128, 128, M).transpose(1, 0, 2).reshape(128, (K // 128) * M))


def host_prep(z):
    h = {}
    x = np.asarray(z["L0_attn_x"])
    h["x_fm"] = np.ascontiguousarray(x.transpose(0, 2, 1))

    for nm, key in [("wqT", "L0_attn_Wq"), ("wkT", "L0_attn_Wk"), ("wvT", "L0_attn_Wv"),
                    ("gwT", "L0_ffn_gate_w")]:
        h[nm] = _tile_kxm(np.ascontiguousarray(np.asarray(z[key]).T))
    h["gbias"] = np.ascontiguousarray(
        np.tile(np.asarray(z["L0_ffn_gate_bias"])[None, :], (128, 1)).astype(np.float32))

    blob = np.zeros((NE, 128, BLOBW), dtype=np.float32)
    Wg, Wu, Wd = np.asarray(z["L0_ffn_Wg"]), np.asarray(z["L0_ffn_Wu"]), np.asarray(z["L0_ffn_Wd"])
    gt = Wg.transpose(0, 2, 1).reshape(64, 8, 128, 704).transpose(0, 2, 1, 3)
    blob[:64, :, :SEC].reshape(64, 128, 8, IPAD)[:, :, :, :704] = gt
    ut = Wu.transpose(0, 2, 1).reshape(64, 8, 128, 704).transpose(0, 2, 1, 3)
    blob[:64, :, SEC:2 * SEC].reshape(64, 128, 8, IPAD)[:, :, :, :704] = ut
    dt_ = np.zeros((64, IPAD, 1024), dtype=np.float32)
    dt_[:, :704, :] = Wd.transpose(0, 2, 1)
    blob[:64, :, 2 * SEC:] = dt_.reshape(64, 6, 128, 1024).transpose(0, 2, 1, 3).reshape(64, 128, SEC)
    Sg, Su, Sd = np.asarray(z["L0_ffn_Sg"]), np.asarray(z["L0_ffn_Su"]), np.asarray(z["L0_ffn_Sd"])
    for hh in range(2):
        e = 64 + hh
        sg = Sg[hh * 704:(hh + 1) * 704].T
        su = Su[hh * 704:(hh + 1) * 704].T
        sd = Sd[:, hh * 704:(hh + 1) * 704].T
        blob[e, :, :SEC].reshape(128, 8, IPAD)[:, :, :704] = sg.reshape(8, 128, 704).transpose(1, 0, 2)
        blob[e, :, SEC:2 * SEC].reshape(128, 8, IPAD)[:, :, :704] = su.reshape(8, 128, 704).transpose(1, 0, 2)
        sdp = np.zeros((IPAD, 1024), dtype=np.float32)
        sdp[:704] = sd
        blob[e, :, 2 * SEC:] = sdp.reshape(6, 128, 1024).transpose(1, 0, 2).reshape(128, SEC)
    h["moeblob"] = blob

    basis = np.zeros((64, 64 * 64), dtype=BF)
    for e in range(64):
        basis[e, e * 64:(e + 1) * 64] = 1
    h["basis64"] = basis
    i2 = np.zeros((64, 128), dtype=BF)
    for k in range(64):
        i2[k, k] = 1
        i2[k, 64 + k] = 1
    h["i2t"] = i2
    h["ident"] = np.eye(128, dtype=np.float32)

    q = np.asarray(z["L1_attn_queries"])
    h["qwT"] = np.ascontiguousarray(
        q.reshape(4, 256, 4, 2, 128).transpose(4, 0, 2, 3, 1).reshape(128, 8192))
    cb = np.asarray(z["L1_attn_combiners"])
    h["combT"] = np.ascontiguousarray(
        cb.reshape(4, 8, 128, 256).transpose(2, 0, 1, 3).reshape(128, 8192))
    for nm, key in [("wrT", "L1_ffn_W_r"), ("wk2T", "L1_ffn_W_k"), ("wv2T", "L1_ffn_W_v")]:
        w = np.asarray(z[key])
        h[nm] = np.ascontiguousarray(
            w.reshape(8, 128, 8, 128).transpose(3, 2, 0, 1).reshape(128, 8192))
    h["mur"] = np.ascontiguousarray(np.asarray(z["L1_ffn_mu_r"]).reshape(8, 128).T)
    h["muk"] = np.ascontiguousarray(np.asarray(z["L1_ffn_mu_k"]).reshape(8, 128).T)
    return h


class Carver:
    def __init__(self, ap_f32):
        self.f = ap_f32
        self.b = ap_f32.bitcast(BF16)

    def f32(self, off, n):
        return self.f[:, off // 4: off // 4 + n]

    def f32v(self, off, d0, d1):
        return self.f[:, off // 4: off // 4 + d0 * d1].rearrange("p (a b) -> p a b", a=d0)

    def bf16(self, off, n):
        return self.b[:, off // 2: off // 2 + n]

    def bf16v(self, off, d0, d1):
        return self.b[:, off // 2: off // 2 + d0 * d1].rearrange("p (a b) -> p a b", a=d0)


class Ctx:
    def B(self, name, small=False):
        if name not in self._bufs:
            self._bufs[name] = Buf(small=small)
        return self._bufs[name]


def declare_io(nc, phases):
    t = {}
    fused = not SPLIT_MODE

    def dram(name, shape, dtype, kind):
        t[name] = nc.dram_tensor(name, shape, dtype, kind=kind).ap()

    has = {p: (p in phases) for p in "ABCDE"}
    if has["A"] or has["C"]:
        dram("x_fm", [BPC, D, S], F32, "ExternalInput")
    if has["A"]:
        dram("gwT", [128, 512], F32, "ExternalInput")
        dram("gbias", [128, 64], F32, "ExternalInput")
    if has["B"]:
        dram("wqT", [128, 512], F32, "ExternalInput")
        dram("wkT", [128, 512], F32, "ExternalInput")
        dram("wvT", [128, 512], F32, "ExternalInput")
    dram("ident", [128, 128], F32, "ExternalInput")
    if has["C"]:
        dram("moeblob", [NE, 128, BLOBW], F32, "ExternalInput")
        dram("basis64", [64, 64 * 64], BF16, "ExternalInput")
        dram("i2t", [64, 128], BF16, "ExternalInput")
    if has["D"] or has["E"]:
        dram("qwT", [128, 8192], F32, "ExternalInput")
        dram("combT", [128, 8192], F32, "ExternalInput")
        dram("wrT", [128, 8192], F32, "ExternalInput")
        dram("wk2T", [128, 8192], F32, "ExternalInput")
        dram("wv2T", [128, 8192], F32, "ExternalInput")
        dram("mur", [128, 8], F32, "ExternalInput")
        dram("muk", [128, 8], F32, "ExternalInput")

    def bridge(name, shape, dtype, produced, consumed):
        if produced and consumed:
            dram(name, shape, dtype, "ExternalOutput" if DEBUG_EXPORT else "Internal")
        elif produced:
            dram(name, shape, dtype, "ExternalOutput")
        elif consumed:
            dram(name, shape, dtype, "ExternalInput")

    if (not SPLIT_MODE) or not has["A"]:
        bridge("xn_h", [128, 8, TOK], BF16, has["A"], has["B"] or has["C"])
    else:
        dram("xn_h", [128, 8, TOK], BF16, "ExternalOutput")
    bridge("rwT_h", [64, TOK], BF16, has["A"], has["C"])
    bridge("sdup_h", [128, TOK], F32, has["B"], has["C"])
    bridge("x1_h", [BPC, D, S], F32, has["C"], has["D"] or has["E"])
    if has["D"] or has["E"]:
        bridge("xn1_h", [128, 8, TOK], BF16, has["D"], has["E"])
    if has["E"]:
        dram("out_fm", [BPC, D, S], F32, "ExternalOutput")
    return t


def emit_static(c, phases):
    P, t = c.P, c.t
    c.bident = c.B("ident")
    c.bidentb = c.B("identb")
    c.bones = c.B("ones", small=True)
    c.beps = c.B("eps", small=True)
    P.op("sync", lambda E, o=c.identf, i=t["ident"]: E.dma_start(out=o, in_=i),
         writes=[c.bident], inc=16)
    P.op("scalar", lambda E, o=c.identb, i=c.identf: E.copy(o, i),
         reads=[c.bident], writes=[c.bidentb])
    P.op("vector", lambda E, o=c.ones_col: E.memset(o, 1.0), writes=[c.bones])
    P.op("vector", lambda E, o=c.ones_row: E.memset(o, 1.0), writes=[c.bones])
    P.op("vector", lambda E, o=c.epsA: E.memset(o, 1e-5), writes=[c.beps])

    if "A" in phases:
        c.bgw = c.B("gw")
        c.bgbias = c.B("gbias")
        c.bncol = c.B("ncol", small=True)
        P.op("sync", lambda E, o=c.gwT, i=t["gwT"]: E.dma_start(out=o, in_=i), writes=[c.bgw], inc=16)
        P.op("sync", lambda E, o=c.gbias, i=t["gbias"]: E.dma_start(out=o, in_=i), writes=[c.bgbias], inc=16)
        pcs = c.psum[0]
        bcs = c.pbuf[0]
        for et in range(8):
            P.op("tensor", lambda E, o=pcs[0:64, 0:1], l=c.gwT[:, et * 64:(et + 1) * 64],
                 r=c.ones_col, s=(et == 0), p=(et == 7): E.matmul(o, l, r, start=s, stop=p),
                 reads=[c.bgw, c.bones], writes=[bcs])
        bnc = c.B("ncol_c", small=True)
        P.op("vector", lambda E, o=c.ncol_c[0:64, 0:1], i=pcs[0:64, 0:1]:
             E.tensor_scalar(o, i, -1.0, None, ALU.mult), reads=[bcs], writes=[bnc])
        P.op("tensor", lambda E, o=c.psum[1][0:1, 0:64], i=c.ncol_c[0:64, 0:1],
             idn=c.identf[0:64, 0:64]: E.transpose(o, i, idn),
             reads=[bnc, c.bident], writes=[bcs])
        P.op("vector", lambda E, o=c.ncol_row, i=c.psum[1][0:1, 0:64]:
             E.tensor_copy(o, i), reads=[bcs], writes=[c.bncol])

    if "B" in phases:
        c.bwqkv = c.B("wqkv")
        for i, nm in enumerate(["wqT", "wkT", "wvT"]):
            P.op("gpsimd", lambda E, o=c.wqkvb[:, i * 512:(i + 1) * 512], src=t[nm]:
                 E.dma_start(out=o, in_=src), writes=[c.bwqkv], inc=16)

    if ("D" in phases) or ("E" in phases):
        c.bmu = c.B("mu", small=True)
        P.op("sync", lambda E, o=c.mur, i=t["mur"]: E.dma_start(out=o, in_=i), writes=[c.bmu], inc=16)
        P.op("sync", lambda E, o=c.muk, i=t["muk"]: E.dma_start(out=o, in_=i), writes=[c.bmu], inc=16)
        P.op("vector", lambda E, o=c.omur, i=c.mur: E.tensor_scalar(o, i, -1.0, 1.0, ALU.mult, ALU.add),
             reads=[c.bmu], writes=[c.bmu])
        P.op("vector", lambda E, o=c.omuk, i=c.muk: E.tensor_scalar(o, i, -1.0, 1.0, ALU.mult, ALU.add),
             reads=[c.bmu], writes=[c.bmu])


LN_O = 72 * KB


def emit_ln_chunk(c, ci, src_dram, src_buf, dst_dram, dst_buf, with_routing):
    P, t, dyn = c.P, c.t, c.dyn
    b = ci // (S // CH)
    off = (ci % (S // CH)) * CH
    goff = ci * CH
    O = LN_O

    xld = dyn.f32v(O, 8, 512)
    x2 = dyn.f32v(O + 16 * KB, 8, 512)
    tmp1 = dyn.f32(O + 32 * KB, 512)
    xnst = dyn.bf16v(O + 34 * KB, 8, 512)
    m_row = dyn.f32(O + 42 * KB, 512)[0:1, :]
    sd_row = dyn.f32(O + 44 * KB, 512)[0:1, :]
    rstd_row = dyn.f32(O + 46 * KB, 512)[0:1, :]
    mr_row = dyn.f32(O + 48 * KB, 512)[0:1, :]
    ex2_row = dyn.f32(O + 50 * KB, 512)[0:1, :]
    tmpn = dyn.f32(O + 52 * KB, 512)

    bxld = c.B("ln_xld")
    bx2 = c.B("ln_x2")
    bxnst = c.B("ln_xnst")
    bstat = c.B("ln_stat", small=True)
    btmpn = c.B("ln_tmpn")
    btmp1 = c.B("ln_tmp1", small=True)
    psx, psx2, pbm, pbr = c.psum[0], c.psum[1], c.psum[2], c.psum[3]
    bpsx, bpsx2, bpbm, bpbr = c.pbuf[0], c.pbuf[1], c.pbuf[2], c.pbuf[3]

    src = src_dram[b, :, off:off + CH].rearrange("(n p) m -> p n m", p=128)
    P.op("sync", lambda E, o=xld, i=src: E.dma_start(out=o, in_=i),
         reads=[src_buf], writes=[bxld], inc=16)
    P.op("scalar", lambda E, o=x2, i=xld: E.activation(o, i, AF.Square),
         reads=[bxld], writes=[bx2])
    for et in range(8):
        P.op("tensor", lambda E, o=psx[0:1, :], l=c.ones_col, r=xld[:, et, :],
             s=(et == 0), p=(et == 7): E.matmul(o, l, r, start=s, stop=p),
             reads=[c.bones, bxld], writes=[bpsx])
    for et in range(8):
        P.op("tensor", lambda E, o=psx2[0:1, :], l=c.ones_col, r=x2[:, et, :],
             s=(et == 0), p=(et == 7): E.matmul(o, l, r, start=s, stop=p),
             reads=[c.bones, bx2], writes=[bpsx2])
    P.op("vector", lambda E, o=m_row, i=psx[0:1, :]:
         E.tensor_scalar(o, i, 1.0 / D, None, ALU.mult), reads=[bpsx], writes=[bstat])
    P.op("vector", lambda E, o=ex2_row, i=psx2[0:1, :]:
         E.tensor_scalar(o, i, 1.0 / D, None, ALU.mult), reads=[bpsx2], writes=[bstat])
    P.op("vector", lambda E, o=tmp1[0:1, :], i=m_row, j=m_row:
         E.tensor_mul(o, i, j), reads=[bstat], writes=[btmp1])
    P.op("vector", lambda E, o=ex2_row, i=ex2_row, j=tmp1[0:1, :]:
         E.tensor_sub(o, i, j), reads=[bstat, btmp1], writes=[bstat])
    P.op("scalar", lambda E, o=sd_row, i=ex2_row, bi=c.epsA[0:1, 0:1]:
         E.activation(o, i, AF.Sqrt, bias=bi), reads=[bstat, c.beps], writes=[bstat])
    P.op("vector", lambda E, o=rstd_row, i=sd_row: E.reciprocal(o, i),
         reads=[bstat], writes=[bstat])
    if with_routing:
        P.op("vector", lambda E, o=mr_row, i=m_row, j=rstd_row:
             E.tensor_mul(o, i, j), reads=[bstat], writes=[bstat])
    P.op("tensor", lambda E, o=pbm[:, :], l=c.ones_row, r=m_row:
         E.matmul(o, l, r, start=True, stop=True), reads=[c.bones, bstat], writes=[bpbm])
    P.op("tensor", lambda E, o=pbr[:, :], l=c.ones_row, r=rstd_row:
         E.matmul(o, l, r, start=True, stop=True), reads=[c.bones, bstat], writes=[bpbr])
    for et in range(8):
        P.op("vector", lambda E, o=tmpn, i=xld[:, et, :], j=pbm[:, :]:
             E.tensor_sub(o, i, j), reads=[bxld, bpbm], writes=[btmpn])
        P.op("vector", lambda E, o=xnst[:, et, :], i=tmpn, j=pbr[:, :]:
             E.tensor_mul(o, i, j), reads=[btmpn, bpbr], writes=[bxnst])
    dst = dst_dram[:, :, goff:goff + CH]
    P.op("sync", lambda E, o=dst, i=xnst: E.dma_start(out=o, in_=i),
         reads=[bxnst], writes=[dst_buf], inc=16)

    if not with_routing:
        return

    plg, pcor = c.psum[4], c.psum[5]
    bplg, bpcor = c.pbuf[4], c.pbuf[5]
    for et in range(8):
        P.op("tensor", lambda E, o=plg[0:64, :], l=c.gwT[:, et * 64:(et + 1) * 64],
             r=xld[:, et, :], s=(et == 0), p=(et == 7): E.matmul(o, l, r, start=s, stop=p),
             reads=[c.bgw, bxld], writes=[bplg])
    P.op("tensor", lambda E, o=pcor[0:64, :], l=c.ncol_row[:, 0:64], r=mr_row:
         E.matmul(o, l, r, start=True, stop=True), reads=[c.bncol, bstat], writes=[bpcor])
    lg0 = dyn.f32(O + 54 * KB, 512)[0:64, :]
    lg1 = dyn.f32(O + 56 * KB, 512)[0:64, :]
    blg = c.B("ln_lg")
    P.op("scalar", lambda E, o=lg0, i=plg[0:64, :]: E.copy(o, i), reads=[bplg], writes=[blg])
    P.op("vector", lambda E, o=lg1, i=lg0, j=pbr[0:64, :]: E.tensor_mul(o, i, j),
         reads=[blg, bpbr], writes=[blg])
    P.op("vector", lambda E, o=lg1, i=lg1, j=pcor[0:64, :]: E.tensor_add(o, i, j),
         reads=[bpcor, blg], writes=[blg])
    ptr = c.psum[6]
    bptr = c.pbuf[6]
    lg_tm = dyn.f32v(O + 58 * KB, 4, 64)
    blgtm = c.B("ln_lgtm")
    for j in range(4):
        P.op("tensor", lambda E, o=ptr[0:128, 0:64], i=lg1[:, j * 128:(j + 1) * 128],
             idn=c.identf[0:64, 0:64]: E.transpose(o, i, idn),
             reads=[blg, c.bident], writes=[bptr])
        P.op("vector", lambda E, o=lg_tm[:, j, :], i=ptr[0:128, 0:64]:
             E.tensor_copy(o, i), reads=[bptr], writes=[blgtm])
    emit_routing(c, ci, lg_tm, blgtm, O + 60 * KB)


def emit_routing(c, ci, lg_tm, blgtm, OB):
    P, t, dyn = c.P, c.t, c.dyn
    goff = ci * CH

    scores = dyn.f32v(OB, 4, 64)
    sfc = dyn.f32v(OB + 1024, 4, 64)
    msk = dyn.f32v(OB + 2048, 4, 64)
    cur = dyn.f32v(OB + 3072, 4, 64)
    sel = dyn.f32v(OB + 4096, 4, 64)
    sel8 = dyn.f32v(OB + 5120, 4, 64)
    w_ = dyn.f32v(OB + 6144, 4, 64)
    rw = dyn.f32v(OB + 7168, 4, 64)
    m1 = dyn.f32v(OB + 8192, 4, 8)
    mtmp = dyn.f32v(OB + 8320, 4, 8)
    gs = dyn.f32v(OB + 8448, 4, 8)
    curg = dyn.f32v(OB + 8576, 4, 8)
    selg = dyn.f32v(OB + 8704, 4, 8)
    gmask = dyn.f32v(OB + 8832, 4, 8)
    mm_ = dyn.f32v(OB + 8960, 4, 1)
    den = dyn.f32v(OB + 9088, 4, 1)
    rcp = dyn.f32v(OB + 9216, 4, 1)
    rwTs = dyn.f32(OB + 9344, 512)[0:64, :]
    rwTb_st = dyn.bf16(OB + 11392, 512)[0:64, :]

    bS = c.B("route", small=True)

    def V(fn):
        P.op("vector", fn, reads=[bS], writes=[bS])

    P.op("scalar", lambda E, o=scores, i=lg_tm: E.activation(o, i, AF.Sigmoid),
         reads=[blgtm], writes=[bS])
    V(lambda E: E.tensor_add(sfc, scores, c.gbias.unsqueeze(1).broadcast_to([128, 4, 64])))
    V(lambda E: E.tensor_reduce(m1, sfc.rearrange("p a (g k) -> p a g k", k=8), AX.X, ALU.max))
    V(lambda E: E.tensor_tensor(msk.rearrange("p a (g k) -> p a g k", k=8),
                                sfc.rearrange("p a (g k) -> p a g k", k=8),
                                m1.unsqueeze(3).broadcast_to([128, 4, 8, 8]), ALU.is_ge))
    V(lambda E: E.scalar_tensor_tensor(msk, msk, -1e30, sfc, ALU.mult, ALU.add))
    V(lambda E: E.tensor_reduce(mtmp, msk.rearrange("p a (g k) -> p a g k", k=8), AX.X, ALU.max))
    V(lambda E: E.tensor_add(gs, m1, mtmp))
    V(lambda E: E.memset(gmask, 0.0))
    V(lambda E: E.tensor_copy(curg, gs))
    for _ in range(4):
        V(lambda E: E.tensor_reduce(mm_, curg, AX.X, ALU.max))
        V(lambda E: E.tensor_tensor(selg, curg, mm_.broadcast_to([128, 4, 8]), ALU.is_ge))
        V(lambda E: E.tensor_add(gmask, gmask, selg))
        V(lambda E: E.scalar_tensor_tensor(curg, selg, -1e30, curg, ALU.mult, ALU.add))
    V(lambda E: E.tensor_tensor(msk.rearrange("p a (g k) -> p a g k", k=8),
                                sfc.rearrange("p a (g k) -> p a g k", k=8),
                                gmask.unsqueeze(3).broadcast_to([128, 4, 8, 8]), ALU.mult))
    V(lambda E: E.memset(sel8, 0.0))
    V(lambda E: E.tensor_copy(cur, msk))
    for _ in range(8):
        V(lambda E: E.tensor_reduce(mm_, cur, AX.X, ALU.max))
        V(lambda E: E.tensor_tensor(sel, cur, mm_.broadcast_to([128, 4, 64]), ALU.is_ge))
        V(lambda E: E.tensor_add(sel8, sel8, sel))
        V(lambda E: E.scalar_tensor_tensor(cur, sel, -1e30, cur, ALU.mult, ALU.add))
    V(lambda E: E.tensor_mul(w_, sel8, scores))
    V(lambda E: E.tensor_reduce(den, w_, AX.X, ALU.add))
    V(lambda E: E.tensor_scalar(den, den, 1e-20, None, ALU.add))
    V(lambda E: E.reciprocal(rcp, den))
    V(lambda E: E.tensor_scalar(rcp, rcp, 2.5, None, ALU.mult))
    V(lambda E: E.tensor_tensor(rw, w_, rcp.broadcast_to([128, 4, 64]), ALU.mult))
    ptr2 = c.psum[7]
    bptr2 = c.pbuf[7]
    for j in range(4):
        P.op("tensor", lambda E, o=ptr2[0:64, 0:128], i=rw[:, j, :], idn=c.identf:
             E.transpose(o, i, idn), reads=[bS, c.bident], writes=[bptr2])
        P.op("vector", lambda E, o=rwTs[:, j * 128:(j + 1) * 128], i=ptr2[0:64, 0:128]:
             E.tensor_copy(o, i), reads=[bptr2], writes=[bS])
    V(lambda E: E.tensor_copy(rwTb_st, rwTs))
    P.op("sync", lambda E, o=t["rwT_h"][:, goff:goff + CH], i=rwTb_st:
         E.dma_start(out=o, in_=i), reads=[bS], writes=[c.dbuf["rwT_h"]], inc=16)


def emit_phase_a(c):
    for ci in range(NCH):
        emit_ln_chunk(c, ci, c.t["x_fm"], c.dbuf["x_fm"], c.t["xn_h"], c.dbuf["xn_h"], True)


def emit_phase_b(c):
    P, t, dyn = c.P, c.t, c.dyn
    O = LN_O
    xnb = dyn.bf16v(O, 8, S)
    scf = dyn.f32(O + 32 * KB, S)
    attn_t = [dyn.bf16(O + 40 * KB + i * 4 * KB, S) for i in range(2)]
    qTb = dyn.bf16(O + 48 * KB, S)[0:64, :]
    kTb = dyn.bf16(O + 52 * KB, S)[0:64, :]
    vTb = dyn.bf16(O + 56 * KB, S)[0:64, :]
    v_tm = dyn.bf16(O + 60 * KB, 16 * 128)
    v_sc = [dyn.bf16(O + 64 * KB + i * 256, 128) for i in range(2)]
    sdst = dyn.f32(O + 66 * KB, S)
    nmx = dyn.f32(O + 74 * KB, 16)[:, 0:1]
    rs = dyn.f32(O + 74 * KB + 64, 16)[:, 0:1]
    rcq = dyn.f32(O + 74 * KB + 128, 16)[:, 0:1]

    pq = c.psum[0]
    ptr = c.psum[1]
    ptrb = ptr.bitcast(BF16)
    psc = [c.psum[2], c.psum[3]]
    pout = [c.psum[4 + i] for i in range(4)]

    bxnb = c.B("b_xnb")
    bq, bk, bv = c.B("b_q"), c.B("b_k"), c.B("b_v")
    bpq = c.pbuf[0]
    bvtm = c.B("b_vtm")
    bptr = c.pbuf[1]
    bscf = c.B("b_scf")
    bnmx = c.B("b_nmx", small=True)
    brs = c.B("b_rs", small=True)
    batt = [c.B("b_att0"), c.B("b_att1")]
    bvsc = [c.B("b_vsc0"), c.B("b_vsc1")]
    bpsc = [c.pbuf[2], c.pbuf[3]]
    bpout = c.pbuf[4]
    bpout2 = [c.pbuf[4], c.pbuf[5], c.pbuf[6], c.pbuf[7]]
    bsd = c.B("b_sd")

    for b in range(BPC):
        P.op("sync", lambda E, o=xnb, i=t["xn_h"][:, :, b * S:(b + 1) * S]:
             E.dma_start(out=o, in_=i), reads=[c.dbuf["xn_h"]], writes=[bxnb], inc=16)
        for wi, (dst, bd) in enumerate([(qTb, bq), (kTb, bk), (vTb, bv)]):
            for ic in range(4):
                for et in range(8):
                    P.op("tensor", lambda E, o=pq[0:64, :],
                         l=c.wqkvb[:, wi * 512 + et * 64: wi * 512 + (et + 1) * 64],
                         r=xnb[:, et, ic * 512:(ic + 1) * 512],
                         s=(et == 0), p=(et == 7): E.matmul(o, l, r, start=s, stop=p),
                         reads=[c.bwqkv, bxnb], writes=[bpq])
                P.op("scalar", lambda E, o=dst[:, ic * 512:(ic + 1) * 512], i=pq[0:64, :]:
                     E.copy(o, i), reads=[bpq], writes=[bd])
        for kt in range(16):
            P.op("tensor", lambda E, o=ptrb[0:128, 0:64], i=vTb[:, kt * 128:(kt + 1) * 128],
                 idn=c.identb[0:64, 0:64]: E.transpose(o, i, idn),
                 reads=[bv, c.bidentb], writes=[bptr])
            P.op("vector", lambda E, o=v_tm[:, kt * 128:kt * 128 + 64], i=ptrb[0:128, 0:64]:
                 E.tensor_copy(o, i), reads=[bptr], writes=[bvtm])
            P.op("vector", lambda E, o=v_tm[:, kt * 128 + 64:(kt + 1) * 128], i=ptrb[0:128, 0:64]:
                 E.tensor_copy(o, i), reads=[bptr], writes=[bvtm])
        for kt in range(16):
            par = kt % 2
            for ic in range(4):
                P.op("tensor", lambda E, o=psc[ic % 2][:, :], l=kTb[:, kt * 128:(kt + 1) * 128],
                     r=qTb[:, ic * 512:(ic + 1) * 512]: E.matmul(o, l, r, start=True, stop=True),
                     reads=[bk, bq], writes=[bpsc[ic % 2]])
                P.op("scalar", lambda E, o=scf[:, ic * 512:(ic + 1) * 512], i=psc[ic % 2][:, :]:
                     E.copy(o, i), reads=[bpsc[ic % 2]], writes=[bscf])
            P.op("vector", lambda E, o=nmx, i=scf:
                 E.reduce_max(o, i, axis=AX.X, negate=True), reads=[bscf], writes=[bnmx])
            P.op("scalar", lambda E, o=attn_t[par], i=scf, bi=nmx, ac=rs:
                 E.activation(o, i, AF.Exp, bias=bi, accum_out=ac),
                 reads=[bscf, bnmx], writes=[batt[par], brs])
            P.op("vector", lambda E, o=rcq, i=rs: E.reciprocal(o, i),
                 reads=[brs], writes=[brs])
            P.op("vector", lambda E, o=v_sc[par], i=v_tm[:, kt * 128:(kt + 1) * 128], sc=rcq:
                 E.tensor_scalar(o, i, sc, None, ALU.mult),
                 reads=[bvtm, brs], writes=[bvsc[par]])
            for ic in range(4):
                P.op("tensor", lambda E, o=pout[ic][:, :], l=v_sc[par],
                     r=attn_t[par][:, ic * 512:(ic + 1) * 512],
                     s=(kt == 0), p=(kt == 15): E.matmul(o, l, r, start=s, stop=p),
                     reads=[bvsc[par], batt[par]], writes=[bpout2[ic]])
        for ic in range(4):
            P.op("scalar", lambda E, o=sdst[:, ic * 512:(ic + 1) * 512], i=pout[ic][:, :]:
                 E.copy(o, i), reads=[bpout2[ic]], writes=[bsd])
        P.op("sync", lambda E, o=t["sdup_h"][:, b * S:(b + 1) * S], i=sdst:
             E.dma_start(out=o, in_=i), reads=[bsd], writes=[c.dbuf["sdup_h"]], inc=16)


def emit_phase_c(c):
    P, t, dyn = c.P, c.t, c.dyn
    wbuf = [dyn.bf16(0 + i * 36864, BLOBW) for i in range(2)]
    routed = dyn.f32(72 * KB, 8 * 1024)
    xnc = [dyn.bf16v(104 * KB + i * 8 * KB, 8, 512) for i in range(2)]
    hb = [dyn.bf16(120 * KB + i * 6 * KB, 6 * 512) for i in range(2)]
    sg = [dyn.f32(132 * KB + i * 2 * KB, 512) for i in range(2)]
    t1 = dyn.f32(136 * KB, 512)
    b64s = dyn.bf16(138 * KB, 512)[0:64, :]
    rwTb = dyn.bf16(139 * KB, TOK)[0:64, :]
    xepi = dyn.f32v(147 * KB, 8, 512)
    sdps = [dyn.f32(163 * KB + i * 2 * KB, 512) for i in range(2)]
    gsil = [dyn.f32(167 * KB + i * 2 * KB, 512) for i in range(2)]
    ttmp = [dyn.f32(171 * KB + i * 2 * KB, 512) for i in range(2)]
    basis = dyn.bf16(175 * KB, 64 * 64)[0:64, :]
    i2sb = dyn.bf16(184 * KB, 128)[0:64, :]

    bwbuf = [c.B("c_wb0"), c.B("c_wb1")]
    bxnc = [c.B("c_xnc0"), c.B("c_xnc1")]
    bhb = [c.B("c_hb0"), c.B("c_hb1")]
    bsg = [c.B("c_sg0"), c.B("c_sg1")]
    bt1 = c.B("c_t1")
    bb64 = c.B("c_b64")
    brwT = c.B("c_rwT")
    bbasis = c.B("c_basis")
    brouted = [[c.B(f"c_rt{ht}_{cc}") for cc in range(GCH)] for ht in range(8)]
    bxepi = c.B("c_xepi")
    bsdps = [c.B("c_sdps0"), c.B("c_sdps1")]
    bgsil = [c.B("c_gsil0"), c.B("c_gsil1")]
    bttmp = [c.B("c_ttmp0"), c.B("c_ttmp1")]

    pag = [c.psum[0], c.psum[1]]
    pau = [c.psum[2], c.psum[3]]
    prt = [c.psum[4], c.psum[5]]
    pb64 = c.psum[6]
    pb128 = c.psum[7]
    bpag = [c.pbuf[0], c.pbuf[1]]
    bpau = [c.pbuf[2], c.pbuf[3]]
    bprt = [c.pbuf[4], c.pbuf[5]]
    bpb64 = c.pbuf[6]
    bpb128 = c.pbuf[7]

    P.op("sync", lambda E, o=rwTb, i=t["rwT_h"]: E.dma_start(out=o, in_=i),
         reads=[c.dbuf["rwT_h"]], writes=[brwT], inc=16)
    P.op("sync", lambda E, o=basis, i=t["basis64"]: E.dma_start(out=o, in_=i),
         writes=[bbasis], inc=16)
    P.op("sync", lambda E, o=i2sb, i=t["i2t"]: E.dma_start(out=o, in_=i),
         writes=[bbasis], inc=16)

    def wdma(s):
        g0, e0 = divmod(s, NE)
        P.op("gpsimd", lambda E, o=wbuf[s % 2], i=t["moeblob"][e0]:
             E.dma_start(out=o, in_=i), writes=[bwbuf[s % 2]], inc=16)

    wdma(0)
    wdma(1)

    slots = [(g, e, cc) for g in range(NGRP) for e in range(NE) for cc in range(GCH)]

    def emit_bcast(k):
        g, e, cc = slots[k]
        if e >= 64:
            return
        goff = g * GTOK + cc * CH
        P.op("tensor", lambda E, o=pb64[0:64, :], l=basis[:, e * 64:(e + 1) * 64],
             r=rwTb[:, goff:goff + CH]: E.matmul(o, l, r, start=True, stop=True),
             reads=[bbasis, brwT], writes=[bpb64])
        P.op("vector", lambda E, o=b64s, i=pb64[0:64, :]: E.tensor_copy(o, i),
             reads=[bpb64], writes=[bb64])
        P.op("tensor", lambda E, o=pb128[:, :], l=i2sb, r=b64s:
             E.matmul(o, l, r, start=True, stop=True),
             reads=[bbasis, bb64], writes=[bpb128])

    def emit_agau(k):
        g, e, cc = slots[k]
        s = g * NE + e
        par = s % 2
        hpar = k % 2
        for it in range(6):
            ap_ = it % 2
            for et in range(8):
                P.op("tensor", lambda E, o=pag[ap_][:, :],
                     l=wbuf[par][:, et * IPAD + it * 128: et * IPAD + (it + 1) * 128],
                     r=xnc[cc][:, et, :], s=(et == 0), p=(et == 7):
                     E.matmul(o, l, r, start=s, stop=p),
                     reads=[bwbuf[par], bxnc[cc]], writes=[bpag[ap_]])
            for et in range(8):
                P.op("tensor", lambda E, o=pau[ap_][:, :],
                     l=wbuf[par][:, SEC + et * IPAD + it * 128: SEC + et * IPAD + (it + 1) * 128],
                     r=xnc[cc][:, et, :], s=(et == 0), p=(et == 7):
                     E.matmul(o, l, r, start=s, stop=p),
                     reads=[bwbuf[par], bxnc[cc]], writes=[bpau[ap_]])
            P.op("scalar", lambda E, o=sg[ap_], i=pag[ap_][:, :]:
                 E.activation(o, i, AF.Silu), reads=[bpag[ap_]], writes=[bsg[ap_]])
            if e < 64:
                P.op("vector", lambda E, o=t1, i=sg[ap_], j=pau[ap_][:, :]:
                     E.tensor_mul(o, i, j), reads=[bsg[ap_], bpau[ap_]], writes=[bt1])
                P.op("vector", lambda E, o=hb[hpar][:, it * 512:(it + 1) * 512], i=t1, j=pb128[:, :]:
                     E.tensor_mul(o, i, j), reads=[bt1, bpb128], writes=[bhb[hpar]])
            else:
                P.op("vector", lambda E, o=hb[hpar][:, it * 512:(it + 1) * 512], i=sg[ap_],
                     j=pau[ap_][:, :]: E.tensor_mul(o, i, j),
                     reads=[bsg[ap_], bpau[ap_]], writes=[bhb[hpar]])

    def emit_wd(k):
        g, e, cc = slots[k]
        s = g * NE + e
        par = s % 2
        hpar = k % 2
        for ht in range(8):
            rp = ht % 2
            for it in range(6):
                P.op("tensor", lambda E, o=prt[rp][:, :],
                     l=wbuf[par][:, 2 * SEC + it * 1024 + ht * 128: 2 * SEC + it * 1024 + (ht + 1) * 128],
                     r=hb[hpar][:, it * 512:(it + 1) * 512], s=(it == 0), p=(it == 5):
                     E.matmul(o, l, r, start=s, stop=p),
                     reads=[bwbuf[par], bhb[hpar]], writes=[bprt[rp]])
            rsl = routed[:, ht * 1024 + cc * 512: ht * 1024 + (cc + 1) * 512]
            if e == 0:
                P.op("vector", lambda E, o=rsl, i=prt[rp][:, :]: E.tensor_copy(o, i),
                     reads=[bprt[rp]], writes=[brouted[ht][cc]])
            else:
                P.op("vector", lambda E, o=rsl, i=rsl, j=prt[rp][:, :]: E.tensor_add(o, i, j),
                     reads=[bprt[rp], brouted[ht][cc]], writes=[brouted[ht][cc]])

    def emit_xnc_loads(g):
        for cc in range(GCH):
            goff = g * GTOK + cc * CH
            P.op("sync", lambda E, o=xnc[cc], i=t["xn_h"][:, :, goff:goff + CH]:
                 E.dma_start(out=o, in_=i), reads=[c.dbuf["xn_h"]], writes=[bxnc[cc]], inc=16)

    def emit_epilogue(g):
        for cc in range(GCH):
            goff = g * GTOK + cc * CH
            b = goff // S
            off = goff % S
            src = t["x_fm"][b, :, off:off + CH].rearrange("(n p) m -> p n m", p=128)
            P.op("sync", lambda E, o=xepi, i=src: E.dma_start(out=o, in_=i),
                 reads=[c.dbuf["x_fm"]], writes=[bxepi], inc=16)
            P.op("sync", lambda E, o=sdps[cc], i=t["sdup_h"][:, goff:goff + CH]:
                 E.dma_start(out=o, in_=i), reads=[c.dbuf["sdup_h"]], writes=[bsdps[cc]], inc=16)
            for ht in range(8):
                gp = ht % 2
                rsl = routed[:, ht * 1024 + cc * 512: ht * 1024 + (cc + 1) * 512]
                P.op("scalar", lambda E, o=gsil[gp], i=rsl: E.activation(o, i, AF.Silu),
                     reads=[brouted[ht][cc]], writes=[bgsil[gp]])
                P.op("vector", lambda E, o=ttmp[gp], i=gsil[gp], j=sdps[cc]:
                     E.tensor_mul(o, i, j), reads=[bgsil[gp], bsdps[cc]], writes=[bttmp[gp]])
                P.op("vector", lambda E, o=ttmp[gp], i=ttmp[gp], j=xepi[:, ht, :]:
                     E.tensor_add(o, i, j), reads=[bxepi, bttmp[gp]], writes=[bttmp[gp]])
                dst = t["x1_h"][b, ht * 128:(ht + 1) * 128, off:off + CH]
                P.op("sync", lambda E, o=dst, i=ttmp[gp]: E.dma_start(out=o, in_=i),
                     reads=[bttmp[gp]], writes=[c.dbuf["x1_h"]], inc=16)

    last_g = -1
    for k, (g, e, cc) in enumerate(slots):
        if g != last_g:
            emit_xnc_loads(g)
            last_g = g
        if k == 0:
            emit_bcast(0)
        emit_agau(k)
        if k > 0:
            emit_wd(k - 1)
            gp, ep, cp = slots[k - 1]
            if ep == NE - 1 and cp == GCH - 1:
                emit_epilogue(gp)
            sp = gp * NE + ep
            if cp == GCH - 1 and sp + 2 < NGRP * NE:
                wdma(sp + 2)
        if k + 1 < len(slots):
            emit_bcast(k + 1)
    emit_wd(len(slots) - 1)
    emit_epilogue(NGRP - 1)


def emit_phase_d(c):
    P, t, dyn = c.P, c.t, c.dyn
    c.qwTb = dyn.bf16(0, 8192)
    c.PaB = dyn.bf16(16 * KB, 2 * 8192)
    c.combTb = dyn.bf16(48 * KB, 8192)
    c.wrkvb = dyn.bf16(152 * KB, 3 * 8192)
    c.bqw, c.bPa, c.bcomb, c.bwrkv = c.B("d_qw"), c.B("d_Pa"), c.B("d_comb"), c.B("d_wrkv")
    P.op("gpsimd", lambda E, o=c.qwTb, i=t["qwT"]: E.dma_start(out=o, in_=i),
         writes=[c.bqw], inc=16)
    P.op("gpsimd", lambda E, o=c.combTb, i=t["combT"]: E.dma_start(out=o, in_=i),
         writes=[c.bcomb], inc=16)

    for ci in range(NCH):
        emit_ln_chunk(c, ci, c.t["x1_h"], c.dbuf["x1_h"], c.t["xn1_h"], c.dbuf["xn1_h"], False)
    c.P.barrier()

    for i, nm in enumerate(["wrT", "wk2T", "wv2T"]):
        P.op("gpsimd", lambda E, o=c.wrkvb[:, i * 8192:(i + 1) * 8192], src=t[nm]:
             E.dma_start(out=o, in_=src), writes=[c.bwrkv], inc=16)

    MqB = dyn.bf16(64 * KB, 4 * 512)
    xnb = dyn.bf16v(68 * KB, 8, S)
    xtm = dyn.bf16(100 * KB, 16 * 1024)
    bxnb, bxtm, bMq = c.B("d_xnb"), c.B("d_xtm"), c.B("d_Mq")
    ptr = c.psum[0]
    ptrb = ptr.bitcast(BF16)
    bptr = c.pbuf[0]
    pmq = [c.psum[1 + q] for q in range(4)]
    bpmq = [c.pbuf[1 + q] for q in range(4)]
    ppp = [c.psum[5], c.psum[6]]
    bppp = [c.pbuf[5], c.pbuf[6]]

    for b in range(BPC):
        P.op("sync", lambda E, o=xnb, i=t["xn1_h"][:, :, b * S:(b + 1) * S]:
             E.dma_start(out=o, in_=i), reads=[c.dbuf["xn1_h"]], writes=[bxnb], inc=16)
        for ut in range(16):
            for ft in range(8):
                P.op("tensor", lambda E, o=ptrb[0:128, 0:128],
                     i=xnb[:, ft, ut * 128:(ut + 1) * 128], idn=c.identb:
                     E.transpose(o, i, idn), reads=[bxnb, c.bidentb], writes=[bptr])
                P.op("vector", lambda E,
                     o=xtm[:, ut * 1024 + ft * 128: ut * 1024 + (ft + 1) * 128],
                     i=ptrb[0:128, 0:128]: E.tensor_copy(o, i), reads=[bptr], writes=[bxtm])
        for fh in range(2):
            for ut in range(16):
                for q in range(4):
                    P.op("tensor", lambda E, o=pmq[q][:, fh * 256:(fh + 1) * 256],
                         l=xtm[:, ut * 1024 + q * 256 + fh * 128: ut * 1024 + q * 256 + (fh + 1) * 128],
                         r=xtm[:, ut * 1024 + q * 256: ut * 1024 + (q + 1) * 256],
                         s=(ut == 0), p=(ut == 15): E.matmul(o, l, r, start=s, stop=p),
                         reads=[bxtm], writes=[bpmq[q]])
            for q in range(4):
                P.op("scalar", lambda E, o=MqB[:, q * 512 + fh * 256: q * 512 + (fh + 1) * 256],
                     i=pmq[q][:, fh * 256:(fh + 1) * 256]:
                     E.copy(o, i), reads=[bpmq[q]], writes=[bMq])
        for a in range(4):
            for q in range(4):
                for eh in range(2):
                    pp = ppp[eh]
                    for fh in range(2):
                        P.op("tensor", lambda E, o=pp[:, 0:256],
                             l=c.qwTb[:, ((a * 4 + q) * 2 + fh) * 256 + eh * 128:
                                      ((a * 4 + q) * 2 + fh) * 256 + (eh + 1) * 128],
                             r=MqB[:, q * 512 + fh * 256: q * 512 + (fh + 1) * 256],
                             s=(fh == 0), p=(fh == 1): E.matmul(o, l, r, start=s, stop=p),
                             reads=[c.bqw, bMq], writes=[bppp[eh]])
                    dst = c.PaB[:, b * 8192 + a * 2048 + eh * 1024 + q * 256:
                                b * 8192 + a * 2048 + eh * 1024 + (q + 1) * 256]
                    P.op("scalar", lambda E, o=dst, i=ppp[eh][:, 0:256]: E.copy(o, i),
                         reads=[bppp[eh]], writes=[c.bPa])


def emit_phase_e(c):
    P, t, dyn = c.P, c.t, c.dyn
    rsb = dyn.f32(0, 8 * 512)
    kb = dyn.bf16(64 * KB, 8 * 512)
    attn_a = dyn.bf16(72 * KB, 8 * 512)
    s1c = dyn.f32(80 * KB, 8 * 512)
    tmpk = dyn.f32(96 * KB, 512)
    fo = [dyn.f32(98 * KB + i * 2 * KB, 512) for i in range(2)]
    oo = [dyn.f32(102 * KB + i * 2 * KB, 512) for i in range(2)]
    t1r = dyn.f32(106 * KB, 512)
    g1s = dyn.f32(108 * KB, 512)
    xe = dyn.bf16v(110 * KB, 8, 513)
    xe1 = dyn.f32v(119 * KB, 8, 512)
    xr = dyn.bf16(135 * KB, 8 * 512)
    xk = dyn.bf16(143 * KB, 8 * 512)

    brsb = c.B("e_rsb")
    bxe = c.B("e_xe")
    bxe1 = c.B("e_xe1")
    bxr, bxk, bkb = c.B("e_xr"), c.B("e_xk"), c.B("e_kb")
    batn = c.B("e_attn")
    bs1c = c.B("e_s1c")
    btmpk = c.B("e_tmpk")
    bfo = [c.B("e_fo0"), c.B("e_fo1")]
    bg1s = c.B("e_g1s")
    boo = [c.B("e_oo0"), c.B("e_oo1")]
    bt1r = c.B("e_t1r")

    pa1 = [c.psum[0], c.psum[1]]
    po = [c.psum[2], c.psum[3]]
    pe_ = [c.psum[4], c.psum[5]]
    bpa1 = [c.pbuf[0], c.pbuf[1]]
    bpo = [c.pbuf[2], c.pbuf[3]]
    bpe = [c.pbuf[4], c.pbuf[5]]

    for b in range(BPC):
        for cc in range(4):
            off = cc * 512
            if cc == 0:
                P.op("vector", lambda E, o=xe[:, :, 0:1]: E.memset(o, 0.0), writes=[bxe])
                P.op("sync", lambda E, o=xe[:, :, 1:513],
                     i=t["xn1_h"][:, :, b * S: b * S + 512]:
                     E.dma_start(out=o, in_=i), reads=[c.dbuf["xn1_h"]], writes=[bxe], inc=16)
            else:
                P.op("sync", lambda E, o=xe[:, :, 0:513],
                     i=t["xn1_h"][:, :, b * S + off - 1: b * S + off + 512]:
                     E.dma_start(out=o, in_=i), reads=[c.dbuf["xn1_h"]], writes=[bxe], inc=16)
            src1 = t["x1_h"][b, :, off:off + 512].rearrange("(n p) m -> p n m", p=128)
            P.op("sync", lambda E, o=xe1, i=src1: E.dma_start(out=o, in_=i),
                 reads=[c.dbuf["x1_h"]], writes=[bxe1], inc=16)

            for a in range(4):
                for qgt in range(8):
                    pp = pa1[qgt % 2]
                    for eh in range(2):
                        P.op("tensor", lambda E, o=pp[:, :],
                             l=c.PaB[:, b * 8192 + a * 2048 + eh * 1024 + qgt * 128:
                                     b * 8192 + a * 2048 + eh * 1024 + (qgt + 1) * 128],
                             r=xe[:, 2 * a + eh, 1:513],
                             s=(eh == 0), p=(eh == 1): E.matmul(o, l, r, start=s, stop=p),
                             reads=[c.bPa, bxe], writes=[bpa1[qgt % 2]])
                    P.op("scalar", lambda E, o=attn_a[:, qgt * 512:(qgt + 1) * 512],
                         i=pa1[qgt % 2][:, :]: E.copy(o, i),
                         reads=[bpa1[qgt % 2]], writes=[batn])
                for gt in range(2):
                    pp = po[gt % 2]
                    for qgt in range(8):
                        P.op("tensor", lambda E, o=pp[:, :],
                             l=c.combTb[:, (a * 8 + qgt) * 256 + gt * 128:
                                        (a * 8 + qgt) * 256 + (gt + 1) * 128],
                             r=attn_a[:, qgt * 512:(qgt + 1) * 512],
                             s=(qgt == 0), p=(qgt == 7): E.matmul(o, l, r, start=s, stop=p),
                             reads=[c.bcomb, batn], writes=[bpo[gt % 2]])
                    P.op("scalar", lambda E, o=s1c[:, (2 * a + gt) * 512:(2 * a + gt + 1) * 512],
                         i=po[gt % 2][:, :]: E.copy(o, i), reads=[bpo[gt % 2]], writes=[bs1c])

            for ft in range(8):
                P.op("vector", lambda E, o=t1r, i=xe[:, ft, 1:513], sc=c.mur[:, ft:ft + 1]:
                     E.tensor_scalar(o, i, sc, None, ALU.mult),
                     reads=[bxe, c.bmu], writes=[bt1r])
                P.op("vector", lambda E, o=xr[:, ft * 512:(ft + 1) * 512], i=xe[:, ft, 0:512],
                     sc=c.omur[:, ft:ft + 1], j=t1r:
                     E.scalar_tensor_tensor(o, i, sc, j, ALU.mult, ALU.add),
                     reads=[bxe, c.bmu, bt1r], writes=[bxr])
                P.op("vector", lambda E, o=t1r, i=xe[:, ft, 1:513], sc=c.muk[:, ft:ft + 1]:
                     E.tensor_scalar(o, i, sc, None, ALU.mult),
                     reads=[bxe, c.bmu], writes=[bt1r])
                P.op("vector", lambda E, o=xk[:, ft * 512:(ft + 1) * 512], i=xe[:, ft, 0:512],
                     sc=c.omuk[:, ft:ft + 1], j=t1r:
                     E.scalar_tensor_tensor(o, i, sc, j, ALU.mult, ALU.add),
                     reads=[bxe, c.bmu, bt1r], writes=[bxk])
            for dt in range(8):
                pp = pe_[dt % 2]
                for ct in range(8):
                    P.op("tensor", lambda E, o=pp[:, :],
                         l=c.wrkvb[:, 0 * 8192 + ct * 1024 + dt * 128: 0 * 8192 + ct * 1024 + (dt + 1) * 128],
                         r=xr[:, ct * 512:(ct + 1) * 512],
                         s=(ct == 0), p=(ct == 7): E.matmul(o, l, r, start=s, stop=p),
                         reads=[c.bwrkv, bxr], writes=[bpe[dt % 2]])
                P.op("scalar", lambda E, o=rsb[:, dt * 512:(dt + 1) * 512], i=pe_[dt % 2][:, :]:
                     E.activation(o, i, AF.Sigmoid), reads=[bpe[dt % 2]], writes=[brsb])
            for dt in range(8):
                pp = pe_[dt % 2]
                for ct in range(8):
                    P.op("tensor", lambda E, o=pp[:, :],
                         l=c.wrkvb[:, 1 * 8192 + ct * 1024 + dt * 128: 1 * 8192 + ct * 1024 + (dt + 1) * 128],
                         r=xk[:, ct * 512:(ct + 1) * 512],
                         s=(ct == 0), p=(ct == 7): E.matmul(o, l, r, start=s, stop=p),
                         reads=[c.bwrkv, bxk], writes=[bpe[dt % 2]])
                P.op("vector", lambda E, o=tmpk, i=pe_[dt % 2][:, :]:
                     E.tensor_scalar_max(o, i, 0.0), reads=[bpe[dt % 2]], writes=[btmpk])
                P.op("scalar", lambda E, o=kb[:, dt * 512:(dt + 1) * 512], i=tmpk:
                     E.activation(o, i, AF.Square), reads=[btmpk], writes=[bkb])
            for dt in range(8):
                pp = pe_[dt % 2]
                par = dt % 2
                for ct in range(8):
                    P.op("tensor", lambda E, o=pp[:, :],
                         l=c.wrkvb[:, 2 * 8192 + ct * 1024 + dt * 128: 2 * 8192 + ct * 1024 + (dt + 1) * 128],
                         r=kb[:, ct * 512:(ct + 1) * 512],
                         s=(ct == 0), p=(ct == 7): E.matmul(o, l, r, start=s, stop=p),
                         reads=[c.bwrkv, bkb], writes=[bpe[dt % 2]])
                P.op("vector", lambda E, o=fo[par], i=rsb[:, dt * 512:(dt + 1) * 512],
                     j=pe_[dt % 2][:, :]: E.tensor_mul(o, i, j),
                     reads=[brsb, bpe[dt % 2]], writes=[bfo[par]])
                P.op("scalar", lambda E, o=g1s, i=fo[par]: E.activation(o, i, AF.Silu),
                     reads=[bfo[par]], writes=[bg1s])
                P.op("vector", lambda E, o=fo[par], i=g1s, j=s1c[:, dt * 512:(dt + 1) * 512]:
                     E.tensor_mul(o, i, j), reads=[bg1s, bs1c], writes=[bfo[par]])
                P.op("vector", lambda E, o=oo[par], i=fo[par], j=xe1[:, dt, :]:
                     E.tensor_add(o, i, j), reads=[bfo[par], bxe1], writes=[boo[par]])
                dst = t["out_fm"][b, dt * 128:(dt + 1) * 128, off:off + 512]
                P.op("sync", lambda E, o=dst, i=oo[par]: E.dma_start(out=o, in_=i),
                     reads=[boo[par]], writes=[c.dbuf["out_fm"]], inc=16)


def build_launch(phases):
    nc = bass.Bass("TRN2", target_bir_lowering=False, debug=False)
    t = declare_io(nc, phases)
    P = Prog()
    c = Ctx()
    c.nc, c.t, c.P = nc, t, P
    c._bufs = {}
    stack = contextlib.ExitStack()

    def SB(nm, shp, dt):
        return stack.enter_context(nc.sbuf_tensor(nm, shp, dt)).ap()

    def PS(nm, shp, dt=F32):
        return stack.enter_context(nc.psum_tensor(nm, shp, dt)).ap()

    c.dyn = Carver(SB("dyn", [128, DYN_BYTES // 4], F32))
    c.identf = SB("identf", [128, 128], F32)
    c.identb = SB("identb", [128, 128], BF16)
    c.ones_col = SB("ones_col", [128, 1], F32)
    c.ones_row = SB("ones_row", [1, 128], F32)
    c.epsA = SB("epsA", [128, 1], F32)
    if "A" in phases:
        c.gwT = SB("gwT_s", [128, 512], F32)
        c.gbias = SB("gbias_s", [128, 64], F32)
        c.ncol_row = SB("ncol_row", [1, 64], F32)
        c.ncol_c = SB("ncol_c", [128, 1], F32)
    if "B" in phases:
        c.wqkvb = SB("wqkvb", [128, 3 * 512], BF16)
    if ("D" in phases) or ("E" in phases):
        c.mur = SB("mur_s", [128, 8], F32)
        c.muk = SB("muk_s", [128, 8], F32)
        c.omur = SB("omur_s", [128, 8], F32)
        c.omuk = SB("omuk_s", [128, 8], F32)
    c.psum = [PS(f"pbk{i}", [128, 512], F32) for i in range(8)]
    c.pbuf = [Buf() for _ in range(8)]

    sems = {e: stack.enter_context(nc.semaphore("sem_" + e)) for e in ENGS}
    c.dbuf = {nm: Buf() for nm in t}

    emit_static(c, phases)
    if "A" in phases:
        emit_phase_a(c)
    if "B" in phases:
        P.barrier()
        emit_phase_b(c)
    if "C" in phases:
        P.barrier(skip=("gpsimd",))
        emit_phase_c(c)
    if "D" in phases:
        P.barrier()
        emit_phase_d(c)
    if "E" in phases:
        P.barrier()
        emit_phase_e(c)

    final = P.cnt["sync"]

    def mk(e, extra_wait=None):
        def body(engobj):
            water = {pe: 0 for pe in ENGS}
            for fd, fn, inc in P.items[e]:
                for pe, v in sorted(fd.items()):
                    if v > water[pe]:
                        engobj.wait_ge(sems[pe], v)
                        water[pe] = v
                inst = fn(engobj)
                inst.then_inc(sems[e], inc)
            if extra_wait is not None:
                engobj.wait_ge(sems[e], extra_wait)
        return body

    with nc.Block() as block:
        block.sync(mk("sync", extra_wait=final))
        block.gpsimd(mk("gpsimd"))
        block.tensor(mk("tensor"))
        block.vector(mk("vector"))
        block.scalar(mk("scalar"))
    stack.close()
    return nc


_NC_CACHE = {}


def _get_nc(phases):
    if phases not in _NC_CACHE:
        _NC_CACHE[phases] = build_launch(phases)
    return _NC_CACHE[phases]


def kernel(**inputs):
    h = host_prep(inputs)
    per_core = []
    for core in range(NCORES):
        m = {"x_fm": np.ascontiguousarray(h["x_fm"][core * BPC:(core + 1) * BPC])}
        per_core.append(m)

    common_ab = {k: h[k] for k in ["gwT", "gbias", "wqT", "wkT", "wvT", "ident"]}
    common_c = {k: h[k] for k in ["moeblob", "basis64", "i2t", "ident"]}
    common_de = {k: h[k] for k in ["qwT", "combT", "wrT", "wk2T", "wv2T", "mur", "muk", "ident"]}

    if SPLIT_MODE:
        nc1 = _get_nc("AB")
        maps1 = [dict(common_ab, x_fm=per_core[i]["x_fm"]) for i in range(NCORES)]
        r1 = run_bass_kernel_spmd(nc1, maps1, core_ids=list(range(NCORES))).results

        nc2 = _get_nc("C")
        maps2 = [dict(common_c, x_fm=per_core[i]["x_fm"],
                      xn_h=r1[i]["xn_h"], rwT_h=r1[i]["rwT_h"], sdup_h=r1[i]["sdup_h"])
                 for i in range(NCORES)]
        r2 = run_bass_kernel_spmd(nc2, maps2, core_ids=list(range(NCORES))).results

        nc3 = _get_nc("DE")
        maps3 = [dict(common_de, x1_h=r2[i]["x1_h"]) for i in range(NCORES)]
        r3 = run_bass_kernel_spmd(nc3, maps3, core_ids=list(range(NCORES))).results
        outs = [r3[i]["out_fm"] for i in range(NCORES)]
    else:
        nc = _get_nc("ABCDE")
        maps = [dict(common_ab) for _ in range(NCORES)]
        for i in range(NCORES):
            maps[i].update(common_c)
            maps[i].update(common_de)
            maps[i]["x_fm"] = per_core[i]["x_fm"]
        rr = run_bass_kernel_spmd(nc, maps, core_ids=list(range(NCORES))).results
        outs = [rr[i]["out_fm"] for i in range(NCORES)]

    out = np.concatenate([o.reshape(BPC, D, S) for o in outs], axis=0)
    return np.ascontiguousarray(out.transpose(0, 2, 1)).astype(np.float32)


# version 04
# speedup vs baseline: 1.4626x; 1.4626x over previous
import contextlib
import numpy as np
import ml_dtypes
import concourse.bass as bass
import concourse.mybir as mybir
from concourse.bass_utils import run_bass_kernel_spmd
from concourse.bass_core_rust import IndirectOffsetOnAxis

F32 = mybir.dt.float32
BF16 = mybir.dt.bfloat16
AF = mybir.ActivationFunctionType
ALU = mybir.AluOpType
AX = mybir.AxisListType
BF = ml_dtypes.bfloat16

NCORES = 8
S = 2048
D = 1024
BPC = 2
TOK = BPC * S
NE = 66
IPAD = 768
SEC = 8 * IPAD
BLOBW = 3 * SEC
CH = 512
NCH = TOK // CH
NGRP = 4
GTOK = TOK // NGRP
GCH = GTOK // CH
KB = 1024
DYN_BYTES = 200 * KB
CAP = 1536
NSLOT = 64 * CAP
TRASH = NSLOT
NROW_CX = NSLOT + 128
NROW_CO = NSLOT + 128
BIGF = 1.0e7

SPLIT_MODE = False
DEBUG_EXPORT = False

ENGS = ("sync", "gpsimd", "tensor", "vector", "scalar")


class Buf:
    __slots__ = ("w", "r", "small")

    def __init__(self, small=False):
        self.w = None
        self.r = []
        self.small = small


class Prog:
    def __init__(self):
        self.items = {e: [] for e in ENGS}
        self.cnt = {e: 0 for e in ENGS}
        self.pend = {e: [] for e in ENGS}
        self.drains = {e: [] for e in ENGS}

    def drain(self, q):
        if self.cnt[q] > 0 and (not self.drains[q] or self.drains[q][-1] != self.cnt[q]):
            self.items[q].append(("DRAIN", self.cnt[q], 0))
            self.drains[q].append(self.cnt[q])

    def barrier(self, skip=()):
        for q in ("sync", "gpsimd"):
            if q not in skip:
                self.drain(q)
        toks = [(e, self.cnt[e]) for e in ENGS if self.cnt[e] > 0]
        for e in ENGS:
            if e in skip:
                continue
            self.pend[e] = self.pend[e] + [(tk, True) for tk in toks]

    def op(self, e, fn, reads=(), writes=(), extra=(), inc=1):
        deps = []
        for b in reads:
            if b.w is not None:
                deps.append((b.w, b.small))
        for b in writes:
            for tk in b.r:
                deps.append((tk, b.small))
            if b.w is not None:
                deps.append((b.w, b.small))
        for tk in extra:
            deps.append((tk, True))
        deps += self.pend[e]
        self.pend[e] = []
        self.cnt[e] += inc
        tok = (e, self.cnt[e])
        fd = {}
        for (pe, v), force in deps:
            if pe == e and (not force or v >= tok[1]):
                continue
            if pe in ("sync", "gpsimd") and self.drains[pe]:
                import bisect
                i = bisect.bisect_left(self.drains[pe], v)
                if i < len(self.drains[pe]):
                    v = self.drains[pe][i]
            if v > fd.get(pe, 0):
                fd[pe] = v
        self.items[e].append((fd, fn, inc))
        for b in reads:
            b.r.append(tok)
        for b in writes:
            b.r = []
            b.w = tok
        return tok


def _tile_kxm(w):
    K, M = w.shape
    return np.ascontiguousarray(
        w.reshape(K // 128, 128, M).transpose(1, 0, 2).reshape(128, (K // 128) * M))


def host_prep(z):
    h = {}
    x = np.asarray(z["L0_attn_x"])
    h["x_fm"] = np.ascontiguousarray(x.transpose(0, 2, 1))

    for nm, key in [("wqT", "L0_attn_Wq"), ("wkT", "L0_attn_Wk"), ("wvT", "L0_attn_Wv"),
                    ("gwT", "L0_ffn_gate_w")]:
        h[nm] = _tile_kxm(np.ascontiguousarray(np.asarray(z[key]).T))
    h["gbias"] = np.ascontiguousarray(
        np.tile(np.asarray(z["L0_ffn_gate_bias"])[None, :], (128, 1)).astype(np.float32))

    blob = np.zeros((NE, 128, BLOBW), dtype=np.float32)
    Wg, Wu, Wd = np.asarray(z["L0_ffn_Wg"]), np.asarray(z["L0_ffn_Wu"]), np.asarray(z["L0_ffn_Wd"])
    gt = Wg.transpose(0, 2, 1).reshape(64, 8, 128, 704).transpose(0, 2, 1, 3)
    blob[:64, :, :SEC].reshape(64, 128, 8, IPAD)[:, :, :, :704] = gt
    ut = Wu.transpose(0, 2, 1).reshape(64, 8, 128, 704).transpose(0, 2, 1, 3)
    blob[:64, :, SEC:2 * SEC].reshape(64, 128, 8, IPAD)[:, :, :, :704] = ut
    dt_ = np.zeros((64, IPAD, 1024), dtype=np.float32)
    dt_[:, :704, :] = Wd.transpose(0, 2, 1)
    blob[:64, :, 2 * SEC:] = dt_.reshape(64, 6, 128, 1024).transpose(0, 2, 1, 3).reshape(64, 128, SEC)
    Sg, Su, Sd = np.asarray(z["L0_ffn_Sg"]), np.asarray(z["L0_ffn_Su"]), np.asarray(z["L0_ffn_Sd"])
    for hh in range(2):
        e = 64 + hh
        sg = Sg[hh * 704:(hh + 1) * 704].T
        su = Su[hh * 704:(hh + 1) * 704].T
        sd = Sd[:, hh * 704:(hh + 1) * 704].T
        blob[e, :, :SEC].reshape(128, 8, IPAD)[:, :, :704] = sg.reshape(8, 128, 704).transpose(1, 0, 2)
        blob[e, :, SEC:2 * SEC].reshape(128, 8, IPAD)[:, :, :704] = su.reshape(8, 128, 704).transpose(1, 0, 2)
        sdp = np.zeros((IPAD, 1024), dtype=np.float32)
        sdp[:704] = sd
        blob[e, :, 2 * SEC:] = sdp.reshape(6, 128, 1024).transpose(1, 0, 2).reshape(128, SEC)
    h["moeblob"] = blob

    basis = np.zeros((64, 64 * 64), dtype=BF)
    for e in range(64):
        basis[e, e * 64:(e + 1) * 64] = 1
    h["basis64"] = basis
    i2 = np.zeros((64, 128), dtype=BF)
    for k in range(64):
        i2[k, k] = 1
        i2[k, 64 + k] = 1
    h["i2t"] = i2
    h["ident"] = np.eye(128, dtype=np.float32)

    q = np.asarray(z["L1_attn_queries"])
    h["qwT"] = np.ascontiguousarray(
        q.reshape(4, 256, 4, 2, 128).transpose(4, 0, 2, 3, 1).reshape(128, 8192))
    cb = np.asarray(z["L1_attn_combiners"])
    h["combT"] = np.ascontiguousarray(
        cb.reshape(4, 8, 128, 256).transpose(2, 0, 1, 3).reshape(128, 8192))
    for nm, key in [("wrT", "L1_ffn_W_r"), ("wk2T", "L1_ffn_W_k"), ("wv2T", "L1_ffn_W_v")]:
        w = np.asarray(z[key])
        h[nm] = np.ascontiguousarray(
            w.reshape(8, 128, 8, 128).transpose(3, 2, 0, 1).reshape(128, 8192))
    h["ecap"] = (np.arange(64, dtype=np.float32) * CAP).reshape(64, 1)
    h["mur"] = np.ascontiguousarray(np.asarray(z["L1_ffn_mu_r"]).reshape(8, 128).T)
    h["muk"] = np.ascontiguousarray(np.asarray(z["L1_ffn_mu_k"]).reshape(8, 128).T)
    return h


class Carver:
    def __init__(self, ap_f32):
        self.f = ap_f32
        self.b = ap_f32.bitcast(BF16)

    def f32(self, off, n):
        return self.f[:, off // 4: off // 4 + n]

    def f32v(self, off, d0, d1):
        return self.f[:, off // 4: off // 4 + d0 * d1].rearrange("p (a b) -> p a b", a=d0)

    def bf16(self, off, n):
        return self.b[:, off // 2: off // 2 + n]

    def bf16v(self, off, d0, d1):
        return self.b[:, off // 2: off // 2 + d0 * d1].rearrange("p (a b) -> p a b", a=d0)


class Ctx:
    def B(self, name, small=False):
        if name not in self._bufs:
            self._bufs[name] = Buf(small=small)
        return self._bufs[name]


def declare_io(nc, phases):
    t = {}
    fused = not SPLIT_MODE

    def dram(name, shape, dtype, kind):
        t[name] = nc.dram_tensor(name, shape, dtype, kind=kind).ap()

    has = {p: (p in phases) for p in "ABCDE"}
    if has["A"] or has["C"]:
        dram("x_fm", [BPC, D, S], F32, "ExternalInput")
    if has["A"]:
        dram("gwT", [128, 512], F32, "ExternalInput")
        dram("gbias", [128, 64], F32, "ExternalInput")
    if has["B"]:
        dram("wqT", [128, 512], F32, "ExternalInput")
        dram("wkT", [128, 512], F32, "ExternalInput")
        dram("wvT", [128, 512], F32, "ExternalInput")
    dram("ident", [128, 128], F32, "ExternalInput")
    if has["C"]:
        dram("moeblob", [NE, 128, BLOBW], F32, "ExternalInput")
        dram("x_tm", [TOK, D], F32, "ExternalInput")
        dram("compact_x", [NROW_CX, D], BF16, "ExternalOutput")
        dram("compact_out", [NROW_CO, D], BF16, "ExternalOutput")
        dram("shared_h", [2 * TOK, D], BF16, "Internal")
    if has["A"]:
        dram("ecap", [64, 1], F32, "ExternalInput")
    if has["D"] or has["E"]:
        dram("qwT", [128, 8192], F32, "ExternalInput")
        dram("combT", [128, 8192], F32, "ExternalInput")
        dram("wrT", [128, 8192], F32, "ExternalInput")
        dram("wk2T", [128, 8192], F32, "ExternalInput")
        dram("wv2T", [128, 8192], F32, "ExternalInput")
        dram("mur", [128, 8], F32, "ExternalInput")
        dram("muk", [128, 8], F32, "ExternalInput")

    def bridge(name, shape, dtype, produced, consumed):
        if produced and consumed:
            dram(name, shape, dtype, "ExternalOutput" if DEBUG_EXPORT else "Internal")
        elif produced:
            dram(name, shape, dtype, "ExternalOutput")
        elif consumed:
            dram(name, shape, dtype, "ExternalInput")

    if (not SPLIT_MODE) or not has["A"]:
        bridge("xn_h", [128, 8, TOK], BF16, has["A"], has["B"] or has["C"])
        bridge("xn_tm_h", [TOK, D], BF16, has["A"], has["C"])
        bridge("idx_h", [TOK, 8], mybir.dt.int32, has["A"], has["C"])
        bridge("rw8_h", [TOK, 8], F32, has["A"], has["C"])
    else:
        dram("xn_h", [128, 8, TOK], BF16, "ExternalOutput")
        dram("xn_tm_h", [TOK, D], BF16, "ExternalOutput")
        dram("idx_h", [TOK, 8], mybir.dt.int32, "ExternalOutput")
        dram("rw8_h", [TOK, 8], F32, "ExternalOutput")
    bridge("rwT_h", [64, TOK], BF16, has["A"], has["C"])
    bridge("sdup_h", [128, TOK], F32, has["B"], has["C"])
    bridge("x1_h", [BPC, D, S], F32, has["C"], has["D"] or has["E"])
    if has["D"] or has["E"]:
        bridge("xn1_h", [128, 8, TOK], BF16, has["D"], has["E"])
    if has["E"]:
        dram("out_fm", [BPC, D, S], F32, "ExternalOutput")
    return t


def emit_static(c, phases):
    P, t = c.P, c.t
    c.bident = c.B("ident")
    c.bidentb = c.B("identb")
    c.bones = c.B("ones", small=True)
    c.beps = c.B("eps", small=True)
    P.op("sync", lambda E, o=c.identf, i=t["ident"]: E.dma_start(out=o, in_=i),
         writes=[c.bident], inc=16)
    P.op("scalar", lambda E, o=c.identb, i=c.identf: E.copy(o, i),
         reads=[c.bident], writes=[c.bidentb])
    P.op("vector", lambda E, o=c.ones_col: E.memset(o, 1.0), writes=[c.bones])
    P.op("vector", lambda E, o=c.ones_row: E.memset(o, 1.0), writes=[c.bones])
    P.op("vector", lambda E, o=c.epsA: E.memset(o, 1e-5), writes=[c.beps])

    if "A" in phases:
        c.bgw = c.B("gw")
        c.bgbias = c.B("gbias")
        c.bncol = c.B("ncol", small=True)
        P.op("sync", lambda E, o=c.gwT, i=t["gwT"]: E.dma_start(out=o, in_=i), writes=[c.bgw], inc=16)
        P.op("sync", lambda E, o=c.gbias, i=t["gbias"]: E.dma_start(out=o, in_=i), writes=[c.bgbias], inc=16)
        pcs = c.psum[0]
        bcs = c.pbuf[0]
        for et in range(8):
            P.op("tensor", lambda E, o=pcs[0:64, 0:1], l=c.gwT[:, et * 64:(et + 1) * 64],
                 r=c.ones_col, s=(et == 0), p=(et == 7): E.matmul(o, l, r, start=s, stop=p),
                 reads=[c.bgw, c.bones], writes=[bcs])
        bnc = c.B("ncol_c", small=True)
        P.op("vector", lambda E, o=c.ncol_c[0:64, 0:1], i=pcs[0:64, 0:1]:
             E.tensor_scalar(o, i, -1.0, None, ALU.mult), reads=[bcs], writes=[bnc])
        P.op("tensor", lambda E, o=c.psum[1][0:1, 0:64], i=c.ncol_c[0:64, 0:1],
             idn=c.identf[0:64, 0:64]: E.transpose(o, i, idn),
             reads=[bnc, c.bident], writes=[bcs])
        P.op("vector", lambda E, o=c.ncol_row, i=c.psum[1][0:1, 0:64]:
             E.tensor_copy(o, i), reads=[bcs], writes=[c.bncol])
        c.becap = c.B("ecap", small=True)
        P.op("sync", lambda E, o=c.ecap_col, i=t["ecap"]: E.dma_start(out=o, in_=i),
             writes=[c.becap], inc=16)

    if "B" in phases:
        c.bwqkv = c.B("wqkv")
        for i, nm in enumerate(["wqT", "wkT", "wvT"]):
            P.op("gpsimd", lambda E, o=c.wqkvb[:, i * 512:(i + 1) * 512], src=t[nm]:
                 E.dma_start(out=o, in_=src), writes=[c.bwqkv], inc=16)

    if ("D" in phases) or ("E" in phases):
        c.bmu = c.B("mu", small=True)
        P.op("sync", lambda E, o=c.mur, i=t["mur"]: E.dma_start(out=o, in_=i), writes=[c.bmu], inc=16)
        P.op("sync", lambda E, o=c.muk, i=t["muk"]: E.dma_start(out=o, in_=i), writes=[c.bmu], inc=16)
        P.op("vector", lambda E, o=c.omur, i=c.mur: E.tensor_scalar(o, i, -1.0, 1.0, ALU.mult, ALU.add),
             reads=[c.bmu], writes=[c.bmu])
        P.op("vector", lambda E, o=c.omuk, i=c.muk: E.tensor_scalar(o, i, -1.0, 1.0, ALU.mult, ALU.add),
             reads=[c.bmu], writes=[c.bmu])


LN_O = 72 * KB


def emit_ln_chunk(c, ci, src_dram, src_buf, dst_dram, dst_buf, with_routing):
    P, t, dyn = c.P, c.t, c.dyn
    b = ci // (S // CH)
    off = (ci % (S // CH)) * CH
    goff = ci * CH
    O = LN_O

    xld = dyn.f32v(O, 8, 512)
    x2 = dyn.f32v(O + 16 * KB, 8, 512)
    tmp1 = dyn.f32(O + 32 * KB, 512)
    xnst = dyn.bf16v(O + 34 * KB, 8, 512)
    m_row = dyn.f32(O + 42 * KB, 512)[0:1, :]
    sd_row = dyn.f32(O + 44 * KB, 512)[0:1, :]
    rstd_row = dyn.f32(O + 46 * KB, 512)[0:1, :]
    mr_row = dyn.f32(O + 48 * KB, 512)[0:1, :]
    ex2_row = dyn.f32(O + 50 * KB, 512)[0:1, :]
    tmpn = dyn.f32(O + 52 * KB, 512)

    bxld = c.B("ln_xld")
    bx2 = c.B("ln_x2")
    bxnst = c.B("ln_xnst")
    bstat = c.B("ln_stat", small=True)
    btmpn = c.B("ln_tmpn")
    btmp1 = c.B("ln_tmp1", small=True)
    psx, psx2, pbm, pbr = c.psum[0], c.psum[1], c.psum[2], c.psum[3]
    bpsx, bpsx2, bpbm, bpbr = c.pbuf[0], c.pbuf[1], c.pbuf[2], c.pbuf[3]

    src = src_dram[b, :, off:off + CH].rearrange("(n p) m -> p n m", p=128)
    P.op("sync", lambda E, o=xld, i=src: E.dma_start(out=o, in_=i),
         reads=[src_buf], writes=[bxld], inc=16)
    P.drain("sync")
    P.op("scalar", lambda E, o=x2, i=xld: E.activation(o, i, AF.Square),
         reads=[bxld], writes=[bx2])
    for et in range(8):
        P.op("tensor", lambda E, o=psx[0:1, :], l=c.ones_col, r=xld[:, et, :],
             s=(et == 0), p=(et == 7): E.matmul(o, l, r, start=s, stop=p),
             reads=[c.bones, bxld], writes=[bpsx])
    for et in range(8):
        P.op("tensor", lambda E, o=psx2[0:1, :], l=c.ones_col, r=x2[:, et, :],
             s=(et == 0), p=(et == 7): E.matmul(o, l, r, start=s, stop=p),
             reads=[c.bones, bx2], writes=[bpsx2])
    P.op("vector", lambda E, o=m_row, i=psx[0:1, :]:
         E.tensor_scalar(o, i, 1.0 / D, None, ALU.mult), reads=[bpsx], writes=[bstat])
    P.op("vector", lambda E, o=ex2_row, i=psx2[0:1, :]:
         E.tensor_scalar(o, i, 1.0 / D, None, ALU.mult), reads=[bpsx2], writes=[bstat])
    P.op("vector", lambda E, o=tmp1[0:1, :], i=m_row, j=m_row:
         E.tensor_mul(o, i, j), reads=[bstat], writes=[btmp1])
    P.op("vector", lambda E, o=ex2_row, i=ex2_row, j=tmp1[0:1, :]:
         E.tensor_sub(o, i, j), reads=[bstat, btmp1], writes=[bstat])
    P.op("scalar", lambda E, o=sd_row, i=ex2_row, bi=c.epsA[0:1, 0:1]:
         E.activation(o, i, AF.Sqrt, bias=bi), reads=[bstat, c.beps], writes=[bstat])
    P.op("vector", lambda E, o=rstd_row, i=sd_row: E.reciprocal(o, i),
         reads=[bstat], writes=[bstat])
    if with_routing:
        P.op("vector", lambda E, o=mr_row, i=m_row, j=rstd_row:
             E.tensor_mul(o, i, j), reads=[bstat], writes=[bstat])
    P.op("tensor", lambda E, o=pbm[:, :], l=c.ones_row, r=m_row:
         E.matmul(o, l, r, start=True, stop=True), reads=[c.bones, bstat], writes=[bpbm])
    P.op("tensor", lambda E, o=pbr[:, :], l=c.ones_row, r=rstd_row:
         E.matmul(o, l, r, start=True, stop=True), reads=[c.bones, bstat], writes=[bpbr])
    for et in range(8):
        P.op("vector", lambda E, o=tmpn, i=xld[:, et, :], j=pbm[:, :]:
             E.tensor_sub(o, i, j), reads=[bxld, bpbm], writes=[btmpn])
        P.op("vector", lambda E, o=xnst[:, et, :], i=tmpn, j=pbr[:, :]:
             E.tensor_mul(o, i, j), reads=[btmpn, bpbr], writes=[bxnst])
    dst = dst_dram[:, :, goff:goff + CH]
    P.op("sync", lambda E, o=dst, i=xnst: E.dma_start(out=o, in_=i),
         reads=[bxnst], writes=[dst_buf], inc=16)

    if with_routing:
        xtm_st = dyn.bf16v(O + 62 * KB, 4, 1024)
        bxtmst = c.B("ln_xtmst")
        ptrxb = [c.psum[6].bitcast(BF16), c.psum[7].bitcast(BF16)]
        bptrx2 = [c.pbuf[6], c.pbuf[7]]
        for tt in range(4):
            bk = tt % 2
            for et in range(8):
                P.op("tensor", lambda E, o=ptrxb[bk][0:128, et * 128:(et + 1) * 128],
                     i=xnst[:, et, tt * 128:(tt + 1) * 128], idn=c.identb:
                     E.transpose(o, i, idn), reads=[bxnst, c.bidentb], writes=[bptrx2[bk]])
            P.op("vector", lambda E, o=xtm_st[:, tt, :], i=ptrxb[bk][0:128, 0:1024]:
                 E.tensor_copy(o, i), reads=[bptrx2[bk]], writes=[bxtmst])
        dst2 = c.t["xn_tm_h"][goff:goff + CH, :].rearrange("(tt p) cc -> p tt cc", p=128)
        P.op("sync", lambda E, o=dst2, i=xtm_st: E.dma_start(out=o, in_=i),
             reads=[bxtmst], writes=[c.dbuf["xn_tm_h"]], inc=16)

    if not with_routing:
        return

    plg, pcor = c.psum[4], c.psum[5]
    bplg, bpcor = c.pbuf[4], c.pbuf[5]
    for et in range(8):
        P.op("tensor", lambda E, o=plg[0:64, :], l=c.gwT[:, et * 64:(et + 1) * 64],
             r=xld[:, et, :], s=(et == 0), p=(et == 7): E.matmul(o, l, r, start=s, stop=p),
             reads=[c.bgw, bxld], writes=[bplg])
    P.op("tensor", lambda E, o=pcor[0:64, :], l=c.ncol_row[:, 0:64], r=mr_row:
         E.matmul(o, l, r, start=True, stop=True), reads=[c.bncol, bstat], writes=[bpcor])
    lg0 = dyn.f32(O + 54 * KB, 512)[0:64, :]
    lg1 = dyn.f32(O + 56 * KB, 512)[0:64, :]
    blg = c.B("ln_lg")
    P.op("scalar", lambda E, o=lg0, i=plg[0:64, :]: E.copy(o, i), reads=[bplg], writes=[blg])
    P.op("vector", lambda E, o=lg1, i=lg0, j=pbr[0:64, :]: E.tensor_mul(o, i, j),
         reads=[blg, bpbr], writes=[blg])
    P.op("vector", lambda E, o=lg1, i=lg1, j=pcor[0:64, :]: E.tensor_add(o, i, j),
         reads=[bpcor, blg], writes=[blg])
    ptr = c.psum[6]
    bptr = c.pbuf[6]
    lg_tm = dyn.f32v(O + 58 * KB, 4, 64)
    blgtm = c.B("ln_lgtm")
    for j in range(4):
        P.op("tensor", lambda E, o=ptr[0:128, 0:64], i=lg1[:, j * 128:(j + 1) * 128],
             idn=c.identf[0:64, 0:64]: E.transpose(o, i, idn),
             reads=[blg, c.bident], writes=[bptr])
        P.op("vector", lambda E, o=lg_tm[:, j, :], i=ptr[0:128, 0:64]:
             E.tensor_copy(o, i), reads=[bptr], writes=[blgtm])
    emit_routing(c, ci, lg_tm, blgtm, O + 71 * KB)


def emit_routing(c, ci, lg_tm, blgtm, OB):
    P, t, dyn = c.P, c.t, c.dyn
    goff = ci * CH

    scores = dyn.f32v(OB, 4, 64)
    sfc = dyn.f32v(OB + 1024, 4, 64)
    msk = dyn.f32v(OB + 2048, 4, 64)
    cur = dyn.f32v(OB + 3072, 4, 64)
    sel = dyn.f32v(OB + 4096, 4, 64)
    sel8 = dyn.f32v(OB + 5120, 4, 64)
    w_ = dyn.f32v(OB + 6144, 4, 64)
    rw = dyn.f32v(OB + 7168, 4, 64)
    m1 = dyn.f32v(OB + 8192, 4, 8)
    mtmp = dyn.f32v(OB + 8320, 4, 8)
    gs = dyn.f32v(OB + 8448, 4, 8)
    curg = dyn.f32v(OB + 8576, 4, 8)
    selg = dyn.f32v(OB + 8704, 4, 8)
    gmask = dyn.f32v(OB + 8832, 4, 8)
    mm_ = dyn.f32v(OB + 8960, 4, 1)
    den = dyn.f32v(OB + 9088, 4, 1)
    rcp = dyn.f32v(OB + 9216, 4, 1)
    rwTs = dyn.f32(OB + 9344, 512)[0:64, :]
    rwTb_st = dyn.bf16(OB + 11392, 512)[0:64, :]

    bS = c.B("route", small=True)

    def V(fn):
        P.op("vector", fn, reads=[bS], writes=[bS])

    P.op("scalar", lambda E, o=scores, i=lg_tm: E.activation(o, i, AF.Sigmoid),
         reads=[blgtm], writes=[bS])
    V(lambda E: E.tensor_add(sfc, scores, c.gbias.unsqueeze(1).broadcast_to([128, 4, 64])))
    V(lambda E: E.tensor_reduce(m1, sfc.rearrange("p a (g k) -> p a g k", k=8), AX.X, ALU.max))
    V(lambda E: E.tensor_tensor(msk.rearrange("p a (g k) -> p a g k", k=8),
                                sfc.rearrange("p a (g k) -> p a g k", k=8),
                                m1.unsqueeze(3).broadcast_to([128, 4, 8, 8]), ALU.is_ge))
    V(lambda E: E.scalar_tensor_tensor(msk, msk, -1e30, sfc, ALU.mult, ALU.add))
    V(lambda E: E.tensor_reduce(mtmp, msk.rearrange("p a (g k) -> p a g k", k=8), AX.X, ALU.max))
    V(lambda E: E.tensor_add(gs, m1, mtmp))
    V(lambda E: E.memset(gmask, 0.0))
    V(lambda E: E.tensor_copy(curg, gs))
    for _ in range(4):
        V(lambda E: E.tensor_reduce(mm_, curg, AX.X, ALU.max))
        V(lambda E: E.tensor_tensor(selg, curg, mm_.broadcast_to([128, 4, 8]), ALU.is_ge))
        V(lambda E: E.tensor_add(gmask, gmask, selg))
        V(lambda E: E.scalar_tensor_tensor(curg, selg, -1e30, curg, ALU.mult, ALU.add))
    V(lambda E: E.tensor_tensor(msk.rearrange("p a (g k) -> p a g k", k=8),
                                sfc.rearrange("p a (g k) -> p a g k", k=8),
                                gmask.unsqueeze(3).broadcast_to([128, 4, 8, 8]), ALU.mult))
    V(lambda E: E.memset(sel8, 0.0))
    V(lambda E: E.tensor_copy(cur, msk))
    for _ in range(8):
        V(lambda E: E.tensor_reduce(mm_, cur, AX.X, ALU.max))
        V(lambda E: E.tensor_tensor(sel, cur, mm_.broadcast_to([128, 4, 64]), ALU.is_ge))
        V(lambda E: E.tensor_add(sel8, sel8, sel))
        V(lambda E: E.scalar_tensor_tensor(cur, sel, -1e30, cur, ALU.mult, ALU.add))
    V(lambda E: E.tensor_mul(w_, sel8, scores))
    V(lambda E: E.tensor_reduce(den, w_, AX.X, ALU.add))
    V(lambda E: E.tensor_scalar(den, den, 1e-20, None, ALU.add))
    V(lambda E: E.reciprocal(rcp, den))
    V(lambda E: E.tensor_scalar(rcp, rcp, 2.5, None, ALU.mult))
    V(lambda E: E.tensor_tensor(rw, w_, rcp.broadcast_to([128, 4, 64]), ALU.mult))
    ptr2 = c.psum[7]
    bptr2 = c.pbuf[7]
    for j in range(4):
        P.op("tensor", lambda E, o=ptr2[0:64, 0:128], i=rw[:, j, :], idn=c.identf:
             E.transpose(o, i, idn), reads=[bS, c.bident], writes=[bptr2])
        P.op("vector", lambda E, o=rwTs[:, j * 128:(j + 1) * 128], i=ptr2[0:64, 0:128]:
             E.tensor_copy(o, i), reads=[bptr2], writes=[bS])
    V(lambda E: E.tensor_copy(rwTb_st, rwTs))
    P.op("sync", lambda E, o=t["rwT_h"][:, goff:goff + CH], i=rwTb_st:
         E.dma_start(out=o, in_=i), reads=[bS], writes=[c.dbuf["rwT_h"]], inc=16)


def emit_phase_a(c):
    for ci in range(NCH):
        emit_ln_chunk(c, ci, c.t["x_fm"], c.dbuf["x_fm"], c.t["xn_h"], c.dbuf["xn_h"], True)




def emit_phase_a2(c):
    P, t, dyn = c.P, c.t, c.dyn
    O = LN_O
    rwt_s = dyn.bf16(O, TOK)[0:64, :]
    mask = dyn.f32(O + 8 * KB, TOK)[0:64, :]
    cumA = dyn.f32(O + 24 * KB, TOK)[0:64, :]
    cumB = dyn.f32(O + 40 * KB, TOK)[0:64, :]
    tgt = dyn.f32(O + 56 * KB, TOK)[0:64, :]
    ovfm = dyn.f32(O + 72 * KB, TOK)[0:64, :]
    tgt_tm = dyn.f32v(O + 88 * KB, 32, 64)
    rw_tm = dyn.f32v(O + 96 * KB, 32, 64)
    cur = dyn.f32v(O + 104 * KB, 32, 64)
    eq = dyn.f32v(O + 112 * KB, 32, 64)
    tmp_e = dyn.f32v(O + 120 * KB, 32, 64)
    SM = O + 24 * KB + 4 * KB
    mn = dyn.f32v(SM, 32, 1)
    gbv = dyn.f32v(SM + 256, 32, 1)
    tgt8 = dyn.f32v(SM + 512, 32, 8)
    rw8 = dyn.f32v(SM + 1536, 32, 8)
    idx_i = dyn.f.bitcast(mybir.dt.int32)[:, (SM + 2560) // 4:(SM + 2560) // 4 + 256].rearrange(
        "p (a b) -> p a b", a=32)

    bS = c.B("a2", small=True)

    def V(fn):
        P.op("vector", fn, reads=[bS], writes=[bS])

    P.op("sync", lambda E, o=rwt_s, i=t["rwT_h"]: E.dma_start(out=o, in_=i),
         reads=[c.dbuf["rwT_h"]], writes=[bS], inc=16)
    P.drain("sync")
    V(lambda E: E.tensor_single_scalar(mask, rwt_s, 0.0, ALU.is_gt))
    V(lambda E: E.tensor_copy(cumA, mask))
    srcb, dstb = cumA, cumB
    sh = 1
    while sh < TOK:
        V(lambda E, s_=srcb, d_=dstb, ss=sh: E.tensor_add(d_[:, ss:], s_[:, ss:], s_[:, :TOK - ss]))
        V(lambda E, s_=srcb, d_=dstb, ss=sh: E.tensor_copy(d_[:, 0:ss], s_[:, 0:ss]))
        srcb, dstb = dstb, srcb
        sh *= 2
    cum = srcb
    V(lambda E: E.tensor_sub(tgt, cum, mask))
    V(lambda E: E.tensor_single_scalar(ovfm, tgt, float(CAP), ALU.is_ge))
    V(lambda E: E.tensor_scalar(tgt, tgt, c.ecap_col[0:64, 0:1], None, ALU.add))
    V(lambda E: E.tensor_scalar(tgt, tgt, BIGF, None, ALU.add))
    V(lambda E: E.scalar_tensor_tensor(tgt, mask, -BIGF, tgt, ALU.mult, ALU.add))
    V(lambda E: E.scalar_tensor_tensor(tgt, ovfm, BIGF, tgt, ALU.mult, ALU.add))
    V(lambda E: E.tensor_scalar_min(tgt, tgt, float(TRASH)))
    ptrf = c.psum[6]
    ptrb2 = c.psum[7].bitcast(BF16)
    bptrf, bptrb = c.pbuf[6], c.pbuf[7]
    for j in range(32):
        P.op("tensor", lambda E, o=ptrf[0:128, 0:64], i=tgt[:, j * 128:(j + 1) * 128],
             idn=c.identf[0:64, 0:64]: E.transpose(o, i, idn),
             reads=[bS, c.bident], writes=[bptrf])
        P.op("vector", lambda E, o=tgt_tm[:, j, :], i=ptrf[0:128, 0:64]:
             E.tensor_copy(o, i), reads=[bptrf], writes=[bS])
        P.op("tensor", lambda E, o=ptrb2[0:128, 0:64], i=rwt_s[:, j * 128:(j + 1) * 128],
             idn=c.identb[0:64, 0:64]: E.transpose(o, i, idn),
             reads=[bS, c.bidentb], writes=[bptrb])
        P.op("vector", lambda E, o=rw_tm[:, j, :], i=ptrb2[0:128, 0:64]:
             E.tensor_copy(o, i), reads=[bptrb], writes=[bS])
    V(lambda E: E.tensor_copy(cur, tgt_tm))
    for j in range(8):
        V(lambda E: E.tensor_reduce(mn, cur, AX.X, ALU.min))
        V(lambda E, jj=j: E.tensor_copy(tgt8[:, :, jj:jj + 1], mn))
        V(lambda E: E.tensor_tensor(eq, cur, mn.broadcast_to([128, 32, 64]), ALU.is_le))
        V(lambda E: E.tensor_mul(tmp_e, eq, rw_tm))
        V(lambda E, jj=j: E.tensor_reduce(rw8[:, :, jj:jj + 1], tmp_e, AX.X, ALU.add))
        V(lambda E: E.tensor_single_scalar(gbv, mn, float(NSLOT), ALU.is_lt))
        V(lambda E, jj=j: E.tensor_tensor(rw8[:, :, jj:jj + 1], rw8[:, :, jj:jj + 1],
                                          gbv, ALU.mult))
        V(lambda E: E.scalar_tensor_tensor(cur, eq, BIGF, cur, ALU.mult, ALU.add))
    V(lambda E: E.tensor_copy(idx_i, tgt8))
    P.op("sync", lambda E, o=t["idx_h"].rearrange("(tt p) j -> p tt j", p=128), i=idx_i:
         E.dma_start(out=o, in_=i), reads=[bS], writes=[c.dbuf["idx_h"]], inc=16)
    P.op("sync", lambda E, o=t["rw8_h"].rearrange("(tt p) j -> p tt j", p=128), i=rw8:
         E.dma_start(out=o, in_=i), reads=[bS], writes=[c.dbuf["rw8_h"]], inc=16)


def emit_phase_b(c):
    P, t, dyn = c.P, c.t, c.dyn
    O = LN_O
    xnb = dyn.bf16v(O, 8, S)
    scf = dyn.f32(O + 32 * KB, S)
    attn_t = [dyn.bf16(O + 40 * KB + i * 4 * KB, S) for i in range(2)]
    qTb = dyn.bf16(O + 48 * KB, S)[0:64, :]
    kTb = dyn.bf16(O + 52 * KB, S)[0:64, :]
    vTb = dyn.bf16(O + 56 * KB, S)[0:64, :]
    v_tm = dyn.bf16(O + 60 * KB, 16 * 128)
    v_sc = [dyn.bf16(O + 64 * KB + i * 256, 128) for i in range(2)]
    sdst = dyn.f32(O + 66 * KB, S)
    nmx = dyn.f32(O + 74 * KB, 16)[:, 0:1]
    rs = dyn.f32(O + 74 * KB + 64, 16)[:, 0:1]
    rcq = dyn.f32(O + 74 * KB + 128, 16)[:, 0:1]

    pq = c.psum[0]
    ptr = c.psum[1]
    ptrb = ptr.bitcast(BF16)
    psc = [c.psum[2], c.psum[3]]
    pout = [c.psum[4 + i] for i in range(4)]

    bxnb = c.B("b_xnb")
    bq, bk, bv = c.B("b_q"), c.B("b_k"), c.B("b_v")
    bpq = c.pbuf[0]
    bvtm = c.B("b_vtm")
    bptr = c.pbuf[1]
    bscf = c.B("b_scf")
    bnmx = c.B("b_nmx", small=True)
    brs = c.B("b_rs", small=True)
    batt = [c.B("b_att0"), c.B("b_att1")]
    bvsc = [c.B("b_vsc0"), c.B("b_vsc1")]
    bpsc = [c.pbuf[2], c.pbuf[3]]
    bpout = c.pbuf[4]
    bpout2 = [c.pbuf[4], c.pbuf[5], c.pbuf[6], c.pbuf[7]]
    bsd = c.B("b_sd")

    for b in range(BPC):
        P.op("sync", lambda E, o=xnb, i=t["xn_h"][:, :, b * S:(b + 1) * S]:
             E.dma_start(out=o, in_=i), reads=[c.dbuf["xn_h"]], writes=[bxnb], inc=16)
        P.drain("sync")
        for wi, (dst, bd) in enumerate([(qTb, bq), (kTb, bk), (vTb, bv)]):
            for ic in range(4):
                for et in range(8):
                    P.op("tensor", lambda E, o=pq[0:64, :],
                         l=c.wqkvb[:, wi * 512 + et * 64: wi * 512 + (et + 1) * 64],
                         r=xnb[:, et, ic * 512:(ic + 1) * 512],
                         s=(et == 0), p=(et == 7): E.matmul(o, l, r, start=s, stop=p),
                         reads=[c.bwqkv, bxnb], writes=[bpq])
                P.op("scalar", lambda E, o=dst[:, ic * 512:(ic + 1) * 512], i=pq[0:64, :]:
                     E.copy(o, i), reads=[bpq], writes=[bd])
        for kt in range(16):
            P.op("tensor", lambda E, o=ptrb[0:128, 0:64], i=vTb[:, kt * 128:(kt + 1) * 128],
                 idn=c.identb[0:64, 0:64]: E.transpose(o, i, idn),
                 reads=[bv, c.bidentb], writes=[bptr])
            P.op("vector", lambda E, o=v_tm[:, kt * 128:kt * 128 + 64], i=ptrb[0:128, 0:64]:
                 E.tensor_copy(o, i), reads=[bptr], writes=[bvtm])
            P.op("vector", lambda E, o=v_tm[:, kt * 128 + 64:(kt + 1) * 128], i=ptrb[0:128, 0:64]:
                 E.tensor_copy(o, i), reads=[bptr], writes=[bvtm])
        for kt in range(16):
            par = kt % 2
            for ic in range(4):
                P.op("tensor", lambda E, o=psc[ic % 2][:, :], l=kTb[:, kt * 128:(kt + 1) * 128],
                     r=qTb[:, ic * 512:(ic + 1) * 512]: E.matmul(o, l, r, start=True, stop=True),
                     reads=[bk, bq], writes=[bpsc[ic % 2]])
                P.op("scalar", lambda E, o=scf[:, ic * 512:(ic + 1) * 512], i=psc[ic % 2][:, :]:
                     E.copy(o, i), reads=[bpsc[ic % 2]], writes=[bscf])
            P.op("vector", lambda E, o=nmx, i=scf:
                 E.reduce_max(o, i, axis=AX.X, negate=True), reads=[bscf], writes=[bnmx])
            P.op("scalar", lambda E, o=attn_t[par], i=scf, bi=nmx, ac=rs:
                 E.activation(o, i, AF.Exp, bias=bi, accum_out=ac),
                 reads=[bscf, bnmx], writes=[batt[par], brs])
            P.op("vector", lambda E, o=rcq, i=rs: E.reciprocal(o, i),
                 reads=[brs], writes=[brs])
            P.op("vector", lambda E, o=v_sc[par], i=v_tm[:, kt * 128:(kt + 1) * 128], sc=rcq:
                 E.tensor_scalar(o, i, sc, None, ALU.mult),
                 reads=[bvtm, brs], writes=[bvsc[par]])
            for ic in range(4):
                P.op("tensor", lambda E, o=pout[ic][:, :], l=v_sc[par],
                     r=attn_t[par][:, ic * 512:(ic + 1) * 512],
                     s=(kt == 0), p=(kt == 15): E.matmul(o, l, r, start=s, stop=p),
                     reads=[bvsc[par], batt[par]], writes=[bpout2[ic]])
        for ic in range(4):
            P.op("scalar", lambda E, o=sdst[:, ic * 512:(ic + 1) * 512], i=pout[ic][:, :]:
                 E.copy(o, i), reads=[bpout2[ic]], writes=[bsd])
        P.op("sync", lambda E, o=t["sdup_h"][:, b * S:(b + 1) * S], i=sdst:
             E.dma_start(out=o, in_=i), reads=[bsd], writes=[c.dbuf["sdup_h"]], inc=16)


def emit_phase_c(c):
    P, t, dyn = c.P, c.t, c.dyn
    wbuf = [dyn.bf16(0 + i * 36864, BLOBW) for i in range(2)]
    cx_tm = [dyn.bf16v(72 * KB + i * 8 * KB, 4, 1024) for i in range(2)]
    rh = [dyn.bf16v(88 * KB + i * 8 * KB, 8, 512) for i in range(2)]
    hb = [dyn.bf16(104 * KB + i * 6 * KB, 6 * 512) for i in range(2)]
    sg = [dyn.f32(116 * KB + i * 2 * KB, 512) for i in range(2)]
    co_st = [dyn.bf16v(120 * KB + i * 8 * KB, 4, 1024) for i in range(2)]
    sc_tile = [dyn.bf16(152 * KB + i * 2 * KB, 1024) for i in range(2)]
    idxt = [dyn.f.bitcast(mybir.dt.int32)[:, (156 * KB + i * 64) // 4:(156 * KB + i * 64) // 4 + 8]
            for i in range(2)]
    sdup_s = dyn.f32(157 * KB, TOK)
    Gt = [dyn.bf16(72 * KB + i * 2 * KB, 1024) for i in range(4)]
    acc = [dyn.f32(80 * KB + i * 4 * KB, 1024) for i in range(2)]
    gsb = [dyn.f32(88 * KB + i * 4 * KB, 1024) for i in range(2)]
    x_tile = [dyn.f32(96 * KB + i * 4 * KB, 1024) for i in range(2)]
    x1f_st = [dyn.f32v(104 * KB + i * 4 * KB, 8, 128) for i in range(2)]
    s_small = [dyn.f32(112 * KB + i * 512, 64) for i in range(2)]
    sht = [dyn.bf16(114 * KB + i * 2 * KB, 1024) for i in range(4)]
    idxt2 = [dyn.f.bitcast(mybir.dt.int32)[:, (123 * KB + i * 64) // 4:(123 * KB + i * 64) // 4 + 8]
             for i in range(2)]
    rw8t = [dyn.f32(124 * KB + i * 512, 8) for i in range(2)]

    bwbuf = [c.B("c_wb0"), c.B("c_wb1")]
    bcx = [c.B("c_cx0"), c.B("c_cx1")]
    brh = [c.B("c_rh0"), c.B("c_rh1")]
    bhb = [c.B("c_hb0"), c.B("c_hb1")]
    bsg = [c.B("c_sg0"), c.B("c_sg1")]
    bco = [c.B("c_co0"), c.B("c_co1")]
    bsc = [c.B("c_sc0"), c.B("c_sc1")]
    bidxt = [c.B("c_idxt0"), c.B("c_idxt1")]
    bsdup = c.B("c_sdup")
    bG = [c.B(f"c_G{i}") for i in range(4)]
    bacc = [c.B("c_acc0"), c.B("c_acc1")]
    bgs = [c.B("c_gs0"), c.B("c_gs1")]
    bxt = [c.B("c_xt0"), c.B("c_xt1")]
    bx1f = [c.B("c_x1f0"), c.B("c_x1f1")]
    bss = [c.B("c_ss0"), c.B("c_ss1")]
    bsh = [c.B(f"c_sh{i}") for i in range(4)]
    bidxt2 = [c.B("c_idxt2a"), c.B("c_idxt2b")]
    brw8t = [c.B("c_rw8t0"), c.B("c_rw8t1")]

    pag = [c.psum[0], c.psum[1]]
    pau = [c.psum[2], c.psum[3]]
    pdn = [c.psum[4], c.psum[5]]
    ptr_ = c.psum[6]
    ptrb_ = ptr_.bitcast(BF16)
    ptr2_ = c.psum[7]
    bpag = [c.pbuf[0], c.pbuf[1]]
    bpau = [c.pbuf[2], c.pbuf[3]]
    bpdn = [c.pbuf[4], c.pbuf[5]]
    bptr = c.pbuf[6]
    bptr2 = c.pbuf[7]

    P.op("sync", lambda E, o=sdup_s, i=t["sdup_h"]: E.dma_start(out=o, in_=i),
         reads=[c.dbuf["sdup_h"]], writes=[bsdup], inc=16)

    def wdma(e):
        P.op("gpsimd", lambda E, o=wbuf[e % 2], i=t["moeblob"][e]:
             E.dma_start(out=o, in_=i), writes=[bwbuf[e % 2]], inc=16)

    wdma(0)
    wdma(1)

    for tt in range(32):
        par = tt % 2
        P.op("sync", lambda E, o=sc_tile[par],
             i=t["xn_tm_h"][tt * 128:(tt + 1) * 128, :]:
             E.dma_start(out=o, in_=i), reads=[c.dbuf["xn_tm_h"]], writes=[bsc[par]], inc=16)
        P.op("sync", lambda E, o=idxt[par], i=t["idx_h"][tt * 128:(tt + 1) * 128, :]:
             E.dma_start(out=o, in_=i), reads=[c.dbuf["idx_h"]], writes=[bidxt[par]], inc=16)
        P.drain("sync")
        for j in range(8):
            P.op("gpsimd", lambda E, o=t["compact_x"], off=idxt[par][:, j:j + 1], i=sc_tile[par]:
                 E.indirect_dma_start(out=o, out_offset=IndirectOffsetOnAxis(off, 0),
                                      in_=i, in_offset=None),
                 reads=[bsc[par], bidxt[par]], writes=[c.dbuf["compact_x"]], inc=16)

    P.drain("gpsimd")

    slots = [("e", e, ch) for e in range(64) for ch in range(3)] +             [("s", 64 + hh, c8) for hh in range(2) for c8 in range(8)]

    def emit_load(k):
        kind, e, ch = slots[k]
        par = k % 2
        if kind == "e":
            base = e * CAP + ch * 512
            P.op("sync", lambda E, o=cx_tm[par],
                 i=t["compact_x"][base:base + 512, :].rearrange("(tt p) cc -> p tt cc", p=128):
                 E.dma_start(out=o, in_=i),
                 reads=[c.dbuf["compact_x"]], writes=[bcx[par]], inc=16)
            P.drain("sync")
        else:
            goff = ch * 512
            P.op("sync", lambda E, o=rh[par], i=t["xn_h"][:, :, goff:goff + 512]:
                 E.dma_start(out=o, in_=i), reads=[c.dbuf["xn_h"]], writes=[brh[par]], inc=16)
            P.drain("sync")

    ptrbanks = [ptr_.bitcast(BF16), ptr2_.bitcast(BF16)]
    bptrx = [c.pbuf[6], c.pbuf[7]]

    def emit_tr(k):
        kind, e, ch = slots[k]
        if kind != "e":
            return
        par = k % 2
        for tt in range(4):
            bk = tt % 2
            pb = ptrbanks[bk]
            for ct in range(8):
                P.op("tensor", lambda E, o=pb[0:128, ct * 128:(ct + 1) * 128],
                     i=cx_tm[par][:, tt, ct * 128:(ct + 1) * 128], idn=c.identb:
                     E.transpose(o, i, idn), reads=[bcx[par], c.bidentb],
                     writes=[bptrx[bk]])
            P.op("vector", lambda E, o=rh[par][:, :, tt * 128:(tt + 1) * 128],
                 i=pb[0:128, 0:1024].rearrange("p (a b) -> p a b", a=8):
                 E.tensor_copy(o, i), reads=[bptrx[bk]], writes=[brh[par]])

    def emit_up(k):
        kind, e, ch = slots[k]
        par = k % 2
        wpar = e % 2
        for it in range(6):
            ap_ = it % 2
            for et in range(8):
                P.op("tensor", lambda E, o=pag[ap_][:, :],
                     l=wbuf[wpar][:, et * IPAD + it * 128: et * IPAD + (it + 1) * 128],
                     r=rh[par][:, et, :], s=(et == 0), p=(et == 7):
                     E.matmul(o, l, r, start=s, stop=p),
                     reads=[bwbuf[wpar], brh[par]], writes=[bpag[ap_]])
            for et in range(8):
                P.op("tensor", lambda E, o=pau[ap_][:, :],
                     l=wbuf[wpar][:, SEC + et * IPAD + it * 128: SEC + et * IPAD + (it + 1) * 128],
                     r=rh[par][:, et, :], s=(et == 0), p=(et == 7):
                     E.matmul(o, l, r, start=s, stop=p),
                     reads=[bwbuf[wpar], brh[par]], writes=[bpau[ap_]])
            P.op("scalar", lambda E, o=sg[ap_], i=pag[ap_][:, :]:
                 E.activation(o, i, AF.Silu), reads=[bpag[ap_]], writes=[bsg[ap_]])
            P.op("vector", lambda E, o=hb[par][:, it * 512:(it + 1) * 512], i=sg[ap_],
                 j=pau[ap_][:, :]: E.tensor_mul(o, i, j),
                 reads=[bsg[ap_], bpau[ap_]], writes=[bhb[par]])

    def emit_down(k):
        kind, e, ch = slots[k]
        par = k % 2
        wpar = e % 2
        for mt in range(4):
            dp = mt % 2
            for nh in range(2):
                for it in range(6):
                    P.op("tensor", lambda E, o=pdn[dp][:, :],
                         l=hb[par][:, it * 512 + mt * 128: it * 512 + (mt + 1) * 128],
                         r=wbuf[wpar][:, 2 * SEC + it * 1024 + nh * 512: 2 * SEC + it * 1024 + (nh + 1) * 512],
                         s=(it == 0), p=(it == 5): E.matmul(o, l, r, start=s, stop=p),
                         reads=[bwbuf[wpar], bhb[par]], writes=[bpdn[dp]])
                P.op("scalar", lambda E, o=co_st[par][:, mt, nh * 512:(nh + 1) * 512],
                     i=pdn[dp][:, :]: E.copy(o, i), reads=[bpdn[dp]], writes=[bco[par]])
        if kind == "e":
            base = e * CAP + ch * 512
            dst = t["compact_out"][base:base + 512, :].rearrange("(tt p) cc -> p tt cc", p=128)
            dbf = c.dbuf["compact_out"]
        else:
            hh = e - 64
            base = hh * TOK + ch * 512
            dst = t["shared_h"][base:base + 512, :].rearrange("(tt p) cc -> p tt cc", p=128)
            dbf = c.dbuf["shared_h"]
        P.op("sync", lambda E, o=dst, i=co_st[par]: E.dma_start(out=o, in_=i),
             reads=[bco[par]], writes=[dbf], inc=16)

    emit_load(0)
    emit_tr(0)
    for k in range(len(slots)):
        if k + 1 < len(slots):
            emit_load(k + 1)
        emit_up(k)
        if k + 1 < len(slots):
            emit_tr(k + 1)
        if k > 0:
            emit_down(k - 1)
            kindp, ep, chp = slots[k - 1]
            last_ch = 2 if kindp == "e" else 7
            if chp == last_ch and ep + 2 < NE:
                wdma(ep + 2)
    emit_down(len(slots) - 1)

    P.barrier()
    bAS = c.B("c_asm", small=True)

    for tt in range(32):
        gp = tt % 2
        goff = tt * 128
        b = goff // S
        off = goff % S
        P.op("sync", lambda E, o=idxt2[gp], i=t["idx_h"][goff:goff + 128, :]:
             E.dma_start(out=o, in_=i), reads=[c.dbuf["idx_h"]], writes=[bidxt2[gp]], inc=16)
        P.op("sync", lambda E, o=x_tile[gp], i=t["x_tm"][goff:goff + 128, :]:
             E.dma_start(out=o, in_=i), writes=[bxt[gp]], inc=16)
        P.op("sync", lambda E, o=rw8t[gp], i=t["rw8_h"][goff:goff + 128, :]:
             E.dma_start(out=o, in_=i), reads=[c.dbuf["rw8_h"]], writes=[brw8t[gp]], inc=16)
        P.op("tensor", lambda E, o=[ptr_, ptr2_][gp][0:128, 0:64],
             i=sdup_s[0:64, goff:goff + 128],
             idn=c.identf[0:64, 0:64]: E.transpose(o, i, idn),
             reads=[bsdup, c.bident], writes=[bptrx[gp]])
        P.op("vector", lambda E, o=s_small[gp], i=[ptr_, ptr2_][gp][0:128, 0:64]:
             E.tensor_copy(o, i), reads=[bptrx[gp]], writes=[bss[gp]])
        for hh in range(2):
            sj = hh + 2 * gp
            P.op("sync", lambda E, o=sht[sj],
                 i=t["shared_h"][hh * TOK + goff: hh * TOK + goff + 128, :]:
                 E.dma_start(out=o, in_=i), reads=[c.dbuf["shared_h"]], writes=[bsh[sj]], inc=16)
        P.drain("sync")
        for j in range(8):
            gj = (j % 2) + 2 * gp
            P.op("gpsimd", lambda E, o=Gt[gj], off=idxt2[gp][:, j:j + 1], i=t["compact_out"]:
                 E.indirect_dma_start(out=o, out_offset=None, in_=i,
                                      in_offset=IndirectOffsetOnAxis(off, 0)),
                 reads=[c.dbuf["compact_out"], bidxt2[gp]], writes=[bG[gj]], inc=16)
            src_acc, dst_acc = acc[gp], gsb[gp]
            if j == 0:
                P.op("vector", lambda E, o=acc[gp], i=Gt[gj], sc=rw8t[gp][:, 0:1]:
                     E.tensor_scalar(o, i, sc, None, ALU.mult),
                     reads=[bG[gj], brw8t[gp], bAS], writes=[bacc[gp], bAS])
            else:
                a_in = acc[gp] if j % 2 == 1 else gsb[gp]
                a_out = gsb[gp] if j % 2 == 1 else acc[gp]
                P.op("vector", lambda E, o=a_out, i=Gt[gj], sc=rw8t[gp][:, j:j + 1], jj=a_in:
                     E.scalar_tensor_tensor(o, i, sc, jj, ALU.mult, ALU.add),
                     reads=[bG[gj], brw8t[gp], bacc[gp], bgs[gp], bAS],
                     writes=[bacc[gp], bgs[gp], bAS])
        for hh in range(2):
            sj = hh + 2 * gp
            a_in = gsb[gp] if hh == 0 else acc[gp]
            a_out = acc[gp] if hh == 0 else gsb[gp]
            P.op("vector", lambda E, o=a_out, i=a_in, j=sht[sj]: E.tensor_add(o, i, j),
                 reads=[bsh[sj], bacc[gp], bgs[gp], bAS], writes=[bacc[gp], bgs[gp], bAS])
        P.op("scalar", lambda E, o=acc[gp], i=gsb[gp]: E.activation(o, i, AF.Silu),
             reads=[bgs[gp], bAS], writes=[bacc[gp], bAS])
        P.op("vector", lambda E, o=gsb[gp].rearrange("p (r d) -> p r d", r=16),
             i=acc[gp].rearrange("p (r d) -> p r d", r=16),
             j=s_small[gp].unsqueeze(1).broadcast_to([128, 16, 64]): E.tensor_mul(o, i, j),
             reads=[bss[gp], bacc[gp], bAS], writes=[bgs[gp], bAS])
        P.op("vector", lambda E, o=acc[gp], i=gsb[gp], j=x_tile[gp]: E.tensor_add(o, i, j),
             reads=[bxt[gp], bgs[gp], bAS], writes=[bacc[gp], bAS])
        for half in range(2):
            bk = (gp + half) % 2
            pbf = [ptr_, ptr2_][bk]
            for ct4 in range(4):
                ct = half * 4 + ct4
                P.op("tensor", lambda E, o=pbf[0:128, ct4 * 128:(ct4 + 1) * 128],
                     i=acc[gp][:, ct * 128:(ct + 1) * 128], idn=c.identf:
                     E.transpose(o, i, idn), reads=[bacc[gp], c.bident], writes=[bptrx[bk]])
            P.op("scalar", lambda E, o=x1f_st[gp][:, half * 4:(half + 1) * 4, :],
                 i=pbf[0:128, 0:512].rearrange("p (a b) -> p a b", a=4):
                 E.copy(o, i), reads=[bptrx[bk], bAS], writes=[bx1f[gp], bAS])
        dst = t["x1_h"][b, :, off:off + 128].rearrange("(n p) m -> p n m", p=128)
        P.op("sync", lambda E, o=dst, i=x1f_st[gp]: E.dma_start(out=o, in_=i),
             reads=[bx1f[gp]], writes=[c.dbuf["x1_h"]], inc=16)


def emit_phase_d(c):
    P, t, dyn = c.P, c.t, c.dyn
    c.qwTb = dyn.bf16(0, 8192)
    c.PaB = dyn.bf16(16 * KB, 2 * 8192)
    c.combTb = dyn.bf16(48 * KB, 8192)
    c.wrkvb = dyn.bf16(152 * KB, 3 * 8192)
    c.bqw, c.bPa, c.bcomb, c.bwrkv = c.B("d_qw"), c.B("d_Pa"), c.B("d_comb"), c.B("d_wrkv")
    P.op("gpsimd", lambda E, o=c.qwTb, i=t["qwT"]: E.dma_start(out=o, in_=i),
         writes=[c.bqw], inc=16)
    P.op("gpsimd", lambda E, o=c.combTb, i=t["combT"]: E.dma_start(out=o, in_=i),
         writes=[c.bcomb], inc=16)

    for ci in range(NCH):
        emit_ln_chunk(c, ci, c.t["x1_h"], c.dbuf["x1_h"], c.t["xn1_h"], c.dbuf["xn1_h"], False)
    c.P.barrier()

    for i, nm in enumerate(["wrT", "wk2T", "wv2T"]):
        P.op("gpsimd", lambda E, o=c.wrkvb[:, i * 8192:(i + 1) * 8192], src=t[nm]:
             E.dma_start(out=o, in_=src), writes=[c.bwrkv], inc=16)

    MqB = dyn.bf16(64 * KB, 4 * 512)
    xnb = dyn.bf16v(68 * KB, 8, S)
    xtm = dyn.bf16(100 * KB, 16 * 1024)
    bxnb, bxtm, bMq = c.B("d_xnb"), c.B("d_xtm"), c.B("d_Mq")
    ptr = c.psum[0]
    ptrb = ptr.bitcast(BF16)
    bptr = c.pbuf[0]
    pmq = [c.psum[1 + q] for q in range(4)]
    bpmq = [c.pbuf[1 + q] for q in range(4)]
    ppp = [c.psum[5], c.psum[6]]
    bppp = [c.pbuf[5], c.pbuf[6]]

    for b in range(BPC):
        P.op("sync", lambda E, o=xnb, i=t["xn1_h"][:, :, b * S:(b + 1) * S]:
             E.dma_start(out=o, in_=i), reads=[c.dbuf["xn1_h"]], writes=[bxnb], inc=16)
        P.drain("sync")
        for ut in range(16):
            for ft in range(8):
                P.op("tensor", lambda E, o=ptrb[0:128, 0:128],
                     i=xnb[:, ft, ut * 128:(ut + 1) * 128], idn=c.identb:
                     E.transpose(o, i, idn), reads=[bxnb, c.bidentb], writes=[bptr])
                P.op("vector", lambda E,
                     o=xtm[:, ut * 1024 + ft * 128: ut * 1024 + (ft + 1) * 128],
                     i=ptrb[0:128, 0:128]: E.tensor_copy(o, i), reads=[bptr], writes=[bxtm])
        for fh in range(2):
            for ut in range(16):
                for q in range(4):
                    P.op("tensor", lambda E, o=pmq[q][:, fh * 256:(fh + 1) * 256],
                         l=xtm[:, ut * 1024 + q * 256 + fh * 128: ut * 1024 + q * 256 + (fh + 1) * 128],
                         r=xtm[:, ut * 1024 + q * 256: ut * 1024 + (q + 1) * 256],
                         s=(ut == 0), p=(ut == 15): E.matmul(o, l, r, start=s, stop=p),
                         reads=[bxtm], writes=[bpmq[q]])
            for q in range(4):
                P.op("scalar", lambda E, o=MqB[:, q * 512 + fh * 256: q * 512 + (fh + 1) * 256],
                     i=pmq[q][:, fh * 256:(fh + 1) * 256]:
                     E.copy(o, i), reads=[bpmq[q]], writes=[bMq])
        for a in range(4):
            for q in range(4):
                for eh in range(2):
                    pp = ppp[eh]
                    for fh in range(2):
                        P.op("tensor", lambda E, o=pp[:, 0:256],
                             l=c.qwTb[:, ((a * 4 + q) * 2 + fh) * 256 + eh * 128:
                                      ((a * 4 + q) * 2 + fh) * 256 + (eh + 1) * 128],
                             r=MqB[:, q * 512 + fh * 256: q * 512 + (fh + 1) * 256],
                             s=(fh == 0), p=(fh == 1): E.matmul(o, l, r, start=s, stop=p),
                             reads=[c.bqw, bMq], writes=[bppp[eh]])
                    dst = c.PaB[:, b * 8192 + a * 2048 + eh * 1024 + q * 256:
                                b * 8192 + a * 2048 + eh * 1024 + (q + 1) * 256]
                    P.op("scalar", lambda E, o=dst, i=ppp[eh][:, 0:256]: E.copy(o, i),
                         reads=[bppp[eh]], writes=[c.bPa])


def emit_phase_e(c):
    P, t, dyn = c.P, c.t, c.dyn
    rsb = dyn.f32(0, 8 * 512)
    kb = dyn.bf16(64 * KB, 8 * 512)
    attn_a = dyn.bf16(72 * KB, 8 * 512)
    s1c = dyn.f32(80 * KB, 8 * 512)
    tmpk = dyn.f32(96 * KB, 512)
    fo = [dyn.f32(98 * KB + i * 2 * KB, 512) for i in range(2)]
    oo = [dyn.f32(102 * KB + i * 2 * KB, 512) for i in range(2)]
    t1r = dyn.f32(106 * KB, 512)
    g1s = dyn.f32(108 * KB, 512)
    xe = dyn.bf16v(110 * KB, 8, 513)
    xe1 = dyn.f32v(119 * KB, 8, 512)
    xr = dyn.bf16(135 * KB, 8 * 512)
    xk = dyn.bf16(143 * KB, 8 * 512)

    brsb = c.B("e_rsb")
    bxe = c.B("e_xe")
    bxe1 = c.B("e_xe1")
    bxr, bxk, bkb = c.B("e_xr"), c.B("e_xk"), c.B("e_kb")
    batn = c.B("e_attn")
    bs1c = c.B("e_s1c")
    btmpk = c.B("e_tmpk")
    bfo = [c.B("e_fo0"), c.B("e_fo1")]
    bg1s = c.B("e_g1s")
    boo = [c.B("e_oo0"), c.B("e_oo1")]
    bt1r = c.B("e_t1r")

    pa1 = [c.psum[0], c.psum[1]]
    po = [c.psum[2], c.psum[3]]
    pe_ = [c.psum[4], c.psum[5]]
    bpa1 = [c.pbuf[0], c.pbuf[1]]
    bpo = [c.pbuf[2], c.pbuf[3]]
    bpe = [c.pbuf[4], c.pbuf[5]]

    for b in range(BPC):
        for cc in range(4):
            off = cc * 512
            if cc == 0:
                P.op("vector", lambda E, o=xe[:, :, 0:1]: E.memset(o, 0.0), writes=[bxe])
                P.op("sync", lambda E, o=xe[:, :, 1:513],
                     i=t["xn1_h"][:, :, b * S: b * S + 512]:
                     E.dma_start(out=o, in_=i), reads=[c.dbuf["xn1_h"]], writes=[bxe], inc=16)
            else:
                P.op("sync", lambda E, o=xe[:, :, 0:513],
                     i=t["xn1_h"][:, :, b * S + off - 1: b * S + off + 512]:
                     E.dma_start(out=o, in_=i), reads=[c.dbuf["xn1_h"]], writes=[bxe], inc=16)
            src1 = t["x1_h"][b, :, off:off + 512].rearrange("(n p) m -> p n m", p=128)
            P.op("sync", lambda E, o=xe1, i=src1: E.dma_start(out=o, in_=i),
                 reads=[c.dbuf["x1_h"]], writes=[bxe1], inc=16)
            P.drain("sync")

            for a in range(4):
                for qgt in range(8):
                    pp = pa1[qgt % 2]
                    for eh in range(2):
                        P.op("tensor", lambda E, o=pp[:, :],
                             l=c.PaB[:, b * 8192 + a * 2048 + eh * 1024 + qgt * 128:
                                     b * 8192 + a * 2048 + eh * 1024 + (qgt + 1) * 128],
                             r=xe[:, 2 * a + eh, 1:513],
                             s=(eh == 0), p=(eh == 1): E.matmul(o, l, r, start=s, stop=p),
                             reads=[c.bPa, bxe], writes=[bpa1[qgt % 2]])
                    P.op("scalar", lambda E, o=attn_a[:, qgt * 512:(qgt + 1) * 512],
                         i=pa1[qgt % 2][:, :]: E.copy(o, i),
                         reads=[bpa1[qgt % 2]], writes=[batn])
                for gt in range(2):
                    pp = po[gt % 2]
                    for qgt in range(8):
                        P.op("tensor", lambda E, o=pp[:, :],
                             l=c.combTb[:, (a * 8 + qgt) * 256 + gt * 128:
                                        (a * 8 + qgt) * 256 + (gt + 1) * 128],
                             r=attn_a[:, qgt * 512:(qgt + 1) * 512],
                             s=(qgt == 0), p=(qgt == 7): E.matmul(o, l, r, start=s, stop=p),
                             reads=[c.bcomb, batn], writes=[bpo[gt % 2]])
                    P.op("scalar", lambda E, o=s1c[:, (2 * a + gt) * 512:(2 * a + gt + 1) * 512],
                         i=po[gt % 2][:, :]: E.copy(o, i), reads=[bpo[gt % 2]], writes=[bs1c])

            for ft in range(8):
                P.op("vector", lambda E, o=t1r, i=xe[:, ft, 1:513], sc=c.mur[:, ft:ft + 1]:
                     E.tensor_scalar(o, i, sc, None, ALU.mult),
                     reads=[bxe, c.bmu], writes=[bt1r])
                P.op("vector", lambda E, o=xr[:, ft * 512:(ft + 1) * 512], i=xe[:, ft, 0:512],
                     sc=c.omur[:, ft:ft + 1], j=t1r:
                     E.scalar_tensor_tensor(o, i, sc, j, ALU.mult, ALU.add),
                     reads=[bxe, c.bmu, bt1r], writes=[bxr])
                P.op("vector", lambda E, o=t1r, i=xe[:, ft, 1:513], sc=c.muk[:, ft:ft + 1]:
                     E.tensor_scalar(o, i, sc, None, ALU.mult),
                     reads=[bxe, c.bmu], writes=[bt1r])
                P.op("vector", lambda E, o=xk[:, ft * 512:(ft + 1) * 512], i=xe[:, ft, 0:512],
                     sc=c.omuk[:, ft:ft + 1], j=t1r:
                     E.scalar_tensor_tensor(o, i, sc, j, ALU.mult, ALU.add),
                     reads=[bxe, c.bmu, bt1r], writes=[bxk])
            for dt in range(8):
                pp = pe_[dt % 2]
                for ct in range(8):
                    P.op("tensor", lambda E, o=pp[:, :],
                         l=c.wrkvb[:, 0 * 8192 + ct * 1024 + dt * 128: 0 * 8192 + ct * 1024 + (dt + 1) * 128],
                         r=xr[:, ct * 512:(ct + 1) * 512],
                         s=(ct == 0), p=(ct == 7): E.matmul(o, l, r, start=s, stop=p),
                         reads=[c.bwrkv, bxr], writes=[bpe[dt % 2]])
                P.op("scalar", lambda E, o=rsb[:, dt * 512:(dt + 1) * 512], i=pe_[dt % 2][:, :]:
                     E.activation(o, i, AF.Sigmoid), reads=[bpe[dt % 2]], writes=[brsb])
            for dt in range(8):
                pp = pe_[dt % 2]
                for ct in range(8):
                    P.op("tensor", lambda E, o=pp[:, :],
                         l=c.wrkvb[:, 1 * 8192 + ct * 1024 + dt * 128: 1 * 8192 + ct * 1024 + (dt + 1) * 128],
                         r=xk[:, ct * 512:(ct + 1) * 512],
                         s=(ct == 0), p=(ct == 7): E.matmul(o, l, r, start=s, stop=p),
                         reads=[c.bwrkv, bxk], writes=[bpe[dt % 2]])
                P.op("vector", lambda E, o=tmpk, i=pe_[dt % 2][:, :]:
                     E.tensor_scalar_max(o, i, 0.0), reads=[bpe[dt % 2]], writes=[btmpk])
                P.op("scalar", lambda E, o=kb[:, dt * 512:(dt + 1) * 512], i=tmpk:
                     E.activation(o, i, AF.Square), reads=[btmpk], writes=[bkb])
            for dt in range(8):
                pp = pe_[dt % 2]
                par = dt % 2
                for ct in range(8):
                    P.op("tensor", lambda E, o=pp[:, :],
                         l=c.wrkvb[:, 2 * 8192 + ct * 1024 + dt * 128: 2 * 8192 + ct * 1024 + (dt + 1) * 128],
                         r=kb[:, ct * 512:(ct + 1) * 512],
                         s=(ct == 0), p=(ct == 7): E.matmul(o, l, r, start=s, stop=p),
                         reads=[c.bwrkv, bkb], writes=[bpe[dt % 2]])
                P.op("vector", lambda E, o=fo[par], i=rsb[:, dt * 512:(dt + 1) * 512],
                     j=pe_[dt % 2][:, :]: E.tensor_mul(o, i, j),
                     reads=[brsb, bpe[dt % 2]], writes=[bfo[par]])
                P.op("scalar", lambda E, o=g1s, i=fo[par]: E.activation(o, i, AF.Silu),
                     reads=[bfo[par]], writes=[bg1s])
                P.op("vector", lambda E, o=fo[par], i=g1s, j=s1c[:, dt * 512:(dt + 1) * 512]:
                     E.tensor_mul(o, i, j), reads=[bg1s, bs1c], writes=[bfo[par]])
                P.op("vector", lambda E, o=oo[par], i=fo[par], j=xe1[:, dt, :]:
                     E.tensor_add(o, i, j), reads=[bfo[par], bxe1], writes=[boo[par]])
                dst = t["out_fm"][b, dt * 128:(dt + 1) * 128, off:off + 512]
                P.op("sync", lambda E, o=dst, i=oo[par]: E.dma_start(out=o, in_=i),
                     reads=[boo[par]], writes=[c.dbuf["out_fm"]], inc=16)


def build_launch(phases):
    nc = bass.Bass("TRN2", target_bir_lowering=False, debug=False)
    t = declare_io(nc, phases)
    P = Prog()
    c = Ctx()
    c.nc, c.t, c.P = nc, t, P
    c._bufs = {}
    stack = contextlib.ExitStack()

    def SB(nm, shp, dt):
        return stack.enter_context(nc.sbuf_tensor(nm, shp, dt)).ap()

    def PS(nm, shp, dt=F32):
        return stack.enter_context(nc.psum_tensor(nm, shp, dt)).ap()

    c.dyn = Carver(SB("dyn", [128, DYN_BYTES // 4], F32))
    c.identf = SB("identf", [128, 128], F32)
    c.identb = SB("identb", [128, 128], BF16)
    c.ones_col = SB("ones_col", [128, 1], F32)
    c.ones_row = SB("ones_row", [1, 128], F32)
    c.epsA = SB("epsA", [128, 1], F32)
    if "A" in phases:
        c.gwT = SB("gwT_s", [128, 512], F32)
        c.gbias = SB("gbias_s", [128, 64], F32)
        c.ncol_row = SB("ncol_row", [1, 64], F32)
        c.ncol_c = SB("ncol_c", [128, 1], F32)
        c.ecap_col = SB("ecap_s", [64, 1], F32)
    if "B" in phases:
        c.wqkvb = SB("wqkvb", [128, 3 * 512], BF16)
    if ("D" in phases) or ("E" in phases):
        c.mur = SB("mur_s", [128, 8], F32)
        c.muk = SB("muk_s", [128, 8], F32)
        c.omur = SB("omur_s", [128, 8], F32)
        c.omuk = SB("omuk_s", [128, 8], F32)
    c.psum = [PS(f"pbk{i}", [128, 512], F32) for i in range(8)]
    c.pbuf = [Buf() for _ in range(8)]

    sems = {e: stack.enter_context(nc.semaphore("sem_" + e)) for e in ENGS}
    c.dbuf = {nm: Buf() for nm in t}

    emit_static(c, phases)
    if "A" in phases:
        emit_phase_a(c)
        P.barrier()
        emit_phase_a2(c)
    if "B" in phases:
        P.barrier()
        emit_phase_b(c)
    if "C" in phases:
        P.barrier(skip=("gpsimd",))
        emit_phase_c(c)
    if "D" in phases:
        P.barrier()
        emit_phase_d(c)
    if "E" in phases:
        P.barrier()
        emit_phase_e(c)

    final = P.cnt["sync"]

    def mk(e, extra_wait=None):
        def body(engobj):
            water = {pe: 0 for pe in ENGS}
            for item in P.items[e]:
                if item[0] == "DRAIN":
                    v = item[1]
                    if v > water[e]:
                        engobj.wait_ge(sems[e], v)
                        water[e] = v
                    continue
                fd, fn, inc = item
                for pe, v in sorted(fd.items()):
                    if v > water[pe]:
                        engobj.wait_ge(sems[pe], v)
                        water[pe] = v
                inst = fn(engobj)
                inst.then_inc(sems[e], inc)
            if extra_wait is not None:
                engobj.wait_ge(sems[e], extra_wait)
        return body

    with nc.Block() as block:
        block.sync(mk("sync", extra_wait=final))
        block.gpsimd(mk("gpsimd"))
        block.tensor(mk("tensor"))
        block.vector(mk("vector"))
        block.scalar(mk("scalar"))
    stack.close()
    return nc


_NC_CACHE = {}


def _get_nc(phases):
    if phases not in _NC_CACHE:
        _NC_CACHE[phases] = build_launch(phases)
    return _NC_CACHE[phases]


def kernel(**inputs):
    h = host_prep(inputs)
    x_all = np.asarray(inputs["L0_attn_x"]).reshape(-1, D)
    per_core = []
    for core in range(NCORES):
        m = {"x_fm": np.ascontiguousarray(h["x_fm"][core * BPC:(core + 1) * BPC]),
             "x_tm": np.ascontiguousarray(x_all[core * TOK:(core + 1) * TOK])}
        per_core.append(m)

    common_ab = {k: h[k] for k in ["gwT", "gbias", "wqT", "wkT", "wvT", "ident", "ecap"]}
    common_c = {k: h[k] for k in ["moeblob", "ident"]}
    common_de = {k: h[k] for k in ["qwT", "combT", "wrT", "wk2T", "wv2T", "mur", "muk", "ident"]}

    if SPLIT_MODE:
        nc1 = _get_nc("AB")
        maps1 = [dict(common_ab, x_fm=per_core[i]["x_fm"]) for i in range(NCORES)]
        r1 = run_bass_kernel_spmd(nc1, maps1, core_ids=list(range(NCORES))).results

        nc2 = _get_nc("C")
        maps2 = [dict(common_c, x_fm=per_core[i]["x_fm"], x_tm=per_core[i]["x_tm"],
                      xn_h=np.asarray(r1[i]["xn_h"]), rwT_h=np.asarray(r1[i]["rwT_h"]),
                      sdup_h=np.asarray(r1[i]["sdup_h"]), xn_tm_h=np.asarray(r1[i]["xn_tm_h"]),
                      idx_h=np.asarray(r1[i]["idx_h"]), rw8_h=np.asarray(r1[i]["rw8_h"]))
                 for i in range(NCORES)]
        r2 = run_bass_kernel_spmd(nc2, maps2, core_ids=list(range(NCORES))).results

        nc3 = _get_nc("DE")
        maps3 = [dict(common_de, x1_h=r2[i]["x1_h"]) for i in range(NCORES)]
        r3 = run_bass_kernel_spmd(nc3, maps3, core_ids=list(range(NCORES))).results
        outs = [r3[i]["out_fm"] for i in range(NCORES)]
    else:
        nc = _get_nc("ABCDE")
        maps = [dict(common_ab) for _ in range(NCORES)]
        for i in range(NCORES):
            maps[i].update(common_c)
            maps[i].update(common_de)
            maps[i]["x_fm"] = per_core[i]["x_fm"]
            maps[i]["x_tm"] = per_core[i]["x_tm"]
        rr = run_bass_kernel_spmd(nc, maps, core_ids=list(range(NCORES))).results
        outs = [rr[i]["out_fm"] for i in range(NCORES)]

    out = np.concatenate([o.reshape(BPC, D, S) for o in outs], axis=0)
    return np.ascontiguousarray(out.transpose(0, 2, 1)).astype(np.float32)


# version 08
# speedup vs baseline: 1.6947x; 1.1587x over previous
import contextlib
import numpy as np
import ml_dtypes
import concourse.bass as bass
import concourse.mybir as mybir
from concourse.bass_utils import run_bass_kernel_spmd
from concourse.bass_core_rust import IndirectOffsetOnAxis

F32 = mybir.dt.float32
BF16 = mybir.dt.bfloat16
AF = mybir.ActivationFunctionType
ALU = mybir.AluOpType
AX = mybir.AxisListType
BF = ml_dtypes.bfloat16

NCORES = 8
S = 2048
D = 1024
BPC = 2
TOK = BPC * S
NE = 66
IPAD = 768
SEC = 8 * IPAD
BLOBW = 3 * SEC
CH = 512
NCH = TOK // CH
NGRP = 4
GTOK = TOK // NGRP
GCH = GTOK // CH
KB = 1024
DYN_BYTES = 200 * KB
CAP = 1536
NSLOT = 64 * CAP
TRASH = NSLOT
NROW_CX = NSLOT + 128
NROW_CO = NSLOT + 128
BIGF = 1.0e7

SPLIT_MODE = False
DEBUG_EXPORT = False

ENGS = ("sync", "gpsimd", "tensor", "vector", "scalar")


class Buf:
    __slots__ = ("w", "r", "small")

    def __init__(self, small=False):
        self.w = None
        self.r = []
        self.small = small


class Prog:
    def __init__(self):
        self.items = {e: [] for e in ENGS}
        self.cnt = {e: 0 for e in ENGS}
        self.pend = {e: [] for e in ENGS}
        self.drains = {e: [] for e in ENGS}

    def drain(self, q):
        if self.cnt[q] > 0 and (not self.drains[q] or self.drains[q][-1] != self.cnt[q]):
            self.items[q].append(("DRAIN", self.cnt[q], 0))
            self.drains[q].append(self.cnt[q])

    def barrier(self, skip=()):
        for q in ("sync", "gpsimd"):
            if q not in skip:
                self.drain(q)
        toks = [(e, self.cnt[e]) for e in ENGS if self.cnt[e] > 0]
        for e in ENGS:
            if e in skip:
                continue
            self.pend[e] = self.pend[e] + [(tk, True) for tk in toks]

    def op(self, e, fn, reads=(), writes=(), extra=(), inc=1):
        deps = []
        for b in reads:
            if b.w is not None:
                deps.append((b.w, b.small))
        for b in writes:
            for tk in b.r:
                deps.append((tk, b.small))
            if b.w is not None:
                deps.append((b.w, b.small))
        for tk in extra:
            deps.append((tk, True))
        deps += self.pend[e]
        self.pend[e] = []
        self.cnt[e] += inc
        tok = (e, self.cnt[e])
        fd = {}
        for (pe, v), force in deps:
            if pe == e and (not force or v >= tok[1]):
                continue
            if pe in ("sync", "gpsimd") and self.drains[pe]:
                import bisect
                i = bisect.bisect_left(self.drains[pe], v)
                if i < len(self.drains[pe]):
                    v = self.drains[pe][i]
            if v > fd.get(pe, 0):
                fd[pe] = v
        self.items[e].append((fd, fn, inc))
        for b in reads:
            b.r.append(tok)
        for b in writes:
            b.r = []
            b.w = tok
        return tok


def _tile_kxm(w):
    K, M = w.shape
    return np.ascontiguousarray(
        w.reshape(K // 128, 128, M).transpose(1, 0, 2).reshape(128, (K // 128) * M))


def host_prep(z):
    h = {}
    x = np.asarray(z["L0_attn_x"])
    h["x_fm"] = np.ascontiguousarray(x.transpose(0, 2, 1))

    for nm, key in [("wqT", "L0_attn_Wq"), ("wkT", "L0_attn_Wk"), ("wvT", "L0_attn_Wv"),
                    ("gwT", "L0_ffn_gate_w")]:
        h[nm] = _tile_kxm(np.ascontiguousarray(np.asarray(z[key]).T))
    h["gbias"] = np.ascontiguousarray(
        np.tile(np.asarray(z["L0_ffn_gate_bias"])[None, :], (128, 1)).astype(np.float32))

    blob = np.zeros((NE, 128, BLOBW), dtype=np.float32)
    Wg, Wu, Wd = np.asarray(z["L0_ffn_Wg"]), np.asarray(z["L0_ffn_Wu"]), np.asarray(z["L0_ffn_Wd"])
    gt = Wg.transpose(0, 2, 1).reshape(64, 8, 128, 704).transpose(0, 2, 1, 3)
    blob[:64, :, :SEC].reshape(64, 128, 8, IPAD)[:, :, :, :704] = gt
    ut = Wu.transpose(0, 2, 1).reshape(64, 8, 128, 704).transpose(0, 2, 1, 3)
    blob[:64, :, SEC:2 * SEC].reshape(64, 128, 8, IPAD)[:, :, :, :704] = ut
    dt_ = np.zeros((64, IPAD, 1024), dtype=np.float32)
    dt_[:, :704, :] = Wd.transpose(0, 2, 1)
    blob[:64, :, 2 * SEC:] = dt_.reshape(64, 6, 128, 1024).transpose(0, 2, 1, 3).reshape(64, 128, SEC)
    Sg, Su, Sd = np.asarray(z["L0_ffn_Sg"]), np.asarray(z["L0_ffn_Su"]), np.asarray(z["L0_ffn_Sd"])
    for hh in range(2):
        e = 64 + hh
        sg = Sg[hh * 704:(hh + 1) * 704].T
        su = Su[hh * 704:(hh + 1) * 704].T
        sd = Sd[:, hh * 704:(hh + 1) * 704].T
        blob[e, :, :SEC].reshape(128, 8, IPAD)[:, :, :704] = sg.reshape(8, 128, 704).transpose(1, 0, 2)
        blob[e, :, SEC:2 * SEC].reshape(128, 8, IPAD)[:, :, :704] = su.reshape(8, 128, 704).transpose(1, 0, 2)
        sdp = np.zeros((IPAD, 1024), dtype=np.float32)
        sdp[:704] = sd
        blob[e, :, 2 * SEC:] = sdp.reshape(6, 128, 1024).transpose(1, 0, 2).reshape(128, SEC)
    h["moeblob"] = blob

    basis = np.zeros((64, 64 * 64), dtype=BF)
    for e in range(64):
        basis[e, e * 64:(e + 1) * 64] = 1
    h["basis64"] = basis
    i2 = np.zeros((64, 128), dtype=BF)
    for k in range(64):
        i2[k, k] = 1
        i2[k, 64 + k] = 1
    h["i2t"] = i2
    h["ident"] = np.eye(128, dtype=np.float32)

    q = np.asarray(z["L1_attn_queries"])
    h["qwT"] = np.ascontiguousarray(
        q.reshape(4, 256, 4, 2, 128).transpose(4, 0, 2, 3, 1).reshape(128, 8192))
    cb = np.asarray(z["L1_attn_combiners"])
    h["combT"] = np.ascontiguousarray(
        cb.reshape(4, 8, 128, 256).transpose(2, 0, 1, 3).reshape(128, 8192))
    for nm, key in [("wrT", "L1_ffn_W_r"), ("wk2T", "L1_ffn_W_k"), ("wv2T", "L1_ffn_W_v")]:
        w = np.asarray(z[key])
        h[nm] = np.ascontiguousarray(
            w.reshape(8, 128, 8, 128).transpose(3, 2, 0, 1).reshape(128, 8192))
    h["ecap"] = (np.arange(64, dtype=np.float32) * CAP).reshape(64, 1)
    h["mur"] = np.ascontiguousarray(np.asarray(z["L1_ffn_mu_r"]).reshape(8, 128).T)
    h["muk"] = np.ascontiguousarray(np.asarray(z["L1_ffn_mu_k"]).reshape(8, 128).T)
    return h


class Carver:
    def __init__(self, ap_f32):
        self.f = ap_f32
        self.b = ap_f32.bitcast(BF16)

    def f32(self, off, n):
        return self.f[:, off // 4: off // 4 + n]

    def f32v(self, off, d0, d1):
        return self.f[:, off // 4: off // 4 + d0 * d1].rearrange("p (a b) -> p a b", a=d0)

    def bf16(self, off, n):
        return self.b[:, off // 2: off // 2 + n]

    def bf16v(self, off, d0, d1):
        return self.b[:, off // 2: off // 2 + d0 * d1].rearrange("p (a b) -> p a b", a=d0)


class Ctx:
    def B(self, name, small=False):
        if name not in self._bufs:
            self._bufs[name] = Buf(small=small)
        return self._bufs[name]


def declare_io(nc, phases):
    t = {}
    fused = not SPLIT_MODE

    def dram(name, shape, dtype, kind):
        t[name] = nc.dram_tensor(name, shape, dtype, kind=kind).ap()

    has = {p: (p in phases) for p in "ABCDE"}
    if has["A"] or has["C"]:
        dram("x_fm", [BPC, D, S], F32, "ExternalInput")
    if has["A"]:
        dram("gwT", [128, 512], F32, "ExternalInput")
        dram("gbias", [128, 64], F32, "ExternalInput")
    if has["B"]:
        dram("wqT", [128, 512], F32, "ExternalInput")
        dram("wkT", [128, 512], F32, "ExternalInput")
        dram("wvT", [128, 512], F32, "ExternalInput")
    dram("ident", [128, 128], F32, "ExternalInput")
    if has["C"]:
        dram("moeblob", [NE, 128, BLOBW], F32, "ExternalInput")
        dram("x_tm", [TOK, D], F32, "ExternalInput")
        dram("compact_x", [NROW_CX, D], BF16, "ExternalOutput")
        dram("compact_out", [NROW_CO, D], BF16, "ExternalOutput")
        dram("shared_h", [2 * TOK, D], BF16, "Internal")
    if has["A"]:
        dram("ecap", [64, 1], F32, "ExternalInput")
    if has["D"] or has["E"]:
        dram("qwT", [128, 8192], F32, "ExternalInput")
        dram("combT", [128, 8192], F32, "ExternalInput")
        dram("wrT", [128, 8192], F32, "ExternalInput")
        dram("wk2T", [128, 8192], F32, "ExternalInput")
        dram("wv2T", [128, 8192], F32, "ExternalInput")
        dram("mur", [128, 8], F32, "ExternalInput")
        dram("muk", [128, 8], F32, "ExternalInput")

    def bridge(name, shape, dtype, produced, consumed):
        if produced and consumed:
            dram(name, shape, dtype, "ExternalOutput" if DEBUG_EXPORT else "Internal")
        elif produced:
            dram(name, shape, dtype, "ExternalOutput")
        elif consumed:
            dram(name, shape, dtype, "ExternalInput")

    if (not SPLIT_MODE) or not has["A"]:
        bridge("xn_h", [128, 8, TOK], BF16, has["A"], has["B"] or has["C"])
        bridge("xn_tm_h", [TOK, D], BF16, has["A"], has["C"])
        bridge("idx_h", [TOK, 8], mybir.dt.int32, has["A"], has["C"])
        bridge("rw8_h", [TOK, 8], F32, has["A"], has["C"])
    else:
        dram("xn_h", [128, 8, TOK], BF16, "ExternalOutput")
        dram("xn_tm_h", [TOK, D], BF16, "ExternalOutput")
        dram("idx_h", [TOK, 8], mybir.dt.int32, "ExternalOutput")
        dram("rw8_h", [TOK, 8], F32, "ExternalOutput")
    bridge("rwT_h", [64, TOK], BF16, has["A"], has["C"])
    bridge("sdup_h", [128, TOK], F32, has["B"], has["C"])
    bridge("x1_h", [BPC, D, S], F32, has["C"], has["D"] or has["E"])
    if has["D"] or has["E"]:
        bridge("xn1_h", [128, 8, TOK], BF16, has["D"], has["E"])
    if has["E"]:
        dram("out_fm", [BPC, D, S], F32, "ExternalOutput")
    return t


def emit_static(c, phases):
    P, t = c.P, c.t
    c.bident = c.B("ident")
    c.bidentb = c.B("identb")
    c.bones = c.B("ones", small=True)
    c.beps = c.B("eps", small=True)
    P.op("sync", lambda E, o=c.identf, i=t["ident"]: E.dma_start(out=o, in_=i),
         writes=[c.bident], inc=16)
    P.op("scalar", lambda E, o=c.identb, i=c.identf: E.copy(o, i),
         reads=[c.bident], writes=[c.bidentb])
    P.op("vector", lambda E, o=c.ones_col: E.memset(o, 1.0), writes=[c.bones])
    P.op("vector", lambda E, o=c.ones_row: E.memset(o, 1.0), writes=[c.bones])
    P.op("vector", lambda E, o=c.epsA: E.memset(o, 1e-5), writes=[c.beps])

    if "A" in phases:
        c.bgw = c.B("gw")
        c.bgbias = c.B("gbias")
        c.bncol = c.B("ncol", small=True)
        P.op("sync", lambda E, o=c.gwT, i=t["gwT"]: E.dma_start(out=o, in_=i), writes=[c.bgw], inc=16)
        P.op("sync", lambda E, o=c.gbias, i=t["gbias"]: E.dma_start(out=o, in_=i), writes=[c.bgbias], inc=16)
        pcs = c.psum[0]
        bcs = c.pbuf[0]
        for et in range(8):
            P.op("tensor", lambda E, o=pcs[0:64, 0:1], l=c.gwT[:, et * 64:(et + 1) * 64],
                 r=c.ones_col, s=(et == 0), p=(et == 7): E.matmul(o, l, r, start=s, stop=p),
                 reads=[c.bgw, c.bones], writes=[bcs])
        bnc = c.B("ncol_c", small=True)
        P.op("vector", lambda E, o=c.ncol_c[0:64, 0:1], i=pcs[0:64, 0:1]:
             E.tensor_scalar(o, i, -1.0, None, ALU.mult), reads=[bcs], writes=[bnc])
        P.op("tensor", lambda E, o=c.psum[1][0:1, 0:64], i=c.ncol_c[0:64, 0:1],
             idn=c.identf[0:64, 0:64]: E.transpose(o, i, idn),
             reads=[bnc, c.bident], writes=[bcs])
        P.op("vector", lambda E, o=c.ncol_row, i=c.psum[1][0:1, 0:64]:
             E.tensor_copy(o, i), reads=[bcs], writes=[c.bncol])
        c.becap = c.B("ecap", small=True)
        P.op("sync", lambda E, o=c.ecap_col, i=t["ecap"]: E.dma_start(out=o, in_=i),
             writes=[c.becap], inc=16)

    if "B" in phases:
        c.bwqkv = c.B("wqkv")
        for i, nm in enumerate(["wqT", "wkT", "wvT"]):
            P.op("gpsimd", lambda E, o=c.wqkvb[:, i * 512:(i + 1) * 512], src=t[nm]:
                 E.dma_start(out=o, in_=src), writes=[c.bwqkv], inc=16)

    if ("D" in phases) or ("E" in phases):
        c.bmu = c.B("mu", small=True)
        P.op("sync", lambda E, o=c.mur, i=t["mur"]: E.dma_start(out=o, in_=i), writes=[c.bmu], inc=16)
        P.op("sync", lambda E, o=c.muk, i=t["muk"]: E.dma_start(out=o, in_=i), writes=[c.bmu], inc=16)
        P.op("vector", lambda E, o=c.omur, i=c.mur: E.tensor_scalar(o, i, -1.0, 1.0, ALU.mult, ALU.add),
             reads=[c.bmu], writes=[c.bmu])
        P.op("vector", lambda E, o=c.omuk, i=c.muk: E.tensor_scalar(o, i, -1.0, 1.0, ALU.mult, ALU.add),
             reads=[c.bmu], writes=[c.bmu])


LN_O = 72 * KB


def emit_ln_chunk(c, ci, src_dram, src_buf, dst_dram, dst_buf, with_routing, O=None):
    P, t, dyn = c.P, c.t, c.dyn
    b = ci // (S // CH)
    off = (ci % (S // CH)) * CH
    goff = ci * CH
    if O is None:
        O = LN_O

    xld = dyn.f32v(O, 8, 512)
    x2 = dyn.f32v(O + 16 * KB, 8, 512)
    tmp1 = dyn.f32(O + 32 * KB, 512)
    xnst = dyn.bf16v(O + 34 * KB, 8, 512)
    m_row = dyn.f32(O + 42 * KB, 512)[0:1, :]
    sd_row = dyn.f32(O + 44 * KB, 512)[0:1, :]
    rstd_row = dyn.f32(O + 46 * KB, 512)[0:1, :]
    mr_row = dyn.f32(O + 48 * KB, 512)[0:1, :]
    ex2_row = dyn.f32(O + 50 * KB, 512)[0:1, :]
    tmpn = dyn.f32(O + 52 * KB, 512)

    bxld = c.B("ln_xld")
    bx2 = c.B("ln_x2")
    bxnst = c.B("ln_xnst")
    bstat = c.B("ln_stat", small=True)
    btmpn = c.B("ln_tmpn", small=True)
    btmp1 = c.B("ln_tmp1", small=True)
    psx, psx2, pbm, pbr = c.psum[0], c.psum[1], c.psum[2], c.psum[3]
    bpsx, bpsx2, bpbm, bpbr = c.pbuf[0], c.pbuf[1], c.pbuf[2], c.pbuf[3]

    src = src_dram[b, :, off:off + CH].rearrange("(n p) m -> p n m", p=128)
    P.op("sync", lambda E, o=xld, i=src: E.dma_start(out=o, in_=i),
         reads=[src_buf], writes=[bxld], inc=16)
    P.drain("sync")
    P.op("scalar", lambda E, o=x2, i=xld: E.activation(o, i, AF.Square),
         reads=[bxld], writes=[bx2])
    for et in range(8):
        P.op("tensor", lambda E, o=psx[0:1, :], l=c.ones_col, r=xld[:, et, :],
             s=(et == 0), p=(et == 7): E.matmul(o, l, r, start=s, stop=p),
             reads=[c.bones, bxld], writes=[bpsx])
    for et in range(8):
        P.op("tensor", lambda E, o=psx2[0:1, :], l=c.ones_col, r=x2[:, et, :],
             s=(et == 0), p=(et == 7): E.matmul(o, l, r, start=s, stop=p),
             reads=[c.bones, bx2], writes=[bpsx2])
    P.op("vector", lambda E, o=m_row, i=psx[0:1, :]:
         E.tensor_scalar(o, i, 1.0 / D, None, ALU.mult), reads=[bpsx], writes=[bstat])
    P.op("vector", lambda E, o=ex2_row, i=psx2[0:1, :]:
         E.tensor_scalar(o, i, 1.0 / D, None, ALU.mult), reads=[bpsx2], writes=[bstat])
    P.op("vector", lambda E, o=tmp1[0:1, :], i=m_row, j=m_row:
         E.tensor_mul(o, i, j), reads=[bstat], writes=[btmp1])
    P.op("vector", lambda E, o=ex2_row, i=ex2_row, j=tmp1[0:1, :]:
         E.tensor_sub(o, i, j), reads=[bstat, btmp1], writes=[bstat])
    P.op("scalar", lambda E, o=sd_row, i=ex2_row, bi=c.epsA[0:1, 0:1]:
         E.activation(o, i, AF.Sqrt, bias=bi), reads=[bstat, c.beps], writes=[bstat])
    P.op("vector", lambda E, o=rstd_row, i=sd_row: E.reciprocal(o, i),
         reads=[bstat], writes=[bstat])
    if with_routing:
        P.op("vector", lambda E, o=mr_row, i=m_row, j=rstd_row:
             E.tensor_mul(o, i, j), reads=[bstat], writes=[bstat])
    P.op("tensor", lambda E, o=pbm[:, :], l=c.ones_row, r=m_row:
         E.matmul(o, l, r, start=True, stop=True), reads=[c.bones, bstat], writes=[bpbm])
    P.op("tensor", lambda E, o=pbr[:, :], l=c.ones_row, r=rstd_row:
         E.matmul(o, l, r, start=True, stop=True), reads=[c.bones, bstat], writes=[bpbr])
    for et in range(8):
        P.op("vector", lambda E, o=tmpn, i=xld[:, et, :], j=pbm[:, :]:
             E.tensor_sub(o, i, j), reads=[bxld, bpbm], writes=[btmpn])
        P.op("vector", lambda E, o=xnst[:, et, :], i=tmpn, j=pbr[:, :]:
             E.tensor_mul(o, i, j), reads=[btmpn, bpbr], writes=[bxnst])
    dst = dst_dram[:, :, goff:goff + CH]
    P.op("sync", lambda E, o=dst, i=xnst: E.dma_start(out=o, in_=i),
         reads=[bxnst], writes=[dst_buf], inc=16)

    if with_routing:
        xtm_st = dyn.bf16v(O + 62 * KB, 4, 1024)
        bxtmst = c.B("ln_xtmst")
        ptrxb = [c.psum[6].bitcast(BF16), c.psum[7].bitcast(BF16)]
        bptrx2 = [c.pbuf[6], c.pbuf[7]]
        for tt in range(4):
            bk = tt % 2
            for et in range(8):
                P.op("tensor", lambda E, o=ptrxb[bk][0:128, et * 128:(et + 1) * 128],
                     i=xnst[:, et, tt * 128:(tt + 1) * 128], idn=c.identb:
                     E.transpose(o, i, idn), reads=[bxnst, c.bidentb], writes=[bptrx2[bk]])
            P.op("vector", lambda E, o=xtm_st[:, tt, :], i=ptrxb[bk][0:128, 0:1024]:
                 E.tensor_copy(o, i), reads=[bptrx2[bk]], writes=[bxtmst])
        dst2 = c.t["xn_tm_h"][goff:goff + CH, :].rearrange("(tt p) cc -> p tt cc", p=128)
        P.op("sync", lambda E, o=dst2, i=xtm_st: E.dma_start(out=o, in_=i),
             reads=[bxtmst], writes=[c.dbuf["xn_tm_h"]], inc=16)

    if not with_routing:
        return

    plg, pcor = c.psum[4], c.psum[5]
    bplg, bpcor = c.pbuf[4], c.pbuf[5]
    for et in range(8):
        P.op("tensor", lambda E, o=plg[0:64, :], l=c.gwT[:, et * 64:(et + 1) * 64],
             r=xld[:, et, :], s=(et == 0), p=(et == 7): E.matmul(o, l, r, start=s, stop=p),
             reads=[c.bgw, bxld], writes=[bplg])
    P.op("tensor", lambda E, o=pcor[0:64, :], l=c.ncol_row[:, 0:64], r=mr_row:
         E.matmul(o, l, r, start=True, stop=True), reads=[c.bncol, bstat], writes=[bpcor])
    lg0 = dyn.f32(O + 54 * KB, 512)[0:64, :]
    lg1 = dyn.f32(O + 56 * KB, 512)[0:64, :]
    blg = c.B("ln_lg")
    P.op("scalar", lambda E, o=lg0, i=plg[0:64, :]: E.copy(o, i), reads=[bplg], writes=[blg])
    P.op("vector", lambda E, o=lg1, i=lg0, j=pbr[0:64, :]: E.tensor_mul(o, i, j),
         reads=[blg, bpbr], writes=[blg])
    P.op("vector", lambda E, o=lg1, i=lg1, j=pcor[0:64, :]: E.tensor_add(o, i, j),
         reads=[bpcor, blg], writes=[blg])
    ptr = c.psum[6]
    bptr = c.pbuf[6]
    lg_tm = dyn.f32v(O + 58 * KB, 4, 64)
    blgtm = c.B("ln_lgtm")
    for j in range(4):
        P.op("tensor", lambda E, o=ptr[0:128, 0:64], i=lg1[:, j * 128:(j + 1) * 128],
             idn=c.identf[0:64, 0:64]: E.transpose(o, i, idn),
             reads=[blg, c.bident], writes=[bptr])
        P.op("vector", lambda E, o=lg_tm[:, j, :], i=ptr[0:128, 0:64]:
             E.tensor_copy(o, i), reads=[bptr], writes=[blgtm])
    emit_routing(c, ci, lg_tm, blgtm, O + 71 * KB)


def emit_routing(c, ci, lg_tm, blgtm, OB):
    P, t, dyn = c.P, c.t, c.dyn
    goff = ci * CH

    scores = dyn.f32v(OB, 4, 64)
    sfc = dyn.f32v(OB + 1024, 4, 64)
    msk = dyn.f32v(OB + 2048, 4, 64)
    cur = dyn.f32v(OB + 3072, 4, 64)
    sel = dyn.f32v(OB + 4096, 4, 64)
    sel8 = dyn.f32v(OB + 5120, 4, 64)
    w_ = dyn.f32v(OB + 6144, 4, 64)
    rw = dyn.f32v(OB + 7168, 4, 64)
    m1 = dyn.f32v(OB + 8192, 4, 8)
    mtmp = dyn.f32v(OB + 8320, 4, 8)
    gs = dyn.f32v(OB + 8448, 4, 8)
    curg = dyn.f32v(OB + 8576, 4, 8)
    selg = dyn.f32v(OB + 8704, 4, 8)
    gmask = dyn.f32v(OB + 8832, 4, 8)
    mm_ = dyn.f32v(OB + 8960, 4, 1)
    den = dyn.f32v(OB + 9088, 4, 1)
    rcp = dyn.f32v(OB + 9216, 4, 1)
    rwTs = dyn.f32(OB + 9344, 512)[0:64, :]
    rwTb_st = dyn.bf16(OB + 11392, 512)[0:64, :]

    bS = c.B("route", small=True)

    def V(fn):
        P.op("vector", fn, reads=[bS], writes=[bS])

    P.op("scalar", lambda E, o=scores, i=lg_tm: E.activation(o, i, AF.Sigmoid),
         reads=[blgtm], writes=[bS])
    V(lambda E: E.tensor_add(sfc, scores, c.gbias.unsqueeze(1).broadcast_to([128, 4, 64])))
    V(lambda E: E.tensor_reduce(m1, sfc.rearrange("p a (g k) -> p a g k", k=8), AX.X, ALU.max))
    V(lambda E: E.tensor_tensor(msk.rearrange("p a (g k) -> p a g k", k=8),
                                sfc.rearrange("p a (g k) -> p a g k", k=8),
                                m1.unsqueeze(3).broadcast_to([128, 4, 8, 8]), ALU.is_ge))
    V(lambda E: E.scalar_tensor_tensor(msk, msk, -1e30, sfc, ALU.mult, ALU.add))
    V(lambda E: E.tensor_reduce(mtmp, msk.rearrange("p a (g k) -> p a g k", k=8), AX.X, ALU.max))
    V(lambda E: E.tensor_add(gs, m1, mtmp))
    V(lambda E: E.memset(gmask, 0.0))
    V(lambda E: E.tensor_copy(curg, gs))
    for _ in range(4):
        V(lambda E: E.tensor_reduce(mm_, curg, AX.X, ALU.max))
        V(lambda E: E.tensor_tensor(selg, curg, mm_.broadcast_to([128, 4, 8]), ALU.is_ge))
        V(lambda E: E.tensor_add(gmask, gmask, selg))
        V(lambda E: E.scalar_tensor_tensor(curg, selg, -1e30, curg, ALU.mult, ALU.add))
    V(lambda E: E.tensor_tensor(msk.rearrange("p a (g k) -> p a g k", k=8),
                                sfc.rearrange("p a (g k) -> p a g k", k=8),
                                gmask.unsqueeze(3).broadcast_to([128, 4, 8, 8]), ALU.mult))
    V(lambda E: E.memset(sel8, 0.0))
    V(lambda E: E.tensor_copy(cur, msk))
    for _ in range(8):
        V(lambda E: E.tensor_reduce(mm_, cur, AX.X, ALU.max))
        V(lambda E: E.tensor_tensor(sel, cur, mm_.broadcast_to([128, 4, 64]), ALU.is_ge))
        V(lambda E: E.tensor_add(sel8, sel8, sel))
        V(lambda E: E.scalar_tensor_tensor(cur, sel, -1e30, cur, ALU.mult, ALU.add))
    V(lambda E: E.tensor_mul(w_, sel8, scores))
    V(lambda E: E.tensor_reduce(den, w_, AX.X, ALU.add))
    V(lambda E: E.tensor_scalar(den, den, 1e-20, None, ALU.add))
    V(lambda E: E.reciprocal(rcp, den))
    V(lambda E: E.tensor_scalar(rcp, rcp, 2.5, None, ALU.mult))
    V(lambda E: E.tensor_tensor(rw, w_, rcp.broadcast_to([128, 4, 64]), ALU.mult))
    ptr2 = c.psum[7]
    bptr2 = c.pbuf[7]
    for j in range(4):
        P.op("tensor", lambda E, o=ptr2[0:64, 0:128], i=rw[:, j, :], idn=c.identf:
             E.transpose(o, i, idn), reads=[bS, c.bident], writes=[bptr2])
        P.op("vector", lambda E, o=rwTs[:, j * 128:(j + 1) * 128], i=ptr2[0:64, 0:128]:
             E.tensor_copy(o, i), reads=[bptr2], writes=[bS])
    V(lambda E: E.tensor_copy(rwTb_st, rwTs))
    P.op("sync", lambda E, o=t["rwT_h"][:, goff:goff + CH], i=rwTb_st:
         E.dma_start(out=o, in_=i), reads=[bS], writes=[c.dbuf["rwT_h"]], inc=16)


def emit_phase_a(c):
    for ci in range(NCH):
        emit_ln_chunk(c, ci, c.t["x_fm"], c.dbuf["x_fm"], c.t["xn_h"], c.dbuf["xn_h"], True)




def emit_phase_a2(c):
    P, t, dyn = c.P, c.t, c.dyn
    O = LN_O
    rwt_s = dyn.bf16(O, TOK)[0:64, :]
    mask = dyn.f32(O + 8 * KB, TOK)[0:64, :]
    cumA = dyn.f32(O + 24 * KB, TOK)[0:64, :]
    cumB = dyn.f32(O + 40 * KB, TOK)[0:64, :]
    tgt = dyn.f32(O + 56 * KB, TOK)[0:64, :]
    ovfm = dyn.f32(O + 40 * KB, TOK)[0:64, :]
    tgt_tm = dyn.f32v(O + 8 * KB, 32, 64)
    rw_tm = dyn.f32v(O + 16 * KB, 32, 64)
    cur = dyn.f32v(O + 24 * KB, 32, 64)
    eq = dyn.f32v(O + 32 * KB, 32, 64)
    tmp_e = dyn.f32v(O + 48 * KB, 32, 64)
    SM = O + 44 * KB
    mn = dyn.f32v(SM, 32, 1)
    gbv = dyn.f32v(SM + 256, 32, 1)
    tgt8 = dyn.f32v(SM + 512, 32, 8)
    rw8 = dyn.f32v(SM + 1536, 32, 8)
    idx_i = dyn.f.bitcast(mybir.dt.int32)[:, (SM + 2560) // 4:(SM + 2560) // 4 + 256].rearrange(
        "p (a b) -> p a b", a=32)

    bS = c.B("a2", small=True)

    def V(fn):
        P.op("vector", fn, reads=[bS], writes=[bS])

    P.op("sync", lambda E, o=rwt_s, i=t["rwT_h"]: E.dma_start(out=o, in_=i),
         reads=[c.dbuf["rwT_h"]], writes=[bS], inc=16)
    P.drain("sync")
    V(lambda E: E.tensor_single_scalar(mask, rwt_s, 0.0, ALU.is_gt))
    V(lambda E: E.tensor_copy(cumA, mask))
    srcb, dstb = cumA, cumB
    sh = 1
    while sh < TOK:
        V(lambda E, s_=srcb, d_=dstb, ss=sh: E.tensor_add(d_[:, ss:], s_[:, ss:], s_[:, :TOK - ss]))
        V(lambda E, s_=srcb, d_=dstb, ss=sh: E.tensor_copy(d_[:, 0:ss], s_[:, 0:ss]))
        srcb, dstb = dstb, srcb
        sh *= 2
    cum = srcb
    V(lambda E: E.tensor_sub(tgt, cum, mask))
    V(lambda E: E.tensor_single_scalar(ovfm, tgt, float(CAP), ALU.is_ge))
    V(lambda E: E.tensor_scalar(tgt, tgt, c.ecap_col[0:64, 0:1], None, ALU.add))
    V(lambda E: E.tensor_scalar(tgt, tgt, BIGF, None, ALU.add))
    V(lambda E: E.scalar_tensor_tensor(tgt, mask, -BIGF, tgt, ALU.mult, ALU.add))
    V(lambda E: E.scalar_tensor_tensor(tgt, ovfm, BIGF, tgt, ALU.mult, ALU.add))
    V(lambda E: E.tensor_scalar_min(tgt, tgt, float(TRASH)))
    ptrf = c.psum[6]
    ptrb2 = c.psum[7].bitcast(BF16)
    bptrf, bptrb = c.pbuf[6], c.pbuf[7]
    for j in range(32):
        P.op("tensor", lambda E, o=ptrf[0:128, 0:64], i=tgt[:, j * 128:(j + 1) * 128],
             idn=c.identf[0:64, 0:64]: E.transpose(o, i, idn),
             reads=[bS, c.bident], writes=[bptrf])
        P.op("vector", lambda E, o=tgt_tm[:, j, :], i=ptrf[0:128, 0:64]:
             E.tensor_copy(o, i), reads=[bptrf], writes=[bS])
        P.op("tensor", lambda E, o=ptrb2[0:128, 0:64], i=rwt_s[:, j * 128:(j + 1) * 128],
             idn=c.identb[0:64, 0:64]: E.transpose(o, i, idn),
             reads=[bS, c.bidentb], writes=[bptrb])
        P.op("vector", lambda E, o=rw_tm[:, j, :], i=ptrb2[0:128, 0:64]:
             E.tensor_copy(o, i), reads=[bptrb], writes=[bS])
    V(lambda E: E.tensor_copy(cur, tgt_tm))
    for j in range(8):
        V(lambda E: E.tensor_reduce(mn, cur, AX.X, ALU.min))
        V(lambda E, jj=j: E.tensor_copy(tgt8[:, :, jj:jj + 1], mn))
        V(lambda E: E.tensor_tensor(eq, cur, mn.broadcast_to([128, 32, 64]), ALU.is_le))
        V(lambda E: E.tensor_mul(tmp_e, eq, rw_tm))
        V(lambda E, jj=j: E.tensor_reduce(rw8[:, :, jj:jj + 1], tmp_e, AX.X, ALU.add))
        V(lambda E: E.tensor_single_scalar(gbv, mn, float(NSLOT), ALU.is_lt))
        V(lambda E, jj=j: E.tensor_tensor(rw8[:, :, jj:jj + 1], rw8[:, :, jj:jj + 1],
                                          gbv, ALU.mult))
        V(lambda E: E.scalar_tensor_tensor(cur, eq, BIGF, cur, ALU.mult, ALU.add))
    V(lambda E: E.tensor_copy(idx_i, tgt8))
    P.op("sync", lambda E, o=t["idx_h"].rearrange("(tt p) j -> p tt j", p=128), i=idx_i:
         E.dma_start(out=o, in_=i), reads=[bS], writes=[c.dbuf["idx_h"]], inc=16)
    P.op("sync", lambda E, o=t["rw8_h"].rearrange("(tt p) j -> p tt j", p=128), i=rw8:
         E.dma_start(out=o, in_=i), reads=[bS], writes=[c.dbuf["rw8_h"]], inc=16)


def emit_phase_b(c):
    P, t, dyn = c.P, c.t, c.dyn
    O = LN_O
    xnb = dyn.bf16v(O, 8, S)
    scf = dyn.f32(O + 32 * KB, S)
    attn_t = [dyn.bf16(O + 40 * KB + i * 4 * KB, S) for i in range(2)]
    qTb = dyn.bf16(O + 48 * KB, S)[0:64, :]
    kTb = dyn.bf16(O + 52 * KB, S)[0:64, :]
    vTb = dyn.bf16(O + 56 * KB, S)[0:64, :]
    v_tm = dyn.bf16(O + 60 * KB, 16 * 128)
    v_sc = [dyn.bf16(O + 64 * KB + i * 256, 128) for i in range(2)]
    sdst = dyn.f32(O + 66 * KB, S)
    nmx = dyn.f32(O + 74 * KB, 16)[:, 0:1]
    rs = dyn.f32(O + 74 * KB + 64, 16)[:, 0:1]
    rcq = dyn.f32(O + 74 * KB + 128, 16)[:, 0:1]

    pq = c.psum[0]
    ptr = c.psum[1]
    ptrb = ptr.bitcast(BF16)
    psc = [c.psum[2], c.psum[3]]
    pout = [c.psum[4 + i] for i in range(4)]

    bxnb = c.B("b_xnb")
    bq, bk, bv = c.B("b_q"), c.B("b_k"), c.B("b_v")
    bpq = c.pbuf[0]
    bvtm = c.B("b_vtm")
    bptr = c.pbuf[1]
    bscf = c.B("b_scf")
    bnmx = c.B("b_nmx", small=True)
    brs = c.B("b_rs", small=True)
    batt = [c.B("b_att0"), c.B("b_att1")]
    bvsc = [c.B("b_vsc0"), c.B("b_vsc1")]
    bpsc = [c.pbuf[2], c.pbuf[3]]
    bpout = c.pbuf[4]
    bpout2 = [c.pbuf[4], c.pbuf[5], c.pbuf[6], c.pbuf[7]]
    bsd = c.B("b_sd")

    for b in range(BPC):
        P.op("sync", lambda E, o=xnb, i=t["xn_h"][:, :, b * S:(b + 1) * S]:
             E.dma_start(out=o, in_=i), reads=[c.dbuf["xn_h"]], writes=[bxnb], inc=16)
        P.drain("sync")
        for wi, (dst, bd) in enumerate([(qTb, bq), (kTb, bk), (vTb, bv)]):
            for ic in range(4):
                for et in range(8):
                    P.op("tensor", lambda E, o=pq[0:64, :],
                         l=c.wqkvb[:, wi * 512 + et * 64: wi * 512 + (et + 1) * 64],
                         r=xnb[:, et, ic * 512:(ic + 1) * 512],
                         s=(et == 0), p=(et == 7): E.matmul(o, l, r, start=s, stop=p),
                         reads=[c.bwqkv, bxnb], writes=[bpq])
                P.op("scalar", lambda E, o=dst[:, ic * 512:(ic + 1) * 512], i=pq[0:64, :]:
                     E.copy(o, i), reads=[bpq], writes=[bd])
        for kt in range(16):
            P.op("tensor", lambda E, o=ptrb[0:128, 0:64], i=vTb[:, kt * 128:(kt + 1) * 128],
                 idn=c.identb[0:64, 0:64]: E.transpose(o, i, idn),
                 reads=[bv, c.bidentb], writes=[bptr])
            P.op("vector", lambda E, o=v_tm[:, kt * 128:kt * 128 + 64], i=ptrb[0:128, 0:64]:
                 E.tensor_copy(o, i), reads=[bptr], writes=[bvtm])
            P.op("vector", lambda E, o=v_tm[:, kt * 128 + 64:(kt + 1) * 128], i=ptrb[0:128, 0:64]:
                 E.tensor_copy(o, i), reads=[bptr], writes=[bvtm])
        for kt in range(16):
            par = kt % 2
            for ic in range(4):
                P.op("tensor", lambda E, o=psc[ic % 2][:, :], l=kTb[:, kt * 128:(kt + 1) * 128],
                     r=qTb[:, ic * 512:(ic + 1) * 512]: E.matmul(o, l, r, start=True, stop=True),
                     reads=[bk, bq], writes=[bpsc[ic % 2]])
                P.op("scalar", lambda E, o=scf[:, ic * 512:(ic + 1) * 512], i=psc[ic % 2][:, :]:
                     E.copy(o, i), reads=[bpsc[ic % 2]], writes=[bscf])
            P.op("vector", lambda E, o=nmx, i=scf:
                 E.reduce_max(o, i, axis=AX.X, negate=True), reads=[bscf], writes=[bnmx])
            P.op("scalar", lambda E, o=attn_t[par], i=scf, bi=nmx, ac=rs:
                 E.activation(o, i, AF.Exp, bias=bi, accum_out=ac),
                 reads=[bscf, bnmx], writes=[batt[par], brs])
            P.op("vector", lambda E, o=rcq, i=rs: E.reciprocal(o, i),
                 reads=[brs], writes=[brs])
            P.op("vector", lambda E, o=v_sc[par], i=v_tm[:, kt * 128:(kt + 1) * 128], sc=rcq:
                 E.tensor_scalar(o, i, sc, None, ALU.mult),
                 reads=[bvtm, brs], writes=[bvsc[par]])
            for ic in range(4):
                P.op("tensor", lambda E, o=pout[ic][:, :], l=v_sc[par],
                     r=attn_t[par][:, ic * 512:(ic + 1) * 512],
                     s=(kt == 0), p=(kt == 15): E.matmul(o, l, r, start=s, stop=p),
                     reads=[bvsc[par], batt[par]], writes=[bpout2[ic]])
        for ic in range(4):
            P.op("scalar", lambda E, o=sdst[:, ic * 512:(ic + 1) * 512], i=pout[ic][:, :]:
                 E.copy(o, i), reads=[bpout2[ic]], writes=[bsd])
        P.op("sync", lambda E, o=t["sdup_h"][:, b * S:(b + 1) * S], i=sdst:
             E.dma_start(out=o, in_=i), reads=[bsd], writes=[c.dbuf["sdup_h"]], inc=16)


def emit_phase_c(c, entry_toks=(), with_d_ln=False):
    P, t, dyn = c.P, c.t, c.dyn
    wbuf = [dyn.bf16(0 + i * 36864, BLOBW) for i in range(2)]
    cx_tm = [dyn.bf16v(72 * KB + i * 8 * KB, 4, 1024) for i in range(2)]
    rh = [dyn.bf16v(88 * KB + i * 8 * KB, 8, 512) for i in range(2)]
    hb = [dyn.bf16(104 * KB + i * 6 * KB, 6 * 512) for i in range(2)]
    sg = [dyn.f32(116 * KB + i * 2 * KB, 512) for i in range(2)]
    co_st = [dyn.bf16v(120 * KB + i * 8 * KB, 4, 1024) for i in range(2)]
    sc_tile = [dyn.bf16(152 * KB + i * 2 * KB, 1024) for i in range(2)]
    idx_all = dyn.f.bitcast(mybir.dt.int32)[:, (192 * KB) // 4:(192 * KB) // 4 + 256].rearrange(
        "p (a b) -> p a b", a=32)
    rw8_all = dyn.f32v(193 * KB, 32, 8)
    sdup_s = dyn.f32(122 * KB, TOK)
    Gt = [dyn.bf16(72 * KB + i * 2 * KB, 1024) for i in range(4)]
    acc = [dyn.f32(80 * KB + i * 4 * KB, 1024) for i in range(2)]
    gsb = [dyn.f32(88 * KB + i * 4 * KB, 1024) for i in range(2)]
    x_tile = [dyn.f32(96 * KB + i * 4 * KB, 1024) for i in range(2)]
    x1f_st = [dyn.f32v(104 * KB + i * 4 * KB, 8, 128) for i in range(2)]
    s_small = [dyn.f32(112 * KB + i * 512, 64) for i in range(2)]
    sht = [dyn.bf16(113 * KB + i * 2 * KB, 1024) for i in range(4)]
    idxt2 = [dyn.f.bitcast(mybir.dt.int32)[:, (121 * KB + i * 64) // 4:(121 * KB + i * 64) // 4 + 8]
             for i in range(2)]
    rw8t = [dyn.f32(121 * KB + 512 + i * 512, 8) for i in range(2)]

    bwbuf = [c.B("c_wb0"), c.B("c_wb1")]
    bcx = [c.B("c_cx0"), c.B("c_cx1")]
    brh = [c.B("c_rh0"), c.B("c_rh1")]
    bhb = [c.B("c_hb0"), c.B("c_hb1")]
    bsg = [c.B("c_sg0"), c.B("c_sg1")]
    bco = [c.B("c_co0"), c.B("c_co1")]
    bsc = [c.B("c_sc0"), c.B("c_sc1")]
    bidx_all = c.B("c_idxall")
    brw8_all = c.B("c_rw8all")
    bsdup = c.B("c_sdup")
    bG = [c.B(f"c_G{i}") for i in range(4)]
    bacc = [c.B("c_acc0", small=True), c.B("c_acc1", small=True)]
    bgs = [c.B("c_gs0", small=True), c.B("c_gs1", small=True)]
    bxt = [c.B("c_xt0"), c.B("c_xt1")]
    bx1f = [c.B("c_x1f0"), c.B("c_x1f1")]
    bss = [c.B("c_ss0"), c.B("c_ss1")]
    bsh = [c.B(f"c_sh{i}") for i in range(4)]
    bidxt2 = [c.B("c_idxt2a"), c.B("c_idxt2b")]
    brw8t = [c.B("c_rw8t0"), c.B("c_rw8t1")]

    pag = [c.psum[0], c.psum[1]]
    pau = [c.psum[2], c.psum[3]]
    pdn = [c.psum[4], c.psum[5]]
    ptr_ = c.psum[6]
    ptrb_ = ptr_.bitcast(BF16)
    ptr2_ = c.psum[7]
    bpag = [c.pbuf[0], c.pbuf[1]]
    bpau = [c.pbuf[2], c.pbuf[3]]
    bpdn = [c.pbuf[4], c.pbuf[5]]
    bptr = c.pbuf[6]
    bptr2 = c.pbuf[7]

    worder = [64, 65] + list(range(64))
    wseq = {e: i for i, e in enumerate(worder)}

    def wdma(e):
        P.op("gpsimd", lambda E, o=wbuf[wseq[e] % 2], i=t["moeblob"][e]:
             E.dma_start(out=o, in_=i), writes=[bwbuf[wseq[e] % 2]], inc=16)

    wdma(64)
    wdma(65)

    P.op("sync", lambda E, o=idx_all, i=t["idx_h"].rearrange("(tt p) j -> p tt j", p=128):
         E.dma_start(out=o, in_=i), reads=[c.dbuf["idx_h"]], writes=[bidx_all], inc=16)
    P.op("sync", lambda E, o=rw8_all, i=t["rw8_h"].rearrange("(tt p) j -> p tt j", p=128):
         E.dma_start(out=o, in_=i), reads=[c.dbuf["rw8_h"]], writes=[brw8_all], inc=16)
    P.drain("sync")
    for tt in range(32):
        par = tt % 2
        P.op("sync", lambda E, o=sc_tile[par],
             i=t["xn_tm_h"][tt * 128:(tt + 1) * 128, :]:
             E.dma_start(out=o, in_=i), reads=[c.dbuf["xn_tm_h"]], writes=[bsc[par]], inc=16)
        for j in range(8):
            P.op("gpsimd", lambda E, o=t["compact_x"], off=idx_all[:, tt, j:j + 1], i=sc_tile[par]:
                 E.indirect_dma_start(out=o, out_offset=IndirectOffsetOnAxis(off, 0),
                                      in_=i, in_offset=None),
                 reads=[bsc[par], bidx_all], writes=[c.dbuf["compact_x"]], inc=16)

    P.drain("gpsimd")

    slots = [("s", 64 + hh, c8) for hh in range(2) for c8 in range(8)] + \
            [("e", e, ch) for e in range(64) for ch in range(3)]

    first_load = [True]

    def emit_load(k):
        kind, e, ch = slots[k]
        par = k % 2
        ext = list(entry_toks) if first_load[0] else ()
        first_load[0] = False
        if kind == "e":
            base = e * CAP + ch * 512
            P.op("sync", lambda E, o=cx_tm[par],
                 i=t["compact_x"][base:base + 512, :].rearrange("(tt p) cc -> p tt cc", p=128):
                 E.dma_start(out=o, in_=i),
                 reads=[c.dbuf["compact_x"]], writes=[bcx[par]], inc=16, extra=ext)
        else:
            goff = ch * 512
            P.op("sync", lambda E, o=rh[par], i=t["xn_h"][:, :, goff:goff + 512]:
                 E.dma_start(out=o, in_=i), reads=[c.dbuf["xn_h"]], writes=[brh[par]],
                 inc=16, extra=ext)

    ptrbanks = [ptr_.bitcast(BF16), ptr2_.bitcast(BF16)]
    bptrx = [c.pbuf[6], c.pbuf[7]]

    def emit_tr(k):
        kind, e, ch = slots[k]
        if kind != "e":
            return
        par = k % 2
        for tt in range(4):
            bk = tt % 2
            pb = ptrbanks[bk]
            for ct in range(8):
                P.op("tensor", lambda E, o=pb[0:128, ct * 128:(ct + 1) * 128],
                     i=cx_tm[par][:, tt, ct * 128:(ct + 1) * 128], idn=c.identb:
                     E.transpose(o, i, idn), reads=[bcx[par], c.bidentb],
                     writes=[bptrx[bk]])
            P.op("vector", lambda E, o=rh[par][:, :, tt * 128:(tt + 1) * 128],
                 i=pb[0:128, 0:1024].rearrange("p (a b) -> p a b", a=8):
                 E.tensor_copy(o, i), reads=[bptrx[bk]], writes=[brh[par]])

    def emit_up(k):
        kind, e, ch = slots[k]
        par = k % 2
        wpar = wseq[e] % 2
        for it in range(6):
            ap_ = it % 2
            for et in range(8):
                P.op("tensor", lambda E, o=pag[ap_][:, :],
                     l=wbuf[wpar][:, et * IPAD + it * 128: et * IPAD + (it + 1) * 128],
                     r=rh[par][:, et, :], s=(et == 0), p=(et == 7):
                     E.matmul(o, l, r, start=s, stop=p),
                     reads=[bwbuf[wpar], brh[par]], writes=[bpag[ap_]])
            for et in range(8):
                P.op("tensor", lambda E, o=pau[ap_][:, :],
                     l=wbuf[wpar][:, SEC + et * IPAD + it * 128: SEC + et * IPAD + (it + 1) * 128],
                     r=rh[par][:, et, :], s=(et == 0), p=(et == 7):
                     E.matmul(o, l, r, start=s, stop=p),
                     reads=[bwbuf[wpar], brh[par]], writes=[bpau[ap_]])
            P.op("scalar", lambda E, o=sg[ap_], i=pag[ap_][:, :]:
                 E.activation(o, i, AF.Silu), reads=[bpag[ap_]], writes=[bsg[ap_]])
            P.op("vector", lambda E, o=hb[par][:, it * 512:(it + 1) * 512], i=sg[ap_],
                 j=pau[ap_][:, :]: E.tensor_mul(o, i, j),
                 reads=[bsg[ap_], bpau[ap_]], writes=[bhb[par]])

    def emit_down(k):
        kind, e, ch = slots[k]
        par = k % 2
        wpar = wseq[e] % 2
        for mt in range(4):
            dp = mt % 2
            for nh in range(2):
                for it in range(6):
                    P.op("tensor", lambda E, o=pdn[dp][:, :],
                         l=hb[par][:, it * 512 + mt * 128: it * 512 + (mt + 1) * 128],
                         r=wbuf[wpar][:, 2 * SEC + it * 1024 + nh * 512: 2 * SEC + it * 1024 + (nh + 1) * 512],
                         s=(it == 0), p=(it == 5): E.matmul(o, l, r, start=s, stop=p),
                         reads=[bwbuf[wpar], bhb[par]], writes=[bpdn[dp]])
                P.op("scalar", lambda E, o=co_st[par][:, mt, nh * 512:(nh + 1) * 512],
                     i=pdn[dp][:, :]: E.copy(o, i), reads=[bpdn[dp]], writes=[bco[par]])
        if kind == "e":
            base = e * CAP + ch * 512
            dst = t["compact_out"][base:base + 512, :].rearrange("(tt p) cc -> p tt cc", p=128)
            dbf = c.dbuf["compact_out"]
        else:
            hh = e - 64
            base = hh * TOK + ch * 512
            dst = t["shared_h"][base:base + 512, :].rearrange("(tt p) cc -> p tt cc", p=128)
            dbf = c.dbuf["shared_h"]
        P.op("sync", lambda E, o=dst, i=co_st[par]: E.dma_start(out=o, in_=i),
             reads=[bco[par]], writes=[dbf], inc=16)

    emit_load(0)
    emit_tr(0)
    for k in range(len(slots)):
        if k + 1 < len(slots):
            emit_load(k + 1)
        emit_up(k)
        if k + 1 < len(slots):
            emit_tr(k + 1)
        if k > 0:
            emit_down(k - 1)
            kindp, ep, chp = slots[k - 1]
            last_ch = 2 if kindp == "e" else 7
            if chp == last_ch and wseq[ep] + 2 < NE:
                wdma(worder[wseq[ep] + 2])
    emit_down(len(slots) - 1)
    c.d_ln_done = False

    P.barrier()
    bAS = c.B("c_asm", small=False)
    P.op("sync", lambda E, o=sdup_s, i=t["sdup_h"]: E.dma_start(out=o, in_=i),
         reads=[c.dbuf["sdup_h"]], writes=[bsdup], inc=16)
    P.drain("sync")

    for tt in range(32):
        gp = tt % 2
        goff = tt * 128
        b = goff // S
        off = goff % S
        P.op("sync", lambda E, o=x_tile[gp], i=t["x_tm"][goff:goff + 128, :]:
             E.dma_start(out=o, in_=i), writes=[bxt[gp]], inc=16)
        P.op("tensor", lambda E, o=[ptr_, ptr2_][gp][0:128, 0:64],
             i=sdup_s[0:64, goff:goff + 128],
             idn=c.identf[0:64, 0:64]: E.transpose(o, i, idn),
             reads=[bsdup, c.bident], writes=[bptrx[gp]])
        P.op("vector", lambda E, o=s_small[gp], i=[ptr_, ptr2_][gp][0:128, 0:64]:
             E.tensor_copy(o, i), reads=[bptrx[gp]], writes=[bss[gp]])
        for hh in range(2):
            sj = hh + 2 * gp
            P.op("sync", lambda E, o=sht[sj],
                 i=t["shared_h"][hh * TOK + goff: hh * TOK + goff + 128, :]:
                 E.dma_start(out=o, in_=i), reads=[c.dbuf["shared_h"]], writes=[bsh[sj]], inc=16)
        P.drain("sync")
        for jp in range(4):
            for j in (2 * jp, 2 * jp + 1):
                gj = (j % 2) + 2 * gp
                P.op("gpsimd", lambda E, o=Gt[gj], off=idx_all[:, tt, j:j + 1], i=t["compact_out"]:
                     E.indirect_dma_start(out=o, out_offset=None, in_=i,
                                          in_offset=IndirectOffsetOnAxis(off, 0)),
                     reads=[c.dbuf["compact_out"], bidx_all], writes=[bG[gj]], inc=16)
            P.drain("gpsimd")
            for j in (2 * jp, 2 * jp + 1):
                gj = (j % 2) + 2 * gp
                if j == 0:
                    P.op("vector", lambda E, o=acc[gp], i=Gt[gj], sc=rw8_all[:, tt, 0:1]:
                         E.tensor_scalar(o, i, sc, None, ALU.mult),
                         reads=[bG[gj], brw8_all], writes=[bacc[gp]])
                else:
                    a_in = acc[gp] if j % 2 == 1 else gsb[gp]
                    a_out = gsb[gp] if j % 2 == 1 else acc[gp]
                    P.op("vector", lambda E, o=a_out, i=Gt[gj], sc=rw8_all[:, tt, j:j + 1], jj=a_in:
                         E.scalar_tensor_tensor(o, i, sc, jj, ALU.mult, ALU.add),
                         reads=[bG[gj], brw8_all, bacc[gp], bgs[gp]],
                         writes=[bacc[gp], bgs[gp]])
        for hh in range(2):
            sj = hh + 2 * gp
            a_in = gsb[gp] if hh == 0 else acc[gp]
            a_out = acc[gp] if hh == 0 else gsb[gp]
            P.op("vector", lambda E, o=a_out, i=a_in, j=sht[sj]: E.tensor_add(o, i, j),
                 reads=[bsh[sj], bacc[gp], bgs[gp]], writes=[bacc[gp], bgs[gp]])
        P.op("scalar", lambda E, o=acc[gp], i=gsb[gp]: E.activation(o, i, AF.Silu),
             reads=[bgs[gp]], writes=[bacc[gp]])
        P.op("vector", lambda E, o=gsb[gp].rearrange("p (r d) -> p r d", r=16),
             i=acc[gp].rearrange("p (r d) -> p r d", r=16),
             j=s_small[gp].unsqueeze(1).broadcast_to([128, 16, 64]): E.tensor_mul(o, i, j),
             reads=[bss[gp], bacc[gp]], writes=[bgs[gp]])
        P.op("vector", lambda E, o=acc[gp], i=gsb[gp], j=x_tile[gp]: E.tensor_add(o, i, j),
             reads=[bxt[gp], bgs[gp]], writes=[bacc[gp]])
        for half in range(2):
            bk = (gp + half) % 2
            pbf = [ptr_, ptr2_][bk]
            for ct4 in range(4):
                ct = half * 4 + ct4
                P.op("tensor", lambda E, o=pbf[0:128, ct4 * 128:(ct4 + 1) * 128],
                     i=acc[gp][:, ct * 128:(ct + 1) * 128], idn=c.identf:
                     E.transpose(o, i, idn), reads=[bacc[gp], c.bident], writes=[bptrx[bk]])
            P.op("scalar", lambda E, o=x1f_st[gp][:, half * 4:(half + 1) * 4, :],
                 i=pbf[0:128, 0:512].rearrange("p (a b) -> p a b", a=4):
                 E.copy(o, i), reads=[bptrx[bk]], writes=[bx1f[gp]])
        dst = t["x1_h"][b, :, off:off + 128].rearrange("(n p) m -> p n m", p=128)
        P.op("sync", lambda E, o=dst, i=x1f_st[gp]: E.dma_start(out=o, in_=i),
             reads=[bx1f[gp]], writes=[c.dbuf["x1_h"]], inc=16)
        if with_d_ln and (tt + 1) % 4 == 0:
            emit_ln_chunk(c, tt // 4, t["x1_h"], c.dbuf["x1_h"], t["xn1_h"],
                          c.dbuf["xn1_h"], False, O=138 * KB)
            c.d_ln_done = True


def emit_phase_d(c):
    P, t, dyn = c.P, c.t, c.dyn
    c.qwTb = dyn.bf16(0, 8192)
    c.PaB = dyn.bf16(16 * KB, 2 * 8192)
    c.combTb = dyn.bf16(48 * KB, 8192)
    c.wrkvb = dyn.bf16(152 * KB, 3 * 8192)
    c.bqw, c.bPa, c.bcomb, c.bwrkv = c.B("d_qw"), c.B("d_Pa"), c.B("d_comb"), c.B("d_wrkv")
    P.op("gpsimd", lambda E, o=c.qwTb, i=t["qwT"]: E.dma_start(out=o, in_=i),
         writes=[c.bqw], inc=16)
    P.op("gpsimd", lambda E, o=c.combTb, i=t["combT"]: E.dma_start(out=o, in_=i),
         writes=[c.bcomb], inc=16)

    if getattr(c, "d_ln_done", False):
        pass
    else:
        for ci in range(NCH):
            emit_ln_chunk(c, ci, c.t["x1_h"], c.dbuf["x1_h"], c.t["xn1_h"], c.dbuf["xn1_h"], False)
    c.P.barrier()

    for i, nm in enumerate(["wrT", "wk2T", "wv2T"]):
        P.op("gpsimd", lambda E, o=c.wrkvb[:, i * 8192:(i + 1) * 8192], src=t[nm]:
             E.dma_start(out=o, in_=src), writes=[c.bwrkv], inc=16)

    MqB = dyn.bf16(64 * KB, 4 * 512)
    xnb = dyn.bf16v(68 * KB, 8, S)
    xtm = dyn.bf16(100 * KB, 16 * 1024)
    bxnb, bxtm, bMq = c.B("d_xnb"), c.B("d_xtm"), c.B("d_Mq")
    ptr = c.psum[0]
    ptrb = ptr.bitcast(BF16)
    bptr = c.pbuf[0]
    pmq = [c.psum[1 + q] for q in range(4)]
    bpmq = [c.pbuf[1 + q] for q in range(4)]
    ppp = [c.psum[5], c.psum[6]]
    bppp = [c.pbuf[5], c.pbuf[6]]

    for b in range(BPC):
        P.op("sync", lambda E, o=xnb, i=t["xn1_h"][:, :, b * S:(b + 1) * S]:
             E.dma_start(out=o, in_=i), reads=[c.dbuf["xn1_h"]], writes=[bxnb], inc=16)
        P.drain("sync")
        for ut in range(16):
            for ft in range(8):
                P.op("tensor", lambda E, o=ptrb[0:128, 0:128],
                     i=xnb[:, ft, ut * 128:(ut + 1) * 128], idn=c.identb:
                     E.transpose(o, i, idn), reads=[bxnb, c.bidentb], writes=[bptr])
                P.op("vector", lambda E,
                     o=xtm[:, ut * 1024 + ft * 128: ut * 1024 + (ft + 1) * 128],
                     i=ptrb[0:128, 0:128]: E.tensor_copy(o, i), reads=[bptr], writes=[bxtm])
        for fh in range(2):
            for ut in range(16):
                for q in range(4):
                    P.op("tensor", lambda E, o=pmq[q][:, fh * 256:(fh + 1) * 256],
                         l=xtm[:, ut * 1024 + q * 256 + fh * 128: ut * 1024 + q * 256 + (fh + 1) * 128],
                         r=xtm[:, ut * 1024 + q * 256: ut * 1024 + (q + 1) * 256],
                         s=(ut == 0), p=(ut == 15): E.matmul(o, l, r, start=s, stop=p),
                         reads=[bxtm], writes=[bpmq[q]])
            for q in range(4):
                P.op("scalar", lambda E, o=MqB[:, q * 512 + fh * 256: q * 512 + (fh + 1) * 256],
                     i=pmq[q][:, fh * 256:(fh + 1) * 256]:
                     E.copy(o, i), reads=[bpmq[q]], writes=[bMq])
        for a in range(4):
            for q in range(4):
                for eh in range(2):
                    pp = ppp[eh]
                    for fh in range(2):
                        P.op("tensor", lambda E, o=pp[:, 0:256],
                             l=c.qwTb[:, ((a * 4 + q) * 2 + fh) * 256 + eh * 128:
                                      ((a * 4 + q) * 2 + fh) * 256 + (eh + 1) * 128],
                             r=MqB[:, q * 512 + fh * 256: q * 512 + (fh + 1) * 256],
                             s=(fh == 0), p=(fh == 1): E.matmul(o, l, r, start=s, stop=p),
                             reads=[c.bqw, bMq], writes=[bppp[eh]])
                    dst = c.PaB[:, b * 8192 + a * 2048 + eh * 1024 + q * 256:
                                b * 8192 + a * 2048 + eh * 1024 + (q + 1) * 256]
                    P.op("scalar", lambda E, o=dst, i=ppp[eh][:, 0:256]: E.copy(o, i),
                         reads=[bppp[eh]], writes=[c.bPa])


def emit_phase_e(c):
    P, t, dyn = c.P, c.t, c.dyn
    rsb = dyn.f32(0, 8 * 512)
    kb = dyn.bf16(64 * KB, 8 * 512)
    attn_a = dyn.bf16(72 * KB, 8 * 512)
    s1c = dyn.f32(80 * KB, 8 * 512)
    tmpk = dyn.f32(96 * KB, 512)
    fo = [dyn.f32(98 * KB + i * 2 * KB, 512) for i in range(2)]
    oo = [dyn.f32(102 * KB + i * 2 * KB, 512) for i in range(2)]
    t1r = dyn.f32(106 * KB, 512)
    g1s = dyn.f32(108 * KB, 512)
    xe = dyn.bf16v(110 * KB, 8, 513)
    xe1 = dyn.f32v(119 * KB, 8, 512)
    xr = dyn.bf16(135 * KB, 8 * 512)
    xk = dyn.bf16(143 * KB, 8 * 512)

    brsb = c.B("e_rsb")
    bxe = c.B("e_xe")
    bxe1 = c.B("e_xe1")
    bxr, bxk, bkb = c.B("e_xr"), c.B("e_xk"), c.B("e_kb")
    batn = c.B("e_attn")
    bs1c = c.B("e_s1c")
    btmpk = c.B("e_tmpk", small=True)
    bfo = [c.B("e_fo0", small=True), c.B("e_fo1", small=True)]
    bg1s = c.B("e_g1s", small=True)
    boo = [c.B("e_oo0"), c.B("e_oo1")]
    bt1r = c.B("e_t1r", small=True)

    pa1 = [c.psum[0], c.psum[1]]
    po = [c.psum[2], c.psum[3]]
    pe_ = [c.psum[4], c.psum[5]]
    bpa1 = [c.pbuf[0], c.pbuf[1]]
    bpo = [c.pbuf[2], c.pbuf[3]]
    bpe = [c.pbuf[4], c.pbuf[5]]

    for b in range(BPC):
        for cc in range(4):
            off = cc * 512
            if cc == 0:
                P.op("vector", lambda E, o=xe[:, :, 0:1]: E.memset(o, 0.0), writes=[bxe])
                P.op("sync", lambda E, o=xe[:, :, 1:513],
                     i=t["xn1_h"][:, :, b * S: b * S + 512]:
                     E.dma_start(out=o, in_=i), reads=[c.dbuf["xn1_h"]], writes=[bxe], inc=16)
            else:
                P.op("sync", lambda E, o=xe[:, :, 0:513],
                     i=t["xn1_h"][:, :, b * S + off - 1: b * S + off + 512]:
                     E.dma_start(out=o, in_=i), reads=[c.dbuf["xn1_h"]], writes=[bxe], inc=16)
            src1 = t["x1_h"][b, :, off:off + 512].rearrange("(n p) m -> p n m", p=128)
            P.op("sync", lambda E, o=xe1, i=src1: E.dma_start(out=o, in_=i),
                 reads=[c.dbuf["x1_h"]], writes=[bxe1], inc=16)
            P.drain("sync")

            for a in range(4):
                for qgt in range(8):
                    pp = pa1[qgt % 2]
                    for eh in range(2):
                        P.op("tensor", lambda E, o=pp[:, :],
                             l=c.PaB[:, b * 8192 + a * 2048 + eh * 1024 + qgt * 128:
                                     b * 8192 + a * 2048 + eh * 1024 + (qgt + 1) * 128],
                             r=xe[:, 2 * a + eh, 1:513],
                             s=(eh == 0), p=(eh == 1): E.matmul(o, l, r, start=s, stop=p),
                             reads=[c.bPa, bxe], writes=[bpa1[qgt % 2]])
                    P.op("scalar", lambda E, o=attn_a[:, qgt * 512:(qgt + 1) * 512],
                         i=pa1[qgt % 2][:, :]: E.copy(o, i),
                         reads=[bpa1[qgt % 2]], writes=[batn])
                for gt in range(2):
                    pp = po[gt % 2]
                    for qgt in range(8):
                        P.op("tensor", lambda E, o=pp[:, :],
                             l=c.combTb[:, (a * 8 + qgt) * 256 + gt * 128:
                                        (a * 8 + qgt) * 256 + (gt + 1) * 128],
                             r=attn_a[:, qgt * 512:(qgt + 1) * 512],
                             s=(qgt == 0), p=(qgt == 7): E.matmul(o, l, r, start=s, stop=p),
                             reads=[c.bcomb, batn], writes=[bpo[gt % 2]])
                    P.op("scalar", lambda E, o=s1c[:, (2 * a + gt) * 512:(2 * a + gt + 1) * 512],
                         i=po[gt % 2][:, :]: E.copy(o, i), reads=[bpo[gt % 2]], writes=[bs1c])

            for ft in range(8):
                P.op("vector", lambda E, o=t1r, i=xe[:, ft, 1:513], sc=c.mur[:, ft:ft + 1]:
                     E.tensor_scalar(o, i, sc, None, ALU.mult),
                     reads=[bxe, c.bmu], writes=[bt1r])
                P.op("vector", lambda E, o=xr[:, ft * 512:(ft + 1) * 512], i=xe[:, ft, 0:512],
                     sc=c.omur[:, ft:ft + 1], j=t1r:
                     E.scalar_tensor_tensor(o, i, sc, j, ALU.mult, ALU.add),
                     reads=[bxe, c.bmu, bt1r], writes=[bxr])
                P.op("vector", lambda E, o=t1r, i=xe[:, ft, 1:513], sc=c.muk[:, ft:ft + 1]:
                     E.tensor_scalar(o, i, sc, None, ALU.mult),
                     reads=[bxe, c.bmu], writes=[bt1r])
                P.op("vector", lambda E, o=xk[:, ft * 512:(ft + 1) * 512], i=xe[:, ft, 0:512],
                     sc=c.omuk[:, ft:ft + 1], j=t1r:
                     E.scalar_tensor_tensor(o, i, sc, j, ALU.mult, ALU.add),
                     reads=[bxe, c.bmu, bt1r], writes=[bxk])
            for dt in range(8):
                pp = pe_[dt % 2]
                for ct in range(8):
                    P.op("tensor", lambda E, o=pp[:, :],
                         l=c.wrkvb[:, 0 * 8192 + ct * 1024 + dt * 128: 0 * 8192 + ct * 1024 + (dt + 1) * 128],
                         r=xr[:, ct * 512:(ct + 1) * 512],
                         s=(ct == 0), p=(ct == 7): E.matmul(o, l, r, start=s, stop=p),
                         reads=[c.bwrkv, bxr], writes=[bpe[dt % 2]])
                P.op("scalar", lambda E, o=rsb[:, dt * 512:(dt + 1) * 512], i=pe_[dt % 2][:, :]:
                     E.activation(o, i, AF.Sigmoid), reads=[bpe[dt % 2]], writes=[brsb])
            for dt in range(8):
                pp = pe_[dt % 2]
                for ct in range(8):
                    P.op("tensor", lambda E, o=pp[:, :],
                         l=c.wrkvb[:, 1 * 8192 + ct * 1024 + dt * 128: 1 * 8192 + ct * 1024 + (dt + 1) * 128],
                         r=xk[:, ct * 512:(ct + 1) * 512],
                         s=(ct == 0), p=(ct == 7): E.matmul(o, l, r, start=s, stop=p),
                         reads=[c.bwrkv, bxk], writes=[bpe[dt % 2]])
                P.op("vector", lambda E, o=tmpk, i=pe_[dt % 2][:, :]:
                     E.tensor_scalar_max(o, i, 0.0), reads=[bpe[dt % 2]], writes=[btmpk])
                P.op("scalar", lambda E, o=kb[:, dt * 512:(dt + 1) * 512], i=tmpk:
                     E.activation(o, i, AF.Square), reads=[btmpk], writes=[bkb])
            for dt in range(8):
                pp = pe_[dt % 2]
                par = dt % 2
                for ct in range(8):
                    P.op("tensor", lambda E, o=pp[:, :],
                         l=c.wrkvb[:, 2 * 8192 + ct * 1024 + dt * 128: 2 * 8192 + ct * 1024 + (dt + 1) * 128],
                         r=kb[:, ct * 512:(ct + 1) * 512],
                         s=(ct == 0), p=(ct == 7): E.matmul(o, l, r, start=s, stop=p),
                         reads=[c.bwrkv, bkb], writes=[bpe[dt % 2]])
                P.op("vector", lambda E, o=fo[par], i=rsb[:, dt * 512:(dt + 1) * 512],
                     j=pe_[dt % 2][:, :]: E.tensor_mul(o, i, j),
                     reads=[brsb, bpe[dt % 2]], writes=[bfo[par]])
                P.op("scalar", lambda E, o=g1s, i=fo[par]: E.activation(o, i, AF.Silu),
                     reads=[bfo[par]], writes=[bg1s])
                P.op("vector", lambda E, o=fo[par], i=g1s, j=s1c[:, dt * 512:(dt + 1) * 512]:
                     E.tensor_mul(o, i, j), reads=[bg1s, bs1c], writes=[bfo[par]])
                P.op("vector", lambda E, o=oo[par], i=fo[par], j=xe1[:, dt, :]:
                     E.tensor_add(o, i, j), reads=[bfo[par], bxe1], writes=[boo[par]])
                dst = t["out_fm"][b, dt * 128:(dt + 1) * 128, off:off + 512]
                P.op("sync", lambda E, o=dst, i=oo[par]: E.dma_start(out=o, in_=i),
                     reads=[boo[par]], writes=[c.dbuf["out_fm"]], inc=16)


def build_launch(phases):
    nc = bass.Bass("TRN2", target_bir_lowering=False, debug=False)
    t = declare_io(nc, phases)
    P = Prog()
    c = Ctx()
    c.nc, c.t, c.P = nc, t, P
    c._bufs = {}
    stack = contextlib.ExitStack()

    def SB(nm, shp, dt):
        return stack.enter_context(nc.sbuf_tensor(nm, shp, dt)).ap()

    def PS(nm, shp, dt=F32):
        return stack.enter_context(nc.psum_tensor(nm, shp, dt)).ap()

    c.dyn = Carver(SB("dyn", [128, DYN_BYTES // 4], F32))
    c.identf = SB("identf", [128, 128], F32)
    c.identb = SB("identb", [128, 128], BF16)
    c.ones_col = SB("ones_col", [128, 1], F32)
    c.ones_row = SB("ones_row", [1, 128], F32)
    c.epsA = SB("epsA", [128, 1], F32)
    if "A" in phases:
        c.gwT = SB("gwT_s", [128, 512], F32)
        c.gbias = SB("gbias_s", [128, 64], F32)
        c.ncol_row = SB("ncol_row", [1, 64], F32)
        c.ncol_c = SB("ncol_c", [128, 1], F32)
        c.ecap_col = SB("ecap_s", [64, 1], F32)
    if "B" in phases:
        c.wqkvb = SB("wqkvb", [128, 3 * 512], BF16)
    if ("D" in phases) or ("E" in phases):
        c.mur = SB("mur_s", [128, 8], F32)
        c.muk = SB("muk_s", [128, 8], F32)
        c.omur = SB("omur_s", [128, 8], F32)
        c.omuk = SB("omuk_s", [128, 8], F32)
    c.psum = [PS(f"pbk{i}", [128, 512], F32) for i in range(8)]
    c.pbuf = [Buf() for _ in range(8)]

    sems = {e: stack.enter_context(nc.semaphore("sem_" + e)) for e in ENGS}
    c.dbuf = {nm: Buf() for nm in t}

    emit_static(c, phases)
    if "A" in phases:
        emit_phase_a(c)
        P.barrier()
        emit_phase_a2(c)
    if "B" in phases:
        P.barrier()
        emit_phase_b(c)
    if "C" in phases:
        toks = [(e, P.cnt[e]) for e in ENGS if P.cnt[e] > 0]
        P.barrier(skip=("gpsimd", "sync"))
        emit_phase_c(c, entry_toks=toks, with_d_ln=(("D" in phases) and not globals().get("NO_DLN_ILV", False)))
    if "D" in phases:
        P.barrier()
        emit_phase_d(c)
    if "E" in phases:
        P.barrier()
        emit_phase_e(c)

    final = P.cnt["sync"]

    def mk(e, extra_wait=None):
        def body(engobj):
            water = {pe: 0 for pe in ENGS}
            for item in P.items[e]:
                if item[0] == "DRAIN":
                    v = item[1]
                    if v > water[e]:
                        engobj.wait_ge(sems[e], v)
                        water[e] = v
                    continue
                fd, fn, inc = item
                for pe, v in sorted(fd.items()):
                    if v > water[pe]:
                        engobj.wait_ge(sems[pe], v)
                        water[pe] = v
                inst = fn(engobj)
                inst.then_inc(sems[e], inc)
            if extra_wait is not None:
                engobj.wait_ge(sems[e], extra_wait)
        return body

    with nc.Block() as block:
        block.sync(mk("sync", extra_wait=final))
        block.gpsimd(mk("gpsimd"))
        block.tensor(mk("tensor"))
        block.vector(mk("vector"))
        block.scalar(mk("scalar"))
    stack.close()
    return nc


_NC_CACHE = {}


def _get_nc(phases):
    if phases not in _NC_CACHE:
        _NC_CACHE[phases] = build_launch(phases)
    return _NC_CACHE[phases]


def kernel(**inputs):
    h = host_prep(inputs)
    x_all = np.asarray(inputs["L0_attn_x"]).reshape(-1, D)
    per_core = []
    for core in range(NCORES):
        m = {"x_fm": np.ascontiguousarray(h["x_fm"][core * BPC:(core + 1) * BPC]),
             "x_tm": np.ascontiguousarray(x_all[core * TOK:(core + 1) * TOK])}
        per_core.append(m)

    common_ab = {k: h[k] for k in ["gwT", "gbias", "wqT", "wkT", "wvT", "ident", "ecap"]}
    common_c = {k: h[k] for k in ["moeblob", "ident"]}
    common_de = {k: h[k] for k in ["qwT", "combT", "wrT", "wk2T", "wv2T", "mur", "muk", "ident"]}

    if SPLIT_MODE:
        nc1 = _get_nc("AB")
        maps1 = [dict(common_ab, x_fm=per_core[i]["x_fm"]) for i in range(NCORES)]
        r1 = run_bass_kernel_spmd(nc1, maps1, core_ids=list(range(NCORES))).results

        nc2 = _get_nc("C")
        maps2 = [dict(common_c, x_fm=per_core[i]["x_fm"], x_tm=per_core[i]["x_tm"],
                      xn_h=np.asarray(r1[i]["xn_h"]), rwT_h=np.asarray(r1[i]["rwT_h"]),
                      sdup_h=np.asarray(r1[i]["sdup_h"]), xn_tm_h=np.asarray(r1[i]["xn_tm_h"]),
                      idx_h=np.asarray(r1[i]["idx_h"]), rw8_h=np.asarray(r1[i]["rw8_h"]))
                 for i in range(NCORES)]
        r2 = run_bass_kernel_spmd(nc2, maps2, core_ids=list(range(NCORES))).results

        nc3 = _get_nc("DE")
        maps3 = [dict(common_de, x1_h=r2[i]["x1_h"]) for i in range(NCORES)]
        r3 = run_bass_kernel_spmd(nc3, maps3, core_ids=list(range(NCORES))).results
        outs = [r3[i]["out_fm"] for i in range(NCORES)]
    else:
        nc = _get_nc("ABCDE")
        maps = [dict(common_ab) for _ in range(NCORES)]
        for i in range(NCORES):
            maps[i].update(common_c)
            maps[i].update(common_de)
            maps[i]["x_fm"] = per_core[i]["x_fm"]
            maps[i]["x_tm"] = per_core[i]["x_tm"]
        rr = run_bass_kernel_spmd(nc, maps, core_ids=list(range(NCORES))).results
        outs = [rr[i]["out_fm"] for i in range(NCORES)]

    out = np.concatenate([o.reshape(BPC, D, S) for o in outs], axis=0)
    return np.ascontiguousarray(out.transpose(0, 2, 1)).astype(np.float32)


# version 09
# speedup vs baseline: 1.8288x; 1.0792x over previous
import contextlib
import numpy as np
import ml_dtypes
import concourse.bass as bass
import concourse.mybir as mybir
from concourse.bass_utils import run_bass_kernel_spmd
from concourse.bass_core_rust import IndirectOffsetOnAxis

F32 = mybir.dt.float32
BF16 = mybir.dt.bfloat16
AF = mybir.ActivationFunctionType
ALU = mybir.AluOpType
AX = mybir.AxisListType
BF = ml_dtypes.bfloat16

NCORES = 8
S = 2048
D = 1024
BPC = 2
TOK = BPC * S
NE = 66
IPAD = 768
SEC = 8 * IPAD
BLOBW = 3 * SEC
CH = 512
NCH = TOK // CH
NGRP = 4
GTOK = TOK // NGRP
GCH = GTOK // CH
KB = 1024
DYN_BYTES = 200 * KB
CAP = 1536
NSLOT = 64 * CAP
TRASH = NSLOT
NROW_CX = NSLOT + 128
NROW_CO = NSLOT + 128
BIGF = 1.0e7

SPLIT_MODE = False
DEBUG_EXPORT = False

ENGS = ("sync", "gpsimd", "tensor", "vector", "scalar")


class Buf:
    __slots__ = ("w", "r", "small")

    def __init__(self, small=False):
        self.w = None
        self.r = []
        self.small = small


class Prog:
    def __init__(self):
        self.items = {e: [] for e in ENGS}
        self.cnt = {e: 0 for e in ENGS}
        self.pend = {e: [] for e in ENGS}
        self.drains = {e: [] for e in ENGS}

    def drain(self, q):
        if self.cnt[q] > 0 and (not self.drains[q] or self.drains[q][-1] != self.cnt[q]):
            self.items[q].append(("DRAIN", self.cnt[q], 0))
            self.drains[q].append(self.cnt[q])

    def barrier(self, skip=()):
        for q in ("sync", "gpsimd"):
            if q not in skip:
                self.drain(q)
        toks = [(e, self.cnt[e]) for e in ENGS if self.cnt[e] > 0]
        for e in ENGS:
            if e in skip:
                continue
            self.pend[e] = self.pend[e] + [(tk, True) for tk in toks]

    def op(self, e, fn, reads=(), writes=(), extra=(), inc=1):
        deps = []
        for b in reads:
            if b.w is not None:
                deps.append((b.w, b.small))
        for b in writes:
            for tk in b.r:
                deps.append((tk, b.small))
            if b.w is not None:
                deps.append((b.w, b.small))
        for tk in extra:
            deps.append((tk, True))
        deps += self.pend[e]
        self.pend[e] = []
        self.cnt[e] += inc
        tok = (e, self.cnt[e])
        fd = {}
        for (pe, v), force in deps:
            if pe == e and (not force or v >= tok[1]):
                continue
            if pe in ("sync", "gpsimd") and self.drains[pe]:
                import bisect
                i = bisect.bisect_left(self.drains[pe], v)
                if i < len(self.drains[pe]):
                    v = self.drains[pe][i]
            if v > fd.get(pe, 0):
                fd[pe] = v
        self.items[e].append((fd, fn, inc))
        for b in reads:
            b.r.append(tok)
        for b in writes:
            b.r = []
            b.w = tok
        return tok


def _tile_kxm(w):
    K, M = w.shape
    return np.ascontiguousarray(
        w.reshape(K // 128, 128, M).transpose(1, 0, 2).reshape(128, (K // 128) * M))


def host_prep(z):
    h = {}
    x = np.asarray(z["L0_attn_x"])
    h["x_fm"] = np.ascontiguousarray(x.transpose(0, 2, 1))

    for nm, key in [("wqT", "L0_attn_Wq"), ("wkT", "L0_attn_Wk"), ("wvT", "L0_attn_Wv"),
                    ("gwT", "L0_ffn_gate_w")]:
        h[nm] = _tile_kxm(np.ascontiguousarray(np.asarray(z[key]).T))
    h["gbias"] = np.ascontiguousarray(
        np.tile(np.asarray(z["L0_ffn_gate_bias"])[None, :], (128, 1)).astype(np.float32))

    blob = np.zeros((NE, 128, BLOBW), dtype=np.float32)
    Wg, Wu, Wd = np.asarray(z["L0_ffn_Wg"]), np.asarray(z["L0_ffn_Wu"]), np.asarray(z["L0_ffn_Wd"])
    gt = Wg.transpose(0, 2, 1).reshape(64, 8, 128, 704).transpose(0, 2, 1, 3)
    blob[:64, :, :SEC].reshape(64, 128, 8, IPAD)[:, :, :, :704] = gt
    ut = Wu.transpose(0, 2, 1).reshape(64, 8, 128, 704).transpose(0, 2, 1, 3)
    blob[:64, :, SEC:2 * SEC].reshape(64, 128, 8, IPAD)[:, :, :, :704] = ut
    dt_ = np.zeros((64, IPAD, 1024), dtype=np.float32)
    dt_[:, :704, :] = Wd.transpose(0, 2, 1)
    blob[:64, :, 2 * SEC:] = dt_.reshape(64, 6, 128, 1024).transpose(0, 2, 1, 3).reshape(64, 128, SEC)
    Sg, Su, Sd = np.asarray(z["L0_ffn_Sg"]), np.asarray(z["L0_ffn_Su"]), np.asarray(z["L0_ffn_Sd"])
    for hh in range(2):
        e = 64 + hh
        sg = Sg[hh * 704:(hh + 1) * 704].T
        su = Su[hh * 704:(hh + 1) * 704].T
        sd = Sd[:, hh * 704:(hh + 1) * 704].T
        blob[e, :, :SEC].reshape(128, 8, IPAD)[:, :, :704] = sg.reshape(8, 128, 704).transpose(1, 0, 2)
        blob[e, :, SEC:2 * SEC].reshape(128, 8, IPAD)[:, :, :704] = su.reshape(8, 128, 704).transpose(1, 0, 2)
        sdp = np.zeros((IPAD, 1024), dtype=np.float32)
        sdp[:704] = sd
        blob[e, :, 2 * SEC:] = sdp.reshape(6, 128, 1024).transpose(1, 0, 2).reshape(128, SEC)
    h["moeblob"] = blob

    basis = np.zeros((64, 64 * 64), dtype=BF)
    for e in range(64):
        basis[e, e * 64:(e + 1) * 64] = 1
    h["basis64"] = basis
    i2 = np.zeros((64, 128), dtype=BF)
    for k in range(64):
        i2[k, k] = 1
        i2[k, 64 + k] = 1
    h["i2t"] = i2
    h["ident"] = np.eye(128, dtype=np.float32)

    q = np.asarray(z["L1_attn_queries"])
    h["qwT"] = np.ascontiguousarray(
        q.reshape(4, 256, 4, 2, 128).transpose(4, 0, 2, 3, 1).reshape(128, 8192))
    cb = np.asarray(z["L1_attn_combiners"])
    h["combT"] = np.ascontiguousarray(
        cb.reshape(4, 8, 128, 256).transpose(2, 0, 1, 3).reshape(128, 8192))
    for nm, key in [("wrT", "L1_ffn_W_r"), ("wk2T", "L1_ffn_W_k"), ("wv2T", "L1_ffn_W_v")]:
        w = np.asarray(z[key])
        h[nm] = np.ascontiguousarray(
            w.reshape(8, 128, 8, 128).transpose(3, 2, 0, 1).reshape(128, 8192))
    h["ecap"] = (np.arange(64, dtype=np.float32) * CAP).reshape(64, 1)
    h["mur"] = np.ascontiguousarray(np.asarray(z["L1_ffn_mu_r"]).reshape(8, 128).T)
    h["muk"] = np.ascontiguousarray(np.asarray(z["L1_ffn_mu_k"]).reshape(8, 128).T)
    return h


class Carver:
    def __init__(self, ap_f32):
        self.f = ap_f32
        self.b = ap_f32.bitcast(BF16)

    def f32(self, off, n):
        return self.f[:, off // 4: off // 4 + n]

    def f32v(self, off, d0, d1):
        return self.f[:, off // 4: off // 4 + d0 * d1].rearrange("p (a b) -> p a b", a=d0)

    def bf16(self, off, n):
        return self.b[:, off // 2: off // 2 + n]

    def bf16v(self, off, d0, d1):
        return self.b[:, off // 2: off // 2 + d0 * d1].rearrange("p (a b) -> p a b", a=d0)


class Ctx:
    def B(self, name, small=False):
        if name not in self._bufs:
            self._bufs[name] = Buf(small=small)
        return self._bufs[name]


def declare_io(nc, phases):
    t = {}
    fused = not SPLIT_MODE

    def dram(name, shape, dtype, kind):
        t[name] = nc.dram_tensor(name, shape, dtype, kind=kind).ap()

    has = {p: (p in phases) for p in "ABCDE"}
    if has["A"] or has["C"]:
        dram("x_fm", [BPC, D, S], F32, "ExternalInput")
    if has["A"]:
        dram("gwT", [128, 512], F32, "ExternalInput")
        dram("gbias", [128, 64], F32, "ExternalInput")
    if has["B"]:
        dram("wqT", [128, 512], F32, "ExternalInput")
        dram("wkT", [128, 512], F32, "ExternalInput")
        dram("wvT", [128, 512], F32, "ExternalInput")
    dram("ident", [128, 128], F32, "ExternalInput")
    if has["C"]:
        dram("moeblob", [NE, 128, BLOBW], F32, "ExternalInput")
        dram("x_tm", [TOK, D], F32, "ExternalInput")
        dram("compact_x", [NROW_CX, D], BF16, "ExternalOutput")
        dram("compact_out", [NROW_CO, D], BF16, "ExternalOutput")
        dram("shared_h", [2 * TOK, D], BF16, "Internal")
    if has["A"]:
        dram("ecap", [64, 1], F32, "ExternalInput")
    if has["D"] or has["E"]:
        dram("qwT", [128, 8192], F32, "ExternalInput")
        dram("combT", [128, 8192], F32, "ExternalInput")
        dram("wrT", [128, 8192], F32, "ExternalInput")
        dram("wk2T", [128, 8192], F32, "ExternalInput")
        dram("wv2T", [128, 8192], F32, "ExternalInput")
        dram("mur", [128, 8], F32, "ExternalInput")
        dram("muk", [128, 8], F32, "ExternalInput")

    def bridge(name, shape, dtype, produced, consumed):
        if produced and consumed:
            dram(name, shape, dtype, "ExternalOutput" if DEBUG_EXPORT else "Internal")
        elif produced:
            dram(name, shape, dtype, "ExternalOutput")
        elif consumed:
            dram(name, shape, dtype, "ExternalInput")

    if (not SPLIT_MODE) or not has["A"]:
        bridge("xn_h", [128, 8, TOK], BF16, has["A"], has["B"] or has["C"])
        bridge("xn_tm_h", [TOK, D], BF16, has["A"], has["C"])
        bridge("idx_h", [TOK, 8], mybir.dt.int32, has["A"], has["C"])
        bridge("rw8_h", [TOK, 8], F32, has["A"], has["C"])
    else:
        dram("xn_h", [128, 8, TOK], BF16, "ExternalOutput")
        dram("xn_tm_h", [TOK, D], BF16, "ExternalOutput")
        dram("idx_h", [TOK, 8], mybir.dt.int32, "ExternalOutput")
        dram("rw8_h", [TOK, 8], F32, "ExternalOutput")
    bridge("rwT_h", [64, TOK], BF16, has["A"], has["C"])
    bridge("sdup_h", [128, TOK], F32, has["B"], has["C"])
    bridge("x1_h", [BPC, D, S], F32, has["C"], has["D"] or has["E"])
    if has["D"] or has["E"]:
        bridge("xn1_h", [128, 8, TOK], BF16, has["D"], has["E"])
    if has["E"]:
        dram("out_fm", [BPC, D, S], F32, "ExternalOutput")
    return t


def emit_static(c, phases):
    P, t = c.P, c.t
    c.bident = c.B("ident")
    c.bidentb = c.B("identb")
    c.bones = c.B("ones", small=True)
    c.beps = c.B("eps", small=True)
    P.op("sync", lambda E, o=c.identf, i=t["ident"]: E.dma_start(out=o, in_=i),
         writes=[c.bident], inc=16)
    P.op("scalar", lambda E, o=c.identb, i=c.identf: E.copy(o, i),
         reads=[c.bident], writes=[c.bidentb])
    P.op("vector", lambda E, o=c.ones_col: E.memset(o, 1.0), writes=[c.bones])
    P.op("vector", lambda E, o=c.ones_row: E.memset(o, 1.0), writes=[c.bones])
    P.op("vector", lambda E, o=c.epsA: E.memset(o, 1e-5), writes=[c.beps])

    if "A" in phases:
        c.bgw = c.B("gw")
        c.bgbias = c.B("gbias")
        c.bncol = c.B("ncol", small=True)
        P.op("sync", lambda E, o=c.gwT, i=t["gwT"]: E.dma_start(out=o, in_=i), writes=[c.bgw], inc=16)
        P.op("sync", lambda E, o=c.gbias, i=t["gbias"]: E.dma_start(out=o, in_=i), writes=[c.bgbias], inc=16)
        pcs = c.psum[0]
        bcs = c.pbuf[0]
        for et in range(8):
            P.op("tensor", lambda E, o=pcs[0:64, 0:1], l=c.gwT[:, et * 64:(et + 1) * 64],
                 r=c.ones_col, s=(et == 0), p=(et == 7): E.matmul(o, l, r, start=s, stop=p),
                 reads=[c.bgw, c.bones], writes=[bcs])
        bnc = c.B("ncol_c", small=True)
        P.op("vector", lambda E, o=c.ncol_c[0:64, 0:1], i=pcs[0:64, 0:1]:
             E.tensor_scalar(o, i, -1.0, None, ALU.mult), reads=[bcs], writes=[bnc])
        P.op("tensor", lambda E, o=c.psum[1][0:1, 0:64], i=c.ncol_c[0:64, 0:1],
             idn=c.identf[0:64, 0:64]: E.transpose(o, i, idn),
             reads=[bnc, c.bident], writes=[bcs])
        P.op("vector", lambda E, o=c.ncol_row, i=c.psum[1][0:1, 0:64]:
             E.tensor_copy(o, i), reads=[bcs], writes=[c.bncol])
        c.becap = c.B("ecap", small=True)
        P.op("sync", lambda E, o=c.ecap_col, i=t["ecap"]: E.dma_start(out=o, in_=i),
             writes=[c.becap], inc=16)

    if "B" in phases:
        c.bwqkv = c.B("wqkv")
        for i, nm in enumerate(["wqT", "wkT", "wvT"]):
            P.op("gpsimd", lambda E, o=c.wqkvb[:, i * 512:(i + 1) * 512], src=t[nm]:
                 E.dma_start(out=o, in_=src), writes=[c.bwqkv], inc=16)

    if ("D" in phases) or ("E" in phases):
        c.bmu = c.B("mu", small=True)
        P.op("sync", lambda E, o=c.mur, i=t["mur"]: E.dma_start(out=o, in_=i), writes=[c.bmu], inc=16)
        P.op("sync", lambda E, o=c.muk, i=t["muk"]: E.dma_start(out=o, in_=i), writes=[c.bmu], inc=16)
        P.op("vector", lambda E, o=c.omur, i=c.mur: E.tensor_scalar(o, i, -1.0, 1.0, ALU.mult, ALU.add),
             reads=[c.bmu], writes=[c.bmu])
        P.op("vector", lambda E, o=c.omuk, i=c.muk: E.tensor_scalar(o, i, -1.0, 1.0, ALU.mult, ALU.add),
             reads=[c.bmu], writes=[c.bmu])


LN_O = 72 * KB


def emit_ln_chunk(c, ci, src_dram, src_buf, dst_dram, dst_buf, with_routing, O=None):
    P, t, dyn = c.P, c.t, c.dyn
    b = ci // (S // CH)
    off = (ci % (S // CH)) * CH
    goff = ci * CH
    if O is None:
        O = LN_O

    xld = dyn.f32v(O, 8, 512)
    x2 = dyn.f32v(O + 16 * KB, 8, 512)
    tmp1 = dyn.f32(O + 32 * KB, 512)
    xnst = dyn.bf16v(O + 34 * KB, 8, 512)
    m_row = dyn.f32(O + 42 * KB, 512)[0:1, :]
    sd_row = dyn.f32(O + 44 * KB, 512)[0:1, :]
    rstd_row = dyn.f32(O + 46 * KB, 512)[0:1, :]
    mr_row = dyn.f32(O + 48 * KB, 512)[0:1, :]
    ex2_row = dyn.f32(O + 50 * KB, 512)[0:1, :]
    tmpn = dyn.f32(O + 52 * KB, 512)

    bxld = c.B("ln_xld")
    bx2 = c.B("ln_x2")
    bxnst = c.B("ln_xnst")
    bstat = c.B("ln_stat", small=True)
    btmpn = c.B("ln_tmpn")
    btmp1 = c.B("ln_tmp1", small=True)
    psx, psx2, pbm, pbr = c.psum[0], c.psum[1], c.psum[2], c.psum[3]
    bpsx, bpsx2, bpbm, bpbr = c.pbuf[0], c.pbuf[1], c.pbuf[2], c.pbuf[3]

    src = src_dram[b, :, off:off + CH].rearrange("(n p) m -> p n m", p=128)
    P.op("sync", lambda E, o=xld, i=src: E.dma_start(out=o, in_=i),
         reads=[src_buf], writes=[bxld], inc=16)
    P.drain("sync")
    P.op("scalar", lambda E, o=x2, i=xld: E.activation(o, i, AF.Square),
         reads=[bxld], writes=[bx2])
    for et in range(8):
        P.op("tensor", lambda E, o=psx[0:1, :], l=c.ones_col, r=xld[:, et, :],
             s=(et == 0), p=(et == 7): E.matmul(o, l, r, start=s, stop=p),
             reads=[c.bones, bxld], writes=[bpsx])
    for et in range(8):
        P.op("tensor", lambda E, o=psx2[0:1, :], l=c.ones_col, r=x2[:, et, :],
             s=(et == 0), p=(et == 7): E.matmul(o, l, r, start=s, stop=p),
             reads=[c.bones, bx2], writes=[bpsx2])
    P.op("vector", lambda E, o=m_row, i=psx[0:1, :]:
         E.tensor_scalar(o, i, 1.0 / D, None, ALU.mult), reads=[bpsx], writes=[bstat])
    P.op("vector", lambda E, o=ex2_row, i=psx2[0:1, :]:
         E.tensor_scalar(o, i, 1.0 / D, None, ALU.mult), reads=[bpsx2], writes=[bstat])
    P.op("vector", lambda E, o=tmp1[0:1, :], i=m_row, j=m_row:
         E.tensor_mul(o, i, j), reads=[bstat], writes=[btmp1])
    P.op("vector", lambda E, o=ex2_row, i=ex2_row, j=tmp1[0:1, :]:
         E.tensor_sub(o, i, j), reads=[bstat, btmp1], writes=[bstat])
    P.op("scalar", lambda E, o=sd_row, i=ex2_row, bi=c.epsA[0:1, 0:1]:
         E.activation(o, i, AF.Sqrt, bias=bi), reads=[bstat, c.beps], writes=[bstat])
    P.op("vector", lambda E, o=rstd_row, i=sd_row: E.reciprocal(o, i),
         reads=[bstat], writes=[bstat])
    if with_routing:
        P.op("vector", lambda E, o=mr_row, i=m_row, j=rstd_row:
             E.tensor_mul(o, i, j), reads=[bstat], writes=[bstat])
    P.op("tensor", lambda E, o=pbm[:, :], l=c.ones_row, r=m_row:
         E.matmul(o, l, r, start=True, stop=True), reads=[c.bones, bstat], writes=[bpbm])
    P.op("tensor", lambda E, o=pbr[:, :], l=c.ones_row, r=rstd_row:
         E.matmul(o, l, r, start=True, stop=True), reads=[c.bones, bstat], writes=[bpbr])
    for et in range(8):
        P.op("vector", lambda E, o=tmpn, i=xld[:, et, :], j=pbm[:, :]:
             E.tensor_sub(o, i, j), reads=[bxld, bpbm], writes=[btmpn])
        P.op("vector", lambda E, o=xnst[:, et, :], i=tmpn, j=pbr[:, :]:
             E.tensor_mul(o, i, j), reads=[btmpn, bpbr], writes=[bxnst])
    dst = dst_dram[:, :, goff:goff + CH]
    P.op("sync", lambda E, o=dst, i=xnst: E.dma_start(out=o, in_=i),
         reads=[bxnst], writes=[dst_buf], inc=16)

    if with_routing:
        xtm_st = dyn.bf16v(O + 62 * KB, 4, 1024)
        bxtmst = c.B("ln_xtmst")
        ptrxb = [c.psum[6].bitcast(BF16), c.psum[7].bitcast(BF16)]
        bptrx2 = [c.pbuf[6], c.pbuf[7]]
        for tt in range(4):
            bk = tt % 2
            for et in range(8):
                P.op("tensor", lambda E, o=ptrxb[bk][0:128, et * 128:(et + 1) * 128],
                     i=xnst[:, et, tt * 128:(tt + 1) * 128], idn=c.identb:
                     E.transpose(o, i, idn), reads=[bxnst, c.bidentb], writes=[bptrx2[bk]])
            P.op("vector", lambda E, o=xtm_st[:, tt, :], i=ptrxb[bk][0:128, 0:1024]:
                 E.tensor_copy(o, i), reads=[bptrx2[bk]], writes=[bxtmst])
        dst2 = c.t["xn_tm_h"][goff:goff + CH, :].rearrange("(tt p) cc -> p tt cc", p=128)
        P.op("sync", lambda E, o=dst2, i=xtm_st: E.dma_start(out=o, in_=i),
             reads=[bxtmst], writes=[c.dbuf["xn_tm_h"]], inc=16)

    if not with_routing:
        return

    plg, pcor = c.psum[4], c.psum[5]
    bplg, bpcor = c.pbuf[4], c.pbuf[5]
    for et in range(8):
        P.op("tensor", lambda E, o=plg[0:64, :], l=c.gwT[:, et * 64:(et + 1) * 64],
             r=xld[:, et, :], s=(et == 0), p=(et == 7): E.matmul(o, l, r, start=s, stop=p),
             reads=[c.bgw, bxld], writes=[bplg])
    P.op("tensor", lambda E, o=pcor[0:64, :], l=c.ncol_row[:, 0:64], r=mr_row:
         E.matmul(o, l, r, start=True, stop=True), reads=[c.bncol, bstat], writes=[bpcor])
    lg0 = dyn.f32(O + 54 * KB, 512)[0:64, :]
    lg1 = dyn.f32(O + 56 * KB, 512)[0:64, :]
    blg = c.B("ln_lg")
    P.op("scalar", lambda E, o=lg0, i=plg[0:64, :]: E.copy(o, i), reads=[bplg], writes=[blg])
    P.op("vector", lambda E, o=lg1, i=lg0, j=pbr[0:64, :]: E.tensor_mul(o, i, j),
         reads=[blg, bpbr], writes=[blg])
    P.op("vector", lambda E, o=lg1, i=lg1, j=pcor[0:64, :]: E.tensor_add(o, i, j),
         reads=[bpcor, blg], writes=[blg])
    ptr = c.psum[6]
    bptr = c.pbuf[6]
    lg_tm = dyn.f32v(O + 58 * KB, 4, 64)
    blgtm = c.B("ln_lgtm")
    for j in range(4):
        P.op("tensor", lambda E, o=ptr[0:128, 0:64], i=lg1[:, j * 128:(j + 1) * 128],
             idn=c.identf[0:64, 0:64]: E.transpose(o, i, idn),
             reads=[blg, c.bident], writes=[bptr])
        P.op("vector", lambda E, o=lg_tm[:, j, :], i=ptr[0:128, 0:64]:
             E.tensor_copy(o, i), reads=[bptr], writes=[blgtm])
    emit_routing(c, ci, lg_tm, blgtm, O + 71 * KB)


def emit_routing(c, ci, lg_tm, blgtm, OB):
    P, t, dyn = c.P, c.t, c.dyn
    goff = ci * CH

    scores = dyn.f32v(OB, 4, 64)
    sfc = dyn.f32v(OB + 1024, 4, 64)
    msk = dyn.f32v(OB + 2048, 4, 64)
    cur = dyn.f32v(OB + 3072, 4, 64)
    sel = dyn.f32v(OB + 4096, 4, 64)
    sel8 = dyn.f32v(OB + 5120, 4, 64)
    w_ = dyn.f32v(OB + 6144, 4, 64)
    rw = dyn.f32v(OB + 7168, 4, 64)
    m1 = dyn.f32v(OB + 8192, 4, 8)
    mtmp = dyn.f32v(OB + 8320, 4, 8)
    gs = dyn.f32v(OB + 8448, 4, 8)
    curg = dyn.f32v(OB + 8576, 4, 8)
    selg = dyn.f32v(OB + 8704, 4, 8)
    gmask = dyn.f32v(OB + 8832, 4, 8)
    mm_ = dyn.f32v(OB + 8960, 4, 1)
    den = dyn.f32v(OB + 9088, 4, 1)
    rcp = dyn.f32v(OB + 9216, 4, 1)
    rwTs = dyn.f32(OB + 9344, 512)[0:64, :]
    rwTb_st = dyn.bf16(OB + 11392, 512)[0:64, :]

    bS = c.B("route", small=True)

    def V(fn):
        P.op("vector", fn, reads=[bS], writes=[bS])

    P.op("scalar", lambda E, o=scores, i=lg_tm: E.activation(o, i, AF.Sigmoid),
         reads=[blgtm], writes=[bS])
    V(lambda E: E.tensor_add(sfc, scores, c.gbias.unsqueeze(1).broadcast_to([128, 4, 64])))
    V(lambda E: E.tensor_reduce(m1, sfc.rearrange("p a (g k) -> p a g k", k=8), AX.X, ALU.max))
    V(lambda E: E.tensor_tensor(msk.rearrange("p a (g k) -> p a g k", k=8),
                                sfc.rearrange("p a (g k) -> p a g k", k=8),
                                m1.unsqueeze(3).broadcast_to([128, 4, 8, 8]), ALU.is_ge))
    V(lambda E: E.scalar_tensor_tensor(msk, msk, -1e30, sfc, ALU.mult, ALU.add))
    V(lambda E: E.tensor_reduce(mtmp, msk.rearrange("p a (g k) -> p a g k", k=8), AX.X, ALU.max))
    V(lambda E: E.tensor_add(gs, m1, mtmp))
    V(lambda E: E.memset(gmask, 0.0))
    V(lambda E: E.tensor_copy(curg, gs))
    for _ in range(4):
        V(lambda E: E.tensor_reduce(mm_, curg, AX.X, ALU.max))
        V(lambda E: E.tensor_tensor(selg, curg, mm_.broadcast_to([128, 4, 8]), ALU.is_ge))
        V(lambda E: E.tensor_add(gmask, gmask, selg))
        V(lambda E: E.scalar_tensor_tensor(curg, selg, -1e30, curg, ALU.mult, ALU.add))
    V(lambda E: E.tensor_tensor(msk.rearrange("p a (g k) -> p a g k", k=8),
                                sfc.rearrange("p a (g k) -> p a g k", k=8),
                                gmask.unsqueeze(3).broadcast_to([128, 4, 8, 8]), ALU.mult))
    V(lambda E: E.memset(sel8, 0.0))
    V(lambda E: E.tensor_copy(cur, msk))
    for _ in range(8):
        V(lambda E: E.tensor_reduce(mm_, cur, AX.X, ALU.max))
        V(lambda E: E.tensor_tensor(sel, cur, mm_.broadcast_to([128, 4, 64]), ALU.is_ge))
        V(lambda E: E.tensor_add(sel8, sel8, sel))
        V(lambda E: E.scalar_tensor_tensor(cur, sel, -1e30, cur, ALU.mult, ALU.add))
    V(lambda E: E.tensor_mul(w_, sel8, scores))
    V(lambda E: E.tensor_reduce(den, w_, AX.X, ALU.add))
    V(lambda E: E.tensor_scalar(den, den, 1e-20, None, ALU.add))
    V(lambda E: E.reciprocal(rcp, den))
    V(lambda E: E.tensor_scalar(rcp, rcp, 2.5, None, ALU.mult))
    V(lambda E: E.tensor_tensor(rw, w_, rcp.broadcast_to([128, 4, 64]), ALU.mult))
    ptr2 = c.psum[7]
    bptr2 = c.pbuf[7]
    for j in range(4):
        P.op("tensor", lambda E, o=ptr2[0:64, 0:128], i=rw[:, j, :], idn=c.identf:
             E.transpose(o, i, idn), reads=[bS, c.bident], writes=[bptr2])
        P.op("vector", lambda E, o=rwTs[:, j * 128:(j + 1) * 128], i=ptr2[0:64, 0:128]:
             E.tensor_copy(o, i), reads=[bptr2], writes=[bS])
    V(lambda E: E.tensor_copy(rwTb_st, rwTs))
    P.op("sync", lambda E, o=t["rwT_h"][:, goff:goff + CH], i=rwTb_st:
         E.dma_start(out=o, in_=i), reads=[bS], writes=[c.dbuf["rwT_h"]], inc=16)


def emit_phase_a(c):
    for ci in range(NCH):
        emit_ln_chunk(c, ci, c.t["x_fm"], c.dbuf["x_fm"], c.t["xn_h"], c.dbuf["xn_h"], True)




def emit_phase_a2(c):
    P, t, dyn = c.P, c.t, c.dyn
    O = LN_O
    rwt_s = dyn.bf16(O, TOK)[0:64, :]
    mask = dyn.f32(O + 8 * KB, TOK)[0:64, :]
    cumA = dyn.f32(O + 24 * KB, TOK)[0:64, :]
    cumB = dyn.f32(O + 40 * KB, TOK)[0:64, :]
    tgt = dyn.f32(O + 56 * KB, TOK)[0:64, :]
    ovfm = dyn.f32(O + 40 * KB, TOK)[0:64, :]
    tgt_tm = dyn.f32v(O + 8 * KB, 32, 64)
    rw_tm = dyn.f32v(O + 16 * KB, 32, 64)
    cur = dyn.f32v(O + 24 * KB, 32, 64)
    eq = dyn.f32v(O + 32 * KB, 32, 64)
    tmp_e = dyn.f32v(O + 48 * KB, 32, 64)
    SM = O + 44 * KB
    mn = dyn.f32v(SM, 32, 1)
    gbv = dyn.f32v(SM + 256, 32, 1)
    tgt8 = dyn.f32v(SM + 512, 32, 8)
    rw8 = dyn.f32v(SM + 1536, 32, 8)
    idx_i = dyn.f.bitcast(mybir.dt.int32)[:, (SM + 2560) // 4:(SM + 2560) // 4 + 256].rearrange(
        "p (a b) -> p a b", a=32)

    bS = c.B("a2", small=True)

    def V(fn):
        P.op("vector", fn, reads=[bS], writes=[bS])

    P.op("sync", lambda E, o=rwt_s, i=t["rwT_h"]: E.dma_start(out=o, in_=i),
         reads=[c.dbuf["rwT_h"]], writes=[bS], inc=16)
    P.drain("sync")
    V(lambda E: E.tensor_single_scalar(mask, rwt_s, 0.0, ALU.is_gt))
    V(lambda E: E.tensor_copy(cumA, mask))
    srcb, dstb = cumA, cumB
    sh = 1
    while sh < TOK:
        V(lambda E, s_=srcb, d_=dstb, ss=sh: E.tensor_add(d_[:, ss:], s_[:, ss:], s_[:, :TOK - ss]))
        V(lambda E, s_=srcb, d_=dstb, ss=sh: E.tensor_copy(d_[:, 0:ss], s_[:, 0:ss]))
        srcb, dstb = dstb, srcb
        sh *= 2
    cum = srcb
    V(lambda E: E.tensor_sub(tgt, cum, mask))
    V(lambda E: E.tensor_single_scalar(ovfm, tgt, float(CAP), ALU.is_ge))
    V(lambda E: E.tensor_scalar(tgt, tgt, c.ecap_col[0:64, 0:1], None, ALU.add))
    V(lambda E: E.tensor_scalar(tgt, tgt, BIGF, None, ALU.add))
    V(lambda E: E.scalar_tensor_tensor(tgt, mask, -BIGF, tgt, ALU.mult, ALU.add))
    V(lambda E: E.scalar_tensor_tensor(tgt, ovfm, BIGF, tgt, ALU.mult, ALU.add))
    V(lambda E: E.tensor_scalar_min(tgt, tgt, float(TRASH)))
    ptrf = c.psum[6]
    ptrb2 = c.psum[7].bitcast(BF16)
    bptrf, bptrb = c.pbuf[6], c.pbuf[7]
    for j in range(32):
        P.op("tensor", lambda E, o=ptrf[0:128, 0:64], i=tgt[:, j * 128:(j + 1) * 128],
             idn=c.identf[0:64, 0:64]: E.transpose(o, i, idn),
             reads=[bS, c.bident], writes=[bptrf])
        P.op("vector", lambda E, o=tgt_tm[:, j, :], i=ptrf[0:128, 0:64]:
             E.tensor_copy(o, i), reads=[bptrf], writes=[bS])
        P.op("tensor", lambda E, o=ptrb2[0:128, 0:64], i=rwt_s[:, j * 128:(j + 1) * 128],
             idn=c.identb[0:64, 0:64]: E.transpose(o, i, idn),
             reads=[bS, c.bidentb], writes=[bptrb])
        P.op("vector", lambda E, o=rw_tm[:, j, :], i=ptrb2[0:128, 0:64]:
             E.tensor_copy(o, i), reads=[bptrb], writes=[bS])
    V(lambda E: E.tensor_copy(cur, tgt_tm))
    for j in range(8):
        V(lambda E: E.tensor_reduce(mn, cur, AX.X, ALU.min))
        V(lambda E, jj=j: E.tensor_copy(tgt8[:, :, jj:jj + 1], mn))
        V(lambda E: E.tensor_tensor(eq, cur, mn.broadcast_to([128, 32, 64]), ALU.is_le))
        V(lambda E: E.tensor_mul(tmp_e, eq, rw_tm))
        V(lambda E, jj=j: E.tensor_reduce(rw8[:, :, jj:jj + 1], tmp_e, AX.X, ALU.add))
        V(lambda E: E.tensor_single_scalar(gbv, mn, float(NSLOT), ALU.is_lt))
        V(lambda E, jj=j: E.tensor_tensor(rw8[:, :, jj:jj + 1], rw8[:, :, jj:jj + 1],
                                          gbv, ALU.mult))
        V(lambda E: E.scalar_tensor_tensor(cur, eq, BIGF, cur, ALU.mult, ALU.add))
    V(lambda E: E.tensor_copy(idx_i, tgt8))
    P.op("sync", lambda E, o=t["idx_h"].rearrange("(tt p) j -> p tt j", p=128), i=idx_i:
         E.dma_start(out=o, in_=i), reads=[bS], writes=[c.dbuf["idx_h"]], inc=16)
    P.op("sync", lambda E, o=t["rw8_h"].rearrange("(tt p) j -> p tt j", p=128), i=rw8:
         E.dma_start(out=o, in_=i), reads=[bS], writes=[c.dbuf["rw8_h"]], inc=16)


def emit_phase_b(c):
    P, t, dyn = c.P, c.t, c.dyn
    O = LN_O
    xnb = dyn.bf16v(O, 8, S)
    scf = dyn.f32(O + 32 * KB, S)
    attn_t = [dyn.bf16(O + 40 * KB + i * 4 * KB, S) for i in range(2)]
    qTb = dyn.bf16(O + 48 * KB, S)[0:64, :]
    kTb = dyn.bf16(O + 52 * KB, S)[0:64, :]
    vTb = dyn.bf16(O + 56 * KB, S)[0:64, :]
    v_tm = dyn.bf16(O + 60 * KB, 16 * 128)
    v_sc = [dyn.bf16(O + 64 * KB + i * 256, 128) for i in range(2)]
    sdst = dyn.f32(O + 66 * KB, S)
    nmx = dyn.f32(O + 74 * KB, 16)[:, 0:1]
    rs = dyn.f32(O + 74 * KB + 64, 16)[:, 0:1]
    rcq = dyn.f32(O + 74 * KB + 128, 16)[:, 0:1]

    pq = c.psum[0]
    ptr = c.psum[1]
    ptrb = ptr.bitcast(BF16)
    psc = [c.psum[2], c.psum[3]]
    pout = [c.psum[4 + i] for i in range(4)]

    bxnb = c.B("b_xnb")
    bq, bk, bv = c.B("b_q"), c.B("b_k"), c.B("b_v")
    bpq = c.pbuf[0]
    bvtm = c.B("b_vtm")
    bptr = c.pbuf[1]
    bscf = c.B("b_scf")
    bnmx = c.B("b_nmx", small=True)
    brs = c.B("b_rs", small=True)
    batt = [c.B("b_att0"), c.B("b_att1")]
    bvsc = [c.B("b_vsc0"), c.B("b_vsc1")]
    bpsc = [c.pbuf[2], c.pbuf[3]]
    bpout = c.pbuf[4]
    bpout2 = [c.pbuf[4], c.pbuf[5], c.pbuf[6], c.pbuf[7]]
    bsd = c.B("b_sd")

    for b in range(BPC):
        P.op("sync", lambda E, o=xnb, i=t["xn_h"][:, :, b * S:(b + 1) * S]:
             E.dma_start(out=o, in_=i), reads=[c.dbuf["xn_h"]], writes=[bxnb], inc=16)
        P.drain("sync")
        for wi, (dst, bd) in enumerate([(qTb, bq), (kTb, bk), (vTb, bv)]):
            for ic in range(4):
                for et in range(8):
                    P.op("tensor", lambda E, o=pq[0:64, :],
                         l=c.wqkvb[:, wi * 512 + et * 64: wi * 512 + (et + 1) * 64],
                         r=xnb[:, et, ic * 512:(ic + 1) * 512],
                         s=(et == 0), p=(et == 7): E.matmul(o, l, r, start=s, stop=p),
                         reads=[c.bwqkv, bxnb], writes=[bpq])
                P.op("scalar", lambda E, o=dst[:, ic * 512:(ic + 1) * 512], i=pq[0:64, :]:
                     E.copy(o, i), reads=[bpq], writes=[bd])
        for kt in range(16):
            P.op("tensor", lambda E, o=ptrb[0:128, 0:64], i=vTb[:, kt * 128:(kt + 1) * 128],
                 idn=c.identb[0:64, 0:64]: E.transpose(o, i, idn),
                 reads=[bv, c.bidentb], writes=[bptr])
            P.op("vector", lambda E, o=v_tm[:, kt * 128:kt * 128 + 64], i=ptrb[0:128, 0:64]:
                 E.tensor_copy(o, i), reads=[bptr], writes=[bvtm])
            P.op("vector", lambda E, o=v_tm[:, kt * 128 + 64:(kt + 1) * 128], i=ptrb[0:128, 0:64]:
                 E.tensor_copy(o, i), reads=[bptr], writes=[bvtm])
        for kt in range(16):
            par = kt % 2
            for ic in range(4):
                P.op("tensor", lambda E, o=psc[ic % 2][:, :], l=kTb[:, kt * 128:(kt + 1) * 128],
                     r=qTb[:, ic * 512:(ic + 1) * 512]: E.matmul(o, l, r, start=True, stop=True),
                     reads=[bk, bq], writes=[bpsc[ic % 2]])
                P.op("scalar", lambda E, o=scf[:, ic * 512:(ic + 1) * 512], i=psc[ic % 2][:, :]:
                     E.copy(o, i), reads=[bpsc[ic % 2]], writes=[bscf])
            P.op("vector", lambda E, o=nmx, i=scf:
                 E.reduce_max(o, i, axis=AX.X, negate=True), reads=[bscf], writes=[bnmx])
            P.op("scalar", lambda E, o=attn_t[par], i=scf, bi=nmx, ac=rs:
                 E.activation(o, i, AF.Exp, bias=bi, accum_out=ac),
                 reads=[bscf, bnmx], writes=[batt[par], brs])
            P.op("vector", lambda E, o=rcq, i=rs: E.reciprocal(o, i),
                 reads=[brs], writes=[brs])
            P.op("vector", lambda E, o=v_sc[par], i=v_tm[:, kt * 128:(kt + 1) * 128], sc=rcq:
                 E.tensor_scalar(o, i, sc, None, ALU.mult),
                 reads=[bvtm, brs], writes=[bvsc[par]])
            for ic in range(4):
                P.op("tensor", lambda E, o=pout[ic][:, :], l=v_sc[par],
                     r=attn_t[par][:, ic * 512:(ic + 1) * 512],
                     s=(kt == 0), p=(kt == 15): E.matmul(o, l, r, start=s, stop=p),
                     reads=[bvsc[par], batt[par]], writes=[bpout2[ic]])
        for ic in range(4):
            P.op("scalar", lambda E, o=sdst[:, ic * 512:(ic + 1) * 512], i=pout[ic][:, :]:
                 E.copy(o, i), reads=[bpout2[ic]], writes=[bsd])
        P.op("sync", lambda E, o=t["sdup_h"][:, b * S:(b + 1) * S], i=sdst:
             E.dma_start(out=o, in_=i), reads=[bsd], writes=[c.dbuf["sdup_h"]], inc=16)


def emit_phase_c(c, entry_toks=(), with_d_ln=False):
    P, t, dyn = c.P, c.t, c.dyn
    wbuf = [dyn.bf16(0 + i * 36864, BLOBW) for i in range(2)]
    cx_tm = [dyn.bf16v(72 * KB, 4, 1024), dyn.bf16v(80 * KB, 4, 1024),
             dyn.bf16v(160 * KB, 4, 1024)]
    rh = [dyn.bf16v(88 * KB, 8, 512), dyn.bf16v(96 * KB, 8, 512),
          dyn.bf16v(168 * KB, 8, 512)]
    hb = [dyn.bf16(104 * KB + i * 6 * KB, 6 * 512) for i in range(2)]
    sg = [dyn.f32(116 * KB + i * 2 * KB, 512) for i in range(2)]
    co_st = [dyn.bf16v(120 * KB + i * 8 * KB, 4, 1024) for i in range(2)]
    sc_tile = [dyn.bf16(152 * KB + i * 2 * KB, 1024) for i in range(2)]
    idx_all = dyn.f.bitcast(mybir.dt.int32)[:, (192 * KB) // 4:(192 * KB) // 4 + 256].rearrange(
        "p (a b) -> p a b", a=32)
    rw8_all = dyn.f32v(193 * KB, 32, 8)
    sdup_s = dyn.f32(122 * KB, TOK)
    Gt = [dyn.bf16(72 * KB + i * 2 * KB, 1024) for i in range(4)]
    acc = [dyn.f32(80 * KB + i * 4 * KB, 1024) for i in range(2)]
    gsb = [dyn.f32(88 * KB + i * 4 * KB, 1024) for i in range(2)]
    x_tile = [dyn.f32(96 * KB + i * 4 * KB, 1024) for i in range(2)]
    x1f_st = [dyn.f32v(104 * KB + i * 4 * KB, 8, 128) for i in range(2)]
    s_small = [dyn.f32(112 * KB + i * 512, 64) for i in range(2)]
    sht = [dyn.bf16(113 * KB + i * 2 * KB, 1024) for i in range(4)]
    idxt2 = [dyn.f.bitcast(mybir.dt.int32)[:, (121 * KB + i * 64) // 4:(121 * KB + i * 64) // 4 + 8]
             for i in range(2)]
    rw8t = [dyn.f32(121 * KB + 512 + i * 512, 8) for i in range(2)]

    bwbuf = [c.B("c_wb0"), c.B("c_wb1")]
    bcx = [c.B("c_cx0"), c.B("c_cx1"), c.B("c_cx2")]
    brh = [c.B("c_rh0"), c.B("c_rh1"), c.B("c_rh2")]
    bhb = [c.B("c_hb0"), c.B("c_hb1")]
    bsg = [c.B("c_sg0"), c.B("c_sg1")]
    bco = [c.B("c_co0"), c.B("c_co1")]
    bsc = [c.B("c_sc0"), c.B("c_sc1")]
    bidx_all = c.B("c_idxall")
    brw8_all = c.B("c_rw8all")
    bsdup = c.B("c_sdup")
    bG = [c.B(f"c_G{i}") for i in range(4)]
    bacc = [c.B("c_acc0"), c.B("c_acc1")]
    bgs = [c.B("c_gs0"), c.B("c_gs1")]
    bxt = [c.B("c_xt0"), c.B("c_xt1")]
    bx1f = [c.B("c_x1f0"), c.B("c_x1f1")]
    bss = [c.B("c_ss0"), c.B("c_ss1")]
    bsh = [c.B(f"c_sh{i}") for i in range(4)]
    bidxt2 = [c.B("c_idxt2a"), c.B("c_idxt2b")]
    brw8t = [c.B("c_rw8t0"), c.B("c_rw8t1")]

    pag = [c.psum[0], c.psum[1]]
    pau = [c.psum[2], c.psum[3]]
    pdn = [c.psum[4], c.psum[5]]
    ptr_ = c.psum[6]
    ptrb_ = ptr_.bitcast(BF16)
    ptr2_ = c.psum[7]
    bpag = [c.pbuf[0], c.pbuf[1]]
    bpau = [c.pbuf[2], c.pbuf[3]]
    bpdn = [c.pbuf[4], c.pbuf[5]]
    bptr = c.pbuf[6]
    bptr2 = c.pbuf[7]

    worder = [64, 65] + list(range(64))
    wseq = {e: i for i, e in enumerate(worder)}

    def wdma(e):
        P.op("gpsimd", lambda E, o=wbuf[wseq[e] % 2], i=t["moeblob"][e]:
             E.dma_start(out=o, in_=i), writes=[bwbuf[wseq[e] % 2]], inc=16)

    wdma(64)
    wdma(65)

    P.op("sync", lambda E, o=idx_all, i=t["idx_h"].rearrange("(tt p) j -> p tt j", p=128):
         E.dma_start(out=o, in_=i), reads=[c.dbuf["idx_h"]], writes=[bidx_all], inc=16)
    P.op("sync", lambda E, o=rw8_all, i=t["rw8_h"].rearrange("(tt p) j -> p tt j", p=128):
         E.dma_start(out=o, in_=i), reads=[c.dbuf["rw8_h"]], writes=[brw8_all], inc=16)
    P.drain("sync")
    for tt in range(32):
        par = tt % 2
        P.op("sync", lambda E, o=sc_tile[par],
             i=t["xn_tm_h"][tt * 128:(tt + 1) * 128, :]:
             E.dma_start(out=o, in_=i), reads=[c.dbuf["xn_tm_h"]], writes=[bsc[par]], inc=16)
        for j in range(8):
            P.op("gpsimd", lambda E, o=t["compact_x"], off=idx_all[:, tt, j:j + 1], i=sc_tile[par]:
                 E.indirect_dma_start(out=o, out_offset=IndirectOffsetOnAxis(off, 0),
                                      in_=i, in_offset=None),
                 reads=[bsc[par], bidx_all], writes=[c.dbuf["compact_x"]], inc=16)

    P.drain("gpsimd")

    slots = [("s", 64 + hh, c8) for hh in range(2) for c8 in range(8)] + \
            [("e", e, ch) for e in range(64) for ch in range(3)]

    first_load = [True]

    def emit_load(k):
        kind, e, ch = slots[k]
        par = k % 3
        ext = list(entry_toks) if first_load[0] else ()
        first_load[0] = False
        if kind == "e":
            base = e * CAP + ch * 512
            P.op("sync", lambda E, o=cx_tm[par],
                 i=t["compact_x"][base:base + 512, :].rearrange("(tt p) cc -> p tt cc", p=128):
                 E.dma_start(out=o, in_=i),
                 reads=[c.dbuf["compact_x"]], writes=[bcx[par]], inc=16, extra=ext)
        else:
            goff = ch * 512
            P.op("sync", lambda E, o=rh[par], i=t["xn_h"][:, :, goff:goff + 512]:
                 E.dma_start(out=o, in_=i), reads=[c.dbuf["xn_h"]], writes=[brh[par]],
                 inc=16, extra=ext)

    ptrbanks = [ptr_.bitcast(BF16), ptr2_.bitcast(BF16)]
    bptrx = [c.pbuf[6], c.pbuf[7]]

    def emit_tr(k):
        kind, e, ch = slots[k]
        if kind != "e":
            return
        par = k % 3
        for tt in range(4):
            bk = tt % 2
            pb = ptrbanks[bk]
            for ct in range(8):
                P.op("tensor", lambda E, o=pb[0:128, ct * 128:(ct + 1) * 128],
                     i=cx_tm[par][:, tt, ct * 128:(ct + 1) * 128], idn=c.identb:
                     E.transpose(o, i, idn), reads=[bcx[par], c.bidentb],
                     writes=[bptrx[bk]])
            P.op("vector", lambda E, o=rh[par][:, :, tt * 128:(tt + 1) * 128],
                 i=pb[0:128, 0:1024].rearrange("p (a b) -> p a b", a=8):
                 E.tensor_copy(o, i), reads=[bptrx[bk]], writes=[brh[par]])

    def emit_up(k):
        kind, e, ch = slots[k]
        par = k % 3
        wpar = wseq[e] % 2
        for it in range(6):
            ap_ = it % 2
            for et in range(8):
                P.op("tensor", lambda E, o=pag[ap_][:, :],
                     l=wbuf[wpar][:, et * IPAD + it * 128: et * IPAD + (it + 1) * 128],
                     r=rh[par][:, et, :], s=(et == 0), p=(et == 7):
                     E.matmul(o, l, r, start=s, stop=p),
                     reads=[bwbuf[wpar], brh[par]], writes=[bpag[ap_]])
            for et in range(8):
                P.op("tensor", lambda E, o=pau[ap_][:, :],
                     l=wbuf[wpar][:, SEC + et * IPAD + it * 128: SEC + et * IPAD + (it + 1) * 128],
                     r=rh[par][:, et, :], s=(et == 0), p=(et == 7):
                     E.matmul(o, l, r, start=s, stop=p),
                     reads=[bwbuf[wpar], brh[par]], writes=[bpau[ap_]])
            P.op("scalar", lambda E, o=sg[ap_], i=pag[ap_][:, :]:
                 E.activation(o, i, AF.Silu), reads=[bpag[ap_]], writes=[bsg[ap_]])
            P.op("vector", lambda E, o=hb[k % 2][:, it * 512:(it + 1) * 512], i=sg[ap_],
                 j=pau[ap_][:, :]: E.tensor_mul(o, i, j),
                 reads=[bsg[ap_], bpau[ap_]], writes=[bhb[k % 2]])

    def emit_down(k):
        kind, e, ch = slots[k]
        par = k % 2
        wpar = wseq[e] % 2
        for mt in range(4):
            dp = mt % 2
            for nh in range(2):
                for it in range(6):
                    P.op("tensor", lambda E, o=pdn[dp][:, :],
                         l=hb[par][:, it * 512 + mt * 128: it * 512 + (mt + 1) * 128],
                         r=wbuf[wpar][:, 2 * SEC + it * 1024 + nh * 512: 2 * SEC + it * 1024 + (nh + 1) * 512],
                         s=(it == 0), p=(it == 5): E.matmul(o, l, r, start=s, stop=p),
                         reads=[bwbuf[wpar], bhb[par]], writes=[bpdn[dp]])
                P.op("scalar", lambda E, o=co_st[par][:, mt, nh * 512:(nh + 1) * 512],
                     i=pdn[dp][:, :]: E.copy(o, i), reads=[bpdn[dp]], writes=[bco[par]])
        if kind == "e":
            base = e * CAP + ch * 512
            dst = t["compact_out"][base:base + 512, :].rearrange("(tt p) cc -> p tt cc", p=128)
            dbf = c.dbuf["compact_out"]
        else:
            hh = e - 64
            base = hh * TOK + ch * 512
            dst = t["shared_h"][base:base + 512, :].rearrange("(tt p) cc -> p tt cc", p=128)
            dbf = c.dbuf["shared_h"]
        P.op("sync", lambda E, o=dst, i=co_st[par]: E.dma_start(out=o, in_=i),
             reads=[bco[par]], writes=[dbf], inc=16)

    emit_load(0)
    emit_load(1)
    emit_tr(0)
    for k in range(len(slots)):
        if k + 2 < len(slots):
            emit_load(k + 2)
        emit_up(k)
        if k + 1 < len(slots):
            emit_tr(k + 1)
        if k > 0:
            emit_down(k - 1)
            kindp, ep, chp = slots[k - 1]
            last_ch = 2 if kindp == "e" else 7
            if chp == last_ch and wseq[ep] + 2 < NE:
                wdma(worder[wseq[ep] + 2])
    emit_down(len(slots) - 1)
    c.d_ln_done = False

    P.barrier()
    bAS = c.B("c_asm", small=False)
    P.op("sync", lambda E, o=sdup_s, i=t["sdup_h"]: E.dma_start(out=o, in_=i),
         reads=[c.dbuf["sdup_h"]], writes=[bsdup], inc=16)
    P.drain("sync")

    for tt in range(32):
        gp = tt % 2
        goff = tt * 128
        b = goff // S
        off = goff % S
        P.op("sync", lambda E, o=x_tile[gp], i=t["x_tm"][goff:goff + 128, :]:
             E.dma_start(out=o, in_=i), writes=[bxt[gp]], inc=16)
        P.op("tensor", lambda E, o=[ptr_, ptr2_][gp][0:128, 0:64],
             i=sdup_s[0:64, goff:goff + 128],
             idn=c.identf[0:64, 0:64]: E.transpose(o, i, idn),
             reads=[bsdup, c.bident], writes=[bptrx[gp]])
        P.op("vector", lambda E, o=s_small[gp], i=[ptr_, ptr2_][gp][0:128, 0:64]:
             E.tensor_copy(o, i), reads=[bptrx[gp]], writes=[bss[gp]])
        for hh in range(2):
            sj = hh + 2 * gp
            P.op("sync", lambda E, o=sht[sj],
                 i=t["shared_h"][hh * TOK + goff: hh * TOK + goff + 128, :]:
                 E.dma_start(out=o, in_=i), reads=[c.dbuf["shared_h"]], writes=[bsh[sj]], inc=16)
        P.drain("sync")
        for jp in range(4):
            for j in (2 * jp, 2 * jp + 1):
                gj = (j % 2) + 2 * gp
                P.op("gpsimd", lambda E, o=Gt[gj], off=idx_all[:, tt, j:j + 1], i=t["compact_out"]:
                     E.indirect_dma_start(out=o, out_offset=None, in_=i,
                                          in_offset=IndirectOffsetOnAxis(off, 0)),
                     reads=[c.dbuf["compact_out"], bidx_all], writes=[bG[gj]], inc=16)
            P.drain("gpsimd")
            for j in (2 * jp, 2 * jp + 1):
                gj = (j % 2) + 2 * gp
                if j == 0:
                    P.op("vector", lambda E, o=acc[gp], i=Gt[gj], sc=rw8_all[:, tt, 0:1]:
                         E.tensor_scalar(o, i, sc, None, ALU.mult),
                         reads=[bG[gj], brw8_all], writes=[bacc[gp]])
                else:
                    a_in = acc[gp] if j % 2 == 1 else gsb[gp]
                    a_out = gsb[gp] if j % 2 == 1 else acc[gp]
                    P.op("vector", lambda E, o=a_out, i=Gt[gj], sc=rw8_all[:, tt, j:j + 1], jj=a_in:
                         E.scalar_tensor_tensor(o, i, sc, jj, ALU.mult, ALU.add),
                         reads=[bG[gj], brw8_all, bacc[gp], bgs[gp]],
                         writes=[bacc[gp], bgs[gp]])
        for hh in range(2):
            sj = hh + 2 * gp
            a_in = gsb[gp] if hh == 0 else acc[gp]
            a_out = acc[gp] if hh == 0 else gsb[gp]
            P.op("vector", lambda E, o=a_out, i=a_in, j=sht[sj]: E.tensor_add(o, i, j),
                 reads=[bsh[sj], bacc[gp], bgs[gp]], writes=[bacc[gp], bgs[gp]])
        P.op("scalar", lambda E, o=acc[gp], i=gsb[gp]: E.activation(o, i, AF.Silu),
             reads=[bgs[gp]], writes=[bacc[gp]])
        P.op("vector", lambda E, o=gsb[gp].rearrange("p (r d) -> p r d", r=16),
             i=acc[gp].rearrange("p (r d) -> p r d", r=16),
             j=s_small[gp].unsqueeze(1).broadcast_to([128, 16, 64]): E.tensor_mul(o, i, j),
             reads=[bss[gp], bacc[gp]], writes=[bgs[gp]])
        P.op("vector", lambda E, o=acc[gp], i=gsb[gp], j=x_tile[gp]: E.tensor_add(o, i, j),
             reads=[bxt[gp], bgs[gp]], writes=[bacc[gp]])
        for half in range(2):
            bk = (gp + half) % 2
            pbf = [ptr_, ptr2_][bk]
            for ct4 in range(4):
                ct = half * 4 + ct4
                P.op("tensor", lambda E, o=pbf[0:128, ct4 * 128:(ct4 + 1) * 128],
                     i=acc[gp][:, ct * 128:(ct + 1) * 128], idn=c.identf:
                     E.transpose(o, i, idn), reads=[bacc[gp], c.bident], writes=[bptrx[bk]])
            P.op("scalar", lambda E, o=x1f_st[gp][:, half * 4:(half + 1) * 4, :],
                 i=pbf[0:128, 0:512].rearrange("p (a b) -> p a b", a=4):
                 E.copy(o, i), reads=[bptrx[bk]], writes=[bx1f[gp]])
        dst = t["x1_h"][b, :, off:off + 128].rearrange("(n p) m -> p n m", p=128)
        P.op("sync", lambda E, o=dst, i=x1f_st[gp]: E.dma_start(out=o, in_=i),
             reads=[bx1f[gp]], writes=[c.dbuf["x1_h"]], inc=16)
        if with_d_ln and (tt + 1) % 4 == 0:
            emit_ln_chunk(c, tt // 4, t["x1_h"], c.dbuf["x1_h"], t["xn1_h"],
                          c.dbuf["xn1_h"], False, O=138 * KB)
            c.d_ln_done = True


def emit_phase_d(c):
    P, t, dyn = c.P, c.t, c.dyn
    c.qwTb = dyn.bf16(0, 8192)
    c.PaB = dyn.bf16(16 * KB, 2 * 8192)
    c.combTb = dyn.bf16(48 * KB, 8192)
    c.wrkvb = dyn.bf16(152 * KB, 3 * 8192)
    c.bqw, c.bPa, c.bcomb, c.bwrkv = c.B("d_qw"), c.B("d_Pa"), c.B("d_comb"), c.B("d_wrkv")
    P.op("gpsimd", lambda E, o=c.qwTb, i=t["qwT"]: E.dma_start(out=o, in_=i),
         writes=[c.bqw], inc=16)
    P.op("gpsimd", lambda E, o=c.combTb, i=t["combT"]: E.dma_start(out=o, in_=i),
         writes=[c.bcomb], inc=16)

    if getattr(c, "d_ln_done", False):
        pass
    else:
        for ci in range(NCH):
            emit_ln_chunk(c, ci, c.t["x1_h"], c.dbuf["x1_h"], c.t["xn1_h"], c.dbuf["xn1_h"], False)
    c.P.barrier()

    for i, nm in enumerate(["wrT", "wk2T", "wv2T"]):
        P.op("gpsimd", lambda E, o=c.wrkvb[:, i * 8192:(i + 1) * 8192], src=t[nm]:
             E.dma_start(out=o, in_=src), writes=[c.bwrkv], inc=16)

    MqB = dyn.bf16(64 * KB, 4 * 512)
    xnb = dyn.bf16v(68 * KB, 8, S)
    xtm = dyn.bf16(100 * KB, 16 * 1024)
    bxnb, bxtm, bMq = c.B("d_xnb"), c.B("d_xtm"), c.B("d_Mq")
    ptr = c.psum[0]
    ptrb = ptr.bitcast(BF16)
    bptr = c.pbuf[0]
    pmq = [c.psum[1 + q] for q in range(4)]
    bpmq = [c.pbuf[1 + q] for q in range(4)]
    ppp = [c.psum[5], c.psum[6]]
    bppp = [c.pbuf[5], c.pbuf[6]]

    for b in range(BPC):
        P.op("sync", lambda E, o=xnb, i=t["xn1_h"][:, :, b * S:(b + 1) * S]:
             E.dma_start(out=o, in_=i), reads=[c.dbuf["xn1_h"]], writes=[bxnb], inc=16)
        P.drain("sync")
        for ut in range(16):
            for ft in range(8):
                P.op("tensor", lambda E, o=ptrb[0:128, 0:128],
                     i=xnb[:, ft, ut * 128:(ut + 1) * 128], idn=c.identb:
                     E.transpose(o, i, idn), reads=[bxnb, c.bidentb], writes=[bptr])
                P.op("vector", lambda E,
                     o=xtm[:, ut * 1024 + ft * 128: ut * 1024 + (ft + 1) * 128],
                     i=ptrb[0:128, 0:128]: E.tensor_copy(o, i), reads=[bptr], writes=[bxtm])
        for fh in range(2):
            for ut in range(16):
                for q in range(4):
                    P.op("tensor", lambda E, o=pmq[q][:, fh * 256:(fh + 1) * 256],
                         l=xtm[:, ut * 1024 + q * 256 + fh * 128: ut * 1024 + q * 256 + (fh + 1) * 128],
                         r=xtm[:, ut * 1024 + q * 256: ut * 1024 + (q + 1) * 256],
                         s=(ut == 0), p=(ut == 15): E.matmul(o, l, r, start=s, stop=p),
                         reads=[bxtm], writes=[bpmq[q]])
            for q in range(4):
                P.op("scalar", lambda E, o=MqB[:, q * 512 + fh * 256: q * 512 + (fh + 1) * 256],
                     i=pmq[q][:, fh * 256:(fh + 1) * 256]:
                     E.copy(o, i), reads=[bpmq[q]], writes=[bMq])
        for a in range(4):
            for q in range(4):
                for eh in range(2):
                    pp = ppp[eh]
                    for fh in range(2):
                        P.op("tensor", lambda E, o=pp[:, 0:256],
                             l=c.qwTb[:, ((a * 4 + q) * 2 + fh) * 256 + eh * 128:
                                      ((a * 4 + q) * 2 + fh) * 256 + (eh + 1) * 128],
                             r=MqB[:, q * 512 + fh * 256: q * 512 + (fh + 1) * 256],
                             s=(fh == 0), p=(fh == 1): E.matmul(o, l, r, start=s, stop=p),
                             reads=[c.bqw, bMq], writes=[bppp[eh]])
                    dst = c.PaB[:, b * 8192 + a * 2048 + eh * 1024 + q * 256:
                                b * 8192 + a * 2048 + eh * 1024 + (q + 1) * 256]
                    P.op("scalar", lambda E, o=dst, i=ppp[eh][:, 0:256]: E.copy(o, i),
                         reads=[bppp[eh]], writes=[c.bPa])


def emit_phase_e(c):
    P, t, dyn = c.P, c.t, c.dyn
    rsb = dyn.f32(0, 8 * 512)
    kb = dyn.bf16(64 * KB, 8 * 512)
    attn_a = dyn.bf16(72 * KB, 8 * 512)
    s1c = dyn.f32(80 * KB, 8 * 512)
    tmpk = dyn.f32(96 * KB, 512)
    fo = [dyn.f32(98 * KB + i * 2 * KB, 512) for i in range(2)]
    oo = [dyn.f32(102 * KB + i * 2 * KB, 512) for i in range(2)]
    t1r = dyn.f32(106 * KB, 512)
    g1s = dyn.f32(108 * KB, 512)
    xe = dyn.bf16v(110 * KB, 8, 513)
    xe1 = dyn.f32v(119 * KB, 8, 512)
    xr = dyn.bf16(135 * KB, 8 * 512)
    xk = dyn.bf16(143 * KB, 8 * 512)

    brsb = c.B("e_rsb")
    bxe = c.B("e_xe")
    bxe1 = c.B("e_xe1")
    bxr, bxk, bkb = c.B("e_xr"), c.B("e_xk"), c.B("e_kb")
    batn = c.B("e_attn")
    bs1c = c.B("e_s1c")
    btmpk = c.B("e_tmpk")
    bfo = [c.B("e_fo0"), c.B("e_fo1")]
    bg1s = c.B("e_g1s")
    boo = [c.B("e_oo0"), c.B("e_oo1")]
    bt1r = c.B("e_t1r")

    pa1 = [c.psum[0], c.psum[1]]
    po = [c.psum[2], c.psum[3]]
    pe_ = [c.psum[4], c.psum[5]]
    bpa1 = [c.pbuf[0], c.pbuf[1]]
    bpo = [c.pbuf[2], c.pbuf[3]]
    bpe = [c.pbuf[4], c.pbuf[5]]

    for b in range(BPC):
        for cc in range(4):
            off = cc * 512
            if cc == 0:
                P.op("vector", lambda E, o=xe[:, :, 0:1]: E.memset(o, 0.0), writes=[bxe])
                P.op("sync", lambda E, o=xe[:, :, 1:513],
                     i=t["xn1_h"][:, :, b * S: b * S + 512]:
                     E.dma_start(out=o, in_=i), reads=[c.dbuf["xn1_h"]], writes=[bxe], inc=16)
            else:
                P.op("sync", lambda E, o=xe[:, :, 0:513],
                     i=t["xn1_h"][:, :, b * S + off - 1: b * S + off + 512]:
                     E.dma_start(out=o, in_=i), reads=[c.dbuf["xn1_h"]], writes=[bxe], inc=16)
            src1 = t["x1_h"][b, :, off:off + 512].rearrange("(n p) m -> p n m", p=128)
            P.op("sync", lambda E, o=xe1, i=src1: E.dma_start(out=o, in_=i),
                 reads=[c.dbuf["x1_h"]], writes=[bxe1], inc=16)
            P.drain("sync")

            for a in range(4):
                for qgt in range(8):
                    pp = pa1[qgt % 2]
                    for eh in range(2):
                        P.op("tensor", lambda E, o=pp[:, :],
                             l=c.PaB[:, b * 8192 + a * 2048 + eh * 1024 + qgt * 128:
                                     b * 8192 + a * 2048 + eh * 1024 + (qgt + 1) * 128],
                             r=xe[:, 2 * a + eh, 1:513],
                             s=(eh == 0), p=(eh == 1): E.matmul(o, l, r, start=s, stop=p),
                             reads=[c.bPa, bxe], writes=[bpa1[qgt % 2]])
                    P.op("scalar", lambda E, o=attn_a[:, qgt * 512:(qgt + 1) * 512],
                         i=pa1[qgt % 2][:, :]: E.copy(o, i),
                         reads=[bpa1[qgt % 2]], writes=[batn])
                for gt in range(2):
                    pp = po[gt % 2]
                    for qgt in range(8):
                        P.op("tensor", lambda E, o=pp[:, :],
                             l=c.combTb[:, (a * 8 + qgt) * 256 + gt * 128:
                                        (a * 8 + qgt) * 256 + (gt + 1) * 128],
                             r=attn_a[:, qgt * 512:(qgt + 1) * 512],
                             s=(qgt == 0), p=(qgt == 7): E.matmul(o, l, r, start=s, stop=p),
                             reads=[c.bcomb, batn], writes=[bpo[gt % 2]])
                    P.op("scalar", lambda E, o=s1c[:, (2 * a + gt) * 512:(2 * a + gt + 1) * 512],
                         i=po[gt % 2][:, :]: E.copy(o, i), reads=[bpo[gt % 2]], writes=[bs1c])

            for ft in range(8):
                P.op("vector", lambda E, o=t1r, i=xe[:, ft, 1:513], sc=c.mur[:, ft:ft + 1]:
                     E.tensor_scalar(o, i, sc, None, ALU.mult),
                     reads=[bxe, c.bmu], writes=[bt1r])
                P.op("vector", lambda E, o=xr[:, ft * 512:(ft + 1) * 512], i=xe[:, ft, 0:512],
                     sc=c.omur[:, ft:ft + 1], j=t1r:
                     E.scalar_tensor_tensor(o, i, sc, j, ALU.mult, ALU.add),
                     reads=[bxe, c.bmu, bt1r], writes=[bxr])
                P.op("vector", lambda E, o=t1r, i=xe[:, ft, 1:513], sc=c.muk[:, ft:ft + 1]:
                     E.tensor_scalar(o, i, sc, None, ALU.mult),
                     reads=[bxe, c.bmu], writes=[bt1r])
                P.op("vector", lambda E, o=xk[:, ft * 512:(ft + 1) * 512], i=xe[:, ft, 0:512],
                     sc=c.omuk[:, ft:ft + 1], j=t1r:
                     E.scalar_tensor_tensor(o, i, sc, j, ALU.mult, ALU.add),
                     reads=[bxe, c.bmu, bt1r], writes=[bxk])
            for dt in range(8):
                pp = pe_[dt % 2]
                for ct in range(8):
                    P.op("tensor", lambda E, o=pp[:, :],
                         l=c.wrkvb[:, 0 * 8192 + ct * 1024 + dt * 128: 0 * 8192 + ct * 1024 + (dt + 1) * 128],
                         r=xr[:, ct * 512:(ct + 1) * 512],
                         s=(ct == 0), p=(ct == 7): E.matmul(o, l, r, start=s, stop=p),
                         reads=[c.bwrkv, bxr], writes=[bpe[dt % 2]])
                P.op("scalar", lambda E, o=rsb[:, dt * 512:(dt + 1) * 512], i=pe_[dt % 2][:, :]:
                     E.activation(o, i, AF.Sigmoid), reads=[bpe[dt % 2]], writes=[brsb])
            for dt in range(8):
                pp = pe_[dt % 2]
                for ct in range(8):
                    P.op("tensor", lambda E, o=pp[:, :],
                         l=c.wrkvb[:, 1 * 8192 + ct * 1024 + dt * 128: 1 * 8192 + ct * 1024 + (dt + 1) * 128],
                         r=xk[:, ct * 512:(ct + 1) * 512],
                         s=(ct == 0), p=(ct == 7): E.matmul(o, l, r, start=s, stop=p),
                         reads=[c.bwrkv, bxk], writes=[bpe[dt % 2]])
                P.op("vector", lambda E, o=tmpk, i=pe_[dt % 2][:, :]:
                     E.tensor_scalar_max(o, i, 0.0), reads=[bpe[dt % 2]], writes=[btmpk])
                P.op("scalar", lambda E, o=kb[:, dt * 512:(dt + 1) * 512], i=tmpk:
                     E.activation(o, i, AF.Square), reads=[btmpk], writes=[bkb])
            for dt in range(8):
                pp = pe_[dt % 2]
                par = dt % 2
                for ct in range(8):
                    P.op("tensor", lambda E, o=pp[:, :],
                         l=c.wrkvb[:, 2 * 8192 + ct * 1024 + dt * 128: 2 * 8192 + ct * 1024 + (dt + 1) * 128],
                         r=kb[:, ct * 512:(ct + 1) * 512],
                         s=(ct == 0), p=(ct == 7): E.matmul(o, l, r, start=s, stop=p),
                         reads=[c.bwrkv, bkb], writes=[bpe[dt % 2]])
                P.op("vector", lambda E, o=fo[par], i=rsb[:, dt * 512:(dt + 1) * 512],
                     j=pe_[dt % 2][:, :]: E.tensor_mul(o, i, j),
                     reads=[brsb, bpe[dt % 2]], writes=[bfo[par]])
                P.op("scalar", lambda E, o=g1s, i=fo[par]: E.activation(o, i, AF.Silu),
                     reads=[bfo[par]], writes=[bg1s])
                P.op("vector", lambda E, o=fo[par], i=g1s, j=s1c[:, dt * 512:(dt + 1) * 512]:
                     E.tensor_mul(o, i, j), reads=[bg1s, bs1c], writes=[bfo[par]])
                P.op("vector", lambda E, o=oo[par], i=fo[par], j=xe1[:, dt, :]:
                     E.tensor_add(o, i, j), reads=[bfo[par], bxe1], writes=[boo[par]])
                dst = t["out_fm"][b, dt * 128:(dt + 1) * 128, off:off + 512]
                P.op("sync", lambda E, o=dst, i=oo[par]: E.dma_start(out=o, in_=i),
                     reads=[boo[par]], writes=[c.dbuf["out_fm"]], inc=16)


def build_launch(phases):
    nc = bass.Bass("TRN2", target_bir_lowering=False, debug=False)
    t = declare_io(nc, phases)
    P = Prog()
    c = Ctx()
    c.nc, c.t, c.P = nc, t, P
    c._bufs = {}
    stack = contextlib.ExitStack()

    def SB(nm, shp, dt):
        return stack.enter_context(nc.sbuf_tensor(nm, shp, dt)).ap()

    def PS(nm, shp, dt=F32):
        return stack.enter_context(nc.psum_tensor(nm, shp, dt)).ap()

    c.dyn = Carver(SB("dyn", [128, DYN_BYTES // 4], F32))
    c.identf = SB("identf", [128, 128], F32)
    c.identb = SB("identb", [128, 128], BF16)
    c.ones_col = SB("ones_col", [128, 1], F32)
    c.ones_row = SB("ones_row", [1, 128], F32)
    c.epsA = SB("epsA", [128, 1], F32)
    if "A" in phases:
        c.gwT = SB("gwT_s", [128, 512], F32)
        c.gbias = SB("gbias_s", [128, 64], F32)
        c.ncol_row = SB("ncol_row", [1, 64], F32)
        c.ncol_c = SB("ncol_c", [128, 1], F32)
        c.ecap_col = SB("ecap_s", [64, 1], F32)
    if "B" in phases:
        c.wqkvb = SB("wqkvb", [128, 3 * 512], BF16)
    if ("D" in phases) or ("E" in phases):
        c.mur = SB("mur_s", [128, 8], F32)
        c.muk = SB("muk_s", [128, 8], F32)
        c.omur = SB("omur_s", [128, 8], F32)
        c.omuk = SB("omuk_s", [128, 8], F32)
    c.psum = [PS(f"pbk{i}", [128, 512], F32) for i in range(8)]
    c.pbuf = [Buf() for _ in range(8)]

    sems = {e: stack.enter_context(nc.semaphore("sem_" + e)) for e in ENGS}
    c.dbuf = {nm: Buf() for nm in t}

    emit_static(c, phases)
    if "A" in phases:
        emit_phase_a(c)
        P.barrier()
        emit_phase_a2(c)
    if "B" in phases:
        P.barrier()
        emit_phase_b(c)
    if "C" in phases:
        toks = [(e, P.cnt[e]) for e in ENGS if P.cnt[e] > 0]
        P.barrier(skip=("gpsimd", "sync"))
        emit_phase_c(c, entry_toks=toks, with_d_ln=(("D" in phases) and not globals().get("NO_DLN_ILV", False)))
    if "D" in phases:
        P.barrier()
        emit_phase_d(c)
    if "E" in phases:
        P.barrier()
        emit_phase_e(c)

    final = P.cnt["sync"]

    def mk(e, extra_wait=None):
        def body(engobj):
            water = {pe: 0 for pe in ENGS}
            for item in P.items[e]:
                if item[0] == "DRAIN":
                    v = item[1]
                    if v > water[e]:
                        engobj.wait_ge(sems[e], v)
                        water[e] = v
                    continue
                fd, fn, inc = item
                for pe, v in sorted(fd.items()):
                    if v > water[pe]:
                        engobj.wait_ge(sems[pe], v)
                        water[pe] = v
                inst = fn(engobj)
                inst.then_inc(sems[e], inc)
            if extra_wait is not None:
                engobj.wait_ge(sems[e], extra_wait)
        return body

    with nc.Block() as block:
        block.sync(mk("sync", extra_wait=final))
        block.gpsimd(mk("gpsimd"))
        block.tensor(mk("tensor"))
        block.vector(mk("vector"))
        block.scalar(mk("scalar"))
    stack.close()
    return nc


_NC_CACHE = {}


def _get_nc(phases):
    if phases not in _NC_CACHE:
        _NC_CACHE[phases] = build_launch(phases)
    return _NC_CACHE[phases]


def kernel(**inputs):
    h = host_prep(inputs)
    x_all = np.asarray(inputs["L0_attn_x"]).reshape(-1, D)
    per_core = []
    for core in range(NCORES):
        m = {"x_fm": np.ascontiguousarray(h["x_fm"][core * BPC:(core + 1) * BPC]),
             "x_tm": np.ascontiguousarray(x_all[core * TOK:(core + 1) * TOK])}
        per_core.append(m)

    common_ab = {k: h[k] for k in ["gwT", "gbias", "wqT", "wkT", "wvT", "ident", "ecap"]}
    common_c = {k: h[k] for k in ["moeblob", "ident"]}
    common_de = {k: h[k] for k in ["qwT", "combT", "wrT", "wk2T", "wv2T", "mur", "muk", "ident"]}

    if SPLIT_MODE:
        nc1 = _get_nc("AB")
        maps1 = [dict(common_ab, x_fm=per_core[i]["x_fm"]) for i in range(NCORES)]
        r1 = run_bass_kernel_spmd(nc1, maps1, core_ids=list(range(NCORES))).results

        nc2 = _get_nc("C")
        maps2 = [dict(common_c, x_fm=per_core[i]["x_fm"], x_tm=per_core[i]["x_tm"],
                      xn_h=np.asarray(r1[i]["xn_h"]), rwT_h=np.asarray(r1[i]["rwT_h"]),
                      sdup_h=np.asarray(r1[i]["sdup_h"]), xn_tm_h=np.asarray(r1[i]["xn_tm_h"]),
                      idx_h=np.asarray(r1[i]["idx_h"]), rw8_h=np.asarray(r1[i]["rw8_h"]))
                 for i in range(NCORES)]
        r2 = run_bass_kernel_spmd(nc2, maps2, core_ids=list(range(NCORES))).results

        nc3 = _get_nc("DE")
        maps3 = [dict(common_de, x1_h=r2[i]["x1_h"]) for i in range(NCORES)]
        r3 = run_bass_kernel_spmd(nc3, maps3, core_ids=list(range(NCORES))).results
        outs = [r3[i]["out_fm"] for i in range(NCORES)]
    else:
        nc = _get_nc("ABCDE")
        maps = [dict(common_ab) for _ in range(NCORES)]
        for i in range(NCORES):
            maps[i].update(common_c)
            maps[i].update(common_de)
            maps[i]["x_fm"] = per_core[i]["x_fm"]
            maps[i]["x_tm"] = per_core[i]["x_tm"]
        rr = run_bass_kernel_spmd(nc, maps, core_ids=list(range(NCORES))).results
        outs = [rr[i]["out_fm"] for i in range(NCORES)]

    out = np.concatenate([o.reshape(BPC, D, S) for o in outs], axis=0)
    return np.ascontiguousarray(out.transpose(0, 2, 1)).astype(np.float32)


# version 10
# speedup vs baseline: 1.8599x; 1.0170x over previous
import contextlib
import numpy as np
import ml_dtypes
import concourse.bass as bass
import concourse.mybir as mybir
from concourse.bass_utils import run_bass_kernel_spmd
from concourse.bass_core_rust import IndirectOffsetOnAxis

F32 = mybir.dt.float32
BF16 = mybir.dt.bfloat16
AF = mybir.ActivationFunctionType
ALU = mybir.AluOpType
AX = mybir.AxisListType
BF = ml_dtypes.bfloat16

NCORES = 8
S = 2048
D = 1024
BPC = 2
TOK = BPC * S
NE = 66
IPAD = 768
SEC = 8 * IPAD
BLOBW = 3 * SEC
CH = 512
NCH = TOK // CH
NGRP = 4
GTOK = TOK // NGRP
GCH = GTOK // CH
KB = 1024
DYN_BYTES = 200 * KB
CAP = 1536
NSLOT = 64 * CAP
TRASH = NSLOT
NROW_CX = NSLOT + 128
NROW_CO = NSLOT + 128
BIGF = 1.0e7

SPLIT_MODE = False
DEBUG_EXPORT = False

ENGS = ("sync", "gpsimd", "tensor", "vector", "scalar")


class Buf:
    __slots__ = ("w", "r", "small")

    def __init__(self, small=False):
        self.w = None
        self.r = []
        self.small = small


class Prog:
    def __init__(self):
        self.items = {e: [] for e in ENGS}
        self.cnt = {e: 0 for e in ENGS}
        self.pend = {e: [] for e in ENGS}
        self.drains = {e: [] for e in ENGS}

    def drain(self, q):
        if self.cnt[q] > 0 and (not self.drains[q] or self.drains[q][-1] != self.cnt[q]):
            self.items[q].append(("DRAIN", self.cnt[q], 0))
            self.drains[q].append(self.cnt[q])

    def barrier(self, skip=()):
        for q in ("sync", "gpsimd"):
            if q not in skip:
                self.drain(q)
        toks = [(e, self.cnt[e]) for e in ENGS if self.cnt[e] > 0]
        for e in ENGS:
            if e in skip:
                continue
            self.pend[e] = self.pend[e] + [(tk, True) for tk in toks]

    def op(self, e, fn, reads=(), writes=(), extra=(), inc=1):
        deps = []
        for b in reads:
            if b.w is not None:
                deps.append((b.w, b.small))
        for b in writes:
            for tk in b.r:
                deps.append((tk, b.small))
            if b.w is not None:
                deps.append((b.w, b.small))
        for tk in extra:
            deps.append((tk, True))
        deps += self.pend[e]
        self.pend[e] = []
        self.cnt[e] += inc
        tok = (e, self.cnt[e])
        fd = {}
        for (pe, v), force in deps:
            if pe == e and (not force or v >= tok[1]):
                continue
            if pe in ("sync", "gpsimd") and self.drains[pe]:
                import bisect
                i = bisect.bisect_left(self.drains[pe], v)
                if i < len(self.drains[pe]):
                    v = self.drains[pe][i]
            if v > fd.get(pe, 0):
                fd[pe] = v
        self.items[e].append((fd, fn, inc))
        for b in reads:
            b.r.append(tok)
        for b in writes:
            b.r = []
            b.w = tok
        return tok


def _tile_kxm(w):
    K, M = w.shape
    return np.ascontiguousarray(
        w.reshape(K // 128, 128, M).transpose(1, 0, 2).reshape(128, (K // 128) * M))


def host_prep(z):
    h = {}
    x = np.asarray(z["L0_attn_x"])
    h["x_fm"] = np.ascontiguousarray(x.transpose(0, 2, 1))

    for nm, key in [("wqT", "L0_attn_Wq"), ("wkT", "L0_attn_Wk"), ("wvT", "L0_attn_Wv"),
                    ("gwT", "L0_ffn_gate_w")]:
        h[nm] = _tile_kxm(np.ascontiguousarray(np.asarray(z[key]).T))
    h["gbias"] = np.ascontiguousarray(
        np.tile(np.asarray(z["L0_ffn_gate_bias"])[None, :], (128, 1)).astype(np.float32))

    blob = np.zeros((NE, 128, BLOBW), dtype=np.float32)
    Wg, Wu, Wd = np.asarray(z["L0_ffn_Wg"]), np.asarray(z["L0_ffn_Wu"]), np.asarray(z["L0_ffn_Wd"])
    gt = Wg.transpose(0, 2, 1).reshape(64, 8, 128, 704).transpose(0, 2, 1, 3)
    blob[:64, :, :SEC].reshape(64, 128, 8, IPAD)[:, :, :, :704] = gt
    ut = Wu.transpose(0, 2, 1).reshape(64, 8, 128, 704).transpose(0, 2, 1, 3)
    blob[:64, :, SEC:2 * SEC].reshape(64, 128, 8, IPAD)[:, :, :, :704] = ut
    dt_ = np.zeros((64, IPAD, 1024), dtype=np.float32)
    dt_[:, :704, :] = Wd.transpose(0, 2, 1)
    blob[:64, :, 2 * SEC:] = dt_.reshape(64, 6, 128, 1024).transpose(0, 2, 1, 3).reshape(64, 128, SEC)
    Sg, Su, Sd = np.asarray(z["L0_ffn_Sg"]), np.asarray(z["L0_ffn_Su"]), np.asarray(z["L0_ffn_Sd"])
    for hh in range(2):
        e = 64 + hh
        sg = Sg[hh * 704:(hh + 1) * 704].T
        su = Su[hh * 704:(hh + 1) * 704].T
        sd = Sd[:, hh * 704:(hh + 1) * 704].T
        blob[e, :, :SEC].reshape(128, 8, IPAD)[:, :, :704] = sg.reshape(8, 128, 704).transpose(1, 0, 2)
        blob[e, :, SEC:2 * SEC].reshape(128, 8, IPAD)[:, :, :704] = su.reshape(8, 128, 704).transpose(1, 0, 2)
        sdp = np.zeros((IPAD, 1024), dtype=np.float32)
        sdp[:704] = sd
        blob[e, :, 2 * SEC:] = sdp.reshape(6, 128, 1024).transpose(1, 0, 2).reshape(128, SEC)
    h["moeblob"] = blob

    basis = np.zeros((64, 64 * 64), dtype=BF)
    for e in range(64):
        basis[e, e * 64:(e + 1) * 64] = 1
    h["basis64"] = basis
    i2 = np.zeros((64, 128), dtype=BF)
    for k in range(64):
        i2[k, k] = 1
        i2[k, 64 + k] = 1
    h["i2t"] = i2
    h["ident"] = np.eye(128, dtype=np.float32)

    q = np.asarray(z["L1_attn_queries"])
    h["qwT"] = np.ascontiguousarray(
        q.reshape(4, 256, 4, 2, 128).transpose(4, 0, 2, 3, 1).reshape(128, 8192))
    cb = np.asarray(z["L1_attn_combiners"])
    h["combT"] = np.ascontiguousarray(
        cb.reshape(4, 8, 128, 256).transpose(2, 0, 1, 3).reshape(128, 8192))
    for nm, key in [("wrT", "L1_ffn_W_r"), ("wk2T", "L1_ffn_W_k"), ("wv2T", "L1_ffn_W_v")]:
        w = np.asarray(z[key])
        h[nm] = np.ascontiguousarray(
            w.reshape(8, 128, 8, 128).transpose(3, 2, 0, 1).reshape(128, 8192))
    h["ecap"] = (np.arange(64, dtype=np.float32) * CAP).reshape(64, 1)
    h["mur"] = np.ascontiguousarray(np.asarray(z["L1_ffn_mu_r"]).reshape(8, 128).T)
    h["muk"] = np.ascontiguousarray(np.asarray(z["L1_ffn_mu_k"]).reshape(8, 128).T)
    return h


class Carver:
    def __init__(self, ap_f32):
        self.f = ap_f32
        self.b = ap_f32.bitcast(BF16)

    def f32(self, off, n):
        return self.f[:, off // 4: off // 4 + n]

    def f32v(self, off, d0, d1):
        return self.f[:, off // 4: off // 4 + d0 * d1].rearrange("p (a b) -> p a b", a=d0)

    def bf16(self, off, n):
        return self.b[:, off // 2: off // 2 + n]

    def bf16v(self, off, d0, d1):
        return self.b[:, off // 2: off // 2 + d0 * d1].rearrange("p (a b) -> p a b", a=d0)


class Ctx:
    def B(self, name, small=False):
        if name not in self._bufs:
            self._bufs[name] = Buf(small=small)
        return self._bufs[name]


def declare_io(nc, phases):
    t = {}
    fused = not SPLIT_MODE

    def dram(name, shape, dtype, kind):
        t[name] = nc.dram_tensor(name, shape, dtype, kind=kind).ap()

    has = {p: (p in phases) for p in "ABCDE"}
    if has["A"] or has["C"]:
        dram("x_fm", [BPC, D, S], F32, "ExternalInput")
    if has["A"]:
        dram("gwT", [128, 512], F32, "ExternalInput")
        dram("gbias", [128, 64], F32, "ExternalInput")
    if has["B"]:
        dram("wqT", [128, 512], F32, "ExternalInput")
        dram("wkT", [128, 512], F32, "ExternalInput")
        dram("wvT", [128, 512], F32, "ExternalInput")
    dram("ident", [128, 128], F32, "ExternalInput")
    if has["C"]:
        dram("moeblob", [NE, 128, BLOBW], F32, "ExternalInput")
        dram("x_tm", [TOK, D], F32, "ExternalInput")
        dram("compact_x", [NROW_CX, D], BF16, "ExternalOutput")
        dram("compact_out", [NROW_CO, D], BF16, "ExternalOutput")
        dram("shared_h", [2 * TOK, D], BF16, "Internal")
    if has["A"]:
        dram("ecap", [64, 1], F32, "ExternalInput")
    if has["D"] or has["E"]:
        dram("qwT", [128, 8192], F32, "ExternalInput")
        dram("combT", [128, 8192], F32, "ExternalInput")
        dram("wrT", [128, 8192], F32, "ExternalInput")
        dram("wk2T", [128, 8192], F32, "ExternalInput")
        dram("wv2T", [128, 8192], F32, "ExternalInput")
        dram("mur", [128, 8], F32, "ExternalInput")
        dram("muk", [128, 8], F32, "ExternalInput")

    def bridge(name, shape, dtype, produced, consumed):
        if produced and consumed:
            dram(name, shape, dtype, "ExternalOutput" if DEBUG_EXPORT else "Internal")
        elif produced:
            dram(name, shape, dtype, "ExternalOutput")
        elif consumed:
            dram(name, shape, dtype, "ExternalInput")

    if has["A"]:
        dram("xn_h", [128, 8, TOK], BF16, "ExternalOutput")
        dram("xn_tm_h", [TOK, D], BF16, "ExternalOutput")
        dram("idx_h", [TOK, 8], mybir.dt.int32, "ExternalOutput")
        dram("rw8_h", [TOK, 8], F32, "ExternalOutput")
    else:
        bridge("xn_h", [128, 8, TOK], BF16, False, has["B"] or has["C"])
        bridge("xn_tm_h", [TOK, D], BF16, False, has["C"])
        bridge("idx_h", [TOK, 8], mybir.dt.int32, False, has["C"])
        bridge("rw8_h", [TOK, 8], F32, False, has["C"])
    bridge("rwT_h", [64, TOK], BF16, has["A"], has["C"])
    bridge("sdup_h", [128, TOK], F32, has["B"], has["C"])
    bridge("x1_h", [BPC, D, S], F32, has["C"], has["D"] or has["E"])
    if has["D"] or has["E"]:
        bridge("xn1_h", [128, 8, TOK], BF16, has["D"], has["E"])
    if has["E"]:
        dram("out_fm", [BPC, D, S], F32, "ExternalOutput")
    return t


def emit_static(c, phases):
    P, t = c.P, c.t
    c.bident = c.B("ident")
    c.bidentb = c.B("identb")
    c.bones = c.B("ones", small=True)
    c.beps = c.B("eps", small=True)
    P.op("sync", lambda E, o=c.identf, i=t["ident"]: E.dma_start(out=o, in_=i),
         writes=[c.bident], inc=16)
    P.op("scalar", lambda E, o=c.identb, i=c.identf: E.copy(o, i),
         reads=[c.bident], writes=[c.bidentb])
    P.op("vector", lambda E, o=c.ones_col: E.memset(o, 1.0), writes=[c.bones])
    P.op("vector", lambda E, o=c.ones_row: E.memset(o, 1.0), writes=[c.bones])
    P.op("vector", lambda E, o=c.epsA: E.memset(o, 1e-5), writes=[c.beps])

    if "A" in phases:
        c.bgw = c.B("gw")
        c.bgbias = c.B("gbias")
        c.bncol = c.B("ncol", small=True)
        P.op("sync", lambda E, o=c.gwT, i=t["gwT"]: E.dma_start(out=o, in_=i), writes=[c.bgw], inc=16)
        P.op("sync", lambda E, o=c.gbias, i=t["gbias"]: E.dma_start(out=o, in_=i), writes=[c.bgbias], inc=16)
        pcs = c.psum[0]
        bcs = c.pbuf[0]
        for et in range(8):
            P.op("tensor", lambda E, o=pcs[0:64, 0:1], l=c.gwT[:, et * 64:(et + 1) * 64],
                 r=c.ones_col, s=(et == 0), p=(et == 7): E.matmul(o, l, r, start=s, stop=p),
                 reads=[c.bgw, c.bones], writes=[bcs])
        bnc = c.B("ncol_c", small=True)
        P.op("vector", lambda E, o=c.ncol_c[0:64, 0:1], i=pcs[0:64, 0:1]:
             E.tensor_scalar(o, i, -1.0, None, ALU.mult), reads=[bcs], writes=[bnc])
        P.op("tensor", lambda E, o=c.psum[1][0:1, 0:64], i=c.ncol_c[0:64, 0:1],
             idn=c.identf[0:64, 0:64]: E.transpose(o, i, idn),
             reads=[bnc, c.bident], writes=[bcs])
        P.op("vector", lambda E, o=c.ncol_row, i=c.psum[1][0:1, 0:64]:
             E.tensor_copy(o, i), reads=[bcs], writes=[c.bncol])
        c.becap = c.B("ecap", small=True)
        P.op("sync", lambda E, o=c.ecap_col, i=t["ecap"]: E.dma_start(out=o, in_=i),
             writes=[c.becap], inc=16)

    if "B" in phases:
        c.bwqkv = c.B("wqkv")
        for i, nm in enumerate(["wqT", "wkT", "wvT"]):
            P.op("gpsimd", lambda E, o=c.wqkvb[:, i * 512:(i + 1) * 512], src=t[nm]:
                 E.dma_start(out=o, in_=src), writes=[c.bwqkv], inc=16)

    if ("D" in phases) or ("E" in phases):
        c.bmu = c.B("mu", small=True)
        P.op("sync", lambda E, o=c.mur, i=t["mur"]: E.dma_start(out=o, in_=i), writes=[c.bmu], inc=16)
        P.op("sync", lambda E, o=c.muk, i=t["muk"]: E.dma_start(out=o, in_=i), writes=[c.bmu], inc=16)
        P.op("vector", lambda E, o=c.omur, i=c.mur: E.tensor_scalar(o, i, -1.0, 1.0, ALU.mult, ALU.add),
             reads=[c.bmu], writes=[c.bmu])
        P.op("vector", lambda E, o=c.omuk, i=c.muk: E.tensor_scalar(o, i, -1.0, 1.0, ALU.mult, ALU.add),
             reads=[c.bmu], writes=[c.bmu])


LN_O = 72 * KB


def emit_ln_chunk(c, ci, src_dram, src_buf, dst_dram, dst_buf, with_routing, O=None):
    P, t, dyn = c.P, c.t, c.dyn
    b = ci // (S // CH)
    off = (ci % (S // CH)) * CH
    goff = ci * CH
    if O is None:
        O = LN_O

    xld = dyn.f32v(O, 8, 512)
    x2 = dyn.f32v(O + 16 * KB, 8, 512)
    tmp1 = dyn.f32(O + 32 * KB, 512)
    xnst = dyn.bf16v(O + 34 * KB, 8, 512)
    m_row = dyn.f32(O + 42 * KB, 512)[0:1, :]
    sd_row = dyn.f32(O + 44 * KB, 512)[0:1, :]
    rstd_row = dyn.f32(O + 46 * KB, 512)[0:1, :]
    mr_row = dyn.f32(O + 48 * KB, 512)[0:1, :]
    ex2_row = dyn.f32(O + 50 * KB, 512)[0:1, :]
    tmpn = dyn.f32(O + 52 * KB, 512)

    bxld = c.B("ln_xld")
    bx2 = c.B("ln_x2")
    bxnst = c.B("ln_xnst")
    bstat = c.B("ln_stat", small=True)
    btmpn = c.B("ln_tmpn")
    btmp1 = c.B("ln_tmp1", small=True)
    psx, psx2, pbm, pbr = c.psum[0], c.psum[1], c.psum[2], c.psum[3]
    bpsx, bpsx2, bpbm, bpbr = c.pbuf[0], c.pbuf[1], c.pbuf[2], c.pbuf[3]

    src = src_dram[b, :, off:off + CH].rearrange("(n p) m -> p n m", p=128)
    P.op("sync", lambda E, o=xld, i=src: E.dma_start(out=o, in_=i),
         reads=[src_buf], writes=[bxld], inc=16)
    P.drain("sync")
    P.op("scalar", lambda E, o=x2, i=xld: E.activation(o, i, AF.Square),
         reads=[bxld], writes=[bx2])
    for et in range(8):
        P.op("tensor", lambda E, o=psx[0:1, :], l=c.ones_col, r=xld[:, et, :],
             s=(et == 0), p=(et == 7): E.matmul(o, l, r, start=s, stop=p),
             reads=[c.bones, bxld], writes=[bpsx])
    for et in range(8):
        P.op("tensor", lambda E, o=psx2[0:1, :], l=c.ones_col, r=x2[:, et, :],
             s=(et == 0), p=(et == 7): E.matmul(o, l, r, start=s, stop=p),
             reads=[c.bones, bx2], writes=[bpsx2])
    P.op("vector", lambda E, o=m_row, i=psx[0:1, :]:
         E.tensor_scalar(o, i, 1.0 / D, None, ALU.mult), reads=[bpsx], writes=[bstat])
    P.op("vector", lambda E, o=ex2_row, i=psx2[0:1, :]:
         E.tensor_scalar(o, i, 1.0 / D, None, ALU.mult), reads=[bpsx2], writes=[bstat])
    P.op("vector", lambda E, o=tmp1[0:1, :], i=m_row, j=m_row:
         E.tensor_mul(o, i, j), reads=[bstat], writes=[btmp1])
    P.op("vector", lambda E, o=ex2_row, i=ex2_row, j=tmp1[0:1, :]:
         E.tensor_sub(o, i, j), reads=[bstat, btmp1], writes=[bstat])
    P.op("scalar", lambda E, o=sd_row, i=ex2_row, bi=c.epsA[0:1, 0:1]:
         E.activation(o, i, AF.Sqrt, bias=bi), reads=[bstat, c.beps], writes=[bstat])
    P.op("vector", lambda E, o=rstd_row, i=sd_row: E.reciprocal(o, i),
         reads=[bstat], writes=[bstat])
    if with_routing:
        P.op("vector", lambda E, o=mr_row, i=m_row, j=rstd_row:
             E.tensor_mul(o, i, j), reads=[bstat], writes=[bstat])
    P.op("tensor", lambda E, o=pbm[:, :], l=c.ones_row, r=m_row:
         E.matmul(o, l, r, start=True, stop=True), reads=[c.bones, bstat], writes=[bpbm])
    P.op("tensor", lambda E, o=pbr[:, :], l=c.ones_row, r=rstd_row:
         E.matmul(o, l, r, start=True, stop=True), reads=[c.bones, bstat], writes=[bpbr])
    for et in range(8):
        P.op("vector", lambda E, o=tmpn, i=xld[:, et, :], j=pbm[:, :]:
             E.tensor_sub(o, i, j), reads=[bxld, bpbm], writes=[btmpn])
        P.op("vector", lambda E, o=xnst[:, et, :], i=tmpn, j=pbr[:, :]:
             E.tensor_mul(o, i, j), reads=[btmpn, bpbr], writes=[bxnst])
    dst = dst_dram[:, :, goff:goff + CH]
    wq = "gpsimd" if with_routing else "sync"
    P.op(wq, lambda E, o=dst, i=xnst: E.dma_start(out=o, in_=i),
         reads=[bxnst], writes=[dst_buf], inc=16)
    if not with_routing:
        P.drain("sync")

    if with_routing:
        xtm_st = dyn.bf16v(O + 62 * KB, 4, 1024)
        bxtmst = c.B("ln_xtmst")
        ptrxb = [c.psum[6].bitcast(BF16), c.psum[7].bitcast(BF16)]
        bptrx2 = [c.pbuf[6], c.pbuf[7]]
        for tt in range(4):
            bk = tt % 2
            for et in range(8):
                P.op("tensor", lambda E, o=ptrxb[bk][0:128, et * 128:(et + 1) * 128],
                     i=xnst[:, et, tt * 128:(tt + 1) * 128], idn=c.identb:
                     E.transpose(o, i, idn), reads=[bxnst, c.bidentb], writes=[bptrx2[bk]])
            P.op("vector", lambda E, o=xtm_st[:, tt, :], i=ptrxb[bk][0:128, 0:1024]:
                 E.tensor_copy(o, i), reads=[bptrx2[bk]], writes=[bxtmst])
        dst2 = c.t["xn_tm_h"][goff:goff + CH, :].rearrange("(tt p) cc -> p tt cc", p=128)
        P.op("gpsimd", lambda E, o=dst2, i=xtm_st: E.dma_start(out=o, in_=i),
             reads=[bxtmst], writes=[c.dbuf["xn_tm_h"]], inc=16)
        P.drain("gpsimd")

    if not with_routing:
        return

    plg, pcor = c.psum[4], c.psum[5]
    bplg, bpcor = c.pbuf[4], c.pbuf[5]
    for et in range(8):
        P.op("tensor", lambda E, o=plg[0:64, :], l=c.gwT[:, et * 64:(et + 1) * 64],
             r=xld[:, et, :], s=(et == 0), p=(et == 7): E.matmul(o, l, r, start=s, stop=p),
             reads=[c.bgw, bxld], writes=[bplg])
    P.op("tensor", lambda E, o=pcor[0:64, :], l=c.ncol_row[:, 0:64], r=mr_row:
         E.matmul(o, l, r, start=True, stop=True), reads=[c.bncol, bstat], writes=[bpcor])
    lg0 = dyn.f32(O + 54 * KB, 512)[0:64, :]
    lg1 = dyn.f32(O + 56 * KB, 512)[0:64, :]
    blg = c.B("ln_lg")
    P.op("scalar", lambda E, o=lg0, i=plg[0:64, :]: E.copy(o, i), reads=[bplg], writes=[blg])
    P.op("vector", lambda E, o=lg1, i=lg0, j=pbr[0:64, :]: E.tensor_mul(o, i, j),
         reads=[blg, bpbr], writes=[blg])
    P.op("vector", lambda E, o=lg1, i=lg1, j=pcor[0:64, :]: E.tensor_add(o, i, j),
         reads=[bpcor, blg], writes=[blg])
    ptr = c.psum[6]
    bptr = c.pbuf[6]
    lg_tm = dyn.f32v(O + 58 * KB, 4, 64)
    blgtm = c.B("ln_lgtm")
    for j in range(4):
        P.op("tensor", lambda E, o=ptr[0:128, 0:64], i=lg1[:, j * 128:(j + 1) * 128],
             idn=c.identf[0:64, 0:64]: E.transpose(o, i, idn),
             reads=[blg, c.bident], writes=[bptr])
        P.op("vector", lambda E, o=lg_tm[:, j, :], i=ptr[0:128, 0:64]:
             E.tensor_copy(o, i), reads=[bptr], writes=[blgtm])
    emit_routing(c, ci, lg_tm, blgtm, O + 71 * KB)


def emit_routing(c, ci, lg_tm, blgtm, OB):
    P, t, dyn = c.P, c.t, c.dyn
    goff = ci * CH

    scores = dyn.f32v(OB, 4, 64)
    sfc = dyn.f32v(OB + 1024, 4, 64)
    msk = dyn.f32v(OB + 2048, 4, 64)
    cur = dyn.f32v(OB + 3072, 4, 64)
    sel = dyn.f32v(OB + 4096, 4, 64)
    sel8 = dyn.f32v(OB + 5120, 4, 64)
    w_ = dyn.f32v(OB + 6144, 4, 64)
    rw = dyn.f32v(OB + 7168, 4, 64)
    m1 = dyn.f32v(OB + 8192, 4, 8)
    mtmp = dyn.f32v(OB + 8320, 4, 8)
    gs = dyn.f32v(OB + 8448, 4, 8)
    curg = dyn.f32v(OB + 8576, 4, 8)
    selg = dyn.f32v(OB + 8704, 4, 8)
    gmask = dyn.f32v(OB + 8832, 4, 8)
    mm_ = dyn.f32v(OB + 8960, 4, 1)
    den = dyn.f32v(OB + 9088, 4, 1)
    rcp = dyn.f32v(OB + 9216, 4, 1)
    rwTs = dyn.f32(OB + 9344, 512)[0:64, :]
    rwTb_st = dyn.bf16(OB + 11392, 512)[0:64, :]

    bS = c.B("route", small=True)

    def V(fn):
        P.op("vector", fn, reads=[bS], writes=[bS])

    P.op("scalar", lambda E, o=scores, i=lg_tm: E.activation(o, i, AF.Sigmoid),
         reads=[blgtm], writes=[bS])
    V(lambda E: E.tensor_add(sfc, scores, c.gbias.unsqueeze(1).broadcast_to([128, 4, 64])))
    V(lambda E: E.tensor_reduce(m1, sfc.rearrange("p a (g k) -> p a g k", k=8), AX.X, ALU.max))
    V(lambda E: E.tensor_tensor(msk.rearrange("p a (g k) -> p a g k", k=8),
                                sfc.rearrange("p a (g k) -> p a g k", k=8),
                                m1.unsqueeze(3).broadcast_to([128, 4, 8, 8]), ALU.is_ge))
    V(lambda E: E.scalar_tensor_tensor(msk, msk, -1e30, sfc, ALU.mult, ALU.add))
    V(lambda E: E.tensor_reduce(mtmp, msk.rearrange("p a (g k) -> p a g k", k=8), AX.X, ALU.max))
    V(lambda E: E.tensor_add(gs, m1, mtmp))
    V(lambda E: E.memset(gmask, 0.0))
    V(lambda E: E.tensor_copy(curg, gs))
    for _ in range(4):
        V(lambda E: E.tensor_reduce(mm_, curg, AX.X, ALU.max))
        V(lambda E: E.tensor_tensor(selg, curg, mm_.broadcast_to([128, 4, 8]), ALU.is_ge))
        V(lambda E: E.tensor_add(gmask, gmask, selg))
        V(lambda E: E.scalar_tensor_tensor(curg, selg, -1e30, curg, ALU.mult, ALU.add))
    V(lambda E: E.tensor_tensor(msk.rearrange("p a (g k) -> p a g k", k=8),
                                sfc.rearrange("p a (g k) -> p a g k", k=8),
                                gmask.unsqueeze(3).broadcast_to([128, 4, 8, 8]), ALU.mult))
    V(lambda E: E.memset(sel8, 0.0))
    V(lambda E: E.tensor_copy(cur, msk))
    for _ in range(8):
        V(lambda E: E.tensor_reduce(mm_, cur, AX.X, ALU.max))
        V(lambda E: E.tensor_tensor(sel, cur, mm_.broadcast_to([128, 4, 64]), ALU.is_ge))
        V(lambda E: E.tensor_add(sel8, sel8, sel))
        V(lambda E: E.scalar_tensor_tensor(cur, sel, -1e30, cur, ALU.mult, ALU.add))
    V(lambda E: E.tensor_mul(w_, sel8, scores))
    V(lambda E: E.tensor_reduce(den, w_, AX.X, ALU.add))
    V(lambda E: E.tensor_scalar(den, den, 1e-20, None, ALU.add))
    V(lambda E: E.reciprocal(rcp, den))
    V(lambda E: E.tensor_scalar(rcp, rcp, 2.5, None, ALU.mult))
    V(lambda E: E.tensor_tensor(rw, w_, rcp.broadcast_to([128, 4, 64]), ALU.mult))
    ptr2 = c.psum[7]
    bptr2 = c.pbuf[7]
    for j in range(4):
        P.op("tensor", lambda E, o=ptr2[0:64, 0:128], i=rw[:, j, :], idn=c.identf:
             E.transpose(o, i, idn), reads=[bS, c.bident], writes=[bptr2])
        P.op("vector", lambda E, o=rwTs[:, j * 128:(j + 1) * 128], i=ptr2[0:64, 0:128]:
             E.tensor_copy(o, i), reads=[bptr2], writes=[bS])
    V(lambda E: E.tensor_copy(rwTb_st, rwTs))
    P.op("sync", lambda E, o=t["rwT_h"][:, goff:goff + CH], i=rwTb_st:
         E.dma_start(out=o, in_=i), reads=[bS], writes=[c.dbuf["rwT_h"]], inc=16)
    P.drain("sync")


def emit_phase_a(c):
    for ci in range(NCH):
        emit_ln_chunk(c, ci, c.t["x_fm"], c.dbuf["x_fm"], c.t["xn_h"], c.dbuf["xn_h"], True)




def emit_phase_a2(c):
    P, t, dyn = c.P, c.t, c.dyn
    O = LN_O
    rwt_s = dyn.bf16(O, TOK)[0:64, :]
    mask = dyn.f32(O + 8 * KB, TOK)[0:64, :]
    cumA = dyn.f32(O + 24 * KB, TOK)[0:64, :]
    cumB = dyn.f32(O + 40 * KB, TOK)[0:64, :]
    tgt = dyn.f32(O + 56 * KB, TOK)[0:64, :]
    ovfm = dyn.f32(O + 40 * KB, TOK)[0:64, :]
    tgt_tm = dyn.f32v(O + 8 * KB, 32, 64)
    rw_tm = dyn.f32v(O + 16 * KB, 32, 64)
    cur = dyn.f32v(O + 24 * KB, 32, 64)
    eq = dyn.f32v(O + 32 * KB, 32, 64)
    tmp_e = dyn.f32v(O + 48 * KB, 32, 64)
    SM = O + 44 * KB
    mn = dyn.f32v(SM, 32, 1)
    gbv = dyn.f32v(SM + 256, 32, 1)
    tgt8 = dyn.f32v(SM + 512, 32, 8)
    rw8 = dyn.f32v(SM + 1536, 32, 8)
    idx_i = dyn.f.bitcast(mybir.dt.int32)[:, (SM + 2560) // 4:(SM + 2560) // 4 + 256].rearrange(
        "p (a b) -> p a b", a=32)

    bS = c.B("a2", small=True)

    def V(fn):
        P.op("vector", fn, reads=[bS], writes=[bS])

    P.op("sync", lambda E, o=rwt_s, i=t["rwT_h"]: E.dma_start(out=o, in_=i),
         reads=[c.dbuf["rwT_h"]], writes=[bS], inc=16)
    P.drain("sync")
    V(lambda E: E.tensor_single_scalar(mask, rwt_s, 0.0, ALU.is_gt))
    V(lambda E: E.tensor_copy(cumA, mask))
    srcb, dstb = cumA, cumB
    sh = 1
    while sh < TOK:
        V(lambda E, s_=srcb, d_=dstb, ss=sh: E.tensor_add(d_[:, ss:], s_[:, ss:], s_[:, :TOK - ss]))
        V(lambda E, s_=srcb, d_=dstb, ss=sh: E.tensor_copy(d_[:, 0:ss], s_[:, 0:ss]))
        srcb, dstb = dstb, srcb
        sh *= 2
    cum = srcb
    V(lambda E: E.tensor_sub(tgt, cum, mask))
    V(lambda E: E.tensor_single_scalar(ovfm, tgt, float(CAP), ALU.is_ge))
    V(lambda E: E.tensor_scalar(tgt, tgt, c.ecap_col[0:64, 0:1], None, ALU.add))
    V(lambda E: E.tensor_scalar(tgt, tgt, BIGF, None, ALU.add))
    V(lambda E: E.scalar_tensor_tensor(tgt, mask, -BIGF, tgt, ALU.mult, ALU.add))
    V(lambda E: E.scalar_tensor_tensor(tgt, ovfm, BIGF, tgt, ALU.mult, ALU.add))
    V(lambda E: E.tensor_scalar_min(tgt, tgt, float(TRASH)))
    ptrf = c.psum[6]
    ptrb2 = c.psum[7].bitcast(BF16)
    bptrf, bptrb = c.pbuf[6], c.pbuf[7]
    for j in range(32):
        P.op("tensor", lambda E, o=ptrf[0:128, 0:64], i=tgt[:, j * 128:(j + 1) * 128],
             idn=c.identf[0:64, 0:64]: E.transpose(o, i, idn),
             reads=[bS, c.bident], writes=[bptrf])
        P.op("vector", lambda E, o=tgt_tm[:, j, :], i=ptrf[0:128, 0:64]:
             E.tensor_copy(o, i), reads=[bptrf], writes=[bS])
        P.op("tensor", lambda E, o=ptrb2[0:128, 0:64], i=rwt_s[:, j * 128:(j + 1) * 128],
             idn=c.identb[0:64, 0:64]: E.transpose(o, i, idn),
             reads=[bS, c.bidentb], writes=[bptrb])
        P.op("vector", lambda E, o=rw_tm[:, j, :], i=ptrb2[0:128, 0:64]:
             E.tensor_copy(o, i), reads=[bptrb], writes=[bS])
    V(lambda E: E.tensor_copy(cur, tgt_tm))
    for j in range(8):
        V(lambda E: E.tensor_reduce(mn, cur, AX.X, ALU.min))
        V(lambda E, jj=j: E.tensor_copy(tgt8[:, :, jj:jj + 1], mn))
        V(lambda E: E.tensor_tensor(eq, cur, mn.broadcast_to([128, 32, 64]), ALU.is_le))
        V(lambda E: E.tensor_mul(tmp_e, eq, rw_tm))
        V(lambda E, jj=j: E.tensor_reduce(rw8[:, :, jj:jj + 1], tmp_e, AX.X, ALU.add))
        V(lambda E: E.tensor_single_scalar(gbv, mn, float(NSLOT), ALU.is_lt))
        V(lambda E, jj=j: E.tensor_tensor(rw8[:, :, jj:jj + 1], rw8[:, :, jj:jj + 1],
                                          gbv, ALU.mult))
        V(lambda E: E.scalar_tensor_tensor(cur, eq, BIGF, cur, ALU.mult, ALU.add))
    V(lambda E: E.tensor_copy(idx_i, tgt8))
    P.op("gpsimd", lambda E, o=t["idx_h"].rearrange("(tt p) j -> p tt j", p=128), i=idx_i:
         E.dma_start(out=o, in_=i), reads=[bS], writes=[c.dbuf["idx_h"]], inc=16)
    P.op("gpsimd", lambda E, o=t["rw8_h"].rearrange("(tt p) j -> p tt j", p=128), i=rw8:
         E.dma_start(out=o, in_=i), reads=[bS], writes=[c.dbuf["rw8_h"]], inc=16)


def emit_phase_b(c):
    P, t, dyn = c.P, c.t, c.dyn
    O = LN_O
    xnb = dyn.bf16v(O, 8, S)
    scf = dyn.f32(O + 32 * KB, S)
    attn_t = [dyn.bf16(O + 40 * KB + i * 4 * KB, S) for i in range(2)]
    qTb = dyn.bf16(O + 48 * KB, S)[0:64, :]
    kTb = dyn.bf16(O + 52 * KB, S)[0:64, :]
    vTb = dyn.bf16(O + 56 * KB, S)[0:64, :]
    v_tm = dyn.bf16(O + 60 * KB, 16 * 128)
    v_sc = [dyn.bf16(O + 64 * KB + i * 256, 128) for i in range(2)]
    sdst = dyn.f32(O + 66 * KB, S)
    nmx = dyn.f32(O + 74 * KB, 16)[:, 0:1]
    rs = dyn.f32(O + 74 * KB + 64, 16)[:, 0:1]
    rcq = dyn.f32(O + 74 * KB + 128, 16)[:, 0:1]

    pq = c.psum[0]
    ptr = c.psum[1]
    ptrb = ptr.bitcast(BF16)
    psc = [c.psum[2], c.psum[3]]
    pout = [c.psum[4 + i] for i in range(4)]

    bxnb = c.B("b_xnb")
    bq, bk, bv = c.B("b_q"), c.B("b_k"), c.B("b_v")
    bpq = c.pbuf[0]
    bvtm = c.B("b_vtm")
    bptr = c.pbuf[1]
    bscf = c.B("b_scf")
    bnmx = c.B("b_nmx", small=True)
    brs = c.B("b_rs", small=True)
    batt = [c.B("b_att0"), c.B("b_att1")]
    bvsc = [c.B("b_vsc0"), c.B("b_vsc1")]
    bpsc = [c.pbuf[2], c.pbuf[3]]
    bpout = c.pbuf[4]
    bpout2 = [c.pbuf[4], c.pbuf[5], c.pbuf[6], c.pbuf[7]]
    bsd = c.B("b_sd")

    for b in range(BPC):
        P.op("sync", lambda E, o=xnb, i=t["xn_h"][:, :, b * S:(b + 1) * S]:
             E.dma_start(out=o, in_=i), reads=[c.dbuf["xn_h"]], writes=[bxnb], inc=16)
        P.drain("sync")
        for wi, (dst, bd) in enumerate([(qTb, bq), (kTb, bk), (vTb, bv)]):
            for ic in range(4):
                for et in range(8):
                    P.op("tensor", lambda E, o=pq[0:64, :],
                         l=c.wqkvb[:, wi * 512 + et * 64: wi * 512 + (et + 1) * 64],
                         r=xnb[:, et, ic * 512:(ic + 1) * 512],
                         s=(et == 0), p=(et == 7): E.matmul(o, l, r, start=s, stop=p),
                         reads=[c.bwqkv, bxnb], writes=[bpq])
                P.op("scalar", lambda E, o=dst[:, ic * 512:(ic + 1) * 512], i=pq[0:64, :]:
                     E.copy(o, i), reads=[bpq], writes=[bd])
        for kt in range(16):
            P.op("tensor", lambda E, o=ptrb[0:128, 0:64], i=vTb[:, kt * 128:(kt + 1) * 128],
                 idn=c.identb[0:64, 0:64]: E.transpose(o, i, idn),
                 reads=[bv, c.bidentb], writes=[bptr])
            P.op("vector", lambda E, o=v_tm[:, kt * 128:kt * 128 + 64], i=ptrb[0:128, 0:64]:
                 E.tensor_copy(o, i), reads=[bptr], writes=[bvtm])
            P.op("vector", lambda E, o=v_tm[:, kt * 128 + 64:(kt + 1) * 128], i=ptrb[0:128, 0:64]:
                 E.tensor_copy(o, i), reads=[bptr], writes=[bvtm])
        for kt in range(16):
            par = kt % 2
            for ic in range(4):
                P.op("tensor", lambda E, o=psc[ic % 2][:, :], l=kTb[:, kt * 128:(kt + 1) * 128],
                     r=qTb[:, ic * 512:(ic + 1) * 512]: E.matmul(o, l, r, start=True, stop=True),
                     reads=[bk, bq], writes=[bpsc[ic % 2]])
                P.op("scalar", lambda E, o=scf[:, ic * 512:(ic + 1) * 512], i=psc[ic % 2][:, :]:
                     E.copy(o, i), reads=[bpsc[ic % 2]], writes=[bscf])
            P.op("vector", lambda E, o=nmx, i=scf:
                 E.reduce_max(o, i, axis=AX.X, negate=True), reads=[bscf], writes=[bnmx])
            P.op("scalar", lambda E, o=attn_t[par], i=scf, bi=nmx, ac=rs:
                 E.activation(o, i, AF.Exp, bias=bi, accum_out=ac),
                 reads=[bscf, bnmx], writes=[batt[par], brs])
            P.op("vector", lambda E, o=rcq, i=rs: E.reciprocal(o, i),
                 reads=[brs], writes=[brs])
            P.op("vector", lambda E, o=v_sc[par], i=v_tm[:, kt * 128:(kt + 1) * 128], sc=rcq:
                 E.tensor_scalar(o, i, sc, None, ALU.mult),
                 reads=[bvtm, brs], writes=[bvsc[par]])
            for ic in range(4):
                P.op("tensor", lambda E, o=pout[ic][:, :], l=v_sc[par],
                     r=attn_t[par][:, ic * 512:(ic + 1) * 512],
                     s=(kt == 0), p=(kt == 15): E.matmul(o, l, r, start=s, stop=p),
                     reads=[bvsc[par], batt[par]], writes=[bpout2[ic]])
        for ic in range(4):
            P.op("scalar", lambda E, o=sdst[:, ic * 512:(ic + 1) * 512], i=pout[ic][:, :]:
                 E.copy(o, i), reads=[bpout2[ic]], writes=[bsd])
        P.op("sync", lambda E, o=t["sdup_h"][:, b * S:(b + 1) * S], i=sdst:
             E.dma_start(out=o, in_=i), reads=[bsd], writes=[c.dbuf["sdup_h"]], inc=16)


def emit_phase_c(c, entry_toks=(), with_d_ln=False):
    P, t, dyn = c.P, c.t, c.dyn
    wbuf = [dyn.bf16(0 + i * 36864, BLOBW) for i in range(2)]
    cx_tm = [dyn.bf16v(72 * KB, 4, 1024), dyn.bf16v(80 * KB, 4, 1024),
             dyn.bf16v(160 * KB, 4, 1024)]
    rh = [dyn.bf16v(88 * KB, 8, 512), dyn.bf16v(96 * KB, 8, 512),
          dyn.bf16v(168 * KB, 8, 512)]
    hb = [dyn.bf16(104 * KB + i * 6 * KB, 6 * 512) for i in range(2)]
    sg = [dyn.f32(116 * KB + i * 2 * KB, 512) for i in range(2)]
    co_st = [dyn.bf16v(120 * KB + i * 8 * KB, 4, 1024) for i in range(2)]
    sc_tile = [dyn.bf16(152 * KB + i * 2 * KB, 1024) for i in range(2)]
    idx_all = dyn.f.bitcast(mybir.dt.int32)[:, (192 * KB) // 4:(192 * KB) // 4 + 256].rearrange(
        "p (a b) -> p a b", a=32)
    rw8_all = dyn.f32v(193 * KB, 32, 8)
    sdup_s = dyn.f32(122 * KB, TOK)
    Gt = [dyn.bf16(72 * KB + i * 2 * KB, 1024) for i in range(4)]
    acc = [dyn.f32(80 * KB + i * 4 * KB, 1024) for i in range(2)]
    gsb = [dyn.f32(88 * KB + i * 4 * KB, 1024) for i in range(2)]
    x_tile = [dyn.f32(96 * KB + i * 4 * KB, 1024) for i in range(2)]
    x1f_st = [dyn.f32v(104 * KB + i * 4 * KB, 8, 128) for i in range(2)]
    s_small = [dyn.f32(112 * KB + i * 512, 64) for i in range(2)]
    sht = [dyn.bf16(113 * KB + i * 2 * KB, 1024) for i in range(4)]
    idxt2 = [dyn.f.bitcast(mybir.dt.int32)[:, (121 * KB + i * 64) // 4:(121 * KB + i * 64) // 4 + 8]
             for i in range(2)]
    rw8t = [dyn.f32(121 * KB + 512 + i * 512, 8) for i in range(2)]

    bwbuf = [c.B("c_wb0"), c.B("c_wb1")]
    bcx = [c.B("c_cx0"), c.B("c_cx1"), c.B("c_cx2")]
    brh = [c.B("c_rh0"), c.B("c_rh1"), c.B("c_rh2")]
    bhb = [c.B("c_hb0"), c.B("c_hb1")]
    bsg = [c.B("c_sg0"), c.B("c_sg1")]
    bco = [c.B("c_co0"), c.B("c_co1")]
    bsc = [c.B("c_sc0"), c.B("c_sc1")]
    bidx_all = c.B("c_idxall")
    brw8_all = c.B("c_rw8all")
    bsdup = c.B("c_sdup")
    bG = [c.B(f"c_G{i}") for i in range(4)]
    bacc = [c.B("c_acc0"), c.B("c_acc1")]
    bgs = [c.B("c_gs0"), c.B("c_gs1")]
    bxt = [c.B("c_xt0"), c.B("c_xt1")]
    bx1f = [c.B("c_x1f0"), c.B("c_x1f1")]
    bss = [c.B("c_ss0"), c.B("c_ss1")]
    bsh = [c.B(f"c_sh{i}") for i in range(4)]
    bidxt2 = [c.B("c_idxt2a"), c.B("c_idxt2b")]
    brw8t = [c.B("c_rw8t0"), c.B("c_rw8t1")]

    pag = [c.psum[0], c.psum[1]]
    pau = [c.psum[2], c.psum[3]]
    pdn = [c.psum[4], c.psum[5]]
    ptr_ = c.psum[6]
    ptrb_ = ptr_.bitcast(BF16)
    ptr2_ = c.psum[7]
    bpag = [c.pbuf[0], c.pbuf[1]]
    bpau = [c.pbuf[2], c.pbuf[3]]
    bpdn = [c.pbuf[4], c.pbuf[5]]
    bptr = c.pbuf[6]
    bptr2 = c.pbuf[7]

    worder = [64, 65] + list(range(64))
    wseq = {e: i for i, e in enumerate(worder)}

    def wdma(e):
        P.op("gpsimd", lambda E, o=wbuf[wseq[e] % 2], i=t["moeblob"][e]:
             E.dma_start(out=o, in_=i), writes=[bwbuf[wseq[e] % 2]], inc=16)

    wdma(64)
    wdma(65)
    P.drain("gpsimd")

    P.op("gpsimd", lambda E, o=idx_all, i=t["idx_h"].rearrange("(tt p) j -> p tt j", p=128):
         E.dma_start(out=o, in_=i), reads=[c.dbuf["idx_h"]], writes=[bidx_all], inc=16)
    P.op("gpsimd", lambda E, o=rw8_all, i=t["rw8_h"].rearrange("(tt p) j -> p tt j", p=128):
         E.dma_start(out=o, in_=i), reads=[c.dbuf["rw8_h"]], writes=[brw8_all], inc=16)
    P.drain("gpsimd")
    for tt in range(32):
        par = tt % 2
        P.op("gpsimd", lambda E, o=sc_tile[par],
             i=t["xn_tm_h"][tt * 128:(tt + 1) * 128, :]:
             E.dma_start(out=o, in_=i), reads=[c.dbuf["xn_tm_h"]], writes=[bsc[par]], inc=16)
        P.drain("gpsimd")
        for j in range(8):
            P.op("gpsimd", lambda E, o=t["compact_x"], off=idx_all[:, tt, j:j + 1], i=sc_tile[par]:
                 E.indirect_dma_start(out=o, out_offset=IndirectOffsetOnAxis(off, 0),
                                      in_=i, in_offset=None),
                 reads=[bsc[par], bidx_all], writes=[c.dbuf["compact_x"]], inc=16)
        P.drain("gpsimd")

    P.drain("gpsimd")

    slots = [("s", 64 + hh, c8) for hh in range(2) for c8 in range(8)] + \
            [("e", e, ch) for e in range(64) for ch in range(3)]

    first_load = [True]

    def emit_load(k):
        kind, e, ch = slots[k]
        par = k % 3
        ext = list(entry_toks) if first_load[0] else ()
        first_load[0] = False
        if kind == "e":
            base = e * CAP + ch * 512
            P.op("sync", lambda E, o=cx_tm[par],
                 i=t["compact_x"][base:base + 512, :].rearrange("(tt p) cc -> p tt cc", p=128):
                 E.dma_start(out=o, in_=i),
                 reads=[c.dbuf["compact_x"]], writes=[bcx[par]], inc=16, extra=ext)
        else:
            goff = ch * 512
            P.op("sync", lambda E, o=rh[par], i=t["xn_h"][:, :, goff:goff + 512]:
                 E.dma_start(out=o, in_=i), reads=[c.dbuf["xn_h"]], writes=[brh[par]],
                 inc=16, extra=ext)

    ptrbanks = [ptr_.bitcast(BF16), ptr2_.bitcast(BF16)]
    bptrx = [c.pbuf[6], c.pbuf[7]]

    def emit_tr(k):
        kind, e, ch = slots[k]
        if kind != "e":
            return
        par = k % 3
        for tt in range(4):
            bk = tt % 2
            pb = ptrbanks[bk]
            for ct in range(8):
                P.op("tensor", lambda E, o=pb[0:128, ct * 128:(ct + 1) * 128],
                     i=cx_tm[par][:, tt, ct * 128:(ct + 1) * 128], idn=c.identb:
                     E.transpose(o, i, idn), reads=[bcx[par], c.bidentb],
                     writes=[bptrx[bk]])
            P.op("vector", lambda E, o=rh[par][:, :, tt * 128:(tt + 1) * 128],
                 i=pb[0:128, 0:1024].rearrange("p (a b) -> p a b", a=8):
                 E.tensor_copy(o, i), reads=[bptrx[bk]], writes=[brh[par]])

    def emit_up(k):
        kind, e, ch = slots[k]
        par = k % 3
        wpar = wseq[e] % 2
        for it in range(6):
            ap_ = it % 2
            for et in range(8):
                P.op("tensor", lambda E, o=pag[ap_][:, :],
                     l=wbuf[wpar][:, et * IPAD + it * 128: et * IPAD + (it + 1) * 128],
                     r=rh[par][:, et, :], s=(et == 0), p=(et == 7):
                     E.matmul(o, l, r, start=s, stop=p),
                     reads=[bwbuf[wpar], brh[par]], writes=[bpag[ap_]])
            for et in range(8):
                P.op("tensor", lambda E, o=pau[ap_][:, :],
                     l=wbuf[wpar][:, SEC + et * IPAD + it * 128: SEC + et * IPAD + (it + 1) * 128],
                     r=rh[par][:, et, :], s=(et == 0), p=(et == 7):
                     E.matmul(o, l, r, start=s, stop=p),
                     reads=[bwbuf[wpar], brh[par]], writes=[bpau[ap_]])
            P.op("scalar", lambda E, o=sg[ap_], i=pag[ap_][:, :]:
                 E.activation(o, i, AF.Silu), reads=[bpag[ap_]], writes=[bsg[ap_]])
            P.op("vector", lambda E, o=hb[k % 2][:, it * 512:(it + 1) * 512], i=sg[ap_],
                 j=pau[ap_][:, :]: E.tensor_mul(o, i, j),
                 reads=[bsg[ap_], bpau[ap_]], writes=[bhb[k % 2]])

    def emit_down(k):
        kind, e, ch = slots[k]
        par = k % 2
        wpar = wseq[e] % 2
        for mt in range(4):
            dp = mt % 2
            for nh in range(2):
                for it in range(6):
                    P.op("tensor", lambda E, o=pdn[dp][:, :],
                         l=hb[par][:, it * 512 + mt * 128: it * 512 + (mt + 1) * 128],
                         r=wbuf[wpar][:, 2 * SEC + it * 1024 + nh * 512: 2 * SEC + it * 1024 + (nh + 1) * 512],
                         s=(it == 0), p=(it == 5): E.matmul(o, l, r, start=s, stop=p),
                         reads=[bwbuf[wpar], bhb[par]], writes=[bpdn[dp]])
                P.op("scalar", lambda E, o=co_st[par][:, mt, nh * 512:(nh + 1) * 512],
                     i=pdn[dp][:, :]: E.copy(o, i), reads=[bpdn[dp]], writes=[bco[par]])
        if kind == "e":
            base = e * CAP + ch * 512
            dst = t["compact_out"][base:base + 512, :].rearrange("(tt p) cc -> p tt cc", p=128)
            dbf = c.dbuf["compact_out"]
        else:
            hh = e - 64
            base = hh * TOK + ch * 512
            dst = t["shared_h"][base:base + 512, :].rearrange("(tt p) cc -> p tt cc", p=128)
            dbf = c.dbuf["shared_h"]
        P.op("sync", lambda E, o=dst, i=co_st[par]: E.dma_start(out=o, in_=i),
             reads=[bco[par]], writes=[dbf], inc=16)

    emit_load(0)
    emit_load(1)
    emit_tr(0)
    for k in range(len(slots)):
        if k + 2 < len(slots):
            emit_load(k + 2)
        emit_up(k)
        if k + 1 < len(slots):
            emit_tr(k + 1)
        if k > 0:
            emit_down(k - 1)
            kindp, ep, chp = slots[k - 1]
            last_ch = 2 if kindp == "e" else 7
            if chp == last_ch and wseq[ep] + 2 < NE:
                wdma(worder[wseq[ep] + 2])
    emit_down(len(slots) - 1)
    c.d_ln_done = False

    P.barrier()
    bAS = c.B("c_asm", small=False)
    P.op("sync", lambda E, o=sdup_s, i=t["sdup_h"]: E.dma_start(out=o, in_=i),
         reads=[c.dbuf["sdup_h"]], writes=[bsdup], inc=16)
    P.drain("sync")

    for tt in range(32):
        gp = tt % 2
        goff = tt * 128
        b = goff // S
        off = goff % S
        P.op("sync", lambda E, o=x_tile[gp], i=t["x_tm"][goff:goff + 128, :]:
             E.dma_start(out=o, in_=i), writes=[bxt[gp]], inc=16)
        P.op("tensor", lambda E, o=[ptr_, ptr2_][gp][0:128, 0:64],
             i=sdup_s[0:64, goff:goff + 128],
             idn=c.identf[0:64, 0:64]: E.transpose(o, i, idn),
             reads=[bsdup, c.bident], writes=[bptrx[gp]])
        P.op("vector", lambda E, o=s_small[gp], i=[ptr_, ptr2_][gp][0:128, 0:64]:
             E.tensor_copy(o, i), reads=[bptrx[gp]], writes=[bss[gp]])
        for hh in range(2):
            sj = hh + 2 * gp
            P.op("sync", lambda E, o=sht[sj],
                 i=t["shared_h"][hh * TOK + goff: hh * TOK + goff + 128, :]:
                 E.dma_start(out=o, in_=i), reads=[c.dbuf["shared_h"]], writes=[bsh[sj]], inc=16)
        P.drain("sync")
        for jp in range(4):
            for j in (2 * jp, 2 * jp + 1):
                gj = (j % 2) + 2 * gp
                P.op("gpsimd", lambda E, o=Gt[gj], off=idx_all[:, tt, j:j + 1], i=t["compact_out"]:
                     E.indirect_dma_start(out=o, out_offset=None, in_=i,
                                          in_offset=IndirectOffsetOnAxis(off, 0)),
                     reads=[c.dbuf["compact_out"], bidx_all], writes=[bG[gj]], inc=16)
            P.drain("gpsimd")
            for j in (2 * jp, 2 * jp + 1):
                gj = (j % 2) + 2 * gp
                if j == 0:
                    P.op("vector", lambda E, o=acc[gp], i=Gt[gj], sc=rw8_all[:, tt, 0:1]:
                         E.tensor_scalar(o, i, sc, None, ALU.mult),
                         reads=[bG[gj], brw8_all], writes=[bacc[gp]])
                else:
                    a_in = acc[gp] if j % 2 == 1 else gsb[gp]
                    a_out = gsb[gp] if j % 2 == 1 else acc[gp]
                    P.op("vector", lambda E, o=a_out, i=Gt[gj], sc=rw8_all[:, tt, j:j + 1], jj=a_in:
                         E.scalar_tensor_tensor(o, i, sc, jj, ALU.mult, ALU.add),
                         reads=[bG[gj], brw8_all, bacc[gp], bgs[gp]],
                         writes=[bacc[gp], bgs[gp]])
        for hh in range(2):
            sj = hh + 2 * gp
            a_in = gsb[gp] if hh == 0 else acc[gp]
            a_out = acc[gp] if hh == 0 else gsb[gp]
            P.op("vector", lambda E, o=a_out, i=a_in, j=sht[sj]: E.tensor_add(o, i, j),
                 reads=[bsh[sj], bacc[gp], bgs[gp]], writes=[bacc[gp], bgs[gp]])
        P.op("scalar", lambda E, o=acc[gp], i=gsb[gp]: E.activation(o, i, AF.Silu),
             reads=[bgs[gp]], writes=[bacc[gp]])
        P.op("vector", lambda E, o=gsb[gp].rearrange("p (r d) -> p r d", r=16),
             i=acc[gp].rearrange("p (r d) -> p r d", r=16),
             j=s_small[gp].unsqueeze(1).broadcast_to([128, 16, 64]): E.tensor_mul(o, i, j),
             reads=[bss[gp], bacc[gp]], writes=[bgs[gp]])
        P.op("vector", lambda E, o=acc[gp], i=gsb[gp], j=x_tile[gp]: E.tensor_add(o, i, j),
             reads=[bxt[gp], bgs[gp]], writes=[bacc[gp]])
        for half in range(2):
            bk = (gp + half) % 2
            pbf = [ptr_, ptr2_][bk]
            for ct4 in range(4):
                ct = half * 4 + ct4
                P.op("tensor", lambda E, o=pbf[0:128, ct4 * 128:(ct4 + 1) * 128],
                     i=acc[gp][:, ct * 128:(ct + 1) * 128], idn=c.identf:
                     E.transpose(o, i, idn), reads=[bacc[gp], c.bident], writes=[bptrx[bk]])
            P.op("scalar", lambda E, o=x1f_st[gp][:, half * 4:(half + 1) * 4, :],
                 i=pbf[0:128, 0:512].rearrange("p (a b) -> p a b", a=4):
                 E.copy(o, i), reads=[bptrx[bk]], writes=[bx1f[gp]])
        dst = t["x1_h"][b, :, off:off + 128].rearrange("(n p) m -> p n m", p=128)
        P.op("sync", lambda E, o=dst, i=x1f_st[gp]: E.dma_start(out=o, in_=i),
             reads=[bx1f[gp]], writes=[c.dbuf["x1_h"]], inc=16)
        if with_d_ln and (tt + 1) % 4 == 0:
            emit_ln_chunk(c, tt // 4, t["x1_h"], c.dbuf["x1_h"], t["xn1_h"],
                          c.dbuf["xn1_h"], False, O=138 * KB)
            c.d_ln_done = True


def emit_phase_d(c):
    P, t, dyn = c.P, c.t, c.dyn
    c.qwTb = dyn.bf16(0, 8192)
    c.PaB = dyn.bf16(16 * KB, 2 * 8192)
    c.combTb = dyn.bf16(48 * KB, 8192)
    c.wrkvb = dyn.bf16(152 * KB, 3 * 8192)
    c.bqw, c.bPa, c.bcomb, c.bwrkv = c.B("d_qw"), c.B("d_Pa"), c.B("d_comb"), c.B("d_wrkv")
    P.op("gpsimd", lambda E, o=c.qwTb, i=t["qwT"]: E.dma_start(out=o, in_=i),
         writes=[c.bqw], inc=16)
    P.op("gpsimd", lambda E, o=c.combTb, i=t["combT"]: E.dma_start(out=o, in_=i),
         writes=[c.bcomb], inc=16)

    if getattr(c, "d_ln_done", False):
        pass
    else:
        for ci in range(NCH):
            emit_ln_chunk(c, ci, c.t["x1_h"], c.dbuf["x1_h"], c.t["xn1_h"], c.dbuf["xn1_h"], False)
    c.P.barrier()

    for i, nm in enumerate(["wrT", "wk2T", "wv2T"]):
        P.op("gpsimd", lambda E, o=c.wrkvb[:, i * 8192:(i + 1) * 8192], src=t[nm]:
             E.dma_start(out=o, in_=src), writes=[c.bwrkv], inc=16)

    MqB = dyn.bf16(64 * KB, 4 * 512)
    xnb = dyn.bf16v(68 * KB, 8, S)
    xtm = dyn.bf16(100 * KB, 16 * 1024)
    bxnb, bxtm, bMq = c.B("d_xnb"), c.B("d_xtm"), c.B("d_Mq")
    ptr = c.psum[0]
    ptrb = ptr.bitcast(BF16)
    bptr = c.pbuf[0]
    pmq = [c.psum[1 + q] for q in range(4)]
    bpmq = [c.pbuf[1 + q] for q in range(4)]
    ppp = [c.psum[5], c.psum[6]]
    bppp = [c.pbuf[5], c.pbuf[6]]

    for b in range(BPC):
        P.op("sync", lambda E, o=xnb, i=t["xn1_h"][:, :, b * S:(b + 1) * S]:
             E.dma_start(out=o, in_=i), reads=[c.dbuf["xn1_h"]], writes=[bxnb], inc=16)
        P.drain("sync")
        for ut in range(16):
            for ft in range(8):
                P.op("tensor", lambda E, o=ptrb[0:128, 0:128],
                     i=xnb[:, ft, ut * 128:(ut + 1) * 128], idn=c.identb:
                     E.transpose(o, i, idn), reads=[bxnb, c.bidentb], writes=[bptr])
                P.op("vector", lambda E,
                     o=xtm[:, ut * 1024 + ft * 128: ut * 1024 + (ft + 1) * 128],
                     i=ptrb[0:128, 0:128]: E.tensor_copy(o, i), reads=[bptr], writes=[bxtm])
        for fh in range(2):
            for ut in range(16):
                for q in range(4):
                    P.op("tensor", lambda E, o=pmq[q][:, fh * 256:(fh + 1) * 256],
                         l=xtm[:, ut * 1024 + q * 256 + fh * 128: ut * 1024 + q * 256 + (fh + 1) * 128],
                         r=xtm[:, ut * 1024 + q * 256: ut * 1024 + (q + 1) * 256],
                         s=(ut == 0), p=(ut == 15): E.matmul(o, l, r, start=s, stop=p),
                         reads=[bxtm], writes=[bpmq[q]])
            for q in range(4):
                P.op("scalar", lambda E, o=MqB[:, q * 512 + fh * 256: q * 512 + (fh + 1) * 256],
                     i=pmq[q][:, fh * 256:(fh + 1) * 256]:
                     E.copy(o, i), reads=[bpmq[q]], writes=[bMq])
        for a in range(4):
            for q in range(4):
                for eh in range(2):
                    pp = ppp[eh]
                    for fh in range(2):
                        P.op("tensor", lambda E, o=pp[:, 0:256],
                             l=c.qwTb[:, ((a * 4 + q) * 2 + fh) * 256 + eh * 128:
                                      ((a * 4 + q) * 2 + fh) * 256 + (eh + 1) * 128],
                             r=MqB[:, q * 512 + fh * 256: q * 512 + (fh + 1) * 256],
                             s=(fh == 0), p=(fh == 1): E.matmul(o, l, r, start=s, stop=p),
                             reads=[c.bqw, bMq], writes=[bppp[eh]])
                    dst = c.PaB[:, b * 8192 + a * 2048 + eh * 1024 + q * 256:
                                b * 8192 + a * 2048 + eh * 1024 + (q + 1) * 256]
                    P.op("scalar", lambda E, o=dst, i=ppp[eh][:, 0:256]: E.copy(o, i),
                         reads=[bppp[eh]], writes=[c.bPa])


def emit_phase_e(c):
    P, t, dyn = c.P, c.t, c.dyn
    rsb = dyn.f32(0, 8 * 512)
    kb = dyn.bf16(64 * KB, 8 * 512)
    attn_a = dyn.bf16(72 * KB, 8 * 512)
    s1c = dyn.f32(80 * KB, 8 * 512)
    tmpk = dyn.f32(96 * KB, 512)
    fo = [dyn.f32(98 * KB + i * 2 * KB, 512) for i in range(2)]
    oo = [dyn.f32(102 * KB + i * 2 * KB, 512) for i in range(2)]
    t1r = dyn.f32(106 * KB, 512)
    g1s = dyn.f32(108 * KB, 512)
    xe = dyn.bf16v(110 * KB, 8, 513)
    xe1 = dyn.f32v(119 * KB, 8, 512)
    xr = dyn.bf16(135 * KB, 8 * 512)
    xk = dyn.bf16(143 * KB, 8 * 512)

    brsb = c.B("e_rsb")
    bxe = c.B("e_xe")
    bxe1 = c.B("e_xe1")
    bxr, bxk, bkb = c.B("e_xr"), c.B("e_xk"), c.B("e_kb")
    batn = c.B("e_attn")
    bs1c = c.B("e_s1c")
    btmpk = c.B("e_tmpk")
    bfo = [c.B("e_fo0"), c.B("e_fo1")]
    bg1s = c.B("e_g1s")
    boo = [c.B("e_oo0"), c.B("e_oo1")]
    bt1r = c.B("e_t1r")

    pa1 = [c.psum[0], c.psum[1]]
    po = [c.psum[2], c.psum[3]]
    pe_ = [c.psum[4], c.psum[5]]
    bpa1 = [c.pbuf[0], c.pbuf[1]]
    bpo = [c.pbuf[2], c.pbuf[3]]
    bpe = [c.pbuf[4], c.pbuf[5]]

    for b in range(BPC):
        for cc in range(4):
            off = cc * 512
            if cc == 0:
                P.op("vector", lambda E, o=xe[:, :, 0:1]: E.memset(o, 0.0), writes=[bxe])
                P.op("sync", lambda E, o=xe[:, :, 1:513],
                     i=t["xn1_h"][:, :, b * S: b * S + 512]:
                     E.dma_start(out=o, in_=i), reads=[c.dbuf["xn1_h"]], writes=[bxe], inc=16)
            else:
                P.op("sync", lambda E, o=xe[:, :, 0:513],
                     i=t["xn1_h"][:, :, b * S + off - 1: b * S + off + 512]:
                     E.dma_start(out=o, in_=i), reads=[c.dbuf["xn1_h"]], writes=[bxe], inc=16)
            src1 = t["x1_h"][b, :, off:off + 512].rearrange("(n p) m -> p n m", p=128)
            P.op("sync", lambda E, o=xe1, i=src1: E.dma_start(out=o, in_=i),
                 reads=[c.dbuf["x1_h"]], writes=[bxe1], inc=16)
            P.drain("sync")

            for a in range(4):
                for qgt in range(8):
                    pp = pa1[qgt % 2]
                    for eh in range(2):
                        P.op("tensor", lambda E, o=pp[:, :],
                             l=c.PaB[:, b * 8192 + a * 2048 + eh * 1024 + qgt * 128:
                                     b * 8192 + a * 2048 + eh * 1024 + (qgt + 1) * 128],
                             r=xe[:, 2 * a + eh, 1:513],
                             s=(eh == 0), p=(eh == 1): E.matmul(o, l, r, start=s, stop=p),
                             reads=[c.bPa, bxe], writes=[bpa1[qgt % 2]])
                    P.op("scalar", lambda E, o=attn_a[:, qgt * 512:(qgt + 1) * 512],
                         i=pa1[qgt % 2][:, :]: E.copy(o, i),
                         reads=[bpa1[qgt % 2]], writes=[batn])
                for gt in range(2):
                    pp = po[gt % 2]
                    for qgt in range(8):
                        P.op("tensor", lambda E, o=pp[:, :],
                             l=c.combTb[:, (a * 8 + qgt) * 256 + gt * 128:
                                        (a * 8 + qgt) * 256 + (gt + 1) * 128],
                             r=attn_a[:, qgt * 512:(qgt + 1) * 512],
                             s=(qgt == 0), p=(qgt == 7): E.matmul(o, l, r, start=s, stop=p),
                             reads=[c.bcomb, batn], writes=[bpo[gt % 2]])
                    P.op("scalar", lambda E, o=s1c[:, (2 * a + gt) * 512:(2 * a + gt + 1) * 512],
                         i=po[gt % 2][:, :]: E.copy(o, i), reads=[bpo[gt % 2]], writes=[bs1c])

            for ft in range(8):
                P.op("vector", lambda E, o=t1r, i=xe[:, ft, 1:513], sc=c.mur[:, ft:ft + 1]:
                     E.tensor_scalar(o, i, sc, None, ALU.mult),
                     reads=[bxe, c.bmu], writes=[bt1r])
                P.op("vector", lambda E, o=xr[:, ft * 512:(ft + 1) * 512], i=xe[:, ft, 0:512],
                     sc=c.omur[:, ft:ft + 1], j=t1r:
                     E.scalar_tensor_tensor(o, i, sc, j, ALU.mult, ALU.add),
                     reads=[bxe, c.bmu, bt1r], writes=[bxr])
                P.op("vector", lambda E, o=t1r, i=xe[:, ft, 1:513], sc=c.muk[:, ft:ft + 1]:
                     E.tensor_scalar(o, i, sc, None, ALU.mult),
                     reads=[bxe, c.bmu], writes=[bt1r])
                P.op("vector", lambda E, o=xk[:, ft * 512:(ft + 1) * 512], i=xe[:, ft, 0:512],
                     sc=c.omuk[:, ft:ft + 1], j=t1r:
                     E.scalar_tensor_tensor(o, i, sc, j, ALU.mult, ALU.add),
                     reads=[bxe, c.bmu, bt1r], writes=[bxk])
            for dt in range(8):
                pp = pe_[dt % 2]
                for ct in range(8):
                    P.op("tensor", lambda E, o=pp[:, :],
                         l=c.wrkvb[:, 0 * 8192 + ct * 1024 + dt * 128: 0 * 8192 + ct * 1024 + (dt + 1) * 128],
                         r=xr[:, ct * 512:(ct + 1) * 512],
                         s=(ct == 0), p=(ct == 7): E.matmul(o, l, r, start=s, stop=p),
                         reads=[c.bwrkv, bxr], writes=[bpe[dt % 2]])
                P.op("scalar", lambda E, o=rsb[:, dt * 512:(dt + 1) * 512], i=pe_[dt % 2][:, :]:
                     E.activation(o, i, AF.Sigmoid), reads=[bpe[dt % 2]], writes=[brsb])
            for dt in range(8):
                pp = pe_[dt % 2]
                for ct in range(8):
                    P.op("tensor", lambda E, o=pp[:, :],
                         l=c.wrkvb[:, 1 * 8192 + ct * 1024 + dt * 128: 1 * 8192 + ct * 1024 + (dt + 1) * 128],
                         r=xk[:, ct * 512:(ct + 1) * 512],
                         s=(ct == 0), p=(ct == 7): E.matmul(o, l, r, start=s, stop=p),
                         reads=[c.bwrkv, bxk], writes=[bpe[dt % 2]])
                P.op("vector", lambda E, o=tmpk, i=pe_[dt % 2][:, :]:
                     E.tensor_scalar_max(o, i, 0.0), reads=[bpe[dt % 2]], writes=[btmpk])
                P.op("scalar", lambda E, o=kb[:, dt * 512:(dt + 1) * 512], i=tmpk:
                     E.activation(o, i, AF.Square), reads=[btmpk], writes=[bkb])
            for dt in range(8):
                pp = pe_[dt % 2]
                par = dt % 2
                for ct in range(8):
                    P.op("tensor", lambda E, o=pp[:, :],
                         l=c.wrkvb[:, 2 * 8192 + ct * 1024 + dt * 128: 2 * 8192 + ct * 1024 + (dt + 1) * 128],
                         r=kb[:, ct * 512:(ct + 1) * 512],
                         s=(ct == 0), p=(ct == 7): E.matmul(o, l, r, start=s, stop=p),
                         reads=[c.bwrkv, bkb], writes=[bpe[dt % 2]])
                P.op("vector", lambda E, o=fo[par], i=rsb[:, dt * 512:(dt + 1) * 512],
                     j=pe_[dt % 2][:, :]: E.tensor_mul(o, i, j),
                     reads=[brsb, bpe[dt % 2]], writes=[bfo[par]])
                P.op("scalar", lambda E, o=g1s, i=fo[par]: E.activation(o, i, AF.Silu),
                     reads=[bfo[par]], writes=[bg1s])
                P.op("vector", lambda E, o=fo[par], i=g1s, j=s1c[:, dt * 512:(dt + 1) * 512]:
                     E.tensor_mul(o, i, j), reads=[bg1s, bs1c], writes=[bfo[par]])
                P.op("vector", lambda E, o=oo[par], i=fo[par], j=xe1[:, dt, :]:
                     E.tensor_add(o, i, j), reads=[bfo[par], bxe1], writes=[boo[par]])
                dst = t["out_fm"][b, dt * 128:(dt + 1) * 128, off:off + 512]
                P.op("sync", lambda E, o=dst, i=oo[par]: E.dma_start(out=o, in_=i),
                     reads=[boo[par]], writes=[c.dbuf["out_fm"]], inc=16)


def build_launch(phases):
    nc = bass.Bass("TRN2", target_bir_lowering=False, debug=False)
    t = declare_io(nc, phases)
    P = Prog()
    c = Ctx()
    c.nc, c.t, c.P = nc, t, P
    c._bufs = {}
    stack = contextlib.ExitStack()

    def SB(nm, shp, dt):
        return stack.enter_context(nc.sbuf_tensor(nm, shp, dt)).ap()

    def PS(nm, shp, dt=F32):
        return stack.enter_context(nc.psum_tensor(nm, shp, dt)).ap()

    c.dyn = Carver(SB("dyn", [128, DYN_BYTES // 4], F32))
    c.identf = SB("identf", [128, 128], F32)
    c.identb = SB("identb", [128, 128], BF16)
    c.ones_col = SB("ones_col", [128, 1], F32)
    c.ones_row = SB("ones_row", [1, 128], F32)
    c.epsA = SB("epsA", [128, 1], F32)
    if "A" in phases:
        c.gwT = SB("gwT_s", [128, 512], F32)
        c.gbias = SB("gbias_s", [128, 64], F32)
        c.ncol_row = SB("ncol_row", [1, 64], F32)
        c.ncol_c = SB("ncol_c", [128, 1], F32)
        c.ecap_col = SB("ecap_s", [64, 1], F32)
    if "B" in phases:
        c.wqkvb = SB("wqkvb", [128, 3 * 512], BF16)
    if ("D" in phases) or ("E" in phases):
        c.mur = SB("mur_s", [128, 8], F32)
        c.muk = SB("muk_s", [128, 8], F32)
        c.omur = SB("omur_s", [128, 8], F32)
        c.omuk = SB("omuk_s", [128, 8], F32)
    c.psum = [PS(f"pbk{i}", [128, 512], F32) for i in range(8)]
    c.pbuf = [Buf() for _ in range(8)]

    sems = {e: stack.enter_context(nc.semaphore("sem_" + e)) for e in ENGS}
    c.dbuf = {nm: Buf() for nm in t}

    emit_static(c, phases)
    if "A" in phases:
        emit_phase_a(c)
        P.barrier()
        emit_phase_a2(c)
    if "B" in phases:
        P.barrier()
        emit_phase_b(c)
    if "C" in phases:
        toks = [(e, P.cnt[e]) for e in ENGS if P.cnt[e] > 0]
        P.barrier(skip=("gpsimd", "sync"))
        emit_phase_c(c, entry_toks=toks, with_d_ln=(("D" in phases) and not globals().get("NO_DLN_ILV", False)))
    if "D" in phases:
        P.barrier()
        emit_phase_d(c)
    if "E" in phases:
        P.barrier()
        emit_phase_e(c)

    final = P.cnt["sync"]

    def mk(e, extra_wait=None):
        def body(engobj):
            water = {pe: 0 for pe in ENGS}
            for item in P.items[e]:
                if item[0] == "DRAIN":
                    v = item[1]
                    if v > water[e]:
                        engobj.wait_ge(sems[e], v)
                        water[e] = v
                    continue
                fd, fn, inc = item
                for pe, v in sorted(fd.items()):
                    if v > water[pe]:
                        engobj.wait_ge(sems[pe], v)
                        water[pe] = v
                inst = fn(engobj)
                inst.then_inc(sems[e], inc)
            if extra_wait is not None:
                engobj.wait_ge(sems[e], extra_wait)
        return body

    with nc.Block() as block:
        block.sync(mk("sync", extra_wait=final))
        block.gpsimd(mk("gpsimd"))
        block.tensor(mk("tensor"))
        block.vector(mk("vector"))
        block.scalar(mk("scalar"))
    stack.close()
    return nc


_NC_CACHE = {}


def _get_nc(phases):
    if phases not in _NC_CACHE:
        _NC_CACHE[phases] = build_launch(phases)
    return _NC_CACHE[phases]


def kernel(**inputs):
    h = host_prep(inputs)
    x_all = np.asarray(inputs["L0_attn_x"]).reshape(-1, D)
    per_core = []
    for core in range(NCORES):
        m = {"x_fm": np.ascontiguousarray(h["x_fm"][core * BPC:(core + 1) * BPC]),
             "x_tm": np.ascontiguousarray(x_all[core * TOK:(core + 1) * TOK])}
        per_core.append(m)

    common_ab = {k: h[k] for k in ["gwT", "gbias", "wqT", "wkT", "wvT", "ident", "ecap"]}
    common_c = {k: h[k] for k in ["moeblob", "ident"]}
    common_de = {k: h[k] for k in ["qwT", "combT", "wrT", "wk2T", "wv2T", "mur", "muk", "ident"]}

    if SPLIT_MODE:
        nc1 = _get_nc("AB")
        maps1 = [dict(common_ab, x_fm=per_core[i]["x_fm"]) for i in range(NCORES)]
        r1 = run_bass_kernel_spmd(nc1, maps1, core_ids=list(range(NCORES))).results

        nc2 = _get_nc("C")
        maps2 = [dict(common_c, x_fm=per_core[i]["x_fm"], x_tm=per_core[i]["x_tm"],
                      xn_h=np.asarray(r1[i]["xn_h"]), rwT_h=np.asarray(r1[i]["rwT_h"]),
                      sdup_h=np.asarray(r1[i]["sdup_h"]), xn_tm_h=np.asarray(r1[i]["xn_tm_h"]),
                      idx_h=np.asarray(r1[i]["idx_h"]), rw8_h=np.asarray(r1[i]["rw8_h"]))
                 for i in range(NCORES)]
        r2 = run_bass_kernel_spmd(nc2, maps2, core_ids=list(range(NCORES))).results

        nc3 = _get_nc("DE")
        maps3 = [dict(common_de, x1_h=r2[i]["x1_h"]) for i in range(NCORES)]
        r3 = run_bass_kernel_spmd(nc3, maps3, core_ids=list(range(NCORES))).results
        outs = [r3[i]["out_fm"] for i in range(NCORES)]
    else:
        nc = _get_nc("ABCDE")
        maps = [dict(common_ab) for _ in range(NCORES)]
        for i in range(NCORES):
            maps[i].update(common_c)
            maps[i].update(common_de)
            maps[i]["x_fm"] = per_core[i]["x_fm"]
            maps[i]["x_tm"] = per_core[i]["x_tm"]
        rr = run_bass_kernel_spmd(nc, maps, core_ids=list(range(NCORES))).results
        outs = [rr[i]["out_fm"] for i in range(NCORES)]

    out = np.concatenate([o.reshape(BPC, D, S) for o in outs], axis=0)
    return np.ascontiguousarray(out.transpose(0, 2, 1)).astype(np.float32)


# version 11 (final)
# speedup vs baseline: 1.9034x; 1.0234x over previous
import contextlib
import numpy as np
import ml_dtypes
import concourse.bass as bass
import concourse.mybir as mybir
from concourse.bass_utils import run_bass_kernel_spmd
from concourse.bass_core_rust import IndirectOffsetOnAxis

F32 = mybir.dt.float32
BF16 = mybir.dt.bfloat16
AF = mybir.ActivationFunctionType
ALU = mybir.AluOpType
AX = mybir.AxisListType
BF = ml_dtypes.bfloat16

NCORES = 8
S = 2048
D = 1024
BPC = 2
TOK = BPC * S
NE = 66
IPAD = 768
SEC = 8 * IPAD
BLOBW = 3 * SEC
CH = 512
NCH = TOK // CH
NGRP = 4
GTOK = TOK // NGRP
GCH = GTOK // CH
KB = 1024
DYN_BYTES = 200 * KB
CAP = 1536
NSLOT = 64 * CAP
TRASH = NSLOT
NROW_CX = NSLOT + 128
NROW_CO = NSLOT + 128
BIGF = 1.0e7

SPLIT_MODE = False
DEBUG_EXPORT = False

ENGS = ("sync", "gpsimd", "tensor", "vector", "scalar")


class Buf:
    __slots__ = ("w", "r", "small")

    def __init__(self, small=False):
        self.w = None
        self.r = []
        self.small = small


class Prog:
    def __init__(self):
        self.items = {e: [] for e in ENGS}
        self.cnt = {e: 0 for e in ENGS}
        self.pend = {e: [] for e in ENGS}
        self.drains = {e: [] for e in ENGS}

    def drain(self, q):
        if self.cnt[q] > 0 and (not self.drains[q] or self.drains[q][-1] != self.cnt[q]):
            self.items[q].append(("DRAIN", self.cnt[q], 0))
            self.drains[q].append(self.cnt[q])

    def barrier(self, skip=()):
        for q in ("sync", "gpsimd"):
            if q not in skip:
                self.drain(q)
        toks = [(e, self.cnt[e]) for e in ENGS if self.cnt[e] > 0]
        for e in ENGS:
            if e in skip:
                continue
            self.pend[e] = self.pend[e] + [(tk, True) for tk in toks]

    def op(self, e, fn, reads=(), writes=(), extra=(), inc=1):
        deps = []
        for b in reads:
            if b.w is not None:
                deps.append((b.w, b.small))
        for b in writes:
            for tk in b.r:
                deps.append((tk, b.small))
            if b.w is not None:
                deps.append((b.w, b.small))
        for tk in extra:
            deps.append((tk, True))
        deps += self.pend[e]
        self.pend[e] = []
        self.cnt[e] += inc
        tok = (e, self.cnt[e])
        fd = {}
        for (pe, v), force in deps:
            if pe == e and (not force or v >= tok[1]):
                continue
            if pe in ("sync", "gpsimd") and self.drains[pe]:
                import bisect
                i = bisect.bisect_left(self.drains[pe], v)
                if i < len(self.drains[pe]):
                    v = self.drains[pe][i]
            if v > fd.get(pe, 0):
                fd[pe] = v
        self.items[e].append((fd, fn, inc))
        for b in reads:
            b.r.append(tok)
        for b in writes:
            b.r = []
            b.w = tok
        return tok


def _tile_kxm(w):
    K, M = w.shape
    return np.ascontiguousarray(
        w.reshape(K // 128, 128, M).transpose(1, 0, 2).reshape(128, (K // 128) * M))


def host_prep(z):
    h = {}
    x = np.asarray(z["L0_attn_x"])
    h["x_fm"] = np.ascontiguousarray(x.transpose(0, 2, 1))

    for nm, key in [("wqT", "L0_attn_Wq"), ("wkT", "L0_attn_Wk"), ("wvT", "L0_attn_Wv"),
                    ("gwT", "L0_ffn_gate_w")]:
        h[nm] = _tile_kxm(np.ascontiguousarray(np.asarray(z[key]).T))
    h["gbias"] = np.ascontiguousarray(
        np.tile(np.asarray(z["L0_ffn_gate_bias"])[None, :], (128, 1)).astype(np.float32))

    blob = np.zeros((NE, 128, BLOBW), dtype=np.float32)
    Wg, Wu, Wd = np.asarray(z["L0_ffn_Wg"]), np.asarray(z["L0_ffn_Wu"]), np.asarray(z["L0_ffn_Wd"])
    gt = Wg.transpose(0, 2, 1).reshape(64, 8, 128, 704).transpose(0, 2, 1, 3)
    blob[:64, :, :SEC].reshape(64, 128, 8, IPAD)[:, :, :, :704] = gt
    ut = Wu.transpose(0, 2, 1).reshape(64, 8, 128, 704).transpose(0, 2, 1, 3)
    blob[:64, :, SEC:2 * SEC].reshape(64, 128, 8, IPAD)[:, :, :, :704] = ut
    dt_ = np.zeros((64, IPAD, 1024), dtype=np.float32)
    dt_[:, :704, :] = Wd.transpose(0, 2, 1)
    blob[:64, :, 2 * SEC:] = dt_.reshape(64, 6, 128, 1024).transpose(0, 2, 1, 3).reshape(64, 128, SEC)
    Sg, Su, Sd = np.asarray(z["L0_ffn_Sg"]), np.asarray(z["L0_ffn_Su"]), np.asarray(z["L0_ffn_Sd"])
    for hh in range(2):
        e = 64 + hh
        sg = Sg[hh * 704:(hh + 1) * 704].T
        su = Su[hh * 704:(hh + 1) * 704].T
        sd = Sd[:, hh * 704:(hh + 1) * 704].T
        blob[e, :, :SEC].reshape(128, 8, IPAD)[:, :, :704] = sg.reshape(8, 128, 704).transpose(1, 0, 2)
        blob[e, :, SEC:2 * SEC].reshape(128, 8, IPAD)[:, :, :704] = su.reshape(8, 128, 704).transpose(1, 0, 2)
        sdp = np.zeros((IPAD, 1024), dtype=np.float32)
        sdp[:704] = sd
        blob[e, :, 2 * SEC:] = sdp.reshape(6, 128, 1024).transpose(1, 0, 2).reshape(128, SEC)
    h["moeblob"] = blob

    basis = np.zeros((64, 64 * 64), dtype=BF)
    for e in range(64):
        basis[e, e * 64:(e + 1) * 64] = 1
    h["basis64"] = basis
    i2 = np.zeros((64, 128), dtype=BF)
    for k in range(64):
        i2[k, k] = 1
        i2[k, 64 + k] = 1
    h["i2t"] = i2
    h["ident"] = np.eye(128, dtype=np.float32)

    q = np.asarray(z["L1_attn_queries"])
    h["qwT"] = np.ascontiguousarray(
        q.reshape(4, 256, 4, 2, 128).transpose(4, 0, 2, 3, 1).reshape(128, 8192))
    cb = np.asarray(z["L1_attn_combiners"])
    h["combT"] = np.ascontiguousarray(
        cb.reshape(4, 8, 128, 256).transpose(2, 0, 1, 3).reshape(128, 8192))
    for nm, key in [("wrT", "L1_ffn_W_r"), ("wk2T", "L1_ffn_W_k"), ("wv2T", "L1_ffn_W_v")]:
        w = np.asarray(z[key])
        h[nm] = np.ascontiguousarray(
            w.reshape(8, 128, 8, 128).transpose(3, 2, 0, 1).reshape(128, 8192))
    h["ecap"] = (np.arange(64, dtype=np.float32) * CAP).reshape(64, 1)
    h["mur"] = np.ascontiguousarray(np.asarray(z["L1_ffn_mu_r"]).reshape(8, 128).T)
    h["muk"] = np.ascontiguousarray(np.asarray(z["L1_ffn_mu_k"]).reshape(8, 128).T)
    return h


class Carver:
    def __init__(self, ap_f32):
        self.f = ap_f32
        self.b = ap_f32.bitcast(BF16)

    def f32(self, off, n):
        return self.f[:, off // 4: off // 4 + n]

    def f32v(self, off, d0, d1):
        return self.f[:, off // 4: off // 4 + d0 * d1].rearrange("p (a b) -> p a b", a=d0)

    def bf16(self, off, n):
        return self.b[:, off // 2: off // 2 + n]

    def bf16v(self, off, d0, d1):
        return self.b[:, off // 2: off // 2 + d0 * d1].rearrange("p (a b) -> p a b", a=d0)


class Ctx:
    def B(self, name, small=False):
        if name not in self._bufs:
            self._bufs[name] = Buf(small=small)
        return self._bufs[name]


def declare_io(nc, phases):
    t = {}
    fused = not SPLIT_MODE

    def dram(name, shape, dtype, kind):
        t[name] = nc.dram_tensor(name, shape, dtype, kind=kind).ap()

    has = {p: (p in phases) for p in "ABCDE"}
    if has["A"] or has["C"]:
        dram("x_fm", [BPC, D, S], F32, "ExternalInput")
    if has["A"]:
        dram("gwT", [128, 512], F32, "ExternalInput")
        dram("gbias", [128, 64], F32, "ExternalInput")
    if has["B"]:
        dram("wqT", [128, 512], F32, "ExternalInput")
        dram("wkT", [128, 512], F32, "ExternalInput")
        dram("wvT", [128, 512], F32, "ExternalInput")
    dram("ident", [128, 128], F32, "ExternalInput")
    if has["C"]:
        dram("moeblob", [NE, 128, BLOBW], F32, "ExternalInput")
        dram("x_tm", [TOK, D], F32, "ExternalInput")
        dram("compact_x", [NROW_CX, D], BF16, "ExternalOutput")
        dram("compact_out", [NROW_CO, D], BF16, "ExternalOutput")
        dram("shared_h", [2 * TOK, D], BF16, "Internal")
    if has["A"]:
        dram("ecap", [64, 1], F32, "ExternalInput")
    if has["D"] or has["E"]:
        dram("qwT", [128, 8192], F32, "ExternalInput")
        dram("combT", [128, 8192], F32, "ExternalInput")
        dram("wrT", [128, 8192], F32, "ExternalInput")
        dram("wk2T", [128, 8192], F32, "ExternalInput")
        dram("wv2T", [128, 8192], F32, "ExternalInput")
        dram("mur", [128, 8], F32, "ExternalInput")
        dram("muk", [128, 8], F32, "ExternalInput")

    def bridge(name, shape, dtype, produced, consumed):
        if produced and consumed:
            dram(name, shape, dtype, "ExternalOutput" if DEBUG_EXPORT else "Internal")
        elif produced:
            dram(name, shape, dtype, "ExternalOutput")
        elif consumed:
            dram(name, shape, dtype, "ExternalInput")

    if has["A"]:
        dram("xn_h", [128, 8, TOK], BF16, "ExternalOutput")
        dram("xn_tm_h", [TOK, D], BF16, "ExternalOutput")
        dram("idx_h", [TOK, 8], mybir.dt.int32, "ExternalOutput")
        dram("rw8_h", [TOK, 8], F32, "ExternalOutput")
    else:
        bridge("xn_h", [128, 8, TOK], BF16, False, has["B"] or has["C"])
        bridge("xn_tm_h", [TOK, D], BF16, False, has["C"])
        bridge("idx_h", [TOK, 8], mybir.dt.int32, False, has["C"])
        bridge("rw8_h", [TOK, 8], F32, False, has["C"])
    bridge("rwT_h", [64, TOK], BF16, has["A"], has["C"])
    bridge("sdup_h", [128, TOK], F32, has["B"], has["C"])
    bridge("x1_h", [BPC, D, S], F32, has["C"], has["D"] or has["E"])
    if has["D"] or has["E"]:
        bridge("xn1_h", [128, 8, TOK], BF16, has["D"], has["E"])
    if has["E"]:
        dram("out_fm", [BPC, D, S], F32, "ExternalOutput")
    return t


def emit_static(c, phases):
    P, t = c.P, c.t
    c.bident = c.B("ident")
    c.bidentb = c.B("identb")
    c.bones = c.B("ones", small=True)
    c.beps = c.B("eps", small=True)
    P.op("sync", lambda E, o=c.identf, i=t["ident"]: E.dma_start(out=o, in_=i),
         writes=[c.bident], inc=16)
    P.op("scalar", lambda E, o=c.identb, i=c.identf: E.copy(o, i),
         reads=[c.bident], writes=[c.bidentb])
    P.op("vector", lambda E, o=c.ones_col: E.memset(o, 1.0), writes=[c.bones])
    P.op("vector", lambda E, o=c.ones_row: E.memset(o, 1.0), writes=[c.bones])
    P.op("vector", lambda E, o=c.epsA: E.memset(o, 1e-5), writes=[c.beps])

    if "A" in phases:
        c.bgw = c.B("gw")
        c.bgbias = c.B("gbias")
        c.bncol = c.B("ncol", small=True)
        P.op("sync", lambda E, o=c.gwT, i=t["gwT"]: E.dma_start(out=o, in_=i), writes=[c.bgw], inc=16)
        P.op("sync", lambda E, o=c.gbias, i=t["gbias"]: E.dma_start(out=o, in_=i), writes=[c.bgbias], inc=16)
        pcs = c.psum[0]
        bcs = c.pbuf[0]
        for et in range(8):
            P.op("tensor", lambda E, o=pcs[0:64, 0:1], l=c.gwT[:, et * 64:(et + 1) * 64],
                 r=c.ones_col, s=(et == 0), p=(et == 7): E.matmul(o, l, r, start=s, stop=p),
                 reads=[c.bgw, c.bones], writes=[bcs])
        bnc = c.B("ncol_c", small=True)
        P.op("vector", lambda E, o=c.ncol_c[0:64, 0:1], i=pcs[0:64, 0:1]:
             E.tensor_scalar(o, i, -1.0, None, ALU.mult), reads=[bcs], writes=[bnc])
        P.op("tensor", lambda E, o=c.psum[1][0:1, 0:64], i=c.ncol_c[0:64, 0:1],
             idn=c.identf[0:64, 0:64]: E.transpose(o, i, idn),
             reads=[bnc, c.bident], writes=[bcs])
        P.op("vector", lambda E, o=c.ncol_row, i=c.psum[1][0:1, 0:64]:
             E.tensor_copy(o, i), reads=[bcs], writes=[c.bncol])
        c.becap = c.B("ecap", small=True)
        P.op("sync", lambda E, o=c.ecap_col, i=t["ecap"]: E.dma_start(out=o, in_=i),
             writes=[c.becap], inc=16)

    if "B" in phases:
        c.bwqkv = c.B("wqkv")
        for i, nm in enumerate(["wqT", "wkT", "wvT"]):
            P.op("gpsimd", lambda E, o=c.wqkvb[:, i * 512:(i + 1) * 512], src=t[nm]:
                 E.dma_start(out=o, in_=src), writes=[c.bwqkv], inc=16)

    if ("D" in phases) or ("E" in phases):
        c.bmu = c.B("mu", small=True)
        P.op("sync", lambda E, o=c.mur, i=t["mur"]: E.dma_start(out=o, in_=i), writes=[c.bmu], inc=16)
        P.op("sync", lambda E, o=c.muk, i=t["muk"]: E.dma_start(out=o, in_=i), writes=[c.bmu], inc=16)
        P.op("vector", lambda E, o=c.omur, i=c.mur: E.tensor_scalar(o, i, -1.0, 1.0, ALU.mult, ALU.add),
             reads=[c.bmu], writes=[c.bmu])
        P.op("vector", lambda E, o=c.omuk, i=c.muk: E.tensor_scalar(o, i, -1.0, 1.0, ALU.mult, ALU.add),
             reads=[c.bmu], writes=[c.bmu])


LN_O = 72 * KB


def emit_ln_load(c, ci, src_dram, src_buf, O=None, alt=None):
    P, dyn = c.P, c.dyn
    if O is None:
        O = LN_O
    b = ci // (S // CH)
    off = (ci % (S // CH)) * CH
    par = ci % 2
    base = O if par == 0 else (156 * KB if alt is None else alt)
    xld = dyn.f32v(base, 8, 512)
    bxld = c.B(f"ln_xld{par}")
    src = src_dram[b, :, off:off + CH].rearrange("(n p) m -> p n m", p=128)
    P.op("sync", lambda E, o=xld, i=src: E.dma_start(out=o, in_=i),
         reads=[src_buf], writes=[bxld], inc=16)
    P.drain("sync")


def emit_ln_chunk(c, ci, src_dram, src_buf, dst_dram, dst_buf, with_routing, O=None,
                  alt=None, single=False):
    P, t, dyn = c.P, c.t, c.dyn
    b = ci // (S // CH)
    off = (ci % (S // CH)) * CH
    goff = ci * CH
    if O is None:
        O = LN_O

    par = 0 if single else (ci % 2)
    xbase = O if par == 0 else (156 * KB if alt is None else alt)
    xld = dyn.f32v(xbase, 8, 512)
    x2 = dyn.f32v(O + 16 * KB, 8, 512)
    tmp1 = dyn.f32(O + 32 * KB, 512)
    xnst = dyn.bf16v(O + 34 * KB, 8, 512)
    m_row = dyn.f32(O + 42 * KB, 512)[0:1, :]
    sd_row = dyn.f32(O + 44 * KB, 512)[0:1, :]
    rstd_row = dyn.f32(O + 46 * KB, 512)[0:1, :]
    mr_row = dyn.f32(O + 48 * KB, 512)[0:1, :]
    ex2_row = dyn.f32(O + 50 * KB, 512)[0:1, :]
    tmpn = dyn.f32(O + 52 * KB, 512)

    bxld = c.B(f"ln_xld{par}")
    bx2 = c.B("ln_x2")
    bxnst = c.B("ln_xnst")
    bstat = c.B("ln_stat", small=True)
    btmpn = c.B("ln_tmpn")
    btmp1 = c.B("ln_tmp1", small=True)
    psx, psx2, pbm, pbr = c.psum[0], c.psum[1], c.psum[2], c.psum[3]
    bpsx, bpsx2, bpbm, bpbr = c.pbuf[0], c.pbuf[1], c.pbuf[2], c.pbuf[3]

    if single:
        srcld = src_dram[b, :, off:off + CH].rearrange("(n p) m -> p n m", p=128)
        P.op("sync", lambda E, o=xld, i=srcld: E.dma_start(out=o, in_=i),
             reads=[src_buf], writes=[bxld], inc=16)
        P.drain("sync")
    P.op("scalar", lambda E, o=x2, i=xld: E.activation(o, i, AF.Square),
         reads=[bxld], writes=[bx2])
    for et in range(8):
        P.op("tensor", lambda E, o=psx[0:1, :], l=c.ones_col, r=xld[:, et, :],
             s=(et == 0), p=(et == 7): E.matmul(o, l, r, start=s, stop=p),
             reads=[c.bones, bxld], writes=[bpsx])
    for et in range(8):
        P.op("tensor", lambda E, o=psx2[0:1, :], l=c.ones_col, r=x2[:, et, :],
             s=(et == 0), p=(et == 7): E.matmul(o, l, r, start=s, stop=p),
             reads=[c.bones, bx2], writes=[bpsx2])
    P.op("vector", lambda E, o=m_row, i=psx[0:1, :]:
         E.tensor_scalar(o, i, 1.0 / D, None, ALU.mult), reads=[bpsx], writes=[bstat])
    P.op("vector", lambda E, o=ex2_row, i=psx2[0:1, :]:
         E.tensor_scalar(o, i, 1.0 / D, None, ALU.mult), reads=[bpsx2], writes=[bstat])
    P.op("vector", lambda E, o=tmp1[0:1, :], i=m_row, j=m_row:
         E.tensor_mul(o, i, j), reads=[bstat], writes=[btmp1])
    P.op("vector", lambda E, o=ex2_row, i=ex2_row, j=tmp1[0:1, :]:
         E.tensor_sub(o, i, j), reads=[bstat, btmp1], writes=[bstat])
    P.op("scalar", lambda E, o=sd_row, i=ex2_row, bi=c.epsA[0:1, 0:1]:
         E.activation(o, i, AF.Sqrt, bias=bi), reads=[bstat, c.beps], writes=[bstat])
    P.op("vector", lambda E, o=rstd_row, i=sd_row: E.reciprocal(o, i),
         reads=[bstat], writes=[bstat])
    if with_routing:
        P.op("vector", lambda E, o=mr_row, i=m_row, j=rstd_row:
             E.tensor_mul(o, i, j), reads=[bstat], writes=[bstat])
    P.op("tensor", lambda E, o=pbm[:, :], l=c.ones_row, r=m_row:
         E.matmul(o, l, r, start=True, stop=True), reads=[c.bones, bstat], writes=[bpbm])
    P.op("tensor", lambda E, o=pbr[:, :], l=c.ones_row, r=rstd_row:
         E.matmul(o, l, r, start=True, stop=True), reads=[c.bones, bstat], writes=[bpbr])
    for et in range(8):
        P.op("vector", lambda E, o=tmpn, i=xld[:, et, :], j=pbm[:, :]:
             E.tensor_sub(o, i, j), reads=[bxld, bpbm], writes=[btmpn])
        P.op("vector", lambda E, o=xnst[:, et, :], i=tmpn, j=pbr[:, :]:
             E.tensor_mul(o, i, j), reads=[btmpn, bpbr], writes=[bxnst])
    dst = dst_dram[:, :, goff:goff + CH]
    wq = "gpsimd" if with_routing else "sync"
    P.op(wq, lambda E, o=dst, i=xnst: E.dma_start(out=o, in_=i),
         reads=[bxnst], writes=[dst_buf], inc=16)
    if not with_routing:
        P.drain("sync")

    if with_routing:
        xtm_st = dyn.bf16v(O + 62 * KB, 4, 1024)
        bxtmst = c.B("ln_xtmst")
        ptrxb = [c.psum[6].bitcast(BF16), c.psum[7].bitcast(BF16)]
        bptrx2 = [c.pbuf[6], c.pbuf[7]]
        for tt in range(4):
            bk = tt % 2
            for et in range(8):
                P.op("tensor", lambda E, o=ptrxb[bk][0:128, et * 128:(et + 1) * 128],
                     i=xnst[:, et, tt * 128:(tt + 1) * 128], idn=c.identb:
                     E.transpose(o, i, idn), reads=[bxnst, c.bidentb], writes=[bptrx2[bk]])
            P.op("vector", lambda E, o=xtm_st[:, tt, :], i=ptrxb[bk][0:128, 0:1024]:
                 E.tensor_copy(o, i), reads=[bptrx2[bk]], writes=[bxtmst])
        dst2 = c.t["xn_tm_h"][goff:goff + CH, :].rearrange("(tt p) cc -> p tt cc", p=128)
        P.op("gpsimd", lambda E, o=dst2, i=xtm_st: E.dma_start(out=o, in_=i),
             reads=[bxtmst], writes=[c.dbuf["xn_tm_h"]], inc=16)
        P.drain("gpsimd")

    if not with_routing:
        return

    plg, pcor = c.psum[4], c.psum[5]
    bplg, bpcor = c.pbuf[4], c.pbuf[5]
    for et in range(8):
        P.op("tensor", lambda E, o=plg[0:64, :], l=c.gwT[:, et * 64:(et + 1) * 64],
             r=xld[:, et, :], s=(et == 0), p=(et == 7): E.matmul(o, l, r, start=s, stop=p),
             reads=[c.bgw, bxld], writes=[bplg])
    P.op("tensor", lambda E, o=pcor[0:64, :], l=c.ncol_row[:, 0:64], r=mr_row:
         E.matmul(o, l, r, start=True, stop=True), reads=[c.bncol, bstat], writes=[bpcor])
    lg0 = dyn.f32(O + 54 * KB, 512)[0:64, :]
    lg1 = dyn.f32(O + 56 * KB, 512)[0:64, :]
    blg = c.B("ln_lg")
    P.op("scalar", lambda E, o=lg0, i=plg[0:64, :]: E.copy(o, i), reads=[bplg], writes=[blg])
    P.op("vector", lambda E, o=lg1, i=lg0, j=pbr[0:64, :]: E.tensor_mul(o, i, j),
         reads=[blg, bpbr], writes=[blg])
    P.op("vector", lambda E, o=lg1, i=lg1, j=pcor[0:64, :]: E.tensor_add(o, i, j),
         reads=[bpcor, blg], writes=[blg])
    ptr = c.psum[6]
    bptr = c.pbuf[6]
    lg_tm = dyn.f32v(O + 58 * KB, 4, 64)
    blgtm = c.B("ln_lgtm")
    for j in range(4):
        P.op("tensor", lambda E, o=ptr[0:128, 0:64], i=lg1[:, j * 128:(j + 1) * 128],
             idn=c.identf[0:64, 0:64]: E.transpose(o, i, idn),
             reads=[blg, c.bident], writes=[bptr])
        P.op("vector", lambda E, o=lg_tm[:, j, :], i=ptr[0:128, 0:64]:
             E.tensor_copy(o, i), reads=[bptr], writes=[blgtm])
    emit_routing(c, ci, lg_tm, blgtm, O + 71 * KB)


def emit_routing(c, ci, lg_tm, blgtm, OB):
    P, t, dyn = c.P, c.t, c.dyn
    goff = ci * CH

    scores = dyn.f32v(OB, 4, 64)
    sfc = dyn.f32v(OB + 1024, 4, 64)
    msk = dyn.f32v(OB + 2048, 4, 64)
    cur = dyn.f32v(OB + 3072, 4, 64)
    sel = dyn.f32v(OB + 4096, 4, 64)
    sel8 = dyn.f32v(OB + 5120, 4, 64)
    w_ = dyn.f32v(OB + 6144, 4, 64)
    rw = dyn.f32v(OB + 7168, 4, 64)
    m1 = dyn.f32v(OB + 8192, 4, 8)
    mtmp = dyn.f32v(OB + 8320, 4, 8)
    gs = dyn.f32v(OB + 8448, 4, 8)
    curg = dyn.f32v(OB + 8576, 4, 8)
    selg = dyn.f32v(OB + 8704, 4, 8)
    gmask = dyn.f32v(OB + 8832, 4, 8)
    mm_ = dyn.f32v(OB + 8960, 4, 1)
    den = dyn.f32v(OB + 9088, 4, 1)
    rcp = dyn.f32v(OB + 9216, 4, 1)
    rwTs = dyn.f32(OB + 9344, 512)[0:64, :]
    rwTb_st = dyn.bf16(OB + 11392, 512)[0:64, :]

    bS = c.B("route", small=True)

    def V(fn):
        P.op("vector", fn, reads=[bS], writes=[bS])

    P.op("scalar", lambda E, o=scores, i=lg_tm: E.activation(o, i, AF.Sigmoid),
         reads=[blgtm], writes=[bS])
    V(lambda E: E.tensor_add(sfc, scores, c.gbias.unsqueeze(1).broadcast_to([128, 4, 64])))
    V(lambda E: E.tensor_reduce(m1, sfc.rearrange("p a (g k) -> p a g k", k=8), AX.X, ALU.max))
    V(lambda E: E.tensor_tensor(msk.rearrange("p a (g k) -> p a g k", k=8),
                                sfc.rearrange("p a (g k) -> p a g k", k=8),
                                m1.unsqueeze(3).broadcast_to([128, 4, 8, 8]), ALU.is_ge))
    V(lambda E: E.scalar_tensor_tensor(msk, msk, -1e30, sfc, ALU.mult, ALU.add))
    V(lambda E: E.tensor_reduce(mtmp, msk.rearrange("p a (g k) -> p a g k", k=8), AX.X, ALU.max))
    V(lambda E: E.tensor_add(gs, m1, mtmp))
    V(lambda E: E.memset(gmask, 0.0))
    V(lambda E: E.tensor_copy(curg, gs))
    for _ in range(4):
        V(lambda E: E.tensor_reduce(mm_, curg, AX.X, ALU.max))
        V(lambda E: E.tensor_tensor(selg, curg, mm_.broadcast_to([128, 4, 8]), ALU.is_ge))
        V(lambda E: E.tensor_add(gmask, gmask, selg))
        V(lambda E: E.scalar_tensor_tensor(curg, selg, -1e30, curg, ALU.mult, ALU.add))
    V(lambda E: E.tensor_tensor(msk.rearrange("p a (g k) -> p a g k", k=8),
                                sfc.rearrange("p a (g k) -> p a g k", k=8),
                                gmask.unsqueeze(3).broadcast_to([128, 4, 8, 8]), ALU.mult))
    V(lambda E: E.memset(sel8, 0.0))
    V(lambda E: E.tensor_copy(cur, msk))
    for _ in range(8):
        V(lambda E: E.tensor_reduce(mm_, cur, AX.X, ALU.max))
        V(lambda E: E.tensor_tensor(sel, cur, mm_.broadcast_to([128, 4, 64]), ALU.is_ge))
        V(lambda E: E.tensor_add(sel8, sel8, sel))
        V(lambda E: E.scalar_tensor_tensor(cur, sel, -1e30, cur, ALU.mult, ALU.add))
    V(lambda E: E.tensor_mul(w_, sel8, scores))
    V(lambda E: E.tensor_reduce(den, w_, AX.X, ALU.add))
    V(lambda E: E.tensor_scalar(den, den, 1e-20, None, ALU.add))
    V(lambda E: E.reciprocal(rcp, den))
    V(lambda E: E.tensor_scalar(rcp, rcp, 2.5, None, ALU.mult))
    V(lambda E: E.tensor_tensor(rw, w_, rcp.broadcast_to([128, 4, 64]), ALU.mult))
    ptr2 = c.psum[7]
    bptr2 = c.pbuf[7]
    for j in range(4):
        P.op("tensor", lambda E, o=ptr2[0:64, 0:128], i=rw[:, j, :], idn=c.identf:
             E.transpose(o, i, idn), reads=[bS, c.bident], writes=[bptr2])
        P.op("vector", lambda E, o=rwTs[:, j * 128:(j + 1) * 128], i=ptr2[0:64, 0:128]:
             E.tensor_copy(o, i), reads=[bptr2], writes=[bS])
    V(lambda E: E.tensor_copy(rwTb_st, rwTs))
    P.op("sync", lambda E, o=t["rwT_h"][:, goff:goff + CH], i=rwTb_st:
         E.dma_start(out=o, in_=i), reads=[bS], writes=[c.dbuf["rwT_h"]], inc=16)
    P.drain("sync")


def emit_phase_a(c):
    emit_ln_load(c, 0, c.t["x_fm"], c.dbuf["x_fm"])
    for ci in range(NCH):
        if ci + 1 < NCH:
            emit_ln_load(c, ci + 1, c.t["x_fm"], c.dbuf["x_fm"])
        emit_ln_chunk(c, ci, c.t["x_fm"], c.dbuf["x_fm"], c.t["xn_h"], c.dbuf["xn_h"], True)




def emit_phase_a2(c):
    P, t, dyn = c.P, c.t, c.dyn
    O = LN_O
    rwt_s = dyn.bf16(O, TOK)[0:64, :]
    mask = dyn.f32(O + 8 * KB, TOK)[0:64, :]
    cumA = dyn.f32(O + 24 * KB, TOK)[0:64, :]
    cumB = dyn.f32(O + 40 * KB, TOK)[0:64, :]
    tgt = dyn.f32(O + 56 * KB, TOK)[0:64, :]
    ovfm = dyn.f32(O + 40 * KB, TOK)[0:64, :]
    tgt_tm = dyn.f32v(O + 8 * KB, 32, 64)
    rw_tm = dyn.f32v(O + 16 * KB, 32, 64)
    cur = dyn.f32v(O + 24 * KB, 32, 64)
    eq = dyn.f32v(O + 32 * KB, 32, 64)
    tmp_e = dyn.f32v(O + 48 * KB, 32, 64)
    SM = O + 44 * KB
    mn = dyn.f32v(SM, 32, 1)
    gbv = dyn.f32v(SM + 256, 32, 1)
    tgt8 = dyn.f32v(SM + 512, 32, 8)
    rw8 = dyn.f32v(SM + 1536, 32, 8)
    idx_i = dyn.f.bitcast(mybir.dt.int32)[:, (SM + 2560) // 4:(SM + 2560) // 4 + 256].rearrange(
        "p (a b) -> p a b", a=32)

    bS = c.B("a2", small=True)

    def V(fn):
        P.op("vector", fn, reads=[bS], writes=[bS])

    P.op("sync", lambda E, o=rwt_s, i=t["rwT_h"]: E.dma_start(out=o, in_=i),
         reads=[c.dbuf["rwT_h"]], writes=[bS], inc=16)
    P.drain("sync")
    V(lambda E: E.tensor_single_scalar(mask, rwt_s, 0.0, ALU.is_gt))
    V(lambda E: E.tensor_copy(cumA, mask))
    srcb, dstb = cumA, cumB
    sh = 1
    while sh < TOK:
        V(lambda E, s_=srcb, d_=dstb, ss=sh: E.tensor_add(d_[:, ss:], s_[:, ss:], s_[:, :TOK - ss]))
        V(lambda E, s_=srcb, d_=dstb, ss=sh: E.tensor_copy(d_[:, 0:ss], s_[:, 0:ss]))
        srcb, dstb = dstb, srcb
        sh *= 2
    cum = srcb
    V(lambda E: E.tensor_sub(tgt, cum, mask))
    V(lambda E: E.tensor_single_scalar(ovfm, tgt, float(CAP), ALU.is_ge))
    V(lambda E: E.tensor_scalar(tgt, tgt, c.ecap_col[0:64, 0:1], None, ALU.add))
    V(lambda E: E.tensor_scalar(tgt, tgt, BIGF, None, ALU.add))
    V(lambda E: E.scalar_tensor_tensor(tgt, mask, -BIGF, tgt, ALU.mult, ALU.add))
    V(lambda E: E.scalar_tensor_tensor(tgt, ovfm, BIGF, tgt, ALU.mult, ALU.add))
    V(lambda E: E.tensor_scalar_min(tgt, tgt, float(TRASH)))
    ptrf = c.psum[6]
    ptrb2 = c.psum[7].bitcast(BF16)
    bptrf, bptrb = c.pbuf[6], c.pbuf[7]
    for j in range(32):
        P.op("tensor", lambda E, o=ptrf[0:128, 0:64], i=tgt[:, j * 128:(j + 1) * 128],
             idn=c.identf[0:64, 0:64]: E.transpose(o, i, idn),
             reads=[bS, c.bident], writes=[bptrf])
        P.op("vector", lambda E, o=tgt_tm[:, j, :], i=ptrf[0:128, 0:64]:
             E.tensor_copy(o, i), reads=[bptrf], writes=[bS])
        P.op("tensor", lambda E, o=ptrb2[0:128, 0:64], i=rwt_s[:, j * 128:(j + 1) * 128],
             idn=c.identb[0:64, 0:64]: E.transpose(o, i, idn),
             reads=[bS, c.bidentb], writes=[bptrb])
        P.op("vector", lambda E, o=rw_tm[:, j, :], i=ptrb2[0:128, 0:64]:
             E.tensor_copy(o, i), reads=[bptrb], writes=[bS])
    V(lambda E: E.tensor_copy(cur, tgt_tm))
    for j in range(8):
        V(lambda E: E.tensor_reduce(mn, cur, AX.X, ALU.min))
        V(lambda E, jj=j: E.tensor_copy(tgt8[:, :, jj:jj + 1], mn))
        V(lambda E: E.tensor_tensor(eq, cur, mn.broadcast_to([128, 32, 64]), ALU.is_le))
        V(lambda E: E.tensor_mul(tmp_e, eq, rw_tm))
        V(lambda E, jj=j: E.tensor_reduce(rw8[:, :, jj:jj + 1], tmp_e, AX.X, ALU.add))
        V(lambda E: E.tensor_single_scalar(gbv, mn, float(NSLOT), ALU.is_lt))
        V(lambda E, jj=j: E.tensor_tensor(rw8[:, :, jj:jj + 1], rw8[:, :, jj:jj + 1],
                                          gbv, ALU.mult))
        V(lambda E: E.scalar_tensor_tensor(cur, eq, BIGF, cur, ALU.mult, ALU.add))
    V(lambda E: E.tensor_copy(idx_i, tgt8))
    P.op("gpsimd", lambda E, o=t["idx_h"].rearrange("(tt p) j -> p tt j", p=128), i=idx_i:
         E.dma_start(out=o, in_=i), reads=[bS], writes=[c.dbuf["idx_h"]], inc=16)
    P.op("gpsimd", lambda E, o=t["rw8_h"].rearrange("(tt p) j -> p tt j", p=128), i=rw8:
         E.dma_start(out=o, in_=i), reads=[bS], writes=[c.dbuf["rw8_h"]], inc=16)


def emit_phase_b(c):
    P, t, dyn = c.P, c.t, c.dyn
    O = LN_O
    xnb = dyn.bf16v(O, 8, S)
    scf = dyn.f32(O + 32 * KB, S)
    attn_t = [dyn.bf16(O + 40 * KB + i * 4 * KB, S) for i in range(2)]
    qTb = dyn.bf16(O + 48 * KB, S)[0:64, :]
    kTb = dyn.bf16(O + 52 * KB, S)[0:64, :]
    vTb = dyn.bf16(O + 56 * KB, S)[0:64, :]
    v_tm = dyn.bf16(O + 60 * KB, 16 * 128)
    v_sc = [dyn.bf16(O + 64 * KB + i * 256, 128) for i in range(2)]
    sdst = dyn.f32(O + 66 * KB, S)
    nmx = dyn.f32(O + 74 * KB, 16)[:, 0:1]
    rs = dyn.f32(O + 74 * KB + 64, 16)[:, 0:1]
    rcq = dyn.f32(O + 74 * KB + 128, 16)[:, 0:1]

    pq = c.psum[0]
    ptr = c.psum[1]
    ptrb = ptr.bitcast(BF16)
    psc = [c.psum[2], c.psum[3]]
    pout = [c.psum[4 + i] for i in range(4)]

    bxnb = c.B("b_xnb")
    bq, bk, bv = c.B("b_q"), c.B("b_k"), c.B("b_v")
    bpq = c.pbuf[0]
    bvtm = c.B("b_vtm")
    bptr = c.pbuf[1]
    bscf = c.B("b_scf")
    bnmx = c.B("b_nmx", small=True)
    brs = c.B("b_rs", small=True)
    batt = [c.B("b_att0"), c.B("b_att1")]
    bvsc = [c.B("b_vsc0"), c.B("b_vsc1")]
    bpsc = [c.pbuf[2], c.pbuf[3]]
    bpout = c.pbuf[4]
    bpout2 = [c.pbuf[4], c.pbuf[5], c.pbuf[6], c.pbuf[7]]
    bsd = c.B("b_sd")

    for b in range(BPC):
        P.op("sync", lambda E, o=xnb, i=t["xn_h"][:, :, b * S:(b + 1) * S]:
             E.dma_start(out=o, in_=i), reads=[c.dbuf["xn_h"]], writes=[bxnb], inc=16)
        P.drain("sync")
        for wi, (dst, bd) in enumerate([(qTb, bq), (kTb, bk), (vTb, bv)]):
            for ic in range(4):
                for et in range(8):
                    P.op("tensor", lambda E, o=pq[0:64, :],
                         l=c.wqkvb[:, wi * 512 + et * 64: wi * 512 + (et + 1) * 64],
                         r=xnb[:, et, ic * 512:(ic + 1) * 512],
                         s=(et == 0), p=(et == 7): E.matmul(o, l, r, start=s, stop=p),
                         reads=[c.bwqkv, bxnb], writes=[bpq])
                P.op("scalar", lambda E, o=dst[:, ic * 512:(ic + 1) * 512], i=pq[0:64, :]:
                     E.copy(o, i), reads=[bpq], writes=[bd])
        for kt in range(16):
            P.op("tensor", lambda E, o=ptrb[0:128, 0:64], i=vTb[:, kt * 128:(kt + 1) * 128],
                 idn=c.identb[0:64, 0:64]: E.transpose(o, i, idn),
                 reads=[bv, c.bidentb], writes=[bptr])
            P.op("vector", lambda E, o=v_tm[:, kt * 128:kt * 128 + 64], i=ptrb[0:128, 0:64]:
                 E.tensor_copy(o, i), reads=[bptr], writes=[bvtm])
            P.op("vector", lambda E, o=v_tm[:, kt * 128 + 64:(kt + 1) * 128], i=ptrb[0:128, 0:64]:
                 E.tensor_copy(o, i), reads=[bptr], writes=[bvtm])
        for kt in range(16):
            par = kt % 2
            for ic in range(4):
                P.op("tensor", lambda E, o=psc[ic % 2][:, :], l=kTb[:, kt * 128:(kt + 1) * 128],
                     r=qTb[:, ic * 512:(ic + 1) * 512]: E.matmul(o, l, r, start=True, stop=True),
                     reads=[bk, bq], writes=[bpsc[ic % 2]])
                P.op("scalar", lambda E, o=scf[:, ic * 512:(ic + 1) * 512], i=psc[ic % 2][:, :]:
                     E.copy(o, i), reads=[bpsc[ic % 2]], writes=[bscf])
            P.op("vector", lambda E, o=nmx, i=scf:
                 E.reduce_max(o, i, axis=AX.X, negate=True), reads=[bscf], writes=[bnmx])
            P.op("scalar", lambda E, o=attn_t[par], i=scf, bi=nmx, ac=rs:
                 E.activation(o, i, AF.Exp, bias=bi, accum_out=ac),
                 reads=[bscf, bnmx], writes=[batt[par], brs])
            P.op("vector", lambda E, o=rcq, i=rs: E.reciprocal(o, i),
                 reads=[brs], writes=[brs])
            P.op("vector", lambda E, o=v_sc[par], i=v_tm[:, kt * 128:(kt + 1) * 128], sc=rcq:
                 E.tensor_scalar(o, i, sc, None, ALU.mult),
                 reads=[bvtm, brs], writes=[bvsc[par]])
            for ic in range(4):
                P.op("tensor", lambda E, o=pout[ic][:, :], l=v_sc[par],
                     r=attn_t[par][:, ic * 512:(ic + 1) * 512],
                     s=(kt == 0), p=(kt == 15): E.matmul(o, l, r, start=s, stop=p),
                     reads=[bvsc[par], batt[par]], writes=[bpout2[ic]])
        for ic in range(4):
            P.op("scalar", lambda E, o=sdst[:, ic * 512:(ic + 1) * 512], i=pout[ic][:, :]:
                 E.copy(o, i), reads=[bpout2[ic]], writes=[bsd])
        P.op("sync", lambda E, o=t["sdup_h"][:, b * S:(b + 1) * S], i=sdst:
             E.dma_start(out=o, in_=i), reads=[bsd], writes=[c.dbuf["sdup_h"]], inc=16)


def emit_phase_c(c, entry_toks=(), with_d_ln=False):
    P, t, dyn = c.P, c.t, c.dyn
    wbuf = [dyn.bf16(0 + i * 36864, BLOBW) for i in range(2)]
    cx_tm = [dyn.bf16v(72 * KB, 4, 1024), dyn.bf16v(80 * KB, 4, 1024),
             dyn.bf16v(160 * KB, 4, 1024)]
    rh = [dyn.bf16v(88 * KB, 8, 512), dyn.bf16v(96 * KB, 8, 512),
          dyn.bf16v(168 * KB, 8, 512)]
    hb = [dyn.bf16(104 * KB + i * 6 * KB, 6 * 512) for i in range(2)]
    sg = [dyn.f32(116 * KB + i * 2 * KB, 512) for i in range(2)]
    co_st = [dyn.bf16v(120 * KB + i * 8 * KB, 4, 1024) for i in range(2)]
    sc_tile = [dyn.bf16(152 * KB + i * 2 * KB, 1024) for i in range(2)]
    idx_all = dyn.f.bitcast(mybir.dt.int32)[:, (192 * KB) // 4:(192 * KB) // 4 + 256].rearrange(
        "p (a b) -> p a b", a=32)
    rw8_all = dyn.f32v(193 * KB, 32, 8)
    sdup_s = dyn.f32(122 * KB, TOK)
    Gt = [dyn.bf16(72 * KB + i * 2 * KB, 1024) for i in range(4)]
    acc = [dyn.f32(80 * KB + i * 4 * KB, 1024) for i in range(2)]
    gsb = [dyn.f32(88 * KB + i * 4 * KB, 1024) for i in range(2)]
    x_tile = [dyn.f32(96 * KB + i * 4 * KB, 1024) for i in range(2)]
    x1f_st = [dyn.f32v(104 * KB + i * 4 * KB, 8, 128) for i in range(2)]
    s_small = [dyn.f32(112 * KB + i * 512, 64) for i in range(2)]
    sht = [dyn.bf16(113 * KB + i * 2 * KB, 1024) for i in range(4)]
    idxt2 = [dyn.f.bitcast(mybir.dt.int32)[:, (121 * KB + i * 64) // 4:(121 * KB + i * 64) // 4 + 8]
             for i in range(2)]
    rw8t = [dyn.f32(121 * KB + 512 + i * 512, 8) for i in range(2)]

    bwbuf = [c.B("c_wb0"), c.B("c_wb1")]
    bcx = [c.B("c_cx0"), c.B("c_cx1"), c.B("c_cx2")]
    brh = [c.B("c_rh0"), c.B("c_rh1"), c.B("c_rh2")]
    bhb = [c.B("c_hb0"), c.B("c_hb1")]
    bsg = [c.B("c_sg0"), c.B("c_sg1")]
    bco = [c.B("c_co0"), c.B("c_co1")]
    bsc = [c.B("c_sc0"), c.B("c_sc1")]
    bidx_all = c.B("c_idxall")
    brw8_all = c.B("c_rw8all")
    bsdup = c.B("c_sdup")
    bG = [c.B(f"c_G{i}") for i in range(4)]
    bacc = [c.B("c_acc0"), c.B("c_acc1")]
    bgs = [c.B("c_gs0"), c.B("c_gs1")]
    bxt = [c.B("c_xt0"), c.B("c_xt1")]
    bx1f = [c.B("c_x1f0"), c.B("c_x1f1")]
    bss = [c.B("c_ss0"), c.B("c_ss1")]
    bsh = [c.B(f"c_sh{i}") for i in range(4)]
    bidxt2 = [c.B("c_idxt2a"), c.B("c_idxt2b")]
    brw8t = [c.B("c_rw8t0"), c.B("c_rw8t1")]

    pag = [c.psum[0], c.psum[1]]
    pau = [c.psum[2], c.psum[3]]
    pdn = [c.psum[4], c.psum[5]]
    ptr_ = c.psum[6]
    ptrb_ = ptr_.bitcast(BF16)
    ptr2_ = c.psum[7]
    bpag = [c.pbuf[0], c.pbuf[1]]
    bpau = [c.pbuf[2], c.pbuf[3]]
    bpdn = [c.pbuf[4], c.pbuf[5]]
    bptr = c.pbuf[6]
    bptr2 = c.pbuf[7]

    worder = [64, 65] + list(range(64))
    wseq = {e: i for i, e in enumerate(worder)}

    def wdma(e):
        P.op("gpsimd", lambda E, o=wbuf[wseq[e] % 2], i=t["moeblob"][e]:
             E.dma_start(out=o, in_=i), writes=[bwbuf[wseq[e] % 2]], inc=16)

    wdma(64)
    wdma(65)
    P.drain("gpsimd")

    P.op("gpsimd", lambda E, o=idx_all, i=t["idx_h"].rearrange("(tt p) j -> p tt j", p=128):
         E.dma_start(out=o, in_=i), reads=[c.dbuf["idx_h"]], writes=[bidx_all], inc=16)
    P.op("gpsimd", lambda E, o=rw8_all, i=t["rw8_h"].rearrange("(tt p) j -> p tt j", p=128):
         E.dma_start(out=o, in_=i), reads=[c.dbuf["rw8_h"]], writes=[brw8_all], inc=16)
    P.drain("gpsimd")
    for tt in range(32):
        par = tt % 2
        P.op("gpsimd", lambda E, o=sc_tile[par],
             i=t["xn_tm_h"][tt * 128:(tt + 1) * 128, :]:
             E.dma_start(out=o, in_=i), reads=[c.dbuf["xn_tm_h"]], writes=[bsc[par]], inc=16)
        P.drain("gpsimd")
        for j in range(8):
            P.op("gpsimd", lambda E, o=t["compact_x"], off=idx_all[:, tt, j:j + 1], i=sc_tile[par]:
                 E.indirect_dma_start(out=o, out_offset=IndirectOffsetOnAxis(off, 0),
                                      in_=i, in_offset=None),
                 reads=[bsc[par], bidx_all], writes=[c.dbuf["compact_x"]], inc=16)
        P.drain("gpsimd")

    P.drain("gpsimd")

    slots = [("s", 64 + hh, c8) for hh in range(2) for c8 in range(8)] + \
            [("e", e, ch) for e in range(64) for ch in range(3)]

    first_load = [True]

    def emit_load(k):
        kind, e, ch = slots[k]
        par = k % 3
        ext = list(entry_toks) if first_load[0] else ()
        first_load[0] = False
        if kind == "e":
            base = e * CAP + ch * 512
            P.op("sync", lambda E, o=cx_tm[par],
                 i=t["compact_x"][base:base + 512, :].rearrange("(tt p) cc -> p tt cc", p=128):
                 E.dma_start(out=o, in_=i),
                 reads=[c.dbuf["compact_x"]], writes=[bcx[par]], inc=16, extra=ext)
        else:
            goff = ch * 512
            P.op("sync", lambda E, o=rh[par], i=t["xn_h"][:, :, goff:goff + 512]:
                 E.dma_start(out=o, in_=i), reads=[c.dbuf["xn_h"]], writes=[brh[par]],
                 inc=16, extra=ext)

    ptrbanks = [ptr_.bitcast(BF16), ptr2_.bitcast(BF16)]
    bptrx = [c.pbuf[6], c.pbuf[7]]

    def emit_tr(k):
        kind, e, ch = slots[k]
        if kind != "e":
            return
        par = k % 3
        for tt in range(4):
            bk = tt % 2
            pb = ptrbanks[bk]
            for ct in range(8):
                P.op("tensor", lambda E, o=pb[0:128, ct * 128:(ct + 1) * 128],
                     i=cx_tm[par][:, tt, ct * 128:(ct + 1) * 128], idn=c.identb:
                     E.transpose(o, i, idn), reads=[bcx[par], c.bidentb],
                     writes=[bptrx[bk]])
            P.op("vector", lambda E, o=rh[par][:, :, tt * 128:(tt + 1) * 128],
                 i=pb[0:128, 0:1024].rearrange("p (a b) -> p a b", a=8):
                 E.tensor_copy(o, i), reads=[bptrx[bk]], writes=[brh[par]])

    def emit_up(k):
        kind, e, ch = slots[k]
        par = k % 3
        wpar = wseq[e] % 2
        for it in range(6):
            ap_ = it % 2
            for et in range(8):
                P.op("tensor", lambda E, o=pag[ap_][:, :],
                     l=wbuf[wpar][:, et * IPAD + it * 128: et * IPAD + (it + 1) * 128],
                     r=rh[par][:, et, :], s=(et == 0), p=(et == 7):
                     E.matmul(o, l, r, start=s, stop=p),
                     reads=[bwbuf[wpar], brh[par]], writes=[bpag[ap_]])
            for et in range(8):
                P.op("tensor", lambda E, o=pau[ap_][:, :],
                     l=wbuf[wpar][:, SEC + et * IPAD + it * 128: SEC + et * IPAD + (it + 1) * 128],
                     r=rh[par][:, et, :], s=(et == 0), p=(et == 7):
                     E.matmul(o, l, r, start=s, stop=p),
                     reads=[bwbuf[wpar], brh[par]], writes=[bpau[ap_]])
            P.op("scalar", lambda E, o=sg[ap_], i=pag[ap_][:, :]:
                 E.activation(o, i, AF.Silu), reads=[bpag[ap_]], writes=[bsg[ap_]])
            P.op("vector", lambda E, o=hb[k % 2][:, it * 512:(it + 1) * 512], i=sg[ap_],
                 j=pau[ap_][:, :]: E.tensor_mul(o, i, j),
                 reads=[bsg[ap_], bpau[ap_]], writes=[bhb[k % 2]])

    def emit_down(k):
        kind, e, ch = slots[k]
        par = k % 2
        wpar = wseq[e] % 2
        for mt in range(4):
            dp = mt % 2
            for nh in range(2):
                for it in range(6):
                    P.op("tensor", lambda E, o=pdn[dp][:, :],
                         l=hb[par][:, it * 512 + mt * 128: it * 512 + (mt + 1) * 128],
                         r=wbuf[wpar][:, 2 * SEC + it * 1024 + nh * 512: 2 * SEC + it * 1024 + (nh + 1) * 512],
                         s=(it == 0), p=(it == 5): E.matmul(o, l, r, start=s, stop=p),
                         reads=[bwbuf[wpar], bhb[par]], writes=[bpdn[dp]])
                P.op("scalar", lambda E, o=co_st[par][:, mt, nh * 512:(nh + 1) * 512],
                     i=pdn[dp][:, :]: E.copy(o, i), reads=[bpdn[dp]], writes=[bco[par]])
        if kind == "e":
            base = e * CAP + ch * 512
            dst = t["compact_out"][base:base + 512, :].rearrange("(tt p) cc -> p tt cc", p=128)
            dbf = c.dbuf["compact_out"]
        else:
            hh = e - 64
            base = hh * TOK + ch * 512
            dst = t["shared_h"][base:base + 512, :].rearrange("(tt p) cc -> p tt cc", p=128)
            dbf = c.dbuf["shared_h"]
        P.op("sync", lambda E, o=dst, i=co_st[par]: E.dma_start(out=o, in_=i),
             reads=[bco[par]], writes=[dbf], inc=16)

    emit_load(0)
    emit_load(1)
    emit_tr(0)
    for k in range(len(slots)):
        if k + 2 < len(slots):
            emit_load(k + 2)
        emit_up(k)
        if k + 1 < len(slots):
            emit_tr(k + 1)
        if k > 0:
            emit_down(k - 1)
            kindp, ep, chp = slots[k - 1]
            last_ch = 2 if kindp == "e" else 7
            if chp == last_ch and wseq[ep] + 2 < NE:
                wdma(worder[wseq[ep] + 2])
    emit_down(len(slots) - 1)
    c.d_ln_done = False

    P.barrier()
    bAS = c.B("c_asm", small=False)
    P.op("sync", lambda E, o=sdup_s, i=t["sdup_h"]: E.dma_start(out=o, in_=i),
         reads=[c.dbuf["sdup_h"]], writes=[bsdup], inc=16)
    P.drain("sync")

    for tt in range(32):
        gp = tt % 2
        goff = tt * 128
        b = goff // S
        off = goff % S
        P.op("sync", lambda E, o=x_tile[gp], i=t["x_tm"][goff:goff + 128, :]:
             E.dma_start(out=o, in_=i), writes=[bxt[gp]], inc=16)
        P.op("tensor", lambda E, o=[ptr_, ptr2_][gp][0:128, 0:64],
             i=sdup_s[0:64, goff:goff + 128],
             idn=c.identf[0:64, 0:64]: E.transpose(o, i, idn),
             reads=[bsdup, c.bident], writes=[bptrx[gp]])
        P.op("vector", lambda E, o=s_small[gp], i=[ptr_, ptr2_][gp][0:128, 0:64]:
             E.tensor_copy(o, i), reads=[bptrx[gp]], writes=[bss[gp]])
        for hh in range(2):
            sj = hh + 2 * gp
            P.op("sync", lambda E, o=sht[sj],
                 i=t["shared_h"][hh * TOK + goff: hh * TOK + goff + 128, :]:
                 E.dma_start(out=o, in_=i), reads=[c.dbuf["shared_h"]], writes=[bsh[sj]], inc=16)
        P.drain("sync")
        for jp in range(4):
            for j in (2 * jp, 2 * jp + 1):
                gj = (j % 2) + 2 * gp
                P.op("gpsimd", lambda E, o=Gt[gj], off=idx_all[:, tt, j:j + 1], i=t["compact_out"]:
                     E.indirect_dma_start(out=o, out_offset=None, in_=i,
                                          in_offset=IndirectOffsetOnAxis(off, 0)),
                     reads=[c.dbuf["compact_out"], bidx_all], writes=[bG[gj]], inc=16)
            P.drain("gpsimd")
            for j in (2 * jp, 2 * jp + 1):
                gj = (j % 2) + 2 * gp
                if j == 0:
                    P.op("vector", lambda E, o=acc[gp], i=Gt[gj], sc=rw8_all[:, tt, 0:1]:
                         E.tensor_scalar(o, i, sc, None, ALU.mult),
                         reads=[bG[gj], brw8_all], writes=[bacc[gp]])
                else:
                    a_in = acc[gp] if j % 2 == 1 else gsb[gp]
                    a_out = gsb[gp] if j % 2 == 1 else acc[gp]
                    P.op("vector", lambda E, o=a_out, i=Gt[gj], sc=rw8_all[:, tt, j:j + 1], jj=a_in:
                         E.scalar_tensor_tensor(o, i, sc, jj, ALU.mult, ALU.add),
                         reads=[bG[gj], brw8_all, bacc[gp], bgs[gp]],
                         writes=[bacc[gp], bgs[gp]])
        for hh in range(2):
            sj = hh + 2 * gp
            a_in = gsb[gp] if hh == 0 else acc[gp]
            a_out = acc[gp] if hh == 0 else gsb[gp]
            P.op("vector", lambda E, o=a_out, i=a_in, j=sht[sj]: E.tensor_add(o, i, j),
                 reads=[bsh[sj], bacc[gp], bgs[gp]], writes=[bacc[gp], bgs[gp]])
        P.op("scalar", lambda E, o=acc[gp], i=gsb[gp]: E.activation(o, i, AF.Silu),
             reads=[bgs[gp]], writes=[bacc[gp]])
        P.op("vector", lambda E, o=gsb[gp].rearrange("p (r d) -> p r d", r=16),
             i=acc[gp].rearrange("p (r d) -> p r d", r=16),
             j=s_small[gp].unsqueeze(1).broadcast_to([128, 16, 64]): E.tensor_mul(o, i, j),
             reads=[bss[gp], bacc[gp]], writes=[bgs[gp]])
        P.op("vector", lambda E, o=acc[gp], i=gsb[gp], j=x_tile[gp]: E.tensor_add(o, i, j),
             reads=[bxt[gp], bgs[gp]], writes=[bacc[gp]])
        for half in range(2):
            bk = (gp + half) % 2
            pbf = [ptr_, ptr2_][bk]
            for ct4 in range(4):
                ct = half * 4 + ct4
                P.op("tensor", lambda E, o=pbf[0:128, ct4 * 128:(ct4 + 1) * 128],
                     i=acc[gp][:, ct * 128:(ct + 1) * 128], idn=c.identf:
                     E.transpose(o, i, idn), reads=[bacc[gp], c.bident], writes=[bptrx[bk]])
            P.op("scalar", lambda E, o=x1f_st[gp][:, half * 4:(half + 1) * 4, :],
                 i=pbf[0:128, 0:512].rearrange("p (a b) -> p a b", a=4):
                 E.copy(o, i), reads=[bptrx[bk]], writes=[bx1f[gp]])
        dst = t["x1_h"][b, :, off:off + 128].rearrange("(n p) m -> p n m", p=128)
        P.op("sync", lambda E, o=dst, i=x1f_st[gp]: E.dma_start(out=o, in_=i),
             reads=[bx1f[gp]], writes=[c.dbuf["x1_h"]], inc=16)
        if with_d_ln and (tt + 1) % 4 == 0:
            emit_ln_chunk(c, tt // 4, t["x1_h"], c.dbuf["x1_h"], t["xn1_h"],
                          c.dbuf["xn1_h"], False, O=138 * KB, single=True)
            c.d_ln_done = True


def emit_phase_d(c):
    P, t, dyn = c.P, c.t, c.dyn
    c.qwTb = dyn.bf16(0, 8192)
    c.PaB = dyn.bf16(16 * KB, 2 * 8192)
    c.combTb = dyn.bf16(48 * KB, 8192)
    c.wrkvb = dyn.bf16(152 * KB, 3 * 8192)
    c.bqw, c.bPa, c.bcomb, c.bwrkv = c.B("d_qw"), c.B("d_Pa"), c.B("d_comb"), c.B("d_wrkv")
    P.op("gpsimd", lambda E, o=c.qwTb, i=t["qwT"]: E.dma_start(out=o, in_=i),
         writes=[c.bqw], inc=16)
    P.op("gpsimd", lambda E, o=c.combTb, i=t["combT"]: E.dma_start(out=o, in_=i),
         writes=[c.bcomb], inc=16)

    if getattr(c, "d_ln_done", False):
        pass
    else:
        emit_ln_load(c, 0, c.t["x1_h"], c.dbuf["x1_h"])
        for ci in range(NCH):
            if ci + 1 < NCH:
                emit_ln_load(c, ci + 1, c.t["x1_h"], c.dbuf["x1_h"])
            emit_ln_chunk(c, ci, c.t["x1_h"], c.dbuf["x1_h"], c.t["xn1_h"], c.dbuf["xn1_h"], False)
    c.P.barrier()

    for i, nm in enumerate(["wrT", "wk2T", "wv2T"]):
        P.op("gpsimd", lambda E, o=c.wrkvb[:, i * 8192:(i + 1) * 8192], src=t[nm]:
             E.dma_start(out=o, in_=src), writes=[c.bwrkv], inc=16)

    MqB = dyn.bf16(64 * KB, 4 * 512)
    xnb = dyn.bf16v(68 * KB, 8, S)
    xtm = dyn.bf16(100 * KB, 16 * 1024)
    bxnb, bxtm, bMq = c.B("d_xnb"), c.B("d_xtm"), c.B("d_Mq")
    ptr = c.psum[0]
    ptrb = ptr.bitcast(BF16)
    bptr = c.pbuf[0]
    pmq = [c.psum[1 + q] for q in range(4)]
    bpmq = [c.pbuf[1 + q] for q in range(4)]
    ppp = [c.psum[5], c.psum[6]]
    bppp = [c.pbuf[5], c.pbuf[6]]

    for b in range(BPC):
        P.op("sync", lambda E, o=xnb, i=t["xn1_h"][:, :, b * S:(b + 1) * S]:
             E.dma_start(out=o, in_=i), reads=[c.dbuf["xn1_h"]], writes=[bxnb], inc=16)
        P.drain("sync")
        for ut in range(16):
            for ft in range(8):
                P.op("tensor", lambda E, o=ptrb[0:128, 0:128],
                     i=xnb[:, ft, ut * 128:(ut + 1) * 128], idn=c.identb:
                     E.transpose(o, i, idn), reads=[bxnb, c.bidentb], writes=[bptr])
                P.op("vector", lambda E,
                     o=xtm[:, ut * 1024 + ft * 128: ut * 1024 + (ft + 1) * 128],
                     i=ptrb[0:128, 0:128]: E.tensor_copy(o, i), reads=[bptr], writes=[bxtm])
        for fh in range(2):
            for ut in range(16):
                for q in range(4):
                    P.op("tensor", lambda E, o=pmq[q][:, fh * 256:(fh + 1) * 256],
                         l=xtm[:, ut * 1024 + q * 256 + fh * 128: ut * 1024 + q * 256 + (fh + 1) * 128],
                         r=xtm[:, ut * 1024 + q * 256: ut * 1024 + (q + 1) * 256],
                         s=(ut == 0), p=(ut == 15): E.matmul(o, l, r, start=s, stop=p),
                         reads=[bxtm], writes=[bpmq[q]])
            for q in range(4):
                P.op("scalar", lambda E, o=MqB[:, q * 512 + fh * 256: q * 512 + (fh + 1) * 256],
                     i=pmq[q][:, fh * 256:(fh + 1) * 256]:
                     E.copy(o, i), reads=[bpmq[q]], writes=[bMq])
        for a in range(4):
            for q in range(4):
                for eh in range(2):
                    pp = ppp[eh]
                    for fh in range(2):
                        P.op("tensor", lambda E, o=pp[:, 0:256],
                             l=c.qwTb[:, ((a * 4 + q) * 2 + fh) * 256 + eh * 128:
                                      ((a * 4 + q) * 2 + fh) * 256 + (eh + 1) * 128],
                             r=MqB[:, q * 512 + fh * 256: q * 512 + (fh + 1) * 256],
                             s=(fh == 0), p=(fh == 1): E.matmul(o, l, r, start=s, stop=p),
                             reads=[c.bqw, bMq], writes=[bppp[eh]])
                    dst = c.PaB[:, b * 8192 + a * 2048 + eh * 1024 + q * 256:
                                b * 8192 + a * 2048 + eh * 1024 + (q + 1) * 256]
                    P.op("scalar", lambda E, o=dst, i=ppp[eh][:, 0:256]: E.copy(o, i),
                         reads=[bppp[eh]], writes=[c.bPa])


def emit_phase_e(c):
    P, t, dyn = c.P, c.t, c.dyn
    rsb = dyn.f32(0, 8 * 512)
    kb = dyn.bf16(64 * KB, 8 * 512)
    attn_a = dyn.bf16(72 * KB, 8 * 512)
    s1c = dyn.f32(80 * KB, 8 * 512)
    tmpk = dyn.f32(96 * KB, 512)
    fo = [dyn.f32(98 * KB + i * 2 * KB, 512) for i in range(2)]
    oo = [dyn.f32(102 * KB + i * 2 * KB, 512) for i in range(2)]
    t1r = dyn.f32(106 * KB, 512)
    g1s = dyn.f32(108 * KB, 512)
    xe = dyn.bf16v(110 * KB, 8, 513)
    xe1 = dyn.f32v(119 * KB, 8, 512)
    xr = dyn.bf16(135 * KB, 8 * 512)
    xk = dyn.bf16(143 * KB, 8 * 512)

    brsb = c.B("e_rsb")
    bxe = c.B("e_xe")
    bxe1 = c.B("e_xe1")
    bxr, bxk, bkb = c.B("e_xr"), c.B("e_xk"), c.B("e_kb")
    batn = c.B("e_attn")
    bs1c = c.B("e_s1c")
    btmpk = c.B("e_tmpk")
    bfo = [c.B("e_fo0"), c.B("e_fo1")]
    bg1s = c.B("e_g1s")
    boo = [c.B("e_oo0"), c.B("e_oo1")]
    bt1r = c.B("e_t1r")

    pa1 = [c.psum[0], c.psum[1]]
    po = [c.psum[2], c.psum[3]]
    pe_ = [c.psum[4], c.psum[5]]
    bpa1 = [c.pbuf[0], c.pbuf[1]]
    bpo = [c.pbuf[2], c.pbuf[3]]
    bpe = [c.pbuf[4], c.pbuf[5]]

    for b in range(BPC):
        for cc in range(4):
            off = cc * 512
            if cc == 0:
                P.op("vector", lambda E, o=xe[:, :, 0:1]: E.memset(o, 0.0), writes=[bxe])
                P.op("sync", lambda E, o=xe[:, :, 1:513],
                     i=t["xn1_h"][:, :, b * S: b * S + 512]:
                     E.dma_start(out=o, in_=i), reads=[c.dbuf["xn1_h"]], writes=[bxe], inc=16)
            else:
                P.op("sync", lambda E, o=xe[:, :, 0:513],
                     i=t["xn1_h"][:, :, b * S + off - 1: b * S + off + 512]:
                     E.dma_start(out=o, in_=i), reads=[c.dbuf["xn1_h"]], writes=[bxe], inc=16)
            src1 = t["x1_h"][b, :, off:off + 512].rearrange("(n p) m -> p n m", p=128)
            P.op("sync", lambda E, o=xe1, i=src1: E.dma_start(out=o, in_=i),
                 reads=[c.dbuf["x1_h"]], writes=[bxe1], inc=16)
            P.drain("sync")

            for a in range(4):
                for qgt in range(8):
                    pp = pa1[qgt % 2]
                    for eh in range(2):
                        P.op("tensor", lambda E, o=pp[:, :],
                             l=c.PaB[:, b * 8192 + a * 2048 + eh * 1024 + qgt * 128:
                                     b * 8192 + a * 2048 + eh * 1024 + (qgt + 1) * 128],
                             r=xe[:, 2 * a + eh, 1:513],
                             s=(eh == 0), p=(eh == 1): E.matmul(o, l, r, start=s, stop=p),
                             reads=[c.bPa, bxe], writes=[bpa1[qgt % 2]])
                    P.op("scalar", lambda E, o=attn_a[:, qgt * 512:(qgt + 1) * 512],
                         i=pa1[qgt % 2][:, :]: E.copy(o, i),
                         reads=[bpa1[qgt % 2]], writes=[batn])
                for gt in range(2):
                    pp = po[gt % 2]
                    for qgt in range(8):
                        P.op("tensor", lambda E, o=pp[:, :],
                             l=c.combTb[:, (a * 8 + qgt) * 256 + gt * 128:
                                        (a * 8 + qgt) * 256 + (gt + 1) * 128],
                             r=attn_a[:, qgt * 512:(qgt + 1) * 512],
                             s=(qgt == 0), p=(qgt == 7): E.matmul(o, l, r, start=s, stop=p),
                             reads=[c.bcomb, batn], writes=[bpo[gt % 2]])
                    P.op("scalar", lambda E, o=s1c[:, (2 * a + gt) * 512:(2 * a + gt + 1) * 512],
                         i=po[gt % 2][:, :]: E.copy(o, i), reads=[bpo[gt % 2]], writes=[bs1c])

            for ft in range(8):
                P.op("vector", lambda E, o=t1r, i=xe[:, ft, 1:513], sc=c.mur[:, ft:ft + 1]:
                     E.tensor_scalar(o, i, sc, None, ALU.mult),
                     reads=[bxe, c.bmu], writes=[bt1r])
                P.op("vector", lambda E, o=xr[:, ft * 512:(ft + 1) * 512], i=xe[:, ft, 0:512],
                     sc=c.omur[:, ft:ft + 1], j=t1r:
                     E.scalar_tensor_tensor(o, i, sc, j, ALU.mult, ALU.add),
                     reads=[bxe, c.bmu, bt1r], writes=[bxr])
                P.op("vector", lambda E, o=t1r, i=xe[:, ft, 1:513], sc=c.muk[:, ft:ft + 1]:
                     E.tensor_scalar(o, i, sc, None, ALU.mult),
                     reads=[bxe, c.bmu], writes=[bt1r])
                P.op("vector", lambda E, o=xk[:, ft * 512:(ft + 1) * 512], i=xe[:, ft, 0:512],
                     sc=c.omuk[:, ft:ft + 1], j=t1r:
                     E.scalar_tensor_tensor(o, i, sc, j, ALU.mult, ALU.add),
                     reads=[bxe, c.bmu, bt1r], writes=[bxk])
            for dt in range(8):
                pp = pe_[dt % 2]
                for ct in range(8):
                    P.op("tensor", lambda E, o=pp[:, :],
                         l=c.wrkvb[:, 0 * 8192 + ct * 1024 + dt * 128: 0 * 8192 + ct * 1024 + (dt + 1) * 128],
                         r=xr[:, ct * 512:(ct + 1) * 512],
                         s=(ct == 0), p=(ct == 7): E.matmul(o, l, r, start=s, stop=p),
                         reads=[c.bwrkv, bxr], writes=[bpe[dt % 2]])
                P.op("scalar", lambda E, o=rsb[:, dt * 512:(dt + 1) * 512], i=pe_[dt % 2][:, :]:
                     E.activation(o, i, AF.Sigmoid), reads=[bpe[dt % 2]], writes=[brsb])
            for dt in range(8):
                pp = pe_[dt % 2]
                for ct in range(8):
                    P.op("tensor", lambda E, o=pp[:, :],
                         l=c.wrkvb[:, 1 * 8192 + ct * 1024 + dt * 128: 1 * 8192 + ct * 1024 + (dt + 1) * 128],
                         r=xk[:, ct * 512:(ct + 1) * 512],
                         s=(ct == 0), p=(ct == 7): E.matmul(o, l, r, start=s, stop=p),
                         reads=[c.bwrkv, bxk], writes=[bpe[dt % 2]])
                P.op("vector", lambda E, o=tmpk, i=pe_[dt % 2][:, :]:
                     E.tensor_scalar_max(o, i, 0.0), reads=[bpe[dt % 2]], writes=[btmpk])
                P.op("scalar", lambda E, o=kb[:, dt * 512:(dt + 1) * 512], i=tmpk:
                     E.activation(o, i, AF.Square), reads=[btmpk], writes=[bkb])
            for dt in range(8):
                pp = pe_[dt % 2]
                par = dt % 2
                for ct in range(8):
                    P.op("tensor", lambda E, o=pp[:, :],
                         l=c.wrkvb[:, 2 * 8192 + ct * 1024 + dt * 128: 2 * 8192 + ct * 1024 + (dt + 1) * 128],
                         r=kb[:, ct * 512:(ct + 1) * 512],
                         s=(ct == 0), p=(ct == 7): E.matmul(o, l, r, start=s, stop=p),
                         reads=[c.bwrkv, bkb], writes=[bpe[dt % 2]])
                P.op("vector", lambda E, o=fo[par], i=rsb[:, dt * 512:(dt + 1) * 512],
                     j=pe_[dt % 2][:, :]: E.tensor_mul(o, i, j),
                     reads=[brsb, bpe[dt % 2]], writes=[bfo[par]])
                P.op("scalar", lambda E, o=g1s, i=fo[par]: E.activation(o, i, AF.Silu),
                     reads=[bfo[par]], writes=[bg1s])
                P.op("vector", lambda E, o=fo[par], i=g1s, j=s1c[:, dt * 512:(dt + 1) * 512]:
                     E.tensor_mul(o, i, j), reads=[bg1s, bs1c], writes=[bfo[par]])
                P.op("vector", lambda E, o=oo[par], i=fo[par], j=xe1[:, dt, :]:
                     E.tensor_add(o, i, j), reads=[bfo[par], bxe1], writes=[boo[par]])
                dst = t["out_fm"][b, dt * 128:(dt + 1) * 128, off:off + 512]
                P.op("sync", lambda E, o=dst, i=oo[par]: E.dma_start(out=o, in_=i),
                     reads=[boo[par]], writes=[c.dbuf["out_fm"]], inc=16)


def build_launch(phases):
    nc = bass.Bass("TRN2", target_bir_lowering=False, debug=False)
    t = declare_io(nc, phases)
    P = Prog()
    c = Ctx()
    c.nc, c.t, c.P = nc, t, P
    c._bufs = {}
    stack = contextlib.ExitStack()

    def SB(nm, shp, dt):
        return stack.enter_context(nc.sbuf_tensor(nm, shp, dt)).ap()

    def PS(nm, shp, dt=F32):
        return stack.enter_context(nc.psum_tensor(nm, shp, dt)).ap()

    c.dyn = Carver(SB("dyn", [128, DYN_BYTES // 4], F32))
    c.identf = SB("identf", [128, 128], F32)
    c.identb = SB("identb", [128, 128], BF16)
    c.ones_col = SB("ones_col", [128, 1], F32)
    c.ones_row = SB("ones_row", [1, 128], F32)
    c.epsA = SB("epsA", [128, 1], F32)
    if "A" in phases:
        c.gwT = SB("gwT_s", [128, 512], F32)
        c.gbias = SB("gbias_s", [128, 64], F32)
        c.ncol_row = SB("ncol_row", [1, 64], F32)
        c.ncol_c = SB("ncol_c", [128, 1], F32)
        c.ecap_col = SB("ecap_s", [64, 1], F32)
    if "B" in phases:
        c.wqkvb = SB("wqkvb", [128, 3 * 512], BF16)
    if ("D" in phases) or ("E" in phases):
        c.mur = SB("mur_s", [128, 8], F32)
        c.muk = SB("muk_s", [128, 8], F32)
        c.omur = SB("omur_s", [128, 8], F32)
        c.omuk = SB("omuk_s", [128, 8], F32)
    c.psum = [PS(f"pbk{i}", [128, 512], F32) for i in range(8)]
    c.pbuf = [Buf() for _ in range(8)]

    sems = {e: stack.enter_context(nc.semaphore("sem_" + e)) for e in ENGS}
    c.dbuf = {nm: Buf() for nm in t}

    emit_static(c, phases)
    if "A" in phases:
        emit_phase_a(c)
        P.barrier()
        emit_phase_a2(c)
    if "B" in phases:
        P.barrier()
        emit_phase_b(c)
    if "C" in phases:
        toks = [(e, P.cnt[e]) for e in ENGS if P.cnt[e] > 0]
        P.barrier(skip=("gpsimd", "sync"))
        emit_phase_c(c, entry_toks=toks, with_d_ln=(("D" in phases) and not globals().get("NO_DLN_ILV", False)))
    if "D" in phases:
        P.barrier()
        emit_phase_d(c)
    if "E" in phases:
        P.barrier()
        emit_phase_e(c)

    final = P.cnt["sync"]

    def mk(e, extra_wait=None):
        def body(engobj):
            water = {pe: 0 for pe in ENGS}
            for item in P.items[e]:
                if item[0] == "DRAIN":
                    v = item[1]
                    if v > water[e]:
                        engobj.wait_ge(sems[e], v)
                        water[e] = v
                    continue
                fd, fn, inc = item
                for pe, v in sorted(fd.items()):
                    if v > water[pe]:
                        engobj.wait_ge(sems[pe], v)
                        water[pe] = v
                inst = fn(engobj)
                inst.then_inc(sems[e], inc)
            if extra_wait is not None:
                engobj.wait_ge(sems[e], extra_wait)
        return body

    with nc.Block() as block:
        block.sync(mk("sync", extra_wait=final))
        block.gpsimd(mk("gpsimd"))
        block.tensor(mk("tensor"))
        block.vector(mk("vector"))
        block.scalar(mk("scalar"))
    stack.close()
    return nc


_NC_CACHE = {}


def _get_nc(phases):
    if phases not in _NC_CACHE:
        _NC_CACHE[phases] = build_launch(phases)
    return _NC_CACHE[phases]


def kernel(**inputs):
    h = host_prep(inputs)
    x_all = np.asarray(inputs["L0_attn_x"]).reshape(-1, D)
    per_core = []
    for core in range(NCORES):
        m = {"x_fm": np.ascontiguousarray(h["x_fm"][core * BPC:(core + 1) * BPC]),
             "x_tm": np.ascontiguousarray(x_all[core * TOK:(core + 1) * TOK])}
        per_core.append(m)

    common_ab = {k: h[k] for k in ["gwT", "gbias", "wqT", "wkT", "wvT", "ident", "ecap"]}
    common_c = {k: h[k] for k in ["moeblob", "ident"]}
    common_de = {k: h[k] for k in ["qwT", "combT", "wrT", "wk2T", "wv2T", "mur", "muk", "ident"]}

    if SPLIT_MODE:
        nc1 = _get_nc("AB")
        maps1 = [dict(common_ab, x_fm=per_core[i]["x_fm"]) for i in range(NCORES)]
        r1 = run_bass_kernel_spmd(nc1, maps1, core_ids=list(range(NCORES))).results

        nc2 = _get_nc("C")
        maps2 = [dict(common_c, x_fm=per_core[i]["x_fm"], x_tm=per_core[i]["x_tm"],
                      xn_h=np.asarray(r1[i]["xn_h"]), rwT_h=np.asarray(r1[i]["rwT_h"]),
                      sdup_h=np.asarray(r1[i]["sdup_h"]), xn_tm_h=np.asarray(r1[i]["xn_tm_h"]),
                      idx_h=np.asarray(r1[i]["idx_h"]), rw8_h=np.asarray(r1[i]["rw8_h"]))
                 for i in range(NCORES)]
        r2 = run_bass_kernel_spmd(nc2, maps2, core_ids=list(range(NCORES))).results

        nc3 = _get_nc("DE")
        maps3 = [dict(common_de, x1_h=r2[i]["x1_h"]) for i in range(NCORES)]
        r3 = run_bass_kernel_spmd(nc3, maps3, core_ids=list(range(NCORES))).results
        outs = [r3[i]["out_fm"] for i in range(NCORES)]
    else:
        nc = _get_nc("ABCDE")
        maps = [dict(common_ab) for _ in range(NCORES)]
        for i in range(NCORES):
            maps[i].update(common_c)
            maps[i].update(common_de)
            maps[i]["x_fm"] = per_core[i]["x_fm"]
            maps[i]["x_tm"] = per_core[i]["x_tm"]
        rr = run_bass_kernel_spmd(nc, maps, core_ids=list(range(NCORES))).results
        outs = [rr[i]["out_fm"] for i in range(NCORES)]

    out = np.concatenate([o.reshape(BPC, D, S) for o in outs], axis=0)
    return np.ascontiguousarray(out.transpose(0, 2, 1)).astype(np.float32)
